# version 02
# baseline (speedup 1.0000x reference)
import contextlib
import numpy as np
import ml_dtypes

import concourse.bass as bass
import concourse.mybir as mybir
from concourse.bass_utils import run_bass_kernel_spmd

F32 = mybir.dt.float32
BF16 = mybir.dt.bfloat16
AF = mybir.ActivationFunctionType
ALU = mybir.AluOpType
AXX = mybir.AxisListType.X

NCORE = 8
BT = 2
S = 2048
DM = 1024
T = BT * S
NT = T // 128
EPS_LN = 1e-5

BF = ml_dtypes.bfloat16



def host_constants():
    c = {}
    c["c_ident"] = np.eye(128, dtype=BF)
    c["c_identf"] = np.eye(128, dtype=np.float32)
    kp = np.arange(128)[:, None]
    ql = np.arange(512)[None, :]
    caus = np.concatenate([(kp + 128 * dt <= ql).astype(BF) for dt in range(4)], axis=1)
    c["c_caus"] = caus
    c["c_ones"] = np.ones((1, 512), dtype=BF)
    pos = np.arange(S, dtype=np.float64)
    inv = 1.0 / (10000.0 ** (np.arange(0, 64, 2, dtype=np.float64) / 64.0))
    ang = pos[:, None] * inv[None, :]
    cosv = np.cos(ang)
    sinv = np.sin(ang)
    cos_full = np.concatenate([cosv, cosv], axis=1)
    sin_full = np.concatenate([-sinv, sinv], axis=1)
    c["c_ropecos"] = np.tile(cos_full, (1, 16)).astype(np.float32)
    c["c_ropesin"] = np.tile(sin_full, (1, 16)).astype(np.float32)
    tau = np.arange(128)[:, None]
    tt = np.arange(128)[None, :]
    c["c_causincl"] = (tau <= tt).astype(BF)
    bd = np.zeros((128, 128), dtype=BF)
    for j in range(8):
        bd[16 * j:16 * j + 16, 16 * j:16 * j + 16] = 1
    c["c_blockdiag"] = bd
    return c


CONST_SPECS = [
    ("c_ident", (128, 128), BF16),
    ("c_identf", (128, 128), F32),
    ("c_caus", (128, 2048), BF16),
    ("c_ones", (1, 512), BF16),
    ("c_causincl", (128, 128), BF16),
    ("c_blockdiag", (128, 128), BF16),
]
DRAM_CONST_SPECS = [
    ("c_ropecos", (S, DM), F32),
    ("c_ropesin", (S, DM), F32),
]



class Sem:
    __slots__ = ("h", "v")

    def __init__(self, h):
        self.h = h
        self.v = 0


class Ctx:
    def __init__(self):
        self.nc = bass.Bass("TRN2", target_bir_lowering=False)
        self.es = contextlib.ExitStack()
        self.sems = {}
        self.consts = {}
        self._uid = 0

    def uid(self):
        self._uid += 1
        return self._uid

    def sem(self, name):
        if name not in self.sems:
            self.sems[name] = Sem(self.es.enter_context(self.nc.semaphore(name)))
        return self.sems[name]

    def pool(self, k):
        return [self.sem(f"p{i}") for i in range(k)]

    def gpool(self, k):
        return [self.sem(f"g{i}") for i in range(k)]

    def sb(self, name, shape, dtype, es=None):
        return (es or self.es).enter_context(self.nc.sbuf_tensor(name, shape, dtype))

    def ps_(self, name, shape, dtype=F32, es=None):
        return (es or self.es).enter_context(self.nc.psum_tensor(name, shape, dtype))

    def ps(self, name, shape, dtype=F32):
        return self.es.enter_context(self.nc.psum_tensor(name, shape, dtype))

    def dram(self, name, shape, dtype, kind="Internal"):
        return self.nc.dram_tensor(name, list(shape), dtype, kind=kind).ap()

    def barrier(self):
        self.nc.all_engine_barrier()

    def close(self):
        self.es.close()

    def load_consts(self):
        ld = self.sem("c_ld")
        aps = {}
        for name, shape, dtype in CONST_SPECS:
            self.consts[name] = self.sb("sb_" + name, list(shape), dtype)
            aps[name] = self.dram(name, shape, dtype, kind="ExternalInput")
        self.dram_consts = {}
        for name, shape, dtype in DRAM_CONST_SPECS:
            self.dram_consts[name] = self.dram(name, shape, dtype, kind="ExternalInput")
        with self.nc.Block() as block:
            @block.sync
            def _(sy):
                for name, shape, dtype in CONST_SPECS:
                    sy.dma_start(out=self.consts[name][:], in_=aps[name]).then_inc(ld.h, 16)
                    ld.v += 16
                sy.wait_ge(ld.h, ld.v)
        self.barrier()


def fm_slice(tensor, C, f0, nf, c0, ncol):
    ti = f0 // 128
    p0 = f0 % 128
    assert p0 + nf <= 128, (f0, nf)
    return tensor[p0:p0 + nf, ti * C + c0: ti * C + c0 + ncol]



def emit_weight_fm(ctx, dst, w_dram, NF, C, transpose):
    nc = ctx.nc
    u = ctx.uid()
    mm, ev = ctx.pool(2)
    ldp = ctx.gpool(2)
    ld = ldp[0]
    les = contextlib.ExitStack()
    if not transpose:
        nrt = NF // 128
        with nc.Block() as block:
            @block.gpsimd
            def _(gp):
                for r in range(nrt):
                    gp.dma_start(out=fm_slice(dst, C, r * 128, 128, 0, C),
                                 in_=w_dram[r * 128:(r + 1) * 128, :]).then_inc(ld.h, 16)
                gp.wait_ge(ld.h, ld.v + nrt * 16)
        ld.v += nrt * 16
    else:
        stg = [ctx.sb(f"wstg{u}_{i}", [128, NF], BF16, es=les) for i in range(2)]
        ptr = [ctx.ps_(f"wps{u}_{i}", [128, 512], BF16, es=les) for i in range(2)]
        nrow_t = C // 128
        ncol_t = NF // 128
        ident = ctx.consts["c_ident"]
        ld0 = [ldp[0].v, ldp[1].v]
        mm0, ev0 = mm.v, ev.v
        with nc.Block() as block:
            @block.gpsimd
            def _(gp):
                for r in range(nrow_t):
                    if r >= 2:
                        gp.wait_ge(mm.h, mm0 + (r - 1) * ncol_t)
                    gp.dma_start(out=stg[r % 2][:, :],
                                 in_=w_dram[r * 128:(r + 1) * 128, :]).then_inc(ldp[r % 2].h, 16)

            @block.tensor
            def _(t):
                for r in range(nrow_t):
                    t.wait_ge(ldp[r % 2].h, ld0[r % 2] + (r // 2 + 1) * 16)
                    for cb in range(ncol_t):
                        i = r * ncol_t + cb
                        if i >= 2:
                            t.wait_ge(ev.h, ev0 + i - 1)
                        t.transpose(ptr[i % 2][:, 0:128],
                                    stg[r % 2][:, cb * 128:(cb + 1) * 128],
                                    ident[:]).then_inc(mm.h, 1)

            @block.vector
            def _(v):
                for r in range(nrow_t):
                    for cb in range(ncol_t):
                        i = r * ncol_t + cb
                        v.wait_ge(mm.h, mm0 + i + 1)
                        v.tensor_copy(fm_slice(dst, C, cb * 128, 128, r * 128, 128),
                                      ptr[i % 2][:, 0:128]).then_inc(ev.h, 1)
        ldp[0].v = ld0[0] + ((nrow_t + 1) // 2) * 16
        ldp[1].v = ld0[1] + (nrow_t // 2) * 16
        mm.v = mm0 + nrow_t * ncol_t
        ev.v = ev0 + nrow_t * ncol_t
    ctx.barrier()
    les.close()


def emit_replicate_rows(ctx, dst, src_dram_rows, n):
    nc = ctx.nc
    u = ctx.uid()
    mm, ev = ctx.pool(2)
    ld = ctx.gpool(1)[0]
    les = contextlib.ExitStack()
    row = ctx.sb(f"rrow{u}", [1, n], BF16, es=les)
    pr = ctx.ps_(f"rps{u}", [128, 512], F32, es=les)
    ones = ctx.consts["c_ones"]
    nchunk = (n + 511) // 512
    src = src_dram_rows
    if len(src.shape) == 1:
        src = src.rearrange("(o n) -> o n", o=1)
    with nc.Block() as block:
        @block.gpsimd
        def _(gp):
            gp.dma_start(out=row[:], in_=src).then_inc(ld.h, 16)

        @block.tensor
        def _(t):
            t.wait_ge(ld.h, ld.v + 16)
            for cidx in range(nchunk):
                w = min(512, n - cidx * 512)
                if cidx >= 1:
                    t.wait_ge(ev.h, ev.v + cidx)
                t.matmul(pr[:, 0:w], ones[:, 0:128], row[:, cidx * 512:cidx * 512 + w],
                         start=True, stop=True).then_inc(mm.h, 1)

        @block.vector
        def _(v):
            for cidx in range(nchunk):
                w = min(512, n - cidx * 512)
                v.wait_ge(mm.h, mm.v + cidx + 1)
                v.tensor_copy(dst[:, cidx * 512:cidx * 512 + w], pr[:, 0:w]).then_inc(ev.h, 1)
    ld.v += 16
    mm.v += nchunk
    ev.v += nchunk
    ctx.barrier()
    les.close()


def emit_norm(ctx, x_dram, xnt_dram, xnf_dram=None, grep=None, brep=None,
              n_tiles=NT, x_is_input=True):
    nc = ctx.nc
    u = ctx.uid()
    _p = ctx.pool(14)
    ld, mm, ev, st = _p[0], _p[1], _p[2], _p[3]
    ee = _p[4:14]
    G = 4
    NG = n_tiles // G
    ident = ctx.consts["c_ident"]
    affine = grep is not None

    les = contextlib.ExitStack()
    xt = [ctx.sb(f"nxt{u}_{j}", [128, DM], F32, es=les) for j in range(G)]
    xc = [ctx.sb(f"nxc{u}_{j}", [128, DM], F32, es=les) for j in range(G)]
    xnbf = [ctx.sb(f"nxb{u}_{j}", [128, DM], BF16, es=les) for j in range(G)]
    xnst = [ctx.sb(f"nxs{u}_{j}", [128, DM], BF16, es=les) for j in range(G)]
    xaf = [ctx.sb(f"nxa{u}_{j}", [128, DM], F32, es=les) for j in range(G)] if (affine or xnf_dram is not None) else None
    sv = ctx.sb(f"nsv{u}", [128, G], F32, es=les)
    nmean = ctx.sb(f"nnm{u}", [128, G], F32, es=les)
    v2 = ctx.sb(f"nv2{u}", [128, G], F32, es=les)
    vart = ctx.sb(f"nvt{u}", [128, G], F32, es=les)
    sdv = ctx.sb(f"nsd{u}", [128, G], F32, es=les)
    rstd = ctx.sb(f"nrs{u}", [128, G], F32, es=les)
    ptr = [ctx.ps_(f"nps{u}_{j}", [128, DM], BF16, es=les) for j in range(G)]

    b = {id(s): s.v for s in [ld, mm, ev, st] + ee}

    def bv(s):
        return b[id(s)]

    with nc.Block() as block:
        @block.sync
        def _(sy):
            for g in range(NG):
                if g >= 1:
                    sy.wait_ge(st.h, bv(st) + g * G * (32 if xnf_dram is not None else 16))
                for j in range(G):
                    i = g * G + j
                    sy.dma_start(out=xt[j][:], in_=x_dram[i * 128:(i + 1) * 128, :]).then_inc(ld.h, 16)
                for j in range(G):
                    i = g * G + j
                    sy.wait_ge(ev.h, bv(ev) + i + 1)
                    dst = xnt_dram.rearrange("(c p) t -> p c t", p=128)
                    sy.dma_start(out=dst[:, :, i * 128:(i + 1) * 128],
                                 in_=xnst[j][:].rearrange("p (c q) -> p c q", c=8)).then_inc(st.h, 16)
                    if xnf_dram is not None:
                        sy.wait_ge(ee[9].h, bv(ee[9]) + g + 1)
                        sy.dma_start(out=xnf_dram[i * 128:(i + 1) * 128, :],
                                     in_=xaf[j][:]).then_inc(st.h, 16)
            sy.wait_ge(st.h, bv(st) + NG * G * (32 if xnf_dram is not None else 16))

        @block.vector
        def _(v):
            for g in range(NG):
                v.wait_ge(ld.h, bv(ld) + (g + 1) * G * 16)
                for j in range(G):
                    ins = v.reduce_sum(sv[:, j:j + 1], xt[j][:], axis=AXX)
                ins.then_inc(ee[1].h, 1)
                v.wait_ge(ee[4].h, bv(ee[4]) + g + 1)
                for j in range(G):
                    ins = v.tensor_scalar(vart[:, j:j + 1], v2[:, j:j + 1], 1.0 / DM, EPS_LN,
                                          ALU.mult, ALU.add)
                ins.then_inc(ee[5].h, 1)
                v.wait_ge(ee[6].h, bv(ee[6]) + g + 1)
                for j in range(G):
                    ins = v.reciprocal(rstd[:, j:j + 1], sdv[:, j:j + 1])
                ins.then_inc(ee[7].h, 1)
                if affine or xnf_dram is not None:
                    v.wait_ge(ee[7].h, bv(ee[7]) + g + 1)
                    for j in range(G):
                        if affine:
                            ins = v.scalar_tensor_tensor(xaf[j][:], xc[j][:], rstd[:, j:j + 1],
                                                         grep[:], ALU.mult, ALU.mult)
                        else:
                            ins = v.tensor_scalar_mul(xaf[j][:], xc[j][:], rstd[:, j:j + 1])
                    ins.then_inc(ee[8].h, 1)
                    if affine:
                        v.wait_ge(ee[8].h, bv(ee[8]) + g + 1)
                        for j in range(G):
                            ins = v.tensor_add(xaf[j][:], xaf[j][:], brep[:])
                        ins.then_inc(ee[9].h, 1)
                    else:
                        v.wait_ge(ee[8].h, bv(ee[8]) + g + 1)
                        v.engine_nop().then_inc(ee[9].h, 1)
                for j in range(G):
                    i = g * G + j
                    v.wait_ge(mm.h, bv(mm) + (i + 1) * 8)
                    v.tensor_copy(xnst[j][:], ptr[j][:]).then_inc(ev.h, 1)

        @block.scalar
        def _(s):
            for g in range(NG):
                s.wait_ge(ee[1].h, bv(ee[1]) + g + 1)
                for j in range(G):
                    ins = s.mul(nmean[:, j:j + 1], sv[:, j:j + 1], -1.0 / DM)
                ins.then_inc(ee[2].h, 1)
                s.wait_ge(ee[2].h, bv(ee[2]) + g + 1)
                for j in range(G):
                    ins = s.activation(xc[j][:], xt[j][:], AF.Identity, bias=nmean[:, j:j + 1])
                ins.then_inc(ee[3].h, 1)
                s.wait_ge(ee[3].h, bv(ee[3]) + g + 1)
                for j in range(G):
                    ins = s.activation(xt[j][:], xc[j][:], AF.Square, accum_out=v2[:, j:j + 1])
                ins.then_inc(ee[4].h, 1)
                s.wait_ge(ee[5].h, bv(ee[5]) + g + 1)
                for j in range(G):
                    ins = s.activation(sdv[:, j:j + 1], vart[:, j:j + 1], AF.Sqrt)
                ins.then_inc(ee[6].h, 1)
                if not affine:
                    s.wait_ge(ee[7].h, bv(ee[7]) + g + 1)
                    for j in range(G):
                        ins = s.activation(xnbf[j][:], xc[j][:], AF.Identity, scale=rstd[:, j:j + 1])
                else:
                    s.wait_ge(ee[9].h, bv(ee[9]) + g + 1)
                    for j in range(G):
                        ins = s.copy(xnbf[j][:], xaf[j][:])
                ins.then_inc(ee[0].h, 1)

        @block.tensor
        def _(t):
            for g in range(NG):
                t.wait_ge(ee[0].h, bv(ee[0]) + g + 1)
                for j in range(G):
                    for cb in range(8):
                        t.transpose(ptr[j][:, cb * 128:(cb + 1) * 128],
                                    xnbf[j][:, cb * 128:(cb + 1) * 128],
                                    ident[:]).then_inc(mm.h, 1)

    ld.v += NG * G * 16
    for k in range(10):
        if k == 9 and not (affine or xnf_dram is not None):
            continue
        if k == 8 and not (affine or xnf_dram is not None):
            continue
        ee[k].v += NG
    mm.v += NG * G * 8
    ev.v += NG * G
    st.v += NG * G * (32 if xnf_dram is not None else 16)
    ctx.barrier()
    les.close()



def emit_l0_qkv(ctx, xnt_dram, b, WQT, KT, VA, k_off=1024, v_off=2048, WC=3072,
                parts=3, wv_spec=None, bk_pp=None, bv_rep=None):
    nc = ctx.nc
    u = ctx.uid()
    les = contextlib.ExitStack()
    ld, km, ke, vm, ve, ms = ctx.pool(6)
    if wv_spec is None:
        wv_spec = (WQT, WC, v_off)
    wv_t, wv_C, wv_off = wv_spec
    xsl = [[ctx.sb(f"qkx{u}_{s}_{i}", [128, 512], BF16, es=les) for i in range(8)]
           for s in range(2)]
    ps01 = [ctx.ps_(f"qkp{u}_{i}", [128, 512], F32, es=les) for i in range(2)]
    pv2 = [ctx.ps_(f"qkv{u}_{i}", [128, 1024], F32, es=les) for i in range(2)]

    b_ld, b_km, b_ke, b_vm, b_ve = ld.v, km.v, ke.v, vm.v, ve.v
    with nc.Block() as block:
        @block.sync
        def _(sy):
            for tb in range(4):
                if tb >= 1:
                    sy.wait_ge(ld.h, b_ld + tb * 128)
                if tb >= 2:
                    if parts & 1:
                        sy.wait_ge(km.h, b_km + (tb - 1) * 8)
                    if parts & 2:
                        sy.wait_ge(vm.h, b_vm + (tb - 1) * 8)
                for it in range(8):
                    sy.dma_start(out=xsl[tb % 2][it][:],
                                 in_=xnt_dram[it * 128:(it + 1) * 128,
                                              b * S + tb * 512: b * S + (tb + 1) * 512]
                                 ).then_inc(ld.h, 16)

        @block.vector
        def _(v):
            v.memset(VA[:], 1.0).then_inc(ms.h, 1)
            for tb in range(4):
                if (parts & 1) and bk_pp is None:
                    for ot in range(8):
                        i = tb * 8 + ot
                        v.wait_ge(km.h, b_km + i + 1)
                        v.tensor_copy(fm_slice(KT, 2048, ot * 128, 128, tb * 512, 512),
                                      ps01[ot % 2][:]).then_inc(ke.h, 1)
                if parts & 2:
                    for tt4 in range(4):
                        tt = tb * 4 + tt4
                        for half in range(2):
                            i = (tt * 2 + half)
                            v.wait_ge(vm.h, b_vm + i + 1)
                            dst = VA[:, tt * 1040 + half * 520: tt * 1040 + (half + 1) * 520]
                            dst = dst.rearrange("p (h j) -> p h j", j=65)[:, :, 0:64]
                            srcp = pv2[tt4 % 2][:, half * 512:(half + 1) * 512]
                            srcp = srcp.rearrange("p (h j) -> p h j", j=64)
                            if bv_rep is not None:
                                bvr = bv_rep[:, half * 512:(half + 1) * 512]
                                bvr = bvr.rearrange("p (h j) -> p h j", j=64)
                                v.tensor_add(dst, srcp, bvr).then_inc(ve.h, 1)
                            else:
                                v.tensor_copy(dst, srcp).then_inc(ve.h, 1)

        @block.scalar
        def _(s):
            if (parts & 1) and bk_pp is not None:
                for tb in range(4):
                    for ot in range(8):
                        i = tb * 8 + ot
                        s.wait_ge(km.h, b_km + i + 1)
                        s.activation(fm_slice(KT, 2048, ot * 128, 128, tb * 512, 512),
                                     ps01[ot % 2][:], AF.Identity,
                                     bias=bk_pp[:, ot:ot + 1]).then_inc(ke.h, 1)

        @block.tensor
        def _(t):
            t.wait_ge(ms.h, ms.v + 1)
            for tb in range(4):
                t.wait_ge(ld.h, b_ld + (tb + 1) * 8 * 16)
                if parts & 1:
                    for ot in range(8):
                        i = tb * 8 + ot
                        if i >= 2:
                            t.wait_ge(ke.h, b_ke + i - 1)
                        for it in range(8):
                            ins = t.matmul(ps01[ot % 2][:],
                                           fm_slice(WQT, WC, it * 128, 128, k_off + ot * 128, 128),
                                           xsl[tb % 2][it][:],
                                           start=(it == 0), stop=(it == 7))
                        ins.then_inc(km.h, 1)
                if parts & 2:
                    for tt4 in range(4):
                        tt = tb * 4 + tt4
                        if tt >= 2:
                            t.wait_ge(ve.h, b_ve + (tt - 1) * 2)
                        for half in range(2):
                            for it in range(8):
                                ins = t.matmul(pv2[tt4 % 2][:, half * 512:(half + 1) * 512],
                                               xsl[tb % 2][it][:, tt4 * 128:(tt4 + 1) * 128],
                                               fm_slice(wv_t, wv_C, it * 128, 128, wv_off + half * 512, 512),
                                               start=(it == 0), stop=(it == 7))
                            ins.then_inc(vm.h, 1)

    ld.v = b_ld + 4 * 8 * 16
    if parts & 1:
        km.v = b_km + 32
        ke.v = b_ke + 32
    if parts & 2:
        vm.v = b_vm + 32
        ve.v = b_ve + 32
    ms.v += 1
    ctx.barrier()
    les.close()


def emit_l0_attn_core(ctx, xnt_dram, b, WQT, KT, VA, WPT, acc_dram,
                      q_off=0, WC=3072, causal=True, scale=0.125,
                      bq_pp=None, qt_full=None, bo_rep=None, addin_dram=None):
    nc = ctx.nc
    u = ctx.uid()
    les = contextlib.ExitStack()
    _names = ["xl", "qmm", "qev", "smm", "ex", "mk", "amm", "rc", "rmm", "cpv",
              "muv", "pmm", "pev", "pst", "lx", "e1"]
    sems = dict(zip(_names, ctx.pool(16)))
    B0 = {n: s.v for n, s in sems.items()}

    xq = [[ctx.sb(f"atx{u}_{s}_{i}", [128, 512], BF16, es=les) for i in range(8)]
          for s in range(2)]
    QT = ctx.sb(f"atqt{u}", [128, 8 * 512], BF16, es=les)
    OTB = ctx.sb(f"atot{u}", [128, 8 * 512], BF16, es=les)
    ptile = [ctx.sb(f"atp{u}_{i}", [128, 512], BF16, es=les) for i in range(2)]
    rden = ctx.sb(f"atr{u}", [1, 512], BF16, es=les)
    prepS = ctx.sb(f"atps{u}", [64, 512], BF16, es=les)
    accst = [ctx.sb(f"atac{u}_{i}", [128, DM], F32, es=les) for i in range(2)]
    axd = [ctx.sb(f"atax{u}_{i}", [128, DM], F32, es=les) for i in range(2)] if addin_dram is not None else None
    ps01 = [ctx.ps_(f"atps01{u}_{i}", [128, 512], F32, es=les) for i in range(2)]
    pav = [ctx.ps_(f"atpav{u}_{i}", [128, 512], F32, es=les) for i in range(2)]
    prep = ctx.ps_(f"atprep{u}", [128, 512], F32, es=les)
    pproj = ctx.ps_(f"atproj{u}", [128, 1024], F32, es=les)
    caus = ctx.consts["c_caus"]
    ones = ctx.consts["c_ones"]

    NH = 16
    kmax = [(4 * (qb + 1)) if causal else 16 for qb in range(4)]

    qmm_after = {}
    qev_after = {}
    smm_at = {}
    ex_at = {}
    mk_at = {}
    amm_at = {}
    rc_at = {}
    rmm_at = {}
    cpv_at = {}
    muv_at = {}
    pmm_at = {}
    pev_at = {}
    pst_at = {}
    nq = B0["qmm"]; nqe = B0["qev"]; ns = B0["smm"]; ne = B0["ex"]
    nm = B0["mk"]; na = B0["amm"]; nr = B0["rc"]; nrm = B0["rmm"]
    ncp = B0["cpv"]; nmu = B0["muv"]; npm = B0["pmm"]; npe = B0["pev"]; nps = B0["pst"]
    for qb in range(4):
        nq += 8
        qmm_after[qb] = nq
        nqe += 8
        qev_after[qb] = nqe
        for h in range(NH):
            for kt in range(kmax[qb]):
                ns += 1; smm_at[(qb, h, kt)] = ns
                ne += 1; ex_at[(qb, h, kt)] = ne
                if causal and kt >= 4 * qb:
                    nm += 1; mk_at[(qb, h, kt)] = nm
                na += 1; amm_at[(qb, h, kt)] = na
            nr += 1; rc_at[(qb, h)] = nr
            nrm += 1; rmm_at[(qb, h)] = nrm
            ncp += 1; cpv_at[(qb, h)] = ncp
            nmu += 1; muv_at[(qb, h)] = nmu
        for tt4 in range(4):
            npm += 1
            pmm_at[(qb, tt4)] = npm
            npe += 1; pev_at[(qb, tt4)] = npe
            nps += 16; pst_at[(qb, tt4)] = nps

    def diag(qb, kt):
        return causal and kt >= 4 * qb

    with nc.Block() as block:
        @block.sync
        def _(sy):
            for qb in range(4):
                if qt_full is None:
                    if qb >= 2:
                        sy.wait_ge(sems["qmm"].h, qmm_after[qb - 2])
                    for it in range(8):
                        sy.dma_start(out=xq[qb % 2][it][:],
                                     in_=xnt_dram[it * 128:(it + 1) * 128,
                                                  b * S + qb * 512: b * S + (qb + 1) * 512]
                                     ).then_inc(sems["xl"].h, 16)
                for tt4 in range(4):
                    g = b * S + qb * 512 + tt4 * 128
                    if addin_dram is not None:
                        i = qb * 4 + tt4
                        if i >= 2:
                            sy.wait_ge(sems["e1"].h, B0["e1"] + (i - 2) + 1)
                        sy.dma_start(out=axd[tt4 % 2][:],
                                     in_=addin_dram[g:g + 128, :]).then_inc(sems["lx"].h, 16)
                    sy.wait_ge(sems["pev"].h, pev_at[(qb, tt4)])
                    sy.dma_start(out=acc_dram[g:g + 128, :],
                                 in_=accst[tt4 % 2][:]).then_inc(sems["pst"].h, 16)
                    sy.wait_ge(sems["pst"].h, pst_at[(qb, tt4)])

        @block.tensor
        def _(t):
            for qb in range(4):
                if qt_full is None:
                    t.wait_ge(sems["xl"].h, B0["xl"] + (qb + 1) * 8 * 16)
                    if qb >= 1:
                        t.wait_ge(sems["ex"].h, ex_at[(qb - 1, NH - 1, kmax[qb - 1] - 1)])
                    for ot in range(8):
                        i = qb * 8 + ot
                        if i >= 2:
                            t.wait_ge(sems["qev"].h, B0["qev"] + i - 1)
                        for it in range(8):
                            ins = t.matmul(ps01[ot % 2][:],
                                           fm_slice(WQT, WC, it * 128, 128, q_off + ot * 128, 128),
                                           xq[qb % 2][it][:],
                                           start=(it == 0), stop=(it == 7))
                        ins.then_inc(sems["qmm"].h, 1)
                elif qb >= 1:
                    t.wait_ge(sems["ex"].h, ex_at[(qb - 1, NH - 1, kmax[qb - 1] - 1)])
                for h in range(NH):
                    if qt_full is None:
                        t.wait_ge(sems["qev"].h, qev_after[qb])
                    if (qb, h) >= (0, 2):
                        hh = h - 2 if h >= 2 else NH - 2 + h
                        qq = qb if h >= 2 else qb - 1
                        t.wait_ge(sems["muv"].h, muv_at[(qq, hh)])
                    for kt in range(kmax[qb] + 1):
                        if kt < kmax[qb]:
                            if kt >= 2:
                                t.wait_ge(sems["ex"].h, ex_at[(qb, h, kt - 2)])
                            elif h >= 1:
                                t.wait_ge(sems["ex"].h, ex_at[(qb, h - 1, kmax[qb] - 1 - (1 - kt))])
                            qsrc = (fm_slice(QT, 512, 64 * h, 64, 0, 512)
                                    if qt_full is None else
                                    fm_slice(qt_full, S, 64 * h, 64, qb * 512, 512))
                            t.matmul(ps01[kt % 2][:],
                                     fm_slice(KT, 2048, 64 * h, 64, kt * 128, 128),
                                     qsrc,
                                     start=True, stop=True).then_inc(sems["smm"].h, 1)
                        if kt >= 1:
                            k2 = kt - 1
                            if diag(qb, k2):
                                t.wait_ge(sems["mk"].h, mk_at[(qb, h, k2)])
                            else:
                                t.wait_ge(sems["ex"].h, ex_at[(qb, h, k2)])
                            t.matmul(pav[h % 2][0:65, :],
                                     VA[:, k2 * 1040 + 65 * h: k2 * 1040 + 65 * h + 65],
                                     ptile[k2 % 2][:],
                                     start=(k2 == 0), stop=(k2 == kmax[qb] - 1)
                                     ).then_inc(sems["amm"].h, 1)
                    t.wait_ge(sems["rc"].h, rc_at[(qb, h)])
                    if (qb, h) > (0, 0):
                        prevh = (qb, h - 1) if h >= 1 else (qb - 1, NH - 1)
                        t.wait_ge(sems["cpv"].h, cpv_at[prevh])
                    t.matmul(prep[0:64, :], ones[0:1, 0:64], rden[:],
                             start=True, stop=True).then_inc(sems["rmm"].h, 1)
                for tt4 in range(4):
                    t.wait_ge(sems["muv"].h, muv_at[(qb, NH - 1)])
                    if (qb, tt4) > (0, 0):
                        prev = (qb, tt4 - 1) if tt4 >= 1 else (qb - 1, 3)
                        t.wait_ge(sems["pev"].h, pev_at[prev])
                    for dt in range(8):
                        for half in range(2):
                            ins = t.matmul(pproj[:, half * 512:(half + 1) * 512],
                                           fm_slice(OTB, 512, dt * 128, 128, tt4 * 128, 128),
                                           fm_slice(WPT, 1024, dt * 128, 128, half * 512, 512),
                                           start=(dt == 0), stop=(dt == 7))
                    ins.then_inc(sems["pmm"].h, 1)

        @block.scalar
        def _(s):
            for qb in range(4):
                if qt_full is None:
                    for ot in range(8):
                        i = qb * 8 + ot
                        s.wait_ge(sems["qmm"].h, B0["qmm"] + i + 1)
                        if qb >= 1:
                            s.wait_ge(sems["smm"].h, smm_at[(qb - 1, NH - 1, kmax[qb - 1] - 1)])
                        if bq_pp is not None:
                            ins = s.activation(fm_slice(QT, 512, ot * 128, 128, 0, 512),
                                               ps01[ot % 2][:], AF.Identity,
                                               bias=bq_pp[:, ot:ot + 1])
                        else:
                            ins = s.activation(fm_slice(QT, 512, ot * 128, 128, 0, 512),
                                               ps01[ot % 2][:], AF.Identity)
                        ins.then_inc(sems["qev"].h, 1)
                for h in range(NH):
                    for kt in range(kmax[qb]):
                        s.wait_ge(sems["smm"].h, smm_at[(qb, h, kt)])
                        if (qb, h, kt) >= (0, 0, 2):
                            if kt >= 2:
                                s.wait_ge(sems["amm"].h, amm_at[(qb, h, kt - 2)])
                            elif (qb, h) > (0, 0):
                                prevh = (qb, h - 1) if h >= 1 else (qb - 1, NH - 1)
                                s.wait_ge(sems["amm"].h,
                                          amm_at[(prevh[0], prevh[1],
                                                  kmax[prevh[0]] - (2 - kt))])
                        s.activation(ptile[kt % 2][:], ps01[kt % 2][:], AF.Exp,
                                     scale=scale).then_inc(sems["ex"].h, 1)

        @block.vector
        def _(v):
            for qb in range(4):
                for h in range(NH):
                    if causal:
                        for kt in range(4 * qb, kmax[qb]):
                            dt = kt - 4 * qb
                            v.wait_ge(sems["ex"].h, ex_at[(qb, h, kt)])
                            v.tensor_mul(ptile[kt % 2][:], ptile[kt % 2][:],
                                         caus[:, dt * 512:(dt + 1) * 512]
                                         ).then_inc(sems["mk"].h, 1)
                    v.wait_ge(sems["amm"].h, amm_at[(qb, h, kmax[qb] - 1)])
                    if (qb, h) > (0, 0):
                        prevh = (qb, h - 1) if h >= 1 else (qb - 1, NH - 1)
                        v.wait_ge(sems["rmm"].h, rmm_at[prevh])
                    with nc.allow_low_precision(reason="1/denominator in bf16 == cast(fp32 recip)"):
                        v.reciprocal(rden[:], pav[h % 2][64:65, :]).then_inc(sems["rc"].h, 1)
                    v.wait_ge(sems["rmm"].h, rmm_at[(qb, h)])
                    v.tensor_copy(prepS[:], prep[0:64, :]).then_inc(sems["cpv"].h, 1)
                    v.wait_ge(sems["cpv"].h, cpv_at[(qb, h)])
                    if qb >= 1:
                        v.wait_ge(sems["pmm"].h, pmm_at[(qb - 1, 3)])
                    v.tensor_mul(fm_slice(OTB, 512, 64 * h, 64, 0, 512),
                                 pav[h % 2][0:64, :], prepS[:]
                                 ).then_inc(sems["muv"].h, 1)
                for tt4 in range(4):
                    v.wait_ge(sems["pmm"].h, pmm_at[(qb, tt4)])
                    i = qb * 4 + tt4
                    if i >= 2:
                        v.wait_ge(sems["pst"].h, B0["pst"] + (i - 1) * 16)
                    if bo_rep is not None and addin_dram is not None:
                        v.tensor_add(accst[tt4 % 2][:], pproj[:], bo_rep[:]).then_inc(sems["e1"].h, 1)
                        v.wait_ge(sems["e1"].h, B0["e1"] + i + 1)
                        v.wait_ge(sems["lx"].h, B0["lx"] + (i + 1) * 16)
                        v.tensor_add(accst[tt4 % 2][:], accst[tt4 % 2][:],
                                     axd[tt4 % 2][:]).then_inc(sems["pev"].h, 1)
                    elif bo_rep is not None:
                        v.tensor_add(accst[tt4 % 2][:], pproj[:], bo_rep[:]).then_inc(sems["pev"].h, 1)
                    else:
                        v.tensor_copy(accst[tt4 % 2][:], pproj[:]).then_inc(sems["pev"].h, 1)

    if qt_full is None:
        sems["xl"].v = B0["xl"] + 4 * 8 * 16
        sems["qmm"].v = nq
        sems["qev"].v = nqe
    if bo_rep is not None and addin_dram is not None:
        sems["e1"].v = B0["e1"] + 16
    if addin_dram is not None:
        sems["lx"].v = B0["lx"] + 16 * 16
    sems["smm"].v = ns
    sems["ex"].v = ne
    sems["mk"].v = nm
    sems["amm"].v = na
    sems["rc"].v = nr
    sems["rmm"].v = nrm
    sems["cpv"].v = ncp
    sems["muv"].v = nmu
    sems["pmm"].v = npm
    sems["pev"].v = npe
    sems["pst"].v = nps
    ctx.barrier()
    les.close()


def emit_colvec_to_rowrep(ctx, col_sbuf, ncols, scratch_dram, dst_rep, scale=1.0):
    nc = ctx.nc
    u = ctx.uid()
    les = contextlib.ExitStack()
    mm, ev, st = ctx.pool(3)
    pt = ctx.ps_(f"cr_p{u}", [128, 128], F32, es=les)
    row = ctx.sb(f"cr_r{u}", [ncols, 128], F32, es=les)
    identf = ctx.consts["c_identf"]
    with nc.Block() as block:
        @block.tensor
        def _(t):
            t.transpose(pt[0:ncols, :], col_sbuf[:, 0:ncols], identf[:]).then_inc(mm.h, 1)

        @block.vector
        def _(v):
            v.wait_ge(mm.h, mm.v + 1)
            if scale != 1.0:
                ins = v.tensor_scalar_mul(row[:], pt[0:ncols, :], float(scale))
            else:
                ins = v.tensor_copy(row[:], pt[0:ncols, :])
            ins.then_inc(ev.h, 1)

        @block.sync
        def _(sy):
            sy.wait_ge(ev.h, ev.v + 1)
            sy.dma_start(out=scratch_dram.rearrange("(a p) -> a p", p=128), in_=row[:]).then_inc(st.h, 16)
            sy.wait_ge(st.h, st.v + 16)
    mm.v += 1
    ev.v += 1
    st.v += 16
    ctx.barrier()
    les.close()
    emit_replicate_rows(ctx, dst_rep, scratch_dram, 128 * ncols)


def emit_load_pp(ctx, dst_pp, vec_dram, ncols):
    nc = ctx.nc
    u = ctx.uid()
    les = contextlib.ExitStack()
    ld, mm, ev = ctx.pool(3)
    stg = ctx.sb(f"pp_s{u}", [ncols, 128], F32, es=les)
    pt = ctx.ps_(f"pp_p{u}", [128, 128], F32, es=les)
    identf = ctx.consts["c_identf"]
    with nc.Block() as block:
        @block.sync
        def _(sy):
            sy.dma_start(out=stg[:], in_=vec_dram.rearrange("(a p) -> a p", p=128)).then_inc(ld.h, 16)

        @block.tensor
        def _(t):
            t.wait_ge(ld.h, ld.v + 16)
            t.transpose(pt[:, 0:ncols], stg[:], identf[0:ncols, 0:ncols]).then_inc(mm.h, 1)

        @block.vector
        def _(v):
            v.wait_ge(mm.h, mm.v + 1)
            v.tensor_copy(dst_pp[:, 0:ncols], pt[:, 0:ncols]).then_inc(ev.h, 1)
    ld.v += 16
    mm.v += 1
    ev.v += 1
    ctx.barrier()
    les.close()


def emit_binarize_prep(ctx, w_dram, nrows, ncol, a_col, les):
    nc = ctx.nc
    u = ctx.uid()
    lles = contextlib.ExitStack()
    rv, sv = ctx.pool(2)
    ldp = ctx.gpool(2)
    nrt = nrows // 128
    stg = [ctx.sb(f"bz_g{u}_{i}", [128, ncol], BF16, es=lles) for i in range(2)]
    b_ld = [ldp[0].v, ldp[1].v]
    b_rv, b_sv = rv.v, sv.v
    with nc.Block() as block:
        @block.gpsimd
        def _(gp):
            for r in range(nrt):
                if r >= 2:
                    gp.wait_ge(rv.h, b_rv + r - 1)
                gp.dma_start(out=stg[r % 2][:], in_=w_dram[r * 128:(r + 1) * 128, :]).then_inc(ldp[r % 2].h, 16)

        @block.vector
        def _(v):
            for r in range(nrt):
                v.wait_ge(ldp[r % 2].h, b_ld[r % 2] + (r // 2 + 1) * 16)
                v.reduce_sum(a_col[:, r:r + 1], stg[r % 2][:], axis=AXX,
                             apply_absolute_value=True).then_inc(rv.h, 1)
    ldp[0].v = b_ld[0] + ((nrt + 1) // 2) * 16
    ldp[1].v = b_ld[1] + (nrt // 2) * 16
    rv.v = b_rv + nrt
    ctx.barrier()
    lles.close()


def emit_sign_fm(ctx, dst_fm, w_dram, nrows, ncol, arep, arep_C):
    nc = ctx.nc
    u = ctx.uid()
    les = contextlib.ExitStack()
    sg, mm, ev = ctx.pool(3)
    ldp = ctx.gpool(2)
    nrt = nrows // 128
    nct = ncol // 128
    GRP = min(nct, 8)
    stg = [ctx.sb(f"sg_g{u}_{i}", [128, ncol], BF16, es=les) for i in range(2)]
    sgb = [ctx.sb(f"sg_b{u}_{i}", [128, ncol], BF16, es=les) for i in range(2)]
    pst = [ctx.ps_(f"sg_p{u}_{i}", [128, GRP * 128], BF16, es=les) for i in range(2)]
    ident = ctx.consts["c_ident"]
    ngrp_per_r = nct // GRP
    b_ld = [ldp[0].v, ldp[1].v]
    b_sg, b_mm, b_ev = sg.v, mm.v, ev.v
    with nc.Block() as block:
        @block.gpsimd
        def _(gp):
            for r in range(nrt):
                if r >= 2:
                    gp.wait_ge(sg.h, b_sg + r - 1)
                gp.dma_start(out=stg[r % 2][:], in_=w_dram[r * 128:(r + 1) * 128, :]).then_inc(ldp[r % 2].h, 16)

        @block.scalar
        def _(s):
            for r in range(nrt):
                s.wait_ge(ldp[r % 2].h, b_ld[r % 2] + (r // 2 + 1) * 16)
                if r >= 2:
                    s.wait_ge(mm.h, b_mm + (r - 1) * nct)
                s.activation(sgb[r % 2][:], stg[r % 2][:], AF.Sign).then_inc(sg.h, 1)

        @block.tensor
        def _(t):
            for r in range(nrt):
                t.wait_ge(sg.h, b_sg + r + 1)
                for g in range(ngrp_per_r):
                    gi = r * ngrp_per_r + g
                    if gi >= 2:
                        t.wait_ge(ev.h, b_ev + (gi - 1) * GRP)
                    for cb in range(GRP):
                        c = g * GRP + cb
                        t.transpose(pst[gi % 2][:, cb * 128:(cb + 1) * 128],
                                    sgb[r % 2][:, c * 128:(c + 1) * 128],
                                    ident[:]).then_inc(mm.h, 1)

        @block.vector
        def _(v):
            for r in range(nrt):
                for g in range(ngrp_per_r):
                    gi = r * ngrp_per_r + g
                    v.wait_ge(mm.h, b_mm + (gi + 1) * GRP)
                    for cb in range(GRP):
                        c = g * GRP + cb
                        v.tensor_mul(fm_slice(dst_fm, nrows, c * 128, 128, r * 128, 128),
                                     pst[gi % 2][:, cb * 128:(cb + 1) * 128],
                                     arep[:, r * 128:(r + 1) * 128]).then_inc(ev.h, 1)
    ldp[0].v = b_ld[0] + ((nrt + 1) // 2) * 16
    ldp[1].v = b_ld[1] + (nrt // 2) * 16
    sg.v = b_sg + nrt
    mm.v = b_mm + nrt * nct
    ev.v = b_ev + nrt * nct
    ctx.barrier()
    les.close()


def emit_l0_ffn_main(ctx, xnt_dram, x_dram, acc_dram, xout_dram,
                     S1T, S2T, b1pp, B2rep, NH1=4096):
    nc = ctx.nc
    u = ctx.uid()
    les = contextlib.ExitStack()
    _names = ["xl", "mmm", "ge", "ym", "e1", "e2", "e3", "lx", "stx"]
    sems = dict(zip(_names, ctx.pool(9)))
    B0 = {n: s.v for n, s in sems.items()}
    NF = NH1 // 128
    NTB = T // 256
    xsl = [[ctx.sb(f"ffx{u}_{s}_{i}", [128, 256], BF16, es=les) for i in range(8)]
           for s in range(2)]
    ghT = ctx.sb(f"ffg{u}", [128, NF * 256], BF16, es=les)
    eb = [ctx.sb(f"ffe{u}_{i}", [128, DM], F32, es=les) for i in range(2)]
    acct = [ctx.sb(f"ffa{u}_{i}", [128, DM], F32, es=les) for i in range(2)]
    xtl = [ctx.sb(f"fft{u}_{i}", [128, DM], F32, es=les) for i in range(2)]
    pm = [ctx.ps_(f"ffp{u}_{i}", [128, 256], F32, es=les) for i in range(2)]
    py = [ctx.ps_(f"ffy{u}_{i}", [128, 1024], F32, es=les) for i in range(2)]

    mmm_at = {}
    ge_at = {}
    ym_at = {}
    e3_at = {}
    n_m = B0["mmm"]; n_g = B0["ge"]; n_y = B0["ym"]; n_3 = B0["e3"]
    for tb in range(NTB):
        for f in range(NF):
            n_m += 1; mmm_at[(tb, f)] = n_m
            n_g += 1; ge_at[(tb, f)] = n_g
        for tt in range(2):
            n_y += 1; ym_at[(tb, tt)] = n_y
            n_3 += 1; e3_at[(tb, tt)] = n_3

    with nc.Block() as block:
        @block.sync
        def _(sy):
            for tb in range(NTB):
                if tb >= 2:
                    sy.wait_ge(sems["mmm"].h, mmm_at[(tb - 2, NF - 1)])
                for it in range(8):
                    sy.dma_start(out=xsl[tb % 2][it][:],
                                 in_=xnt_dram[it * 128:(it + 1) * 128,
                                              tb * 256:(tb + 1) * 256]).then_inc(sems["xl"].h, 16)
                for tt in range(2):
                    i = tb * 2 + tt
                    g = i * 128
                    if i >= 2:
                        sy.wait_ge(sems["e3"].h, B0["e3"] + (i - 2) + 1)
                    sy.dma_start(out=acct[tt][:], in_=acc_dram[g:g + 128, :]).then_inc(sems["lx"].h, 16)
                    sy.dma_start(out=xtl[tt][:], in_=x_dram[g:g + 128, :]).then_inc(sems["lx"].h, 16)
                for tt in range(2):
                    i = tb * 2 + tt
                    sy.wait_ge(sems["e3"].h, e3_at[(tb, tt)])
                    sy.dma_start(out=xout_dram[i * 128:(i + 1) * 128, :],
                                 in_=eb[tt][:]).then_inc(sems["stx"].h, 16)
                    sy.wait_ge(sems["stx"].h, B0["stx"] + (i + 1) * 16)

        @block.tensor
        def _(t):
            for tb in range(NTB):
                t.wait_ge(sems["xl"].h, B0["xl"] + (tb + 1) * 8 * 16)
                for f in range(NF):
                    if (tb, f) >= (0, 2):
                        if f >= 2:
                            t.wait_ge(sems["ge"].h, ge_at[(tb, f - 2)])
                        else:
                            t.wait_ge(sems["ge"].h, ge_at[(tb - 1, NF - 2 + f)])
                    for it in range(8):
                        ins = t.matmul(pm[f % 2][:],
                                       fm_slice(S1T, NH1, it * 128, 128, f * 128, 128),
                                       xsl[tb % 2][it][:],
                                       start=(it == 0), stop=(it == 7))
                    ins.then_inc(sems["mmm"].h, 1)
                for tt in range(2):
                    if tb >= 1:
                        t.wait_ge(sems["e1"].h, B0["e1"] + (tb - 1) * 2 + tt + 1)
                    for f in range(NF):
                        t.wait_ge(sems["ge"].h, ge_at[(tb, f)])
                        for half in range(2):
                            ins = t.matmul(py[tt][:, half * 512:(half + 1) * 512],
                                           fm_slice(ghT, 256, f * 128, 128, tt * 128, 128),
                                           fm_slice(S2T, 1024, f * 128, 128, half * 512, 512),
                                           start=(f == 0), stop=(f == NF - 1))
                    ins.then_inc(sems["ym"].h, 1)

        @block.scalar
        def _(s):
            for tb in range(NTB):
                if tb >= 1:
                    s.wait_ge(sems["ym"].h, ym_at[(tb - 1, 1)])
                for f in range(NF):
                    s.wait_ge(sems["mmm"].h, mmm_at[(tb, f)])
                    s.activation(fm_slice(ghT, 256, f * 128, 128, 0, 256),
                                 pm[f % 2][:], AF.Gelu,
                                 bias=b1pp[:, f:f + 1]).then_inc(sems["ge"].h, 1)

        @block.vector
        def _(v):
            for tb in range(NTB):
                for tt in range(2):
                    i = tb * 2 + tt
                    v.wait_ge(sems["ym"].h, ym_at[(tb, tt)])
                    if i >= 2:
                        v.wait_ge(sems["stx"].h, B0["stx"] + (i - 1) * 16)
                    v.tensor_add(eb[tt][:], py[tt][:], B2rep[:]).then_inc(sems["e1"].h, 1)
                    v.wait_ge(sems["e1"].h, B0["e1"] + i + 1)
                    v.wait_ge(sems["lx"].h, B0["lx"] + (tb + 1) * 64)
                    v.tensor_add(eb[tt][:], eb[tt][:], acct[tt][:]).then_inc(sems["e2"].h, 1)
                    v.wait_ge(sems["e2"].h, B0["e2"] + i + 1)
                    v.tensor_add(eb[tt][:], eb[tt][:], xtl[tt][:]).then_inc(sems["e3"].h, 1)

    sems["xl"].v = B0["xl"] + NTB * 8 * 16
    sems["mmm"].v = n_m
    sems["ge"].v = n_g
    sems["ym"].v = n_y
    sems["e1"].v = B0["e1"] + NTB * 2
    sems["e2"].v = B0["e2"] + NTB * 2
    sems["e3"].v = n_3
    sems["lx"].v = B0["lx"] + NTB * 2 * 32
    sems["stx"].v = B0["stx"] + NTB * 2 * 16
    ctx.barrier()
    les.close()


def emit_l0_layer(ctx, x_dram, xout_dram, W, upto="ffn", dbg=None):
    xnt = ctx.dram("l0_xnt", [DM, T], BF16)
    acc = dbg["acc"] if dbg and "acc" in dbg else ctx.dram("l0_acc", [T, DM], F32)
    emit_norm(ctx, x_dram, xnt)
    if upto == "norm":
        return
    les = contextlib.ExitStack()
    WQT = ctx.sb("l0_wqt", [128, 8 * 3072], BF16, es=les)
    WPT = ctx.sb("l0_wpt", [128, 8 * 1024], BF16, es=les)
    emit_weight_fm(ctx, WQT, W["L0_attn_w_qkv"], 1024, 3072, transpose=True)
    emit_weight_fm(ctx, WPT, W["L0_attn_w_proj"], 1024, 1024, transpose=True)
    KT = ctx.sb("l0_kt", [128, 8 * 2048], BF16, es=les)
    VA = ctx.sb("l0_va", [128, 16 * 1040], BF16, es=les)
    if upto == "weights":
        if dbg is not None and "kt" in dbg:
            emit_dump_sbuf(ctx, dbg["kt"], WQT, 8 * 2048)
        les.close()
        return
    for b in range(BT):
        emit_l0_qkv(ctx, xnt, b, WQT, KT, VA)
        if upto == "qkv" and b == 0:
            if dbg is not None and "kt" in dbg:
                emit_dump_sbuf(ctx, dbg["kt"], KT, 8 * 2048)
                emit_dump_sbuf(ctx, dbg["va"], VA, 16 * 1040)
            les.close()
            return
        emit_l0_attn_core(ctx, xnt, b, WQT, KT, VA, WPT, acc)
        if upto == "attn0" and b == 0:
            les.close()
            return
    les.close()
    if upto == "attn":
        return
    les = contextlib.ExitStack()
    a1col = ctx.sb("l0_a1c", [128, 32], F32, es=les)
    a2col = ctx.sb("l0_a2c", [128, 8], F32, es=les)
    A1rep = ctx.sb("l0_a1r", [128, 4096], BF16, es=les)
    A2rep = ctx.sb("l0_a2r", [128, 1024], BF16, es=les)
    B2rep = ctx.sb("l0_b2r", [128, 1024], F32, es=les)
    b1pp = ctx.sb("l0_b1p", [128, 32], F32, es=les)
    a1row_d = ctx.dram("l0_a1row", [4096], F32)
    a2row_d = ctx.dram("l0_a2row", [1024], F32)
    emit_binarize_prep(ctx, W["L0_ffn_w1"], 4096, 1024, a1col, les)
    emit_colvec_to_rowrep(ctx, a1col, 32, a1row_d, A1rep, scale=1.0 / 1024)
    if upto == "fprep1":
        les.close()
        return
    emit_binarize_prep(ctx, W["L0_ffn_w2"], 1024, 4096, a2col, les)
    emit_colvec_to_rowrep(ctx, a2col, 8, a2row_d, A2rep, scale=1.0 / 4096)
    emit_replicate_rows(ctx, B2rep, W["L0_ffn_b2"], 1024)
    emit_load_pp(ctx, b1pp, W["L0_ffn_b1"], 32)
    if upto == "fprep2":
        les.close()
        return
    S1T = ctx.sb("l0_s1t", [128, 8 * 4096], BF16, es=les)
    S2T = ctx.sb("l0_s2t", [128, 32 * 1024], BF16, es=les)
    emit_sign_fm(ctx, S1T, W["L0_ffn_w1"], 4096, 1024, A1rep, 4096)
    if upto == "fsign1":
        les.close()
        return
    emit_sign_fm(ctx, S2T, W["L0_ffn_w2"], 1024, 4096, A2rep, 1024)
    if upto == "fsign":
        les.close()
        return
    emit_l0_ffn_main(ctx, xnt, x_dram, acc, xout_dram, S1T, S2T, b1pp, B2rep)
    les.close()


def emit_dump_sbuf(ctx, dram_out, sbuf, ncols):
    sems = ctx.pool(1)
    st = sems[0]
    with ctx.nc.Block() as block:
        @block.sync
        def _(sy):
            sy.dma_start(out=dram_out, in_=sbuf[:, 0:ncols]).then_inc(st.h, 16)
            sy.wait_ge(st.h, st.v + 16)
    st.v += 16
    ctx.barrier()


def build_l0(upto="ffn", debug=False):
    ctx = Ctx()
    x_in = ctx.dram("x_in", [T, DM], F32, kind="ExternalInput")
    x_out = ctx.dram("x_out", [T, DM], F32, kind="ExternalOutput")
    W = {}
    for nm, shp in [("L0_attn_w_qkv", (3072, 1024)), ("L0_attn_w_proj", (1024, 1024)),
                    ("L0_ffn_w1", (4096, 1024)), ("L0_ffn_b1", (4096,)),
                    ("L0_ffn_w2", (1024, 4096)), ("L0_ffn_b2", (1024,))]:
        W[nm] = ctx.dram(nm, list(shp), F32, kind="ExternalInput")
    ctx.load_consts()
    dbg = None
    if debug:
        dbg = {"acc": ctx.dram("dbg_acc", [T, DM], F32, kind="ExternalOutput")}
        if upto == "qkv":
            dbg["kt"] = ctx.dram("dbg_kt", [128, 8 * 2048], BF16, kind="ExternalOutput")
            dbg["va"] = ctx.dram("dbg_va", [128, 16 * 1040], BF16, kind="ExternalOutput")
    emit_l0_layer(ctx, x_in, x_out, W, upto=upto, dbg=dbg)
    ctx.close()
    return ctx



def emit_l1_ropeqk(ctx, xnt_dram, b, WQfm, WKfm, bq_rep, bk_rep, QTfull, KTfull):
    nc = ctx.nc
    u = ctx.uid()
    les = contextlib.ExitStack()
    (ld, cs, qm, va_, vb, vc, tq, eq, km2, vk, tk, ek) = ctx.pool(12)
    xsl = [[ctx.sb(f"rqx{u}_{s}_{i}", [128, 512], BF16, es=les) for i in range(8)]
           for s in range(2)]
    cosb = [ctx.sb(f"rqc{u}_{i}", [128, DM], F32, es=les) for i in range(2)]
    sinb = [ctx.sb(f"rqs{u}_{i}", [128, DM], F32, es=les) for i in range(2)]
    xnf = ctx.sb(f"rqn{u}", [128, DM], F32, es=les)
    t1f = ctx.sb(f"rq1{u}", [128, DM], F32, es=les)
    t2f = ctx.sb(f"rq2{u}", [128, DM], F32, es=les)
    xbf = [ctx.sb(f"rqb{u}_{i}", [128, DM], BF16, es=les) for i in range(2)]
    pqk = [ctx.ps_(f"rqp{u}_{i}", [128, 1024], F32, es=les) for i in range(2)]
    ptr = [ctx.ps_(f"rqt{u}_{i}", [128, 1024], BF16, es=les) for i in range(2)]
    ident = ctx.consts["c_ident"]
    cosd = ctx.dram_consts["c_ropecos"]
    sind = ctx.dram_consts["c_ropesin"]

    b_ = {s: s.v for s in (ld, cs, qm, va_, vb, vc, tq, eq, km2, vk, tk, ek)}

    with nc.Block() as block:
        @block.sync
        def _(sy):
            for tb in range(4):
                if tb >= 1:
                    sy.wait_ge(ld.h, b_[ld] + tb * 128)
                if tb >= 2:
                    sy.wait_ge(qm.h, b_[qm] + (tb - 1) * 8)
                for it in range(8):
                    sy.dma_start(out=xsl[tb % 2][it][:],
                                 in_=xnt_dram[it * 128:(it + 1) * 128,
                                              b * S + tb * 512: b * S + (tb + 1) * 512]
                                 ).then_inc(ld.h, 16)
                for tt4 in range(4):
                    tt = tb * 4 + tt4
                    if tt >= 2:
                        sy.wait_ge(vc.h, b_[vc] + (tt - 1) * 2)
                    sy.dma_start(out=cosb[tt % 2][:],
                                 in_=cosd[tt * 128:(tt + 1) * 128, :]).then_inc(cs.h, 16)
                    sy.wait_ge(cs.h, b_[cs] + (tt * 2 + 1) * 16)
                    sy.dma_start(out=sinb[tt % 2][:],
                                 in_=sind[tt * 128:(tt + 1) * 128, :]).then_inc(cs.h, 16)
                    sy.wait_ge(cs.h, b_[cs] + (tt * 2 + 2) * 16)

        @block.tensor
        def _(t):
            for tb in range(4):
                t.wait_ge(ld.h, b_[ld] + (tb + 1) * 128)
                for tt4 in range(4):
                    tt = tb * 4 + tt4
                    for p in range(2):
                        W = WQfm if p == 0 else WKfm
                        if tt >= 1:
                            t.wait_ge(va_.h, b_[va_] + (tt - 1) * 2 + p + 1)
                        for half in range(2):
                            for it in range(8):
                                ins = t.matmul(pqk[p][:, half * 512:(half + 1) * 512],
                                               xsl[tb % 2][it][:, tt4 * 128:(tt4 + 1) * 128],
                                               fm_slice(W, 1024, it * 128, 128, half * 512, 512),
                                               start=(it == 0), stop=(it == 7))
                        ins.then_inc(qm.h, 1)
                    for p in range(2):
                        t.wait_ge(vb.h, b_[vb] + tt * 2 + p + 1)
                        if tt >= 1:
                            t.wait_ge((eq if p == 0 else ek).h,
                                      b_[eq if p == 0 else ek] + tt)
                        for cb in range(8):
                            ins = t.transpose(ptr[p][:, cb * 128:(cb + 1) * 128],
                                              xbf[p][:, cb * 128:(cb + 1) * 128],
                                              ident[:])
                        ins.then_inc((tq if p == 0 else tk).h, 1)

        @block.vector
        def _(v):
            for tb in range(4):
                for tt4 in range(4):
                    tt = tb * 4 + tt4
                    for p in range(2):
                        brep = bq_rep if p == 0 else bk_rep
                        v.wait_ge(qm.h, b_[qm] + tt * 2 + p + 1)
                        v.wait_ge(cs.h, b_[cs] + (tt + 1) * 32)
                        if tt >= 1:
                            v.wait_ge((tq if p == 0 else tk).h,
                                      b_[tq if p == 0 else tk] + tt)
                        v.tensor_add(xnf[:], pqk[p][:], brep[:]).then_inc(va_.h, 1)
                        v.wait_ge(va_.h, b_[va_] + tt * 2 + p + 1)
                        v.tensor_mul(t1f[:], xnf[:], cosb[tt % 2][:])
                        x4 = xnf[:].rearrange("q (a k j) -> q a k j", k=2, j=32)
                        s4 = sinb[tt % 2][:].rearrange("q (a k j) -> q a k j", k=2, j=32)
                        t4 = t2f[:].rearrange("q (a k j) -> q a k j", k=2, j=32)
                        v.tensor_mul(t4[:, :, 0, :], x4[:, :, 1, :], s4[:, :, 0, :])
                        ins = v.tensor_mul(t4[:, :, 1, :], x4[:, :, 0, :], s4[:, :, 1, :])
                        ins.then_inc(vc.h, 1)
                        v.wait_ge(vc.h, b_[vc] + tt * 2 + p + 1)
                        v.tensor_add(xbf[p][:], t1f[:], t2f[:]).then_inc(vb.h, 1)
                    for p in range(2):
                        v.wait_ge((tq if p == 0 else tk).h,
                                  b_[tq if p == 0 else tk] + tt + 1)
                        dst = QTfull if p == 0 else KTfull
                        ins = v.tensor_copy(
                            dst[:, :].rearrange("q (c z) -> q c z", c=8)[:, :, tt * 128:(tt + 1) * 128],
                            ptr[p][:].rearrange("q (c z) -> q c z", c=8))
                        ins.then_inc((eq if p == 0 else ek).h, 1)

    ld.v = b_[ld] + 4 * 128
    cs.v = b_[cs] + 16 * 32
    qm.v = b_[qm] + 32
    va_.v = b_[va_] + 32
    vb.v = b_[vb] + 32
    vc.v = b_[vc] + 32
    tq.v = b_[tq] + 16
    tk.v = b_[tk] + 16
    eq.v = b_[eq] + 16
    ek.v = b_[ek] + 16
    ctx.barrier()
    les.close()


def emit_gate(ctx, hnt_dram, GWfm, gb_rep, MSK, gate_dim=8, with_scores=False,
              SCO=None):
    nc = ctx.nc
    u = ctx.uid()
    les = contextlib.ExitStack()
    gd = gate_dim
    (ld, gm, e1, e2, e3, e4, e5) = ctx.pool(7)
    xsl = [[ctx.sb(f"gx{u}_{s}_{i}", [128, 128], BF16, es=les) for i in range(8)]
           for s in range(2)]
    lg = [ctx.sb(f"glg{u}_{i}", [128, gd], F32, es=les) for i in range(2)]
    mx = ctx.sb(f"gmx{u}", [128, 2], F32, es=les)
    ee = ctx.sb(f"gee{u}", [128, gd], F32, es=les)
    sume = ctx.sb(f"gsu{u}", [128, 1], F32, es=les)
    pg = [ctx.ps_(f"gpg{u}_{i}", [128, 512], F32, es=les) for i in range(2)]
    b_ = {s: s.v for s in (ld, gm, e1, e2, e3, e4, e5)}
    with nc.Block() as block:
        @block.sync
        def _(sy):
            for i in range(NT):
                if i >= 1:
                    sy.wait_ge(ld.h, b_[ld] + i * 128)
                if i >= 2:
                    sy.wait_ge(gm.h, b_[gm] + i - 1)
                for it in range(8):
                    sy.dma_start(out=xsl[i % 2][it][:],
                                 in_=hnt_dram[it * 128:(it + 1) * 128,
                                              i * 128:(i + 1) * 128]).then_inc(ld.h, 16)

        @block.tensor
        def _(t):
            for i in range(NT):
                t.wait_ge(ld.h, b_[ld] + (i + 1) * 128)
                if i >= 2:
                    t.wait_ge(e1.h, b_[e1] + i - 1)
                for it in range(8):
                    ins = t.matmul(pg[i % 2][:, 0:gd],
                                   xsl[i % 2][it][:],
                                   fm_slice(GWfm, gd, it * 128, 128, 0, gd),
                                   start=(it == 0), stop=(it == 7))
                ins.then_inc(gm.h, 1)

        @block.vector
        def _(v):
            for i in range(NT):
                v.wait_ge(gm.h, b_[gm] + i + 1)
                v.tensor_add(lg[i % 2][:], pg[i % 2][:, 0:gd], gb_rep[:, 0:gd]).then_inc(e1.h, 1)
                v.wait_ge(e1.h, b_[e1] + i + 1)
                v.reduce_max(mx[:, 0:1], lg[i % 2][:], axis=AXX).then_inc(e2.h, 1)
                v.wait_ge(e2.h, b_[e2] + i + 1)
                if not with_scores:
                    v.tensor_single_scalar(MSK[:, i * gd:(i + 1) * gd], lg[i % 2][:],
                                           mx[:, 0:1], ALU.is_ge).then_inc(e3.h, 1)
                else:
                    v.wait_ge(e4.h, b_[e4] + i)
                    v.tensor_scalar_mul(mx[:, 1:2], mx[:, 0:1], -1.0).then_inc(e3.h, 1)
                    v.wait_ge(e5.h, b_[e5] + i + 1)
                    v.tensor_single_scalar(MSK[:, i * gd:(i + 1) * gd], ee[:],
                                           1.0, ALU.is_ge)
                    with nc.allow_low_precision(reason="score=1/sum fp32 recip"):
                        v.reciprocal(SCO[:, i:i + 1], sume[:]).then_inc(e4.h, 1)

        @block.scalar
        def _(s):
            if with_scores:
                for i in range(NT):
                    s.wait_ge(e3.h, b_[e3] + i + 1)
                    s.activation(ee[:], lg[i % 2][:], AF.Exp, bias=mx[:, 1:2],
                                 accum_out=sume[:]).then_inc(e5.h, 1)

    ld.v = b_[ld] + NT * 128
    gm.v = b_[gm] + NT
    e1.v = b_[e1] + NT
    e2.v = b_[e2] + NT
    e3.v = b_[e3] + NT
    if with_scores:
        e4.v = b_[e4] + NT
        e5.v = b_[e5] + NT
    ctx.barrier()
    les.close()


def emit_moe_expert(ctx, hnt_dram, W1fm, W2fm, b1pp, B2rep, coef_sb, e, gd,
                    macc_dram, first, last, final_addends, xout_dram, act):
    nc = ctx.nc
    u = ctx.uid()
    les = contextlib.ExitStack()
    _names = ["xl", "mmm", "ge", "ym", "e1", "e2", "e3", "lx", "stx"]
    sems = dict(zip(_names, ctx.pool(9)))
    B0 = {n: s.v for n, s in sems.items()}
    NF = 32
    NTB = T // 256
    nadd = (1 if not first else 0) + (len(final_addends) if last else 0)
    xsl = [[ctx.sb(f"mex{u}_{s}_{i}", [128, 256], BF16, es=les) for i in range(8)]
           for s in range(2)]
    ghT = ctx.sb(f"meg{u}", [128, NF * 256], BF16, es=les)
    eb = [ctx.sb(f"mee{u}_{i}", [128, DM], F32, es=les) for i in range(2)]
    adt = [[ctx.sb(f"mea{u}_{s}_{i}", [128, DM], F32, es=les) for i in range(nadd)]
           for s in range(2)] if nadd else None
    pm = [ctx.ps_(f"mep{u}_{i}", [128, 256], F32, es=les) for i in range(2)]
    py = [ctx.ps_(f"mey{u}_{i}", [128, 1024], F32, es=les) for i in range(2)]

    mmm_at = {}
    ge_at = {}
    ym_at = {}
    e3_at = {}
    n_m = B0["mmm"]; n_g = B0["ge"]; n_y = B0["ym"]; n_3 = B0["e3"]
    for tb in range(NTB):
        for f in range(NF):
            n_m += 1; mmm_at[(tb, f)] = n_m
            n_g += 1; ge_at[(tb, f)] = n_g
        for tt in range(2):
            n_y += 1; ym_at[(tb, tt)] = n_y
            n_3 += max(1, nadd); e3_at[(tb, tt)] = n_3
    out_dram = xout_dram if last else macc_dram

    with nc.Block() as block:
        @block.sync
        def _(sy):
            for tb in range(NTB):
                if tb >= 1:
                    sy.wait_ge(sems["xl"].h, B0["xl"] + tb * 128)
                if tb >= 2:
                    sy.wait_ge(sems["mmm"].h, mmm_at[(tb - 2, NF - 1)])
                for it in range(8):
                    sy.dma_start(out=xsl[tb % 2][it][:],
                                 in_=hnt_dram[it * 128:(it + 1) * 128,
                                              tb * 256:(tb + 1) * 256]).then_inc(sems["xl"].h, 16)
                for tt in range(2):
                    i = tb * 2 + tt
                    g = i * 128
                    if nadd:
                        if i >= 2:
                            sy.wait_ge(sems["e3"].h, e3_at[((i - 2) // 2, (i - 2) % 2)])
                        srcs = ([] if first else [macc_dram]) + (final_addends if last else [])
                        for a_i, src in enumerate(srcs):
                            sy.dma_start(out=adt[tt][a_i][:], in_=src[g:g + 128, :]).then_inc(sems["lx"].h, 16)
                for tt in range(2):
                    i = tb * 2 + tt
                    sy.wait_ge(sems["e3"].h, e3_at[(tb, tt)])
                    sy.dma_start(out=out_dram[i * 128:(i + 1) * 128, :],
                                 in_=eb[tt][:]).then_inc(sems["stx"].h, 16)
                    sy.wait_ge(sems["stx"].h, B0["stx"] + (i + 1) * 16)

        @block.tensor
        def _(t):
            for tb in range(NTB):
                t.wait_ge(sems["xl"].h, B0["xl"] + (tb + 1) * 8 * 16)
                for f in range(NF):
                    if (tb, f) >= (0, 2):
                        if f >= 2:
                            t.wait_ge(sems["ge"].h, ge_at[(tb, f - 2)])
                        else:
                            t.wait_ge(sems["ge"].h, ge_at[(tb - 1, NF - 2 + f)])
                    for it in range(8):
                        ins = t.matmul(pm[f % 2][:],
                                       fm_slice(W1fm, 4096, it * 128, 128, f * 128, 128),
                                       xsl[tb % 2][it][:],
                                       start=(it == 0), stop=(it == 7))
                    ins.then_inc(sems["mmm"].h, 1)
                for tt in range(2):
                    if tb >= 1:
                        t.wait_ge(sems["e1"].h, B0["e1"] + (tb - 1) * 2 + tt + 1)
                    for f in range(NF):
                        t.wait_ge(sems["ge"].h, ge_at[(tb, f)])
                        for half in range(2):
                            ins = t.matmul(py[tt][:, half * 512:(half + 1) * 512],
                                           fm_slice(ghT, 256, f * 128, 128, tt * 128, 128),
                                           fm_slice(W2fm, 1024, f * 128, 128, half * 512, 512),
                                           start=(f == 0), stop=(f == NF - 1))
                    ins.then_inc(sems["ym"].h, 1)

        @block.scalar
        def _(s):
            for tb in range(NTB):
                if tb >= 1:
                    s.wait_ge(sems["ym"].h, ym_at[(tb - 1, 1)])
                for f in range(NF):
                    s.wait_ge(sems["mmm"].h, mmm_at[(tb, f)])
                    s.activation(fm_slice(ghT, 256, f * 128, 128, 0, 256),
                                 pm[f % 2][:], act,
                                 bias=b1pp[:, f:f + 1]).then_inc(sems["ge"].h, 1)

        @block.vector
        def _(v):
            for tb in range(NTB):
                for tt in range(2):
                    i = tb * 2 + tt
                    v.wait_ge(sems["ym"].h, ym_at[(tb, tt)])
                    if i >= 2:
                        v.wait_ge(sems["stx"].h, B0["stx"] + (i - 1) * 16)
                    v.tensor_add(eb[tt][:], py[tt][:], B2rep[:]).then_inc(sems["e1"].h, 1)
                    v.wait_ge(sems["e1"].h, B0["e1"] + i + 1)
                    ins = v.tensor_scalar_mul(eb[tt][:], eb[tt][:],
                                              coef_sb[:, i * gd + e:i * gd + e + 1])
                    if nadd:
                        ins.then_inc(sems["e2"].h, 1)
                        v.wait_ge(sems["e2"].h, B0["e2"] + i + 1)
                        v.wait_ge(sems["lx"].h, B0["lx"] + (tb + 1) * nadd * 2 * 16)
                        base3 = e3_at[(tb, tt)] - nadd
                        for a_i in range(nadd):
                            v.tensor_add(eb[tt][:], eb[tt][:],
                                         adt[tt][a_i][:]).then_inc(sems["e3"].h, 1)
                            if a_i < nadd - 1:
                                v.wait_ge(sems["e3"].h, base3 + a_i + 1)
                    else:
                        ins.then_inc(sems["e3"].h, 1)

    sems["xl"].v = B0["xl"] + NTB * 128
    sems["mmm"].v = n_m
    sems["ge"].v = n_g
    sems["ym"].v = n_y
    sems["e1"].v = B0["e1"] + NTB * 2
    if nadd:
        sems["e2"].v = B0["e2"] + NTB * 2
    sems["e3"].v = B0["e3"] + NTB * 2 * max(1, nadd)
    if nadd:
        sems["lx"].v = B0["lx"] + NTB * 2 * nadd * 16
    sems["stx"].v = B0["stx"] + NTB * 2 * 16
    ctx.barrier()
    les.close()


L1_WNAMES = ["L1_attn_Wq", "L1_attn_bq", "L1_attn_Wk", "L1_attn_bk", "L1_attn_Wv",
             "L1_attn_bv", "L1_attn_Wo", "L1_attn_bo", "L1_ffn_ln1_g", "L1_ffn_ln1_b",
             "L1_ffn_ln2_g", "L1_ffn_ln2_b", "L1_ffn_wq", "L1_ffn_bq", "L1_ffn_wk",
             "L1_ffn_bk", "L1_ffn_wv", "L1_ffn_bv", "L1_ffn_wo", "L1_ffn_bo",
             "L1_ffn_gate_w", "L1_ffn_gate_b", "L1_ffn_w1", "L1_ffn_b1",
             "L1_ffn_w2", "L1_ffn_b2"]


def emit_l1_layer(ctx, x_dram, xout_dram, W, upto="all", dbg=None):
    xnt = ctx.dram("l1_xnt", [DM, T], BF16)
    xnf = ctx.dram("l1_xnf", [T, DM], F32)
    acc = dbg["acc"] if dbg and "acc" in dbg else ctx.dram("l1_acc", [T, DM], F32)
    u2d = dbg["u2"] if dbg and "u2" in dbg else ctx.dram("l1_u2", [T, DM], F32)
    hnt = ctx.dram("l1_hnt", [DM, T], BF16)
    hn2t = ctx.dram("l1_hn2t", [DM, T], BF16)
    macc = ctx.dram("l1_macc", [T, DM], F32)

    emit_norm(ctx, x_dram, xnt, xnf_dram=xnf)
    if upto == "norm":
        return
    les = contextlib.ExitStack()
    QTfull = ctx.sb("l1_qt", [128, 8 * 2048], BF16, es=les)
    KT = ctx.sb("l1_kt", [128, 8 * 2048], BF16, es=les)
    VA = ctx.sb("l1_va", [128, 16 * 1040], BF16, es=les)
    bo_rep = ctx.sb("l1_bor", [128, DM], F32, es=les)
    emit_replicate_rows(ctx, bo_rep, W["L1_attn_bo"], 1024)
    for b in range(BT):
        lesb = contextlib.ExitStack()
        WQfm = ctx.sb(f"l1_wq{b}", [128, 8 * 1024], BF16, es=lesb)
        WKfm = ctx.sb(f"l1_wk{b}", [128, 8 * 1024], BF16, es=lesb)
        bq_rep = ctx.sb(f"l1_bqr{b}", [128, DM], F32, es=lesb)
        bk_rep = ctx.sb(f"l1_bkr{b}", [128, DM], F32, es=lesb)
        emit_weight_fm(ctx, WQfm, W["L1_attn_Wq"], 1024, 1024, transpose=False)
        emit_weight_fm(ctx, WKfm, W["L1_attn_Wk"], 1024, 1024, transpose=False)
        emit_replicate_rows(ctx, bq_rep, W["L1_attn_bq"], 1024)
        emit_replicate_rows(ctx, bk_rep, W["L1_attn_bk"], 1024)
        emit_l1_ropeqk(ctx, xnt, b, WQfm, WKfm, bq_rep, bk_rep, QTfull, KT)
        lesb.close()
        lesv = contextlib.ExitStack()
        WVfm = ctx.sb(f"l1_wv{b}", [128, 8 * 1024], BF16, es=lesv)
        bv_rep = ctx.sb(f"l1_bvr{b}", [128, DM], F32, es=lesv)
        emit_weight_fm(ctx, WVfm, W["L1_attn_Wv"], 1024, 1024, transpose=False)
        emit_replicate_rows(ctx, bv_rep, W["L1_attn_bv"], 1024)
        emit_l0_qkv(ctx, xnt, b, WVfm, KT, VA, WC=1024, parts=2,
                    wv_spec=(WVfm, 1024, 0), bv_rep=bv_rep)
        lesv.close()
        lesc = contextlib.ExitStack()
        WOfm = ctx.sb(f"l1_wo{b}", [128, 8 * 1024], BF16, es=lesc)
        emit_weight_fm(ctx, WOfm, W["L1_attn_Wo"], 1024, 1024, transpose=False)
        emit_l0_attn_core(ctx, xnt, b, None, KT, VA, WOfm, acc, WC=1024,
                          causal=False, scale=0.125, qt_full=QTfull, bo_rep=bo_rep)
        lesc.close()
    les.close()
    if upto == "oattn":
        return
    les = contextlib.ExitStack()
    G1 = ctx.sb("l1_g1", [128, DM], F32, es=les)
    B1 = ctx.sb("l1_b1", [128, DM], F32, es=les)
    emit_replicate_rows(ctx, G1, W["L1_ffn_ln1_g"], 1024)
    emit_replicate_rows(ctx, B1, W["L1_ffn_ln1_b"], 1024)
    emit_norm(ctx, xnf, hnt, grep=G1, brep=B1)
    les.close()
    les = contextlib.ExitStack()
    KT2 = ctx.sb("l1_kt2", [128, 8 * 2048], BF16, es=les)
    VA2 = ctx.sb("l1_va2", [128, 16 * 1040], BF16, es=les)
    bo2 = ctx.sb("l1_bo2", [128, DM], F32, es=les)
    emit_replicate_rows(ctx, bo2, W["L1_ffn_bo"], 1024)
    for b in range(BT):
        lesb = contextlib.ExitStack()
        WKfm2 = ctx.sb(f"l1_iwk{b}", [128, 8 * 1024], BF16, es=lesb)
        WVfm2 = ctx.sb(f"l1_iwv{b}", [128, 8 * 1024], BF16, es=lesb)
        bk2pp = ctx.sb(f"l1_ibk{b}", [128, 8], F32, es=lesb)
        bv2rep = ctx.sb(f"l1_ibv{b}", [128, DM], F32, es=lesb)
        emit_weight_fm(ctx, WKfm2, W["L1_ffn_wk"], 1024, 1024, transpose=False)
        emit_weight_fm(ctx, WVfm2, W["L1_ffn_wv"], 1024, 1024, transpose=False)
        emit_load_pp(ctx, bk2pp, W["L1_ffn_bk"], 8)
        emit_replicate_rows(ctx, bv2rep, W["L1_ffn_bv"], 1024)
        emit_l0_qkv(ctx, hnt, b, WKfm2, KT2, VA2, k_off=0, WC=1024, parts=3,
                    wv_spec=(WVfm2, 1024, 0), bk_pp=bk2pp, bv_rep=bv2rep)
        lesb.close()
        lesc = contextlib.ExitStack()
        WQfm2 = ctx.sb(f"l1_iwq{b}", [128, 8 * 1024], BF16, es=lesc)
        WOfm2 = ctx.sb(f"l1_iwo{b}", [128, 8 * 1024], BF16, es=lesc)
        bq2pp = ctx.sb(f"l1_ibq{b}", [128, 8], F32, es=lesc)
        emit_weight_fm(ctx, WQfm2, W["L1_ffn_wq"], 1024, 1024, transpose=False)
        emit_weight_fm(ctx, WOfm2, W["L1_ffn_wo"], 1024, 1024, transpose=False)
        emit_load_pp(ctx, bq2pp, W["L1_ffn_bq"], 8)
        emit_l0_attn_core(ctx, hnt, b, WQfm2, KT2, VA2, WOfm2, u2d, q_off=0,
                          WC=1024, causal=False, scale=0.125, bq_pp=bq2pp,
                          bo_rep=bo2, addin_dram=xnf)
        lesc.close()
    les.close()
    if upto == "inner":
        return
    les = contextlib.ExitStack()
    G2 = ctx.sb("l1_g2", [128, DM], F32, es=les)
    B2 = ctx.sb("l1_b2", [128, DM], F32, es=les)
    emit_replicate_rows(ctx, G2, W["L1_ffn_ln2_g"], 1024)
    emit_replicate_rows(ctx, B2, W["L1_ffn_ln2_b"], 1024)
    emit_norm(ctx, u2d, hn2t, grep=G2, brep=B2)
    les.close()
    lay = contextlib.ExitStack()
    MSK = ctx.sb("l1_msk", [128, NT * 8], F32, es=lay)
    les = contextlib.ExitStack()
    GWfm = ctx.sb("l1_gw", [128, 8 * 8], BF16, es=les)
    gbrep = ctx.sb("l1_gb", [128, 8], F32, es=les)
    emit_weight_fm(ctx, GWfm, W["L1_ffn_gate_w"], 1024, 8, transpose=False)
    emit_replicate_rows(ctx, gbrep, W["L1_ffn_gate_b"], 8)
    emit_gate(ctx, hn2t, GWfm, gbrep, MSK, gate_dim=8, with_scores=False)
    les.close()
    if upto == "gate":
        lay.close()
        return
    for e in range(8):
        lese = contextlib.ExitStack()
        W1fm = ctx.sb(f"l1_e{e}w1", [128, 8 * 4096], BF16, es=lese)
        W2fm = ctx.sb(f"l1_e{e}w2", [128, 32 * 1024], BF16, es=lese)
        b1pp = ctx.sb(f"l1_e{e}b1", [128, 32], F32, es=lese)
        B2e = ctx.sb(f"l1_e{e}b2", [128, DM], F32, es=lese)
        emit_weight_fm(ctx, W1fm, W["L1_ffn_w1"][e], 1024, 4096, transpose=False)
        emit_weight_fm(ctx, W2fm, W["L1_ffn_w2"][e], 4096, 1024, transpose=False)
        emit_load_pp(ctx, b1pp, W["L1_ffn_b1"][e], 32)
        emit_replicate_rows(ctx, B2e, W["L1_ffn_b2"][e], 1024)
        emit_moe_expert(ctx, hn2t, W1fm, W2fm, b1pp, B2e, MSK, e, 8, macc,
                        first=(e == 0), last=(e == 7),
                        final_addends=[x_dram, acc, u2d],
                        xout_dram=xout_dram, act=AF.Relu)
        lese.close()
    lay.close()


def build_l1(upto="all", debug=False):
    ctx = Ctx()
    x_in = ctx.dram("x_in", [T, DM], F32, kind="ExternalInput")
    x_out = ctx.dram("x_out", [T, DM], F32, kind="ExternalOutput")
    W = {}
    shapes = {"L1_attn_Wq": (1024, 1024), "L1_attn_bq": (1024,),
              "L1_attn_Wk": (1024, 1024), "L1_attn_bk": (1024,),
              "L1_attn_Wv": (1024, 1024), "L1_attn_bv": (1024,),
              "L1_attn_Wo": (1024, 1024), "L1_attn_bo": (1024,),
              "L1_ffn_ln1_g": (1024,), "L1_ffn_ln1_b": (1024,),
              "L1_ffn_ln2_g": (1024,), "L1_ffn_ln2_b": (1024,),
              "L1_ffn_wq": (1024, 1024), "L1_ffn_bq": (1024,),
              "L1_ffn_wk": (1024, 1024), "L1_ffn_bk": (1024,),
              "L1_ffn_wv": (1024, 1024), "L1_ffn_bv": (1024,),
              "L1_ffn_wo": (1024, 1024), "L1_ffn_bo": (1024,),
              "L1_ffn_gate_w": (1024, 8), "L1_ffn_gate_b": (8,),
              "L1_ffn_w1": (8, 1024, 4096), "L1_ffn_b1": (8, 4096),
              "L1_ffn_w2": (8, 4096, 1024), "L1_ffn_b2": (8, 1024)}
    for nm in L1_WNAMES:
        W[nm] = ctx.dram(nm, list(shapes[nm]), F32, kind="ExternalInput")
    ctx.load_consts()
    dbg = None
    if debug:
        dbg = {"acc": ctx.dram("dbg_acc", [T, DM], F32, kind="ExternalOutput"),
               "u2": ctx.dram("dbg_u2", [T, DM], F32, kind="ExternalOutput")}
    emit_l1_layer(ctx, x_in, x_out, W, upto=upto, dbg=dbg)
    ctx.close()
    return ctx



PSL = S // NCORE


def emit_l2_gather_xpm(ctx, xn_all_dram, core_p0, XPM):
    nc = ctx.nc
    u = ctx.uid()
    les = contextlib.ExitStack()
    ld, cp = ctx.pool(2)
    stg = [ctx.sb(f"gx{u}_{i}", [128, PSL], BF16, es=les) for i in range(4)]
    b_ld, b_cp = ld.v, cp.v
    with nc.Block() as block:
        @block.sync
        def _(sy):
            n = 0
            for ft in range(8):
                for cb in range(8):
                    for lb in range(2):
                        if n >= 4:
                            sy.wait_ge(cp.h, b_cp + n - 3)
                        sy.dma_start(out=stg[n % 4][:],
                                     in_=xn_all_dram[cb * 1024 + ft * 128: cb * 1024 + (ft + 1) * 128,
                                                     lb * S + core_p0: lb * S + core_p0 + PSL]
                                     ).then_inc(ld.h, 16)
                        sy.wait_ge(ld.h, b_ld + (n + 1) * 16)
                        n += 1

        @block.vector
        def _(v):
            n = 0
            for ft in range(8):
                for cb in range(8):
                    for lb in range(2):
                        v.wait_ge(ld.h, b_ld + (n + 1) * 16)
                        bidx = cb * 2 + lb
                        dst = XPM[:, ft * 4096 + bidx: ft * 4096 + bidx + (PSL - 1) * 16 + 1]
                        dst = bass.AP(dst.tensor, dst.offset, [list(dst.ap[0]), [16, PSL]])
                        v.tensor_copy(dst, stg[n % 4][:]).then_inc(cp.h, 1)
                        n += 1

    ld.v = b_ld + 128 * 16
    cp.v = b_cp + 128
    ctx.barrier()
    les.close()


def emit_l2_attn(ctx, XPM, WQT2, WKT2, WVT2, bq_pp, bk_pp, bv_rep, o_out_dram):
    nc = ctx.nc
    u = ctx.uid()
    les = contextlib.ExitStack()
    _n = ["qm", "qe", "km", "ke", "vm", "ve", "sm", "ex", "mk", "am", "d1",
          "d2", "mu", "st", "ms"]
    sems = dict(zip(_n, ctx.pool(15)))
    B0 = {k: s.v for k, s in sems.items()}
    QTt = ctx.sb(f"l2q{u}", [128, 1024], BF16, es=les)
    KTt = ctx.sb(f"l2k{u}", [128, 1024], BF16, es=les)
    VAt = ctx.sb(f"l2v{u}", [128, 8 * 129], BF16, es=les)
    pt = [ctx.sb(f"l2p{u}_{i}", [128, 128], BF16, es=les) for i in range(2)]
    dnr = ctx.sb(f"l2d{u}", [128, 2], F32, es=les)
    onat = [ctx.sb(f"l2o{u}_{i}", [128, DM], F32, es=les) for i in range(2)]
    psQK = ctx.ps_(f"l2pq{u}", [128, 1024], F32, es=les)
    psV = ctx.ps_(f"l2pv{u}", [128, 1024], F32, es=les)
    psS = [ctx.ps_(f"l2ps{u}_{i}", [128, 512], F32, es=les) for i in range(2)]
    psO = [ctx.ps_(f"l2po{u}_{i}", [128, 512], F32, es=les) for i in range(2)]
    bd = ctx.consts["c_blockdiag"]
    SC = 0.08838834764831845

    c = {k: B0[k] for k in _n}
    qm_at = {}; km_at = {}; vm_at = {}; qe_at = {}; ke_at = {}; ve_at = {}
    sm_at = {}; ex_at = {}; mk_at = {}; am_at = {}; d1_at = {}; d2_at = {}
    mu_at = {}; st_at = {}
    for i in range(NT):
        c["qm"] += 1; qm_at[i] = c["qm"]
        c["qe"] += 1; qe_at[i] = c["qe"]
        c["km"] += 1; km_at[i] = c["km"]
        c["ke"] += 1; ke_at[i] = c["ke"]
        c["vm"] += 1; vm_at[i] = c["vm"]
        c["ve"] += 1; ve_at[i] = c["ve"]
        for h in range(8):
            c["sm"] += 1; sm_at[(i, h)] = c["sm"]
            c["ex"] += 1; ex_at[(i, h)] = c["ex"]
            c["mk"] += 1; mk_at[(i, h)] = c["mk"]
            c["am"] += 1; am_at[(i, h)] = c["am"]
            c["d1"] += 1; d1_at[(i, h)] = c["d1"]
            c["d2"] += 1; d2_at[(i, h)] = c["d2"]
            c["mu"] += 1; mu_at[(i, h)] = c["mu"]
        c["st"] += 16; st_at[i] = c["st"]

    with nc.Block() as block:
        @block.tensor
        def _(t):
            t.wait_ge(sems["ms"].h, B0["ms"] + 1)
            for i in range(NT):
                for p, (W, grp) in enumerate([(WQT2, "qm"), (WKT2, "km")]):
                    if p == 0:
                        if i >= 1:
                            t.wait_ge(sems["ke"].h, ke_at[i - 1])
                    else:
                        t.wait_ge(sems["qe"].h, qe_at[i])
                    for ot in range(8):
                        for it in range(8):
                            ins = t.matmul(psQK[:, ot * 128:(ot + 1) * 128],
                                           fm_slice(W, 1024, it * 128, 128, ot * 128, 128),
                                           XPM[:, it * 4096 + i * 128: it * 4096 + (i + 1) * 128],
                                           start=(it == 0), stop=(it == 7))
                    ins.then_inc(sems[grp].h, 1)
                if i >= 1:
                    t.wait_ge(sems["ve"].h, ve_at[i - 1])
                for half in range(2):
                    for it in range(8):
                        ins = t.matmul(psV[:, half * 512:(half + 1) * 512],
                                       XPM[:, it * 4096 + i * 128: it * 4096 + (i + 1) * 128],
                                       fm_slice(WVT2, 1024, it * 128, 128, half * 512, 512),
                                       start=(it == 0), stop=(it == 7))
                ins.then_inc(sems["vm"].h, 1)
                for h in range(8):
                    t.wait_ge(sems["qe"].h, qe_at[i])
                    t.wait_ge(sems["ke"].h, ke_at[i])
                    if (i, h) > (0, 1):
                        ph = (i, h - 2) if h >= 2 else (i - 1, 6 + h)
                        t.wait_ge(sems["ex"].h, ex_at[ph])
                    t.matmul(psS[h % 2][:, 0:128],
                             KTt[:, h * 128:(h + 1) * 128],
                             QTt[:, h * 128:(h + 1) * 128],
                             start=True, stop=True).then_inc(sems["sm"].h, 1)
                    t.wait_ge(sems["mk"].h, mk_at[(i, h)])
                    if (i, h) > (0, 1):
                        ph = (i, h - 2) if h >= 2 else (i - 1, 6 + h)
                        t.wait_ge(sems["mu"].h, mu_at[ph])
                    t.matmul(psO[h % 2][:, 0:129],
                             pt[h % 2][:],
                             VAt[:, h * 129:(h + 1) * 129],
                             start=True, stop=True).then_inc(sems["am"].h, 1)

        @block.scalar
        def _(s):
            for i in range(NT):
                s.wait_ge(sems["qm"].h, qm_at[i])
                for ot in range(8):
                    ins = s.activation(QTt[:, ot * 128:(ot + 1) * 128],
                                       psQK[:, ot * 128:(ot + 1) * 128], AF.Identity,
                                       bias=bq_pp[:, ot:ot + 1])
                ins.then_inc(sems["qe"].h, 1)
                s.wait_ge(sems["km"].h, km_at[i])
                for ot in range(8):
                    ins = s.activation(KTt[:, ot * 128:(ot + 1) * 128],
                                       psQK[:, ot * 128:(ot + 1) * 128], AF.Identity,
                                       bias=bk_pp[:, ot:ot + 1])
                ins.then_inc(sems["ke"].h, 1)
                for h in range(8):
                    s.wait_ge(sems["sm"].h, sm_at[(i, h)])
                    if (i, h) > (0, 1):
                        ph = (i, h - 2) if h >= 2 else (i - 1, 6 + h)
                        s.wait_ge(sems["am"].h, am_at[ph])
                    s.activation(pt[h % 2][:], psS[h % 2][:, 0:128], AF.Exp,
                                 scale=SC).then_inc(sems["ex"].h, 1)

        @block.vector
        def _(v):
            v.memset(VAt[:], 1.0).then_inc(sems["ms"].h, 1)
            for i in range(NT):
                v.wait_ge(sems["vm"].h, vm_at[i])
                dst = VAt[:, :].rearrange("q (h j) -> q h j", j=129)[:, :, 0:128]
                srcp = psV[:, :].rearrange("q (h j) -> q h j", j=128)
                bvr = bv_rep[:, :].rearrange("q (h j) -> q h j", j=128)
                v.tensor_add(dst, srcp, bvr).then_inc(sems["ve"].h, 1)
                for h in range(8):
                    if h == 0 and i >= 2:
                        v.wait_ge(sems["st"].h, st_at[i - 2])
                    v.wait_ge(sems["ex"].h, ex_at[(i, h)])
                    v.tensor_mul(pt[h % 2][:], pt[h % 2][:], bd[:]).then_inc(sems["mk"].h, 1)
                    v.wait_ge(sems["am"].h, am_at[(i, h)])
                    v.tensor_scalar_add(dnr[:, 0:1], psO[h % 2][:, 128:129], 497.0).then_inc(sems["d1"].h, 1)
                    v.wait_ge(sems["d1"].h, d1_at[(i, h)])
                    v.reciprocal(dnr[:, 1:2], dnr[:, 0:1]).then_inc(sems["d2"].h, 1)
                    v.wait_ge(sems["d2"].h, d2_at[(i, h)])
                    v.tensor_scalar_mul(onat[i % 2][:, h * 128:(h + 1) * 128],
                                        psO[h % 2][:, 0:128],
                                        dnr[:, 1:2]).then_inc(sems["mu"].h, 1)

        @block.sync
        def _(sy):
            for i in range(NT):
                sy.wait_ge(sems["mu"].h, mu_at[(i, 7)])
                sy.dma_start(out=o_out_dram[i * 128:(i + 1) * 128, :],
                             in_=onat[i % 2][:]).then_inc(sems["st"].h, 16)
                sy.wait_ge(sems["st"].h, st_at[i])

    for k in _n:
        sems[k].v = c[k] if k != "ms" else B0["ms"] + 1
    ctx.barrier()
    les.close()


def emit_l2_ffn(ctx, xnt_dram, x_dram, xout_dram, WRT, WKT, WVT, mur_pp, muk_pp):
    nc = ctx.nc
    u = ctx.uid()
    les = contextlib.ExitStack()
    _n = ["ld", "pz", "sh", "km", "kr", "ks", "rm", "rs", "vm", "fm", "fx", "lx", "st"]
    sems = dict(zip(_n, ctx.pool(13)))
    B0 = {k: s.v for k, s in sems.items()}
    xc = [[ctx.sb(f"ffc{u}_{s}_{i}", [128, 512], BF16, es=les) for i in range(8)]
          for s in range(2)]
    xp = [[ctx.sb(f"ffp{u}_{s}_{i}", [128, 512], BF16, es=les) for i in range(8)]
          for s in range(2)]
    dtb = ctx.sb(f"ffd{u}", [128, 512], BF16, es=les)
    XRT = ctx.sb(f"ffr{u}", [128, 8 * 512], BF16, es=les)
    XKT = ctx.sb(f"ffk{u}", [128, 8 * 512], BF16, es=les)
    KAT = ctx.sb(f"ffa{u}", [128, 8 * 512], BF16, es=les)
    rb = ctx.sb(f"ffb{u}", [128, 512], BF16, es=les)
    rnat = ctx.sb(f"ffn{u}", [128, DM], BF16, es=les)
    fout = [ctx.sb(f"ffo{u}_{i}", [128, DM], F32, es=les) for i in range(2)]
    xtl = [ctx.sb(f"ffx{u}_{i}", [128, DM], F32, es=les) for i in range(2)]
    pk = [ctx.ps_(f"ffpk{u}_{i}", [128, 512], F32, es=les) for i in range(2)]
    pr = ctx.ps_(f"ffpr{u}", [128, 1024], F32, es=les)
    pv = ctx.ps_(f"ffpv{u}", [128, 1024], F32, es=les)

    NTB = T // 512
    c = {k: B0[k] for k in _n}
    sh_at = {}; km_at = {}; kr_at = {}; ks_at = {}; rm_at = {}; rs_at = {}
    vm_at = {}; fm_at = {}; fx_at = {}; st_at = {}; pz_at = {}
    for tb in range(NTB):
        if tb % 4 == 0:
            c["pz"] += 1
        pz_at[tb] = c["pz"]
        c["sh"] += 24; sh_at[tb] = c["sh"]
        for dt in range(8):
            c["km"] += 1; km_at[(tb, dt)] = c["km"]
            c["kr"] += 1; kr_at[(tb, dt)] = c["kr"]
            c["ks"] += 1; ks_at[(tb, dt)] = c["ks"]
        for tt in range(4):
            c["rm"] += 1; rm_at[(tb, tt)] = c["rm"]
            c["rs"] += 1; rs_at[(tb, tt)] = c["rs"]
            c["vm"] += 1; vm_at[(tb, tt)] = c["vm"]
            c["fm"] += 1; fm_at[(tb, tt)] = c["fm"]
            c["fx"] += 1; fx_at[(tb, tt)] = c["fx"]
            c["st"] += 16; st_at[(tb, tt)] = c["st"]

    with nc.Block() as block:
        @block.sync
        def _(sy):
            for tb in range(NTB):
                c0 = tb * 512
                first = (tb % 4 == 0)
                if tb >= 1:
                    sy.wait_ge(sems["ld"].h, B0["ld"] + tb * 256)
                if tb >= 2:
                    sy.wait_ge(sems["sh"].h, sh_at[tb - 1])
                for it in range(8):
                    sy.dma_start(out=xc[tb % 2][it][:],
                                 in_=xnt_dram[it * 128:(it + 1) * 128, c0:c0 + 512]
                                 ).then_inc(sems["ld"].h, 16)
                for it in range(8):
                    if first:
                        sy.dma_start(out=xp[tb % 2][it][:, 1:512],
                                     in_=xnt_dram[it * 128:(it + 1) * 128, c0:c0 + 511]
                                     ).then_inc(sems["ld"].h, 16)
                    else:
                        sy.dma_start(out=xp[tb % 2][it][:],
                                     in_=xnt_dram[it * 128:(it + 1) * 128, c0 - 1:c0 + 511]
                                     ).then_inc(sems["ld"].h, 16)
                for tt in range(4):
                    i = tb * 4 + tt
                    if i >= 2:
                        sy.wait_ge(sems["fx"].h, fx_at[((i - 2) // 4, (i - 2) % 4)])
                    sy.dma_start(out=xtl[tt % 2][:],
                                 in_=x_dram[i * 128:(i + 1) * 128, :]).then_inc(sems["lx"].h, 16)
                    sy.wait_ge(sems["fx"].h, fx_at[(tb, tt)])
                    sy.dma_start(out=xout_dram[i * 128:(i + 1) * 128, :],
                                 in_=fout[tt % 2][:]).then_inc(sems["st"].h, 16)
                    sy.wait_ge(sems["st"].h, st_at[(tb, tt)])

        @block.vector
        def _(v):
            for tb in range(NTB):
                first = (tb % 4 == 0)
                v.wait_ge(sems["ld"].h, B0["ld"] + (tb + 1) * 256)
                if first:
                    ins = None
                    for it in range(8):
                        ins = v.memset(xp[tb % 2][it][:, 0:1], 0.0)
                    ins.then_inc(sems["pz"].h, 1)
                    v.wait_ge(sems["pz"].h, pz_at[tb])
                if tb >= 2:
                    v.wait_ge(sems["vm"].h, vm_at[(tb - 1, 3)])
                for it in range(8):
                    if it >= 1:
                        v.wait_ge(sems["sh"].h, sh_at[tb] - 24 + it * 3)
                    v.tensor_sub(dtb[:], xc[tb % 2][it][:], xp[tb % 2][it][:]).then_inc(sems["sh"].h, 1)
                    v.wait_ge(sems["sh"].h, sh_at[tb] - 24 + it * 3 + 1)
                    v.scalar_tensor_tensor(XRT[:, it * 512:(it + 1) * 512], dtb[:],
                                           mur_pp[:, it:it + 1], xp[tb % 2][it][:],
                                           ALU.mult, ALU.add).then_inc(sems["sh"].h, 1)
                    v.scalar_tensor_tensor(XKT[:, it * 512:(it + 1) * 512], dtb[:],
                                           muk_pp[:, it:it + 1], xp[tb % 2][it][:],
                                           ALU.mult, ALU.add).then_inc(sems["sh"].h, 1)
                for dt in range(8):
                    v.wait_ge(sems["kr"].h, kr_at[(tb, dt)])
                    v.tensor_mul(KAT[:, dt * 512:(dt + 1) * 512], rb[:], rb[:]).then_inc(sems["ks"].h, 1)
                for tt in range(4):
                    i = tb * 4 + tt
                    v.wait_ge(sems["rs"].h, rs_at[(tb, tt)])
                    v.wait_ge(sems["vm"].h, vm_at[(tb, tt)])
                    if i >= 2:
                        v.wait_ge(sems["st"].h, st_at[((i - 2) // 4, (i - 2) % 4)])
                    v.tensor_mul(fout[tt % 2][:], pv[:], rnat[:]).then_inc(sems["fm"].h, 1)
                    v.wait_ge(sems["fm"].h, fm_at[(tb, tt)])
                    v.wait_ge(sems["lx"].h, B0["lx"] + (i + 1) * 16)
                    v.tensor_add(fout[tt % 2][:], fout[tt % 2][:], xtl[tt % 2][:]).then_inc(sems["fx"].h, 1)

        @block.scalar
        def _(s):
            for tb in range(NTB):
                for dt in range(8):
                    s.wait_ge(sems["km"].h, km_at[(tb, dt)])
                    if (tb, dt) > (0, 0):
                        pd = (tb, dt - 1) if dt >= 1 else (tb - 1, 7)
                        s.wait_ge(sems["ks"].h, ks_at[pd])
                    s.activation(rb[:], pk[dt % 2][:], AF.Relu).then_inc(sems["kr"].h, 1)
                for tt in range(4):
                    s.wait_ge(sems["rm"].h, rm_at[(tb, tt)])
                    if (tb, tt) > (0, 0):
                        pt_ = (tb, tt - 1) if tt >= 1 else (tb - 1, 3)
                        s.wait_ge(sems["fm"].h, fm_at[pt_])
                    s.activation(rnat[:], pr[:], AF.Sigmoid).then_inc(sems["rs"].h, 1)

        @block.tensor
        def _(t):
            for tb in range(NTB):
                t.wait_ge(sems["sh"].h, sh_at[tb])
                for dt in range(8):
                    if (tb, dt) >= (0, 2):
                        pd = (tb, dt - 2) if dt >= 2 else (tb - 1, 6 + dt)
                        t.wait_ge(sems["kr"].h, kr_at[pd])
                    for it in range(8):
                        ins = t.matmul(pk[dt % 2][:],
                                       fm_slice(WKT, 1024, it * 128, 128, dt * 128, 128),
                                       XKT[:, it * 512:(it + 1) * 512],
                                       start=(it == 0), stop=(it == 7))
                    ins.then_inc(sems["km"].h, 1)
                for tt in range(4):
                    if (tb, tt) > (0, 0):
                        pt_ = (tb, tt - 1) if tt >= 1 else (tb - 1, 3)
                        t.wait_ge(sems["rs"].h, rs_at[pt_])
                    for it in range(8):
                        for half in range(2):
                            ins = t.matmul(pr[:, half * 512:(half + 1) * 512],
                                           XRT[:, it * 512 + tt * 128: it * 512 + tt * 128 + 128],
                                           fm_slice(WRT, 1024, it * 128, 128, half * 512, 512),
                                           start=(it == 0), stop=(it == 7))
                    ins.then_inc(sems["rm"].h, 1)
                    t.wait_ge(sems["ks"].h, ks_at[(tb, 7)])
                    if (tb, tt) > (0, 0):
                        pt_ = (tb, tt - 1) if tt >= 1 else (tb - 1, 3)
                        t.wait_ge(sems["fm"].h, fm_at[pt_])
                    for dt in range(8):
                        for half in range(2):
                            ins = t.matmul(pv[:, half * 512:(half + 1) * 512],
                                           KAT[:, dt * 512 + tt * 128: dt * 512 + tt * 128 + 128],
                                           fm_slice(WVT, 1024, dt * 128, 128, half * 512, 512),
                                           start=(dt == 0), stop=(dt == 7))
                    ins.then_inc(sems["vm"].h, 1)

    for k in _n:
        if k == "ld":
            sems[k].v = B0[k] + NTB * 256
        elif k == "lx":
            sems[k].v = B0[k] + NTB * 4 * 16
        else:
            sems[k].v = c[k]
    ctx.barrier()
    les.close()


def emit_add(ctx, a_dram, b_dram, out_dram):
    nc = ctx.nc
    u = ctx.uid()
    les = contextlib.ExitStack()
    ld, ad, st = ctx.pool(3)
    at = [ctx.sb(f"ad{u}a{i}", [128, DM], F32, es=les) for i in range(2)]
    bt = [ctx.sb(f"ad{u}b{i}", [128, DM], F32, es=les) for i in range(2)]
    ot = [ctx.sb(f"ad{u}o{i}", [128, DM], F32, es=les) for i in range(2)]
    b_ld, b_ad, b_st = ld.v, ad.v, st.v
    with nc.Block() as block:
        @block.sync
        def _(sy):
            for i in range(NT):
                if i >= 2:
                    sy.wait_ge(ad.h, b_ad + i - 1)
                sy.dma_start(out=at[i % 2][:], in_=a_dram[i * 128:(i + 1) * 128, :]).then_inc(ld.h, 16)
                sy.dma_start(out=bt[i % 2][:], in_=b_dram[i * 128:(i + 1) * 128, :]).then_inc(ld.h, 16)
                sy.wait_ge(ld.h, b_ld + (i + 1) * 32)
                sy.wait_ge(ad.h, b_ad + i + 1)
                sy.dma_start(out=out_dram[i * 128:(i + 1) * 128, :], in_=ot[i % 2][:]).then_inc(st.h, 16)
                sy.wait_ge(st.h, b_st + (i + 1) * 16)

        @block.vector
        def _(v):
            for i in range(NT):
                v.wait_ge(ld.h, b_ld + (i + 1) * 32)
                if i >= 2:
                    v.wait_ge(st.h, b_st + (i - 1) * 16)
                v.tensor_add(ot[i % 2][:], at[i % 2][:], bt[i % 2][:]).then_inc(ad.h, 1)
    ld.v = b_ld + NT * 32
    ad.v = b_ad + NT
    st.v = b_st + NT * 16
    ctx.barrier()
    les.close()


L2_WNAMES = ["L2_attn_Wq", "L2_attn_bq", "L2_attn_Wk", "L2_attn_bk",
             "L2_attn_Wv", "L2_attn_bv", "L2_ffn_W_r", "L2_ffn_W_k",
             "L2_ffn_W_v", "L2_ffn_mu_r", "L2_ffn_mu_k"]


def build_l2a(debug=False):
    ctx = Ctx()
    x_in = ctx.dram("x_in", [T, DM], F32, kind="ExternalInput")
    xnt = ctx.dram("xnt_out", [DM, T], BF16, kind="ExternalOutput")
    xo = ctx.dram("xo_out", [T, DM], F32, kind="ExternalOutput")
    W = {nm: ctx.dram(nm, [1024, 1024] if "W_" in nm or "_W" in nm else [1024],
                      F32, kind="ExternalInput")
         for nm in ["L2_ffn_W_r", "L2_ffn_W_k", "L2_ffn_W_v", "L2_ffn_mu_r", "L2_ffn_mu_k"]}
    ctx.load_consts()
    emit_norm(ctx, x_in, xnt)
    les = contextlib.ExitStack()
    WRT = ctx.sb("l2_wrt", [128, 8 * 1024], BF16, es=les)
    WKT = ctx.sb("l2_wkt", [128, 8 * 1024], BF16, es=les)
    WVT = ctx.sb("l2_wvt", [128, 8 * 1024], BF16, es=les)
    mur = ctx.sb("l2_mur", [128, 8], F32, es=les)
    muk = ctx.sb("l2_muk", [128, 8], F32, es=les)
    emit_weight_fm(ctx, WRT, W["L2_ffn_W_r"], 1024, 1024, transpose=True)
    emit_weight_fm(ctx, WKT, W["L2_ffn_W_k"], 1024, 1024, transpose=True)
    emit_weight_fm(ctx, WVT, W["L2_ffn_W_v"], 1024, 1024, transpose=True)
    emit_load_pp(ctx, mur, W["L2_ffn_mu_r"], 8)
    emit_load_pp(ctx, muk, W["L2_ffn_mu_k"], 8)
    emit_l2_ffn(ctx, xnt, x_in, xo, WRT, WKT, WVT, mur, muk)
    les.close()
    ctx.close()
    return ctx


def build_l2b():
    ctx = Ctx()
    xpm_in = ctx.dram("xpm_in", [DM, T], BF16, kind="ExternalInput")
    o_out = ctx.dram("o_out", [T, DM], F32, kind="ExternalOutput")
    W = {}
    for nm in ["L2_attn_Wq", "L2_attn_Wk", "L2_attn_Wv"]:
        W[nm] = ctx.dram(nm, [1024, 1024], F32, kind="ExternalInput")
    for nm in ["L2_attn_bq", "L2_attn_bk", "L2_attn_bv"]:
        W[nm] = ctx.dram(nm, [1024], F32, kind="ExternalInput")
    ctx.load_consts()
    les = contextlib.ExitStack()
    XPM = ctx.sb("l2_xpm", [128, 8 * 4096], BF16, es=les)
    WQT2 = ctx.sb("l2_wq", [128, 8 * 1024], BF16, es=les)
    WKT2 = ctx.sb("l2_wk", [128, 8 * 1024], BF16, es=les)
    WVT2 = ctx.sb("l2_wv", [128, 8 * 1024], BF16, es=les)
    bqp = ctx.sb("l2_bq", [128, 8], F32, es=les)
    bkp = ctx.sb("l2_bk", [128, 8], F32, es=les)
    bvr = ctx.sb("l2_bv", [128, DM], F32, es=les)
    s0 = ctx.pool(1)[0]
    with ctx.nc.Block() as block:
        @block.sync
        def _(sy):
            for ft in range(8):
                sy.dma_start(out=XPM[:, ft * 4096:(ft + 1) * 4096],
                             in_=xpm_in[ft * 128:(ft + 1) * 128, :]).then_inc(s0.h, 16)
            sy.wait_ge(s0.h, s0.v + 128)
    s0.v += 128
    ctx.barrier()
    emit_weight_fm(ctx, WQT2, W["L2_attn_Wq"], 1024, 1024, transpose=True)
    emit_weight_fm(ctx, WKT2, W["L2_attn_Wk"], 1024, 1024, transpose=True)
    emit_weight_fm(ctx, WVT2, W["L2_attn_Wv"], 1024, 1024, transpose=True)
    emit_load_pp(ctx, bqp, W["L2_attn_bq"], 8)
    emit_load_pp(ctx, bkp, W["L2_attn_bk"], 8)
    emit_replicate_rows(ctx, bvr, W["L2_attn_bv"], 1024)
    emit_l2_attn(ctx, XPM, WQT2, WKT2, WVT2, bqp, bkp, bvr, o_out)
    les.close()
    ctx.close()
    return ctx



def emit_l3_qk(ctx, xnt_dram, b, WQfm, WKfm, bq_pp, bk_pp, QTf, KTf, KNat):
    nc = ctx.nc
    u = ctx.uid()
    les = contextlib.ExitStack()
    _n = ["ld", "qm", "f1", "f2", "f3", "f4", "tm", "te"]
    sems = dict(zip(_n, ctx.pool(8)))
    B0 = {k: s.v for k, s in sems.items()}
    xsl = [[ctx.sb(f"l3x{u}_{s}_{i}", [128, 512], BF16, es=les) for i in range(8)]
           for s in range(2)]
    tmn = ctx.sb(f"l3n{u}", [128, 512], F32, es=les)
    tme = ctx.sb(f"l3e{u}", [128, 512], F32, es=les)
    tmr = ctx.sb(f"l3r{u}", [128, 512], F32, es=les)
    ps01 = [ctx.ps_(f"l3p{u}_{i}", [128, 512], F32, es=les) for i in range(2)]
    ptr = [ctx.ps_(f"l3t{u}_{i}", [128, 1024], BF16, es=les) for i in range(2)]
    ident = ctx.consts["c_ident"]

    c = {k: B0[k] for k in _n}
    qm_at = {}; f4_at = {}; tm_at = {}; te_at = {}
    for tb in range(4):
        for p in range(2):
            for ot in range(8):
                c["qm"] += 1; qm_at[(tb, p, ot)] = c["qm"]
                c["f4"] += 1; f4_at[(tb, p, ot)] = c["f4"]
        for tt4 in range(4):
            c["tm"] += 1; tm_at[(tb, tt4)] = c["tm"]
            c["te"] += 1; te_at[(tb, tt4)] = c["te"]

    with nc.Block() as block:
        @block.sync
        def _(sy):
            for tb in range(4):
                if tb >= 1:
                    sy.wait_ge(sems["ld"].h, B0["ld"] + tb * 128)
                if tb >= 2:
                    sy.wait_ge(sems["qm"].h, qm_at[(tb - 1, 1, 7)])
                for it in range(8):
                    sy.dma_start(out=xsl[tb % 2][it][:],
                                 in_=xnt_dram[it * 128:(it + 1) * 128,
                                              b * S + tb * 512: b * S + (tb + 1) * 512]
                                 ).then_inc(sems["ld"].h, 16)

        @block.tensor
        def _(t):
            for tb in range(4):
                t.wait_ge(sems["ld"].h, B0["ld"] + (tb + 1) * 128)
                for p in range(2):
                    W = WQfm if p == 0 else WKfm
                    for ot in range(8):
                        gi = tb * 16 + p * 8 + ot
                        if gi >= 2:
                            pk = gi - 2
                            pot = pk % 8
                            pp = (pk // 8) % 2
                            ptb = pk // 16
                            t.wait_ge(sems["f4"].h, f4_at[(ptb, pp, pot)])
                        for it in range(8):
                            ins = t.matmul(ps01[gi % 2][:],
                                           fm_slice(W, 1024, it * 128, 128, ot * 128, 128),
                                           xsl[tb % 2][it][:],
                                           start=(it == 0), stop=(it == 7))
                        ins.then_inc(sems["qm"].h, 1)
                for tt4 in range(4):
                    t.wait_ge(sems["f4"].h, f4_at[(tb, 1, 7)])
                    if tb * 4 + tt4 >= 2:
                        i2 = tb * 4 + tt4 - 2
                        t.wait_ge(sems["te"].h, te_at[(i2 // 4, i2 % 4)])
                    tok0 = tb * 512 + tt4 * 128
                    for ft in range(8):
                        ins = t.transpose(ptr[tt4 % 2][:, ft * 128:(ft + 1) * 128],
                                          fm_slice(KTf, 2048, ft * 128, 128, tok0, 128),
                                          ident[:])
                    ins.then_inc(sems["tm"].h, 1)

        @block.vector
        def _(v):
            for tb in range(4):
                for p in range(2):
                    bpp = bq_pp if p == 0 else bk_pp
                    dstT = QTf if p == 0 else KTf
                    for ot in range(8):
                        gi = tb * 16 + p * 8 + ot
                        v.wait_ge(sems["qm"].h, qm_at[(tb, p, ot)])
                        if gi >= 1:
                            v.wait_ge(sems["f4"].h, B0["f4"] + gi)
                            v.wait_ge(sems["f3"].h, B0["f3"] + gi)
                        v.tensor_scalar(tmn[:], ps01[gi % 2][:], bpp[:, ot:ot + 1], 0.0,
                                        ALU.add, ALU.min).then_inc(sems["f1"].h, 1)
                        v.tensor_scalar(tmr[:], ps01[gi % 2][:], bpp[:, ot:ot + 1], 0.0,
                                        ALU.add, ALU.max).then_inc(sems["f2"].h, 1)
                        v.wait_ge(sems["f3"].h, B0["f3"] + gi + 1)
                        v.wait_ge(sems["f2"].h, B0["f2"] + gi + 1)
                        v.tensor_add(fm_slice(dstT, 2048, ot * 128, 128, tb * 512, 512),
                                     tme[:], tmr[:]).then_inc(sems["f4"].h, 1)
                for tt4 in range(4):
                    v.wait_ge(sems["tm"].h, tm_at[(tb, tt4)])
                    tok0 = tb * 512 + tt4 * 128
                    v.tensor_copy(
                        KNat[:, (tb * 4 + tt4) * 1024:(tb * 4 + tt4 + 1) * 1024],
                        ptr[tt4 % 2][:]).then_inc(sems["te"].h, 1)

        @block.scalar
        def _(s):
            for tb in range(4):
                for p in range(2):
                    for ot in range(8):
                        gi = tb * 16 + p * 8 + ot
                        s.wait_ge(sems["f1"].h, B0["f1"] + gi + 1)
                        if gi >= 1:
                            s.wait_ge(sems["f4"].h, B0["f4"] + gi)
                        s.activation(tme[:], tmn[:], AF.Exp).then_inc(sems["f3"].h, 1)

    for k in _n:
        if k == "ld":
            sems[k].v = B0[k] + 4 * 128
        elif k in ("f1", "f2", "f3"):
            sems[k].v = B0[k] + 64
        elif k == "qm":
            sems[k].v = c["qm"]
        elif k == "f4":
            sems[k].v = c["f4"]
        elif k in ("tm", "te"):
            sems[k].v = c[k]
    ctx.barrier()
    les.close()


def emit_l3_scan(ctx, QTf, KTf, KNat, VA, ONAT):
    nc = ctx.nc
    u = ctx.uid()
    les = contextlib.ExitStack()
    _n = ["c1", "a1", "a2", "n1", "s1", "d1", "d2", "d3"]
    sems = dict(zip(_n, ctx.pool(8)))
    B0 = {k: s.v for k, s in sems.items()}
    Sprev = [ctx.sb(f"scS{u}_{i}", [128, 65], BF16, es=les) for i in range(2)]
    Amb = [ctx.sb(f"scA{u}_{i}", [128, 128], BF16, es=les) for i in range(2)]
    dnr = ctx.sb(f"scD{u}", [128, 2], F32, es=les)
    psS = [ctx.ps_(f"scPS{u}_{i}", [128, 512], F32, es=les) for i in range(2)]
    psA = [ctx.ps_(f"scPA{u}_{i}", [128, 512], F32, es=les) for i in range(2)]
    psN = [ctx.ps_(f"scPN{u}_{i}", [128, 512], F32, es=les) for i in range(2)]
    cmask = ctx.consts["c_causincl"]
    NHD = 16
    NCH = 16
    EPS3 = 1e-6

    c = {k: B0[k] for k in _n}
    at = {k: {} for k in _n}
    for h in range(NHD):
        for j in range(NCH):
            for k in ("c1", "a1", "a2", "n1", "s1", "d1", "d2", "d3"):
                c[k] += 1
                at[k][(h, j)] = c[k]

    with nc.Block() as block:
        @block.vector
        def _(v):
            for h in range(NHD):
                for j in range(NCH):
                    sp = Sprev[j % 2][64 * (h % 2):64 * (h % 2) + 64, :]
                    spo = Sprev[(j + 1) % 2][64 * (h % 2):64 * (h % 2) + 64, :]
                    if j == 0:
                        if h > 0:
                            v.wait_ge(sems["n1"].h, at["n1"][(h - 1, NCH - 1)])
                        v.memset(sp, 0.0).then_inc(sems["c1"].h, 1)
                    else:
                        v.wait_ge(sems["s1"].h, at["s1"][(h, j - 1)])
                        v.wait_ge(sems["n1"].h, at["n1"][(h, j - 1)])
                        v.tensor_add(sp, psS[(j - 1) % 2][0:64, 0:65],
                                     spo).then_inc(sems["c1"].h, 1)
                    v.wait_ge(sems["a1"].h, at["a1"][(h, j)])
                    gia = h * NCH + j
                    if gia >= 2:
                        p2 = gia - 2
                        v.wait_ge(sems["n1"].h, at["n1"][(p2 // NCH, p2 % NCH)])
                    v.tensor_mul(Amb[j % 2][:], psA[j % 2][0:128, 0:128],
                                 cmask[:]).then_inc(sems["a2"].h, 1)
                    v.wait_ge(sems["n1"].h, at["n1"][(h, j)])
                    gi = h * NCH + j
                    if gi >= 1:
                        p1 = gi - 1
                        v.wait_ge(sems["d2"].h, at["d2"][(p1 // NCH, p1 % NCH)])
                    v.tensor_scalar_add(dnr[:, 0:1], psN[j % 2][0:128, 64:65],
                                        EPS3).then_inc(sems["d1"].h, 1)
                    v.wait_ge(sems["d1"].h, at["d1"][(h, j)])
                    if gi >= 1:
                        p1 = gi - 1
                        v.wait_ge(sems["d3"].h, at["d3"][(p1 // NCH, p1 % NCH)])
                    v.reciprocal(dnr[:, 1:2], dnr[:, 0:1]).then_inc(sems["d2"].h, 1)
                    v.wait_ge(sems["d2"].h, at["d2"][(h, j)])
                    v.tensor_scalar_mul(ONAT[:, j * 1024 + 64 * h: j * 1024 + 64 * h + 64],
                                        psN[j % 2][0:128, 0:64],
                                        dnr[:, 1:2]).then_inc(sems["d3"].h, 1)

        @block.tensor
        def _(t):
            for h in range(NHD):
                for j in range(NCH):
                    gi = h * NCH + j
                    if gi >= 2:
                        p2 = gi - 2
                        t.wait_ge(sems["a2"].h, at["a2"][(p2 // NCH, p2 % NCH)])
                    t.matmul(psA[j % 2][0:128, 0:128],
                             fm_slice(KTf, 2048, 64 * h, 64, j * 128, 128),
                             fm_slice(QTf, 2048, 64 * h, 64, j * 128, 128),
                             start=True, stop=True).then_inc(sems["a1"].h, 1)
                    t.wait_ge(sems["a2"].h, at["a2"][(h, j)])
                    t.wait_ge(sems["c1"].h, at["c1"][(h, j)])
                    if gi >= 2:
                        p2 = gi - 2
                        t.wait_ge(sems["d3"].h, at["d3"][(p2 // NCH, p2 % NCH)])
                    t.matmul(psN[j % 2][0:128, 0:65],
                             Amb[j % 2][:],
                             VA[:, j * 1040 + 65 * h: j * 1040 + 65 * h + 65],
                             start=True, stop=False)
                    t.matmul(psN[j % 2][0:128, 0:65],
                             fm_slice(QTf, 2048, 64 * h, 64, j * 128, 128),
                             Sprev[j % 2][64 * (h % 2):64 * (h % 2) + 64, :],
                             start=False, stop=True).then_inc(sems["n1"].h, 1)
                    if h * NCH + j >= 2:
                        p2 = h * NCH + j - 2
                        t.wait_ge(sems["c1"].h, at["c1"][(p2 // NCH, p2 % NCH)] + 1
                                  if False else at["c1"][((h * NCH + j - 1) // NCH,
                                                          (h * NCH + j - 1) % NCH)])
                    t.matmul(psS[j % 2][0:64, 0:65],
                             KNat[:, j * 1024 + 64 * h: j * 1024 + 64 * h + 64],
                             VA[:, j * 1040 + 65 * h: j * 1040 + 65 * h + 65],
                             start=True, stop=True).then_inc(sems["s1"].h, 1)

    for k in _n:
        sems[k].v = c[k]
    ctx.barrier()
    les.close()


def emit_l3_proj(ctx, ONAT, WOfm, bo_rep, b, acc_dram):
    nc = ctx.nc
    u = ctx.uid()
    les = contextlib.ExitStack()
    (tm, te, pm, pe, st) = ctx.pool(5)
    OTt = ctx.sb(f"pjO{u}", [128, 1024], BF16, es=les)
    accst = [ctx.sb(f"pjA{u}_{i}", [128, DM], F32, es=les) for i in range(2)]
    ptr = [ctx.ps_(f"pjT{u}_{i}", [128, 1024], BF16, es=les) for i in range(2)]
    pproj = [ctx.ps_(f"pjP{u}_{i}", [128, 1024], F32, es=les) for i in range(2)]
    ident = ctx.consts["c_ident"]
    b_ = {s: s.v for s in (tm, te, pm, pe, st)}
    NCH = 16
    with nc.Block() as block:
        @block.tensor
        def _(t):
            for j in range(NCH):
                if j >= 2:
                    t.wait_ge(te.h, b_[te] + j - 1)
                for ft in range(8):
                    ins = t.transpose(ptr[j % 2][:, ft * 128:(ft + 1) * 128],
                                      ONAT[:, j * 1024 + ft * 128: j * 1024 + (ft + 1) * 128],
                                      ident[:])
                ins.then_inc(tm.h, 1)
                t.wait_ge(te.h, b_[te] + j + 1)
                if j >= 2:
                    t.wait_ge(pe.h, b_[pe] + j - 1)
                for et in range(8):
                    for half in range(2):
                        ins = t.matmul(pproj[j % 2][:, half * 512:(half + 1) * 512],
                                       OTt[:, et * 128:(et + 1) * 128],
                                       fm_slice(WOfm, 1024, et * 128, 128, half * 512, 512),
                                       start=(et == 0), stop=(et == 7))
                ins.then_inc(pm.h, 1)

        @block.vector
        def _(v):
            for j in range(NCH):
                v.wait_ge(tm.h, b_[tm] + j + 1)
                if j >= 1:
                    v.wait_ge(pm.h, b_[pm] + j)
                v.tensor_copy(OTt[:], ptr[j % 2][:]).then_inc(te.h, 1)
                v.wait_ge(pm.h, b_[pm] + j + 1)
                if j >= 2:
                    v.wait_ge(st.h, b_[st] + (j - 1) * 16)
                v.tensor_add(accst[j % 2][:], pproj[j % 2][:], bo_rep[:]).then_inc(pe.h, 1)

        @block.sync
        def _(sy):
            for j in range(NCH):
                sy.wait_ge(pe.h, b_[pe] + j + 1)
                g = b * S + j * 128
                sy.dma_start(out=acc_dram[g:g + 128, :], in_=accst[j % 2][:]).then_inc(st.h, 16)
                sy.wait_ge(st.h, b_[st] + (j + 1) * 16)

    tm.v = b_[tm] + NCH
    te.v = b_[te] + NCH
    pm.v = b_[pm] + NCH
    pe.v = b_[pe] + NCH
    st.v = b_[st] + NCH * 16
    ctx.barrier()
    les.close()


def emit_l3_gate(ctx, xnt_dram, wgred_dram, wg_dram, MSK, SCO, COEF):
    nc = ctx.nc
    u = ctx.uid()
    les = contextlib.ExitStack()
    _n = ["ld", "w1", "w2", "w3", "w4", "w5", "xm", "xe", "lm", "e1", "e2",
          "e3", "e4", "e5", "lw"]
    sems = dict(zip(_n, ctx.pool(15)))
    B0 = {k: s.v for k, s in sems.items()}
    WGR = ctx.sb(f"g3r{u}", [16, 1024], F32, es=les)
    WGRT = ctx.sb(f"g3t{u}", [128, 8 * 16], BF16, es=les)
    WGU = ctx.sb(f"g3u{u}", [8, 16], F32, es=les)
    WGUt = ctx.sb(f"g3v{u}", [16, 8], BF16, es=les)
    wgs = ctx.sb(f"g3w{u}", [8, 16], F32, es=les)
    n2 = ctx.sb(f"g3n{u}", [8, 3], F32, es=les)
    xrT = ctx.sb(f"g3x{u}", [16, 512], BF16, es=les)
    lg = ctx.sb(f"g3l{u}", [128, 8], F32, es=les)
    mx = ctx.sb(f"g3m{u}", [128, 2], F32, es=les)
    ee = ctx.sb(f"g3e{u}", [128, 8], F32, es=les)
    sume = ctx.sb(f"g3s{u}", [128, 1], F32, es=les)
    xsl = [[ctx.sb(f"g3a{u}_{s}_{i}", [128, 512], BF16, es=les) for i in range(8)]
           for s in range(2)]
    pg = [ctx.ps_(f"g3p{u}_{i}", [128, 512], F32, es=les) for i in range(2)]
    identf = ctx.consts["c_identf"]
    ident = ctx.consts["c_ident"]
    NB = T // 512

    with nc.Block() as block:
        @block.sync
        def _(sy):
            sy.dma_start(out=WGR[:], in_=wgred_dram).then_inc(sems["lw"].h, 16)
            sy.dma_start(out=wgs[:], in_=wg_dram).then_inc(sems["lw"].h, 16)
            for nb in range(NB):
                if nb >= 1:
                    sy.wait_ge(sems["ld"].h, B0["ld"] + nb * 128)
                if nb >= 2:
                    sy.wait_ge(sems["xm"].h, B0["xm"] + nb - 1)
                for it in range(8):
                    sy.dma_start(out=xsl[nb % 2][it][:],
                                 in_=xnt_dram[it * 128:(it + 1) * 128,
                                              nb * 512:(nb + 1) * 512]).then_inc(sems["ld"].h, 16)

        @block.tensor
        def _(t):
            t.wait_ge(sems["lw"].h, B0["lw"] + 32)
            for cb in range(8):
                ins = t.transpose(pg[0][:, cb * 16:(cb + 1) * 16],
                                  WGR[:, cb * 128:(cb + 1) * 128],
                                  identf[0:16, 0:16])
            ins.then_inc(sems["w1"].h, 1)
            t.wait_ge(sems["w4"].h, B0["w4"] + 1)
            t.transpose(pg[1][0:16, 0:8], WGU[:], identf[0:8, 0:8]).then_inc(sems["w5"].h, 1)
            t.wait_ge(sems["w2"].h, B0["w2"] + 2)
            for nb in range(NB):
                t.wait_ge(sems["ld"].h, B0["ld"] + (nb + 1) * 128)
                if nb >= 1:
                    t.wait_ge(sems["xe"].h, B0["xe"] + nb)
                for it in range(8):
                    ins = t.matmul(pg[0][0:16, :],
                                   fm_slice(WGRT, 16, it * 128, 128, 0, 16),
                                   xsl[nb % 2][it][:],
                                   start=(it == 0), stop=(it == 7))
                ins.then_inc(sems["xm"].h, 1)
                for q in range(4):
                    i = nb * 4 + q
                    t.wait_ge(sems["xe"].h, B0["xe"] + nb + 1)
                    if i >= 1:
                        t.wait_ge(sems["e1"].h, B0["e1"] + i)
                    t.matmul(pg[1][0:128, 0:8], xrT[:, q * 128:(q + 1) * 128],
                             WGUt[:], start=True, stop=True).then_inc(sems["lm"].h, 1)

        @block.vector
        def _(v):
            v.wait_ge(sems["w1"].h, B0["w1"] + 1)
            for cb in range(8):
                ins = v.tensor_copy(WGRT[:, cb * 16:(cb + 1) * 16],
                                    pg[0][:, cb * 16:(cb + 1) * 16])
            ins.then_inc(sems["w2"].h, 1)
            v.wait_ge(sems["lw"].h, B0["lw"] + 32)
            v.tensor_mul(WGU[:], wgs[:], wgs[:]).then_inc(sems["w3"].h, 1)
            v.wait_ge(sems["w3"].h, B0["w3"] + 1)
            v.reduce_sum(n2[:, 0:1], WGU[:], axis=AXX).then_inc(sems["e5"].h, 1)
            v.wait_ge(sems["w3"].h, B0["w3"] + 2)
            v.tensor_scalar_max(n2[:, 1:2], n2[:, 1:2], 1e-4).then_inc(sems["w3"].h, 1)
            v.wait_ge(sems["w3"].h, B0["w3"] + 3)
            v.reciprocal(n2[:, 2:3], n2[:, 1:2]).then_inc(sems["w3"].h, 1)
            v.wait_ge(sems["w3"].h, B0["w3"] + 4)
            v.tensor_scalar_mul(WGU[:], wgs[:], n2[:, 2:3]).then_inc(sems["w4"].h, 1)
            v.wait_ge(sems["w5"].h, B0["w5"] + 1)
            v.tensor_copy(WGUt[:], pg[1][0:16, 0:8]).then_inc(sems["w2"].h, 1)
            for nb in range(NB):
                for q in range(4):
                    i = nb * 4 + q
                    v.wait_ge(sems["lm"].h, B0["lm"] + i + 1)
                    if i >= 1:
                        v.wait_ge(sems["e3"].h, B0["e3"] + i)
                    v.tensor_copy(lg[:], pg[1][0:128, 0:8]).then_inc(sems["e1"].h, 1)
                    v.wait_ge(sems["e1"].h, B0["e1"] + i + 1)
                    v.reduce_max(mx[:, 0:1], lg[:], axis=AXX).then_inc(sems["e5"].h, 1)
                    v.wait_ge(sems["e5"].h, B0["e5"] + 1 + i + 1)
                    v.tensor_scalar_mul(mx[:, 1:2], mx[:, 0:1], -1.0).then_inc(sems["e2"].h, 1)
                    v.wait_ge(sems["e3"].h, B0["e3"] + i + 1)
                    v.tensor_single_scalar(MSK[:, i * 8:(i + 1) * 8], ee[:], 1.0,
                                           ALU.is_ge).then_inc(sems["e4"].h, 1)
                    with nc.allow_low_precision(reason="scores recip"):
                        v.reciprocal(SCO[:, i:i + 1], sume[:]).then_inc(sems["e4"].h, 1)
                    v.wait_ge(sems["e4"].h, B0["e4"] + 3 * i + 2)
                    v.tensor_scalar_mul(COEF[:, i * 8:(i + 1) * 8],
                                        MSK[:, i * 8:(i + 1) * 8],
                                        SCO[:, i:i + 1]).then_inc(sems["e4"].h, 1)

        @block.scalar
        def _(s):
            s.wait_ge(sems["e5"].h, B0["e5"] + 1)
            s.activation(n2[:, 1:2], n2[:, 0:1], AF.Sqrt).then_inc(sems["w3"].h, 1)
            for nb in range(NB):
                s.wait_ge(sems["xm"].h, B0["xm"] + nb + 1)
                if nb >= 1:
                    s.wait_ge(sems["lm"].h, B0["lm"] + nb * 4)
                s.activation(xrT[:], pg[0][0:16, :], AF.Identity).then_inc(sems["xe"].h, 1)
                for q in range(4):
                    i = nb * 4 + q
                    s.wait_ge(sems["e2"].h, B0["e2"] + i + 1)
                    if i >= 1:
                        s.wait_ge(sems["e4"].h, B0["e4"] + 3 * i)
                    s.activation(ee[:], lg[:], AF.Exp, bias=mx[:, 1:2],
                                 accum_out=sume[:]).then_inc(sems["e3"].h, 1)

    sems["ld"].v = B0["ld"] + NB * 128
    sems["lw"].v = B0["lw"] + 32
    sems["w1"].v = B0["w1"] + 1
    sems["w2"].v = B0["w2"] + 2
    sems["w3"].v = B0["w3"] + 4
    sems["w4"].v = B0["w4"] + 1
    sems["w5"].v = B0["w5"] + 1
    sems["xm"].v = B0["xm"] + NB
    sems["xe"].v = B0["xe"] + NB
    sems["lm"].v = B0["lm"] + NB * 4
    sems["e1"].v = B0["e1"] + NT
    sems["e2"].v = B0["e2"] + NT
    sems["e3"].v = B0["e3"] + NT
    sems["e4"].v = B0["e4"] + NT * 3
    sems["e5"].v = B0["e5"] + 1 + NT
    ctx.barrier()
    les.close()


L3_WNAMES = ["L3_attn_wq", "L3_attn_bq", "L3_attn_wk", "L3_attn_bk",
             "L3_attn_wv", "L3_attn_bv", "L3_attn_wo", "L3_attn_bo",
             "L3_ffn_wg_red", "L3_ffn_wg", "L3_ffn_w1", "L3_ffn_b1",
             "L3_ffn_w2", "L3_ffn_b2"]


def emit_l3_layer(ctx, x_dram, xout_dram, W, upto="all", dbg=None):
    xnt = ctx.dram("l3_xnt", [DM, T], BF16)
    acc = dbg["acc"] if dbg and "acc" in dbg else ctx.dram("l3_acc", [T, DM], F32)
    macc = ctx.dram("l3_macc", [T, DM], F32)

    emit_norm(ctx, x_dram, xnt)
    if upto == "norm":
        return
    lay = contextlib.ExitStack()
    QTf = ctx.sb("l3_qtf", [128, 8 * 2048], BF16, es=lay)
    KTf = ctx.sb("l3_ktf", [128, 8 * 2048], BF16, es=lay)
    KNat = ctx.sb("l3_kn", [128, 16 * 1024], BF16, es=lay)
    for b in range(BT):
        les = contextlib.ExitStack()
        WQfm = ctx.sb(f"l3_wq{b}", [128, 8 * 1024], BF16, es=les)
        WKfm = ctx.sb(f"l3_wk{b}", [128, 8 * 1024], BF16, es=les)
        bqp = ctx.sb(f"l3_bq{b}", [128, 8], F32, es=les)
        bkp = ctx.sb(f"l3_bk{b}", [128, 8], F32, es=les)
        emit_weight_fm(ctx, WQfm, W["L3_attn_wq"], 1024, 1024, transpose=False)
        emit_weight_fm(ctx, WKfm, W["L3_attn_wk"], 1024, 1024, transpose=False)
        emit_load_pp(ctx, bqp, W["L3_attn_bq"], 8)
        emit_load_pp(ctx, bkp, W["L3_attn_bk"], 8)
        emit_l3_qk(ctx, xnt, b, WQfm, WKfm, bqp, bkp, QTf, KTf, KNat)
        les.close()
        les = contextlib.ExitStack()
        VA = ctx.sb(f"l3_va{b}", [128, 16 * 1040], BF16, es=les)
        lesv = contextlib.ExitStack()
        WVfm = ctx.sb(f"l3_wv{b}", [128, 8 * 1024], BF16, es=lesv)
        bvr = ctx.sb(f"l3_bv{b}", [128, DM], F32, es=lesv)
        emit_weight_fm(ctx, WVfm, W["L3_attn_wv"], 1024, 1024, transpose=False)
        emit_replicate_rows(ctx, bvr, W["L3_attn_bv"], 1024)
        emit_l0_qkv(ctx, xnt, b, WVfm, None, VA, WC=1024, parts=2,
                    wv_spec=(WVfm, 1024, 0), bv_rep=bvr)
        lesv.close()
        ONAT = ctx.sb(f"l3_on{b}", [128, 16 * 1024], BF16, es=les)
        emit_l3_scan(ctx, QTf, KTf, KNat, VA, ONAT)
        lesp = contextlib.ExitStack()
        WOfm = ctx.sb(f"l3_wo{b}", [128, 8 * 1024], BF16, es=lesp)
        bor = ctx.sb(f"l3_bo{b}", [128, DM], F32, es=lesp)
        emit_weight_fm(ctx, WOfm, W["L3_attn_wo"], 1024, 1024, transpose=False)
        emit_replicate_rows(ctx, bor, W["L3_attn_bo"], 1024)
        emit_l3_proj(ctx, ONAT, WOfm, bor, b, acc)
        lesp.close()
        les.close()
    lay.close()
    if upto == "attn":
        return
    lay = contextlib.ExitStack()
    MSK = ctx.sb("l3_msk", [128, NT * 8], F32, es=lay)
    SCO = ctx.sb("l3_sco", [128, NT], F32, es=lay)
    COEF = ctx.sb("l3_coef", [128, NT * 8], F32, es=lay)
    emit_l3_gate(ctx, xnt, W["L3_ffn_wg_red"], W["L3_ffn_wg"], MSK, SCO, COEF)
    if upto == "gate":
        lay.close()
        return
    for e in range(8):
        lese = contextlib.ExitStack()
        W1fm = ctx.sb(f"l3_e{e}w1", [128, 8 * 4096], BF16, es=lese)
        W2fm = ctx.sb(f"l3_e{e}w2", [128, 32 * 1024], BF16, es=lese)
        b1pp = ctx.sb(f"l3_e{e}b1", [128, 32], F32, es=lese)
        B2e = ctx.sb(f"l3_e{e}b2", [128, DM], F32, es=lese)
        emit_weight_fm(ctx, W1fm, W["L3_ffn_w1"][e], 1024, 4096, transpose=False)
        emit_weight_fm(ctx, W2fm, W["L3_ffn_w2"][e], 4096, 1024, transpose=False)
        emit_load_pp(ctx, b1pp, W["L3_ffn_b1"][e], 32)
        emit_replicate_rows(ctx, B2e, W["L3_ffn_b2"][e], 1024)
        emit_moe_expert(ctx, xnt, W1fm, W2fm, b1pp, B2e, COEF, e, 8, macc,
                        first=(e == 0), last=(e == 7),
                        final_addends=[x_dram, acc],
                        xout_dram=xout_dram, act=AF.Gelu)
        lese.close()
    lay.close()


def build_l3(upto="all", debug=False):
    ctx = Ctx()
    xo_in = ctx.dram("xo_in", [T, DM], F32, kind="ExternalInput")
    o_in = ctx.dram("o_in", [T, DM], F32, kind="ExternalInput")
    x_out = ctx.dram("x_out", [T, DM], F32, kind="ExternalOutput")
    shapes = {"L3_attn_wq": (1024, 1024), "L3_attn_bq": (1024,),
              "L3_attn_wk": (1024, 1024), "L3_attn_bk": (1024,),
              "L3_attn_wv": (1024, 1024), "L3_attn_bv": (1024,),
              "L3_attn_wo": (1024, 1024), "L3_attn_bo": (1024,),
              "L3_ffn_wg_red": (16, 1024), "L3_ffn_wg": (8, 16),
              "L3_ffn_w1": (8, 1024, 4096), "L3_ffn_b1": (8, 4096),
              "L3_ffn_w2": (8, 4096, 1024), "L3_ffn_b2": (8, 1024)}
    W = {nm: ctx.dram(nm, list(shapes[nm]), F32, kind="ExternalInput")
         for nm in L3_WNAMES}
    ctx.load_consts()
    x3 = ctx.dram("l3_x3", [T, DM], F32)
    emit_add(ctx, xo_in, o_in, x3)
    dbg = None
    if debug:
        dbg = {"acc": ctx.dram("dbg_acc", [T, DM], F32, kind="ExternalOutput")}
    emit_l3_layer(ctx, x3, x_out, W, upto=upto, dbg=dbg)
    ctx.close()
    return ctx



L0_WNAMES = ["L0_attn_w_qkv", "L0_attn_w_proj", "L0_ffn_w1", "L0_ffn_b1",
             "L0_ffn_w2", "L0_ffn_b2"]
L2A_WNAMES = ["L2_ffn_W_r", "L2_ffn_W_k", "L2_ffn_W_v", "L2_ffn_mu_r", "L2_ffn_mu_k"]
L2B_WNAMES = ["L2_attn_Wq", "L2_attn_bq", "L2_attn_Wk", "L2_attn_bk",
              "L2_attn_Wv", "L2_attn_bv"]

_BUILD_CACHE = {}


def _get(name, fn):
    if name not in _BUILD_CACHE:
        _BUILD_CACHE[name] = fn()
    return _BUILD_CACHE[name]


def kernel(**inputs):
    consts = host_constants()
    x = np.ascontiguousarray(inputs["L0_attn_x"].reshape(16, 2048, 1024))

    def launch(ctx, per_core_extra, wnames):
        in_maps = []
        for c in range(NCORE):
            m = dict(consts)
            for nm in wnames:
                m[nm] = inputs[nm]
            m.update(per_core_extra[c])
            in_maps.append(m)
        return run_bass_kernel_spmd(ctx.nc, in_maps, core_ids=list(range(NCORE)))

    ctx = _get("l0", build_l0)
    res = launch(ctx, [{"x_in": x[2 * c:2 * c + 2].reshape(T, DM)} for c in range(NCORE)],
                 L0_WNAMES)
    x1 = [np.asarray(res.results[c]["x_out"]) for c in range(NCORE)]
    ctx = _get("l1", lambda: build_l1())
    res = launch(ctx, [{"x_in": x1[c]} for c in range(NCORE)], L1_WNAMES)
    x2 = [np.asarray(res.results[c]["x_out"]) for c in range(NCORE)]
    ctx = _get("l2a", build_l2a)
    res = launch(ctx, [{"x_in": x2[c]} for c in range(NCORE)], L2A_WNAMES)
    xo = [np.asarray(res.results[c]["xo_out"]) for c in range(NCORE)]
    xnt = [np.asarray(res.results[c]["xnt_out"]) for c in range(NCORE)]
    xnt_all = np.stack(xnt)
    xnt_all = xnt_all.reshape(8, DM, 2, S)
    xpm_full = xnt_all.transpose(1, 3, 0, 2).reshape(DM, S, 16)
    xpm = [np.ascontiguousarray(
        xpm_full[:, c * PSL:(c + 1) * PSL, :].reshape(DM, T))
        for c in range(NCORE)]
    ctx = _get("l2b", build_l2b)
    res = launch(ctx, [{"xpm_in": xpm[c]} for c in range(NCORE)], L2B_WNAMES)
    o_pm = [np.asarray(res.results[c]["o_out"]) for c in range(NCORE)]
    o_all = np.stack(o_pm).reshape(8, PSL, 16, DM)
    o_all = o_all.transpose(2, 0, 1, 3).reshape(16, S, DM)
    ctx = _get("l3", lambda: build_l3())
    res = launch(ctx, [{"xo_in": xo[c],
                        "o_in": o_all[2 * c:2 * c + 2].reshape(T, DM)}
                       for c in range(NCORE)], L3_WNAMES)
    x4 = [np.asarray(res.results[c]["x_out"]) for c in range(NCORE)]
    out = np.stack(x4).reshape(16, 2048, 1024).astype(np.float32, copy=False)
    return out


# version 03
# speedup vs baseline: 1.0684x; 1.0684x over previous
import contextlib
import numpy as np
import ml_dtypes

import concourse.bass as bass
import concourse.mybir as mybir
from concourse.bass_utils import run_bass_kernel_spmd

F32 = mybir.dt.float32
BF16 = mybir.dt.bfloat16
AF = mybir.ActivationFunctionType
ALU = mybir.AluOpType
AXX = mybir.AxisListType.X

NCORE = 8
BT = 2
S = 2048
DM = 1024
T = BT * S
NT = T // 128
EPS_LN = 1e-5

BF = ml_dtypes.bfloat16



def host_constants():
    c = {}
    c["c_ident"] = np.eye(128, dtype=BF)
    c["c_identf"] = np.eye(128, dtype=np.float32)
    kp = np.arange(128)[:, None]
    ql = np.arange(512)[None, :]
    caus = np.concatenate([(kp + 128 * dt <= ql).astype(BF) for dt in range(4)], axis=1)
    c["c_caus"] = caus
    c["c_ones"] = np.ones((1, 512), dtype=BF)
    pos = np.arange(S, dtype=np.float64)
    inv = 1.0 / (10000.0 ** (np.arange(0, 64, 2, dtype=np.float64) / 64.0))
    ang = pos[:, None] * inv[None, :]
    cosv = np.cos(ang)
    sinv = np.sin(ang)
    cos_full = np.concatenate([cosv, cosv], axis=1)
    sin_full = np.concatenate([-sinv, sinv], axis=1)
    c["c_ropecos"] = np.tile(cos_full, (1, 16)).astype(np.float32)
    c["c_ropesin"] = np.tile(sin_full, (1, 16)).astype(np.float32)
    tau = np.arange(128)[:, None]
    tt = np.arange(128)[None, :]
    c["c_causincl"] = (tau <= tt).astype(BF)
    bd = np.zeros((128, 128), dtype=BF)
    for j in range(8):
        bd[16 * j:16 * j + 16, 16 * j:16 * j + 16] = 1
    c["c_blockdiag"] = bd
    return c


CONST_SPECS = [
    ("c_ident", (128, 128), BF16),
    ("c_identf", (128, 128), F32),
    ("c_caus", (128, 2048), BF16),
    ("c_ones", (1, 512), BF16),
    ("c_causincl", (128, 128), BF16),
    ("c_blockdiag", (128, 128), BF16),
]
DRAM_CONST_SPECS = [
    ("c_ropecos", (S, DM), F32),
    ("c_ropesin", (S, DM), F32),
]



class Sem:
    __slots__ = ("h", "v")

    def __init__(self, h):
        self.h = h
        self.v = 0


class Ctx:
    def __init__(self):
        self.nc = bass.Bass("TRN2", target_bir_lowering=False)
        self.es = contextlib.ExitStack()
        self.sems = {}
        self.consts = {}
        self._uid = 0

    def uid(self):
        self._uid += 1
        return self._uid

    def sem(self, name):
        if name not in self.sems:
            self.sems[name] = Sem(self.es.enter_context(self.nc.semaphore(name)))
        return self.sems[name]

    def pool(self, k):
        return [self.sem(f"p{i}") for i in range(k)]

    def gpool(self, k):
        return [self.sem(f"g{i}") for i in range(k)]

    def sb(self, name, shape, dtype, es=None):
        return (es or self.es).enter_context(self.nc.sbuf_tensor(name, shape, dtype))

    def ps_(self, name, shape, dtype=F32, es=None):
        return (es or self.es).enter_context(self.nc.psum_tensor(name, shape, dtype))

    def ps(self, name, shape, dtype=F32):
        return self.es.enter_context(self.nc.psum_tensor(name, shape, dtype))

    def dram(self, name, shape, dtype, kind="Internal"):
        return self.nc.dram_tensor(name, list(shape), dtype, kind=kind).ap()

    def barrier(self):
        self.nc.all_engine_barrier()

    def close(self):
        self.es.close()

    def load_consts(self):
        ld = self.sem("c_ld")
        aps = {}
        for name, shape, dtype in CONST_SPECS:
            self.consts[name] = self.sb("sb_" + name, list(shape), dtype)
            aps[name] = self.dram(name, shape, dtype, kind="ExternalInput")
        self.dram_consts = {}
        for name, shape, dtype in DRAM_CONST_SPECS:
            self.dram_consts[name] = self.dram(name, shape, dtype, kind="ExternalInput")
        with self.nc.Block() as block:
            @block.sync
            def _(sy):
                for name, shape, dtype in CONST_SPECS:
                    sy.dma_start(out=self.consts[name][:], in_=aps[name]).then_inc(ld.h, 16)
                    ld.v += 16
                sy.wait_ge(ld.h, ld.v)
        self.barrier()


def fm_slice(tensor, C, f0, nf, c0, ncol):
    ti = f0 // 128
    p0 = f0 % 128
    assert p0 + nf <= 128, (f0, nf)
    return tensor[p0:p0 + nf, ti * C + c0: ti * C + c0 + ncol]



def emit_weight_fm(ctx, dst, w_dram, NF, C, transpose):
    nc = ctx.nc
    u = ctx.uid()
    mm, ev = ctx.pool(2)
    ldp = ctx.gpool(2)
    ld = ldp[0]
    les = contextlib.ExitStack()
    if not transpose:
        nrt = NF // 128
        with nc.Block() as block:
            @block.gpsimd
            def _(gp):
                for r in range(nrt):
                    gp.dma_start(out=fm_slice(dst, C, r * 128, 128, 0, C),
                                 in_=w_dram[r * 128:(r + 1) * 128, :]).then_inc(ld.h, 16)
                gp.wait_ge(ld.h, ld.v + nrt * 16)
        ld.v += nrt * 16
    else:
        stg = [ctx.sb(f"wstg{u}_{i}", [128, NF], BF16, es=les) for i in range(2)]
        ptr = [ctx.ps_(f"wps{u}_{i}", [128, 512], BF16, es=les) for i in range(2)]
        nrow_t = C // 128
        ncol_t = NF // 128
        ident = ctx.consts["c_ident"]
        ld0 = [ldp[0].v, ldp[1].v]
        mm0, ev0 = mm.v, ev.v
        with nc.Block() as block:
            @block.gpsimd
            def _(gp):
                for r in range(nrow_t):
                    if r >= 2:
                        gp.wait_ge(mm.h, mm0 + (r - 1) * ncol_t)
                    gp.dma_start(out=stg[r % 2][:, :],
                                 in_=w_dram[r * 128:(r + 1) * 128, :]).then_inc(ldp[r % 2].h, 16)

            @block.tensor
            def _(t):
                for r in range(nrow_t):
                    t.wait_ge(ldp[r % 2].h, ld0[r % 2] + (r // 2 + 1) * 16)
                    for cb in range(ncol_t):
                        i = r * ncol_t + cb
                        if i >= 2:
                            t.wait_ge(ev.h, ev0 + i - 1)
                        t.transpose(ptr[i % 2][:, 0:128],
                                    stg[r % 2][:, cb * 128:(cb + 1) * 128],
                                    ident[:]).then_inc(mm.h, 1)

            @block.vector
            def _(v):
                for r in range(nrow_t):
                    for cb in range(ncol_t):
                        i = r * ncol_t + cb
                        v.wait_ge(mm.h, mm0 + i + 1)
                        v.tensor_copy(fm_slice(dst, C, cb * 128, 128, r * 128, 128),
                                      ptr[i % 2][:, 0:128]).then_inc(ev.h, 1)
        ldp[0].v = ld0[0] + ((nrow_t + 1) // 2) * 16
        ldp[1].v = ld0[1] + (nrow_t // 2) * 16
        mm.v = mm0 + nrow_t * ncol_t
        ev.v = ev0 + nrow_t * ncol_t
    ctx.barrier()
    les.close()


def emit_replicate_rows(ctx, dst, src_dram_rows, n):
    nc = ctx.nc
    u = ctx.uid()
    mm, ev = ctx.pool(2)
    ld = ctx.gpool(1)[0]
    les = contextlib.ExitStack()
    row = ctx.sb(f"rrow{u}", [1, n], BF16, es=les)
    pr = ctx.ps_(f"rps{u}", [128, 512], F32, es=les)
    ones = ctx.consts["c_ones"]
    nchunk = (n + 511) // 512
    src = src_dram_rows
    if len(src.shape) == 1:
        src = src.rearrange("(o n) -> o n", o=1)
    with nc.Block() as block:
        @block.gpsimd
        def _(gp):
            gp.dma_start(out=row[:], in_=src).then_inc(ld.h, 16)

        @block.tensor
        def _(t):
            t.wait_ge(ld.h, ld.v + 16)
            for cidx in range(nchunk):
                w = min(512, n - cidx * 512)
                if cidx >= 1:
                    t.wait_ge(ev.h, ev.v + cidx)
                t.matmul(pr[:, 0:w], ones[:, 0:128], row[:, cidx * 512:cidx * 512 + w],
                         start=True, stop=True).then_inc(mm.h, 1)

        @block.vector
        def _(v):
            for cidx in range(nchunk):
                w = min(512, n - cidx * 512)
                v.wait_ge(mm.h, mm.v + cidx + 1)
                v.tensor_copy(dst[:, cidx * 512:cidx * 512 + w], pr[:, 0:w]).then_inc(ev.h, 1)
    ld.v += 16
    mm.v += nchunk
    ev.v += nchunk
    ctx.barrier()
    les.close()


def emit_norm(ctx, x_dram, xnt_dram, xnf_dram=None, grep=None, brep=None,
              n_tiles=NT, x_is_input=True, a2a_dram=None):
    nc = ctx.nc
    u = ctx.uid()
    _p = ctx.pool(14)
    ld, mm, ev, st = _p[0], _p[1], _p[2], _p[3]
    ee = _p[4:14]
    G = 4
    NG = n_tiles // G
    ident = ctx.consts["c_ident"]
    affine = grep is not None

    les = contextlib.ExitStack()
    xt = [ctx.sb(f"nxt{u}_{j}", [128, DM], F32, es=les) for j in range(G)]
    xc = [ctx.sb(f"nxc{u}_{j}", [128, DM], F32, es=les) for j in range(G)]
    xnbf = [ctx.sb(f"nxb{u}_{j}", [128, DM], BF16, es=les) for j in range(G)]
    xnst = [ctx.sb(f"nxs{u}_{j}", [128, DM], BF16, es=les) for j in range(G)]
    xaf = [ctx.sb(f"nxa{u}_{j}", [128, DM], F32, es=les) for j in range(G)] if (affine or xnf_dram is not None) else None
    sv = ctx.sb(f"nsv{u}", [128, G], F32, es=les)
    nmean = ctx.sb(f"nnm{u}", [128, G], F32, es=les)
    v2 = ctx.sb(f"nv2{u}", [128, G], F32, es=les)
    vart = ctx.sb(f"nvt{u}", [128, G], F32, es=les)
    sdv = ctx.sb(f"nsd{u}", [128, G], F32, es=les)
    rstd = ctx.sb(f"nrs{u}", [128, G], F32, es=les)
    ptr = [ctx.ps_(f"nps{u}_{j}", [128, DM], BF16, es=les) for j in range(G)]

    n_stores = 1 + (1 if a2a_dram is not None else 0) + (1 if xnf_dram is not None else 0)
    b = {id(s): s.v for s in [ld, mm, ev, st] + ee}

    def bv(s):
        return b[id(s)]

    with nc.Block() as block:
        @block.sync
        def _(sy):
            for g in range(NG):
                if g >= 1:
                    sy.wait_ge(st.h, bv(st) + g * G * 16 * n_stores)
                for j in range(G):
                    i = g * G + j
                    sy.dma_start(out=xt[j][:], in_=x_dram[i * 128:(i + 1) * 128, :]).then_inc(ld.h, 16)
                for j in range(G):
                    i = g * G + j
                    sy.wait_ge(ev.h, bv(ev) + i + 1)
                    dst = xnt_dram.rearrange("(c p) t -> p c t", p=128)
                    sy.dma_start(out=dst[:, :, i * 128:(i + 1) * 128],
                                 in_=xnst[j][:].rearrange("p (c q) -> p c q", c=8)).then_inc(st.h, 16)
                    if a2a_dram is not None:
                        jj = (i % 16) // 2
                        cb0 = (i // 16) * 256 + 128 * (i % 2)
                        dst2 = a2a_dram.rearrange("(jx c p) q -> p c jx q", c=8, p=128)
                        sy.dma_start(out=dst2[:, :, jj, cb0:cb0 + 128],
                                     in_=xnst[j][:].rearrange("p (c q) -> p c q", c=8)).then_inc(st.h, 16)
                    if xnf_dram is not None:
                        sy.wait_ge(ee[9].h, bv(ee[9]) + g + 1)
                        sy.dma_start(out=xnf_dram[i * 128:(i + 1) * 128, :],
                                     in_=xaf[j][:]).then_inc(st.h, 16)
            sy.wait_ge(st.h, bv(st) + NG * G * 16 * n_stores)

        @block.vector
        def _(v):
            for g in range(NG):
                v.wait_ge(ld.h, bv(ld) + (g + 1) * G * 16)
                for j in range(G):
                    ins = v.reduce_sum(sv[:, j:j + 1], xt[j][:], axis=AXX)
                ins.then_inc(ee[1].h, 1)
                v.wait_ge(ee[4].h, bv(ee[4]) + g + 1)
                for j in range(G):
                    ins = v.tensor_scalar(vart[:, j:j + 1], v2[:, j:j + 1], 1.0 / DM, EPS_LN,
                                          ALU.mult, ALU.add)
                ins.then_inc(ee[5].h, 1)
                v.wait_ge(ee[6].h, bv(ee[6]) + g + 1)
                for j in range(G):
                    ins = v.reciprocal(rstd[:, j:j + 1], sdv[:, j:j + 1])
                ins.then_inc(ee[7].h, 1)
                if affine or xnf_dram is not None:
                    v.wait_ge(ee[7].h, bv(ee[7]) + g + 1)
                    for j in range(G):
                        if affine:
                            ins = v.scalar_tensor_tensor(xaf[j][:], xc[j][:], rstd[:, j:j + 1],
                                                         grep[:], ALU.mult, ALU.mult)
                        else:
                            ins = v.tensor_scalar_mul(xaf[j][:], xc[j][:], rstd[:, j:j + 1])
                    ins.then_inc(ee[8].h, 1)
                    if affine:
                        v.wait_ge(ee[8].h, bv(ee[8]) + g + 1)
                        for j in range(G):
                            ins = v.tensor_add(xaf[j][:], xaf[j][:], brep[:])
                        ins.then_inc(ee[9].h, 1)
                    else:
                        v.wait_ge(ee[8].h, bv(ee[8]) + g + 1)
                        v.engine_nop().then_inc(ee[9].h, 1)
                for j in range(G):
                    i = g * G + j
                    v.wait_ge(mm.h, bv(mm) + (i + 1) * 8)
                    v.tensor_copy(xnst[j][:], ptr[j][:]).then_inc(ev.h, 1)

        @block.scalar
        def _(s):
            for g in range(NG):
                s.wait_ge(ee[1].h, bv(ee[1]) + g + 1)
                for j in range(G):
                    ins = s.mul(nmean[:, j:j + 1], sv[:, j:j + 1], -1.0 / DM)
                ins.then_inc(ee[2].h, 1)
                s.wait_ge(ee[2].h, bv(ee[2]) + g + 1)
                for j in range(G):
                    ins = s.activation(xc[j][:], xt[j][:], AF.Identity, bias=nmean[:, j:j + 1])
                ins.then_inc(ee[3].h, 1)
                s.wait_ge(ee[3].h, bv(ee[3]) + g + 1)
                for j in range(G):
                    ins = s.activation(xt[j][:], xc[j][:], AF.Square, accum_out=v2[:, j:j + 1])
                ins.then_inc(ee[4].h, 1)
                s.wait_ge(ee[5].h, bv(ee[5]) + g + 1)
                for j in range(G):
                    ins = s.activation(sdv[:, j:j + 1], vart[:, j:j + 1], AF.Sqrt)
                ins.then_inc(ee[6].h, 1)
                if not affine:
                    s.wait_ge(ee[7].h, bv(ee[7]) + g + 1)
                    for j in range(G):
                        ins = s.activation(xnbf[j][:], xc[j][:], AF.Identity, scale=rstd[:, j:j + 1])
                else:
                    s.wait_ge(ee[9].h, bv(ee[9]) + g + 1)
                    for j in range(G):
                        ins = s.copy(xnbf[j][:], xaf[j][:])
                ins.then_inc(ee[0].h, 1)

        @block.tensor
        def _(t):
            for g in range(NG):
                t.wait_ge(ee[0].h, bv(ee[0]) + g + 1)
                for j in range(G):
                    for cb in range(8):
                        t.transpose(ptr[j][:, cb * 128:(cb + 1) * 128],
                                    xnbf[j][:, cb * 128:(cb + 1) * 128],
                                    ident[:]).then_inc(mm.h, 1)

    ld.v += NG * G * 16
    for k in range(10):
        if k == 9 and not (affine or xnf_dram is not None):
            continue
        if k == 8 and not (affine or xnf_dram is not None):
            continue
        ee[k].v += NG
    mm.v += NG * G * 8
    ev.v += NG * G
    st.v += NG * G * 16 * n_stores
    ctx.barrier()
    les.close()



def emit_l0_qkv(ctx, xnt_dram, b, WQT, KT, VA, k_off=1024, v_off=2048, WC=3072,
                parts=3, wv_spec=None, bk_pp=None, bv_rep=None):
    nc = ctx.nc
    u = ctx.uid()
    les = contextlib.ExitStack()
    ld, km, ke, vm, ve, ms = ctx.pool(6)
    if wv_spec is None:
        wv_spec = (WQT, WC, v_off)
    wv_t, wv_C, wv_off = wv_spec
    xsl = [[ctx.sb(f"qkx{u}_{s}_{i}", [128, 512], BF16, es=les) for i in range(8)]
           for s in range(2)]
    ps01 = [ctx.ps_(f"qkp{u}_{i}", [128, 512], F32, es=les) for i in range(2)]
    pv2 = [ctx.ps_(f"qkv{u}_{i}", [128, 1024], F32, es=les) for i in range(2)]

    b_ld, b_km, b_ke, b_vm, b_ve = ld.v, km.v, ke.v, vm.v, ve.v
    with nc.Block() as block:
        @block.sync
        def _(sy):
            for tb in range(4):
                if tb >= 1:
                    sy.wait_ge(ld.h, b_ld + tb * 128)
                if tb >= 2:
                    if parts & 1:
                        sy.wait_ge(km.h, b_km + (tb - 1) * 8)
                    if parts & 2:
                        sy.wait_ge(vm.h, b_vm + (tb - 1) * 8)
                for it in range(8):
                    sy.dma_start(out=xsl[tb % 2][it][:],
                                 in_=xnt_dram[it * 128:(it + 1) * 128,
                                              b * S + tb * 512: b * S + (tb + 1) * 512]
                                 ).then_inc(ld.h, 16)

        @block.vector
        def _(v):
            v.memset(VA[:], 1.0).then_inc(ms.h, 1)
            for tb in range(4):
                if (parts & 1) and bk_pp is None:
                    for ot in range(8):
                        i = tb * 8 + ot
                        v.wait_ge(km.h, b_km + i + 1)
                        v.tensor_copy(fm_slice(KT, 2048, ot * 128, 128, tb * 512, 512),
                                      ps01[ot % 2][:]).then_inc(ke.h, 1)
                if parts & 2:
                    for tt4 in range(4):
                        tt = tb * 4 + tt4
                        for half in range(2):
                            i = (tt * 2 + half)
                            v.wait_ge(vm.h, b_vm + i + 1)
                            dst = VA[:, tt * 1040 + half * 520: tt * 1040 + (half + 1) * 520]
                            dst = dst.rearrange("p (h j) -> p h j", j=65)[:, :, 0:64]
                            srcp = pv2[tt4 % 2][:, half * 512:(half + 1) * 512]
                            srcp = srcp.rearrange("p (h j) -> p h j", j=64)
                            if bv_rep is not None:
                                bvr = bv_rep[:, half * 512:(half + 1) * 512]
                                bvr = bvr.rearrange("p (h j) -> p h j", j=64)
                                v.tensor_add(dst, srcp, bvr).then_inc(ve.h, 1)
                            else:
                                v.tensor_copy(dst, srcp).then_inc(ve.h, 1)

        @block.scalar
        def _(s):
            if (parts & 1) and bk_pp is not None:
                for tb in range(4):
                    for ot in range(8):
                        i = tb * 8 + ot
                        s.wait_ge(km.h, b_km + i + 1)
                        s.activation(fm_slice(KT, 2048, ot * 128, 128, tb * 512, 512),
                                     ps01[ot % 2][:], AF.Identity,
                                     bias=bk_pp[:, ot:ot + 1]).then_inc(ke.h, 1)

        @block.tensor
        def _(t):
            t.wait_ge(ms.h, ms.v + 1)
            for tb in range(4):
                t.wait_ge(ld.h, b_ld + (tb + 1) * 8 * 16)
                if parts & 1:
                    for ot in range(8):
                        i = tb * 8 + ot
                        if i >= 2:
                            t.wait_ge(ke.h, b_ke + i - 1)
                        for it in range(8):
                            ins = t.matmul(ps01[ot % 2][:],
                                           fm_slice(WQT, WC, it * 128, 128, k_off + ot * 128, 128),
                                           xsl[tb % 2][it][:],
                                           start=(it == 0), stop=(it == 7))
                        ins.then_inc(km.h, 1)
                if parts & 2:
                    for tt4 in range(4):
                        tt = tb * 4 + tt4
                        if tt >= 2:
                            t.wait_ge(ve.h, b_ve + (tt - 1) * 2)
                        for half in range(2):
                            for it in range(8):
                                ins = t.matmul(pv2[tt4 % 2][:, half * 512:(half + 1) * 512],
                                               xsl[tb % 2][it][:, tt4 * 128:(tt4 + 1) * 128],
                                               fm_slice(wv_t, wv_C, it * 128, 128, wv_off + half * 512, 512),
                                               start=(it == 0), stop=(it == 7))
                            ins.then_inc(vm.h, 1)

    ld.v = b_ld + 4 * 8 * 16
    if parts & 1:
        km.v = b_km + 32
        ke.v = b_ke + 32
    if parts & 2:
        vm.v = b_vm + 32
        ve.v = b_ve + 32
    ms.v += 1
    ctx.barrier()
    les.close()


def emit_l0_attn_core(ctx, xnt_dram, b, WQT, KT, VA, WPT, acc_dram,
                      q_off=0, WC=3072, causal=True, scale=0.125,
                      bq_pp=None, qt_full=None, bo_rep=None, addin_dram=None):
    nc = ctx.nc
    u = ctx.uid()
    les = contextlib.ExitStack()
    _names = ["xl", "qmm", "qev", "smm", "ex", "mk", "amm", "rc", "rmm", "cpv",
              "muv", "pmm", "pev", "pst", "lx", "e1"]
    sems = dict(zip(_names, ctx.pool(16)))
    B0 = {n: s.v for n, s in sems.items()}

    xq = [[ctx.sb(f"atx{u}_{s}_{i}", [128, 512], BF16, es=les) for i in range(8)]
          for s in range(2)]
    QT = ctx.sb(f"atqt{u}", [128, 8 * 512], BF16, es=les)
    OTB = ctx.sb(f"atot{u}", [128, 8 * 512], BF16, es=les)
    ptile = [ctx.sb(f"atp{u}_{i}", [128, 512], BF16, es=les) for i in range(2)]
    rden = ctx.sb(f"atr{u}", [1, 512], BF16, es=les)
    prepS = ctx.sb(f"atps{u}", [64, 512], BF16, es=les)
    accst = [ctx.sb(f"atac{u}_{i}", [128, DM], F32, es=les) for i in range(2)]
    axd = [ctx.sb(f"atax{u}_{i}", [128, DM], F32, es=les) for i in range(2)] if addin_dram is not None else None
    ps01 = [ctx.ps_(f"atps01{u}_{i}", [128, 512], F32, es=les) for i in range(2)]
    pav = [ctx.ps_(f"atpav{u}_{i}", [128, 512], F32, es=les) for i in range(2)]
    prep = ctx.ps_(f"atprep{u}", [128, 512], F32, es=les)
    pproj = ctx.ps_(f"atproj{u}", [128, 1024], F32, es=les)
    caus = ctx.consts["c_caus"]
    ones = ctx.consts["c_ones"]

    NH = 16
    kmax = [(4 * (qb + 1)) if causal else 16 for qb in range(4)]

    qmm_after = {}
    qev_after = {}
    smm_at = {}
    ex_at = {}
    mk_at = {}
    amm_at = {}
    rc_at = {}
    rmm_at = {}
    cpv_at = {}
    muv_at = {}
    pmm_at = {}
    pev_at = {}
    pst_at = {}
    nq = B0["qmm"]; nqe = B0["qev"]; ns = B0["smm"]; ne = B0["ex"]
    nm = B0["mk"]; na = B0["amm"]; nr = B0["rc"]; nrm = B0["rmm"]
    ncp = B0["cpv"]; nmu = B0["muv"]; npm = B0["pmm"]; npe = B0["pev"]; nps = B0["pst"]
    for qb in range(4):
        nq += 8
        qmm_after[qb] = nq
        nqe += 8
        qev_after[qb] = nqe
        for h in range(NH):
            for kt in range(kmax[qb]):
                ns += 1; smm_at[(qb, h, kt)] = ns
                ne += 1; ex_at[(qb, h, kt)] = ne
                if causal and kt >= 4 * qb:
                    nm += 1; mk_at[(qb, h, kt)] = nm
                na += 1; amm_at[(qb, h, kt)] = na
            nr += 1; rc_at[(qb, h)] = nr
            nrm += 1; rmm_at[(qb, h)] = nrm
            ncp += 1; cpv_at[(qb, h)] = ncp
            nmu += 1; muv_at[(qb, h)] = nmu
        for tt4 in range(4):
            npm += 1
            pmm_at[(qb, tt4)] = npm
            npe += 1; pev_at[(qb, tt4)] = npe
            nps += 16; pst_at[(qb, tt4)] = nps

    def diag(qb, kt):
        return causal and kt >= 4 * qb

    with nc.Block() as block:
        @block.sync
        def _(sy):
            for qb in range(4):
                if qt_full is None:
                    if qb >= 2:
                        sy.wait_ge(sems["qmm"].h, qmm_after[qb - 2])
                    for it in range(8):
                        sy.dma_start(out=xq[qb % 2][it][:],
                                     in_=xnt_dram[it * 128:(it + 1) * 128,
                                                  b * S + qb * 512: b * S + (qb + 1) * 512]
                                     ).then_inc(sems["xl"].h, 16)
                for tt4 in range(4):
                    g = b * S + qb * 512 + tt4 * 128
                    if addin_dram is not None:
                        i = qb * 4 + tt4
                        if i >= 2:
                            sy.wait_ge(sems["e1"].h, B0["e1"] + (i - 2) + 1)
                        sy.dma_start(out=axd[tt4 % 2][:],
                                     in_=addin_dram[g:g + 128, :]).then_inc(sems["lx"].h, 16)
                    sy.wait_ge(sems["pev"].h, pev_at[(qb, tt4)])
                    sy.dma_start(out=acc_dram[g:g + 128, :],
                                 in_=accst[tt4 % 2][:]).then_inc(sems["pst"].h, 16)
                    sy.wait_ge(sems["pst"].h, pst_at[(qb, tt4)])

        @block.tensor
        def _(t):
            for qb in range(4):
                if qt_full is None:
                    t.wait_ge(sems["xl"].h, B0["xl"] + (qb + 1) * 8 * 16)
                    if qb >= 1:
                        t.wait_ge(sems["ex"].h, ex_at[(qb - 1, NH - 1, kmax[qb - 1] - 1)])
                    for ot in range(8):
                        i = qb * 8 + ot
                        if i >= 2:
                            t.wait_ge(sems["qev"].h, B0["qev"] + i - 1)
                        for it in range(8):
                            ins = t.matmul(ps01[ot % 2][:],
                                           fm_slice(WQT, WC, it * 128, 128, q_off + ot * 128, 128),
                                           xq[qb % 2][it][:],
                                           start=(it == 0), stop=(it == 7))
                        ins.then_inc(sems["qmm"].h, 1)
                elif qb >= 1:
                    t.wait_ge(sems["ex"].h, ex_at[(qb - 1, NH - 1, kmax[qb - 1] - 1)])
                for h in range(NH):
                    if qt_full is None:
                        t.wait_ge(sems["qev"].h, qev_after[qb])
                    if (qb, h) >= (0, 2):
                        hh = h - 2 if h >= 2 else NH - 2 + h
                        qq = qb if h >= 2 else qb - 1
                        t.wait_ge(sems["muv"].h, muv_at[(qq, hh)])
                    for kt in range(kmax[qb] + 1):
                        if kt < kmax[qb]:
                            if kt >= 2:
                                t.wait_ge(sems["ex"].h, ex_at[(qb, h, kt - 2)])
                            elif h >= 1:
                                t.wait_ge(sems["ex"].h, ex_at[(qb, h - 1, kmax[qb] - 1 - (1 - kt))])
                            qsrc = (fm_slice(QT, 512, 64 * h, 64, 0, 512)
                                    if qt_full is None else
                                    fm_slice(qt_full, S, 64 * h, 64, qb * 512, 512))
                            t.matmul(ps01[kt % 2][:],
                                     fm_slice(KT, 2048, 64 * h, 64, kt * 128, 128),
                                     qsrc,
                                     start=True, stop=True).then_inc(sems["smm"].h, 1)
                        if kt >= 1:
                            k2 = kt - 1
                            if diag(qb, k2):
                                t.wait_ge(sems["mk"].h, mk_at[(qb, h, k2)])
                            else:
                                t.wait_ge(sems["ex"].h, ex_at[(qb, h, k2)])
                            t.matmul(pav[h % 2][0:65, :],
                                     VA[:, k2 * 1040 + 65 * h: k2 * 1040 + 65 * h + 65],
                                     ptile[k2 % 2][:],
                                     start=(k2 == 0), stop=(k2 == kmax[qb] - 1)
                                     ).then_inc(sems["amm"].h, 1)
                    t.wait_ge(sems["rc"].h, rc_at[(qb, h)])
                    if (qb, h) > (0, 0):
                        prevh = (qb, h - 1) if h >= 1 else (qb - 1, NH - 1)
                        t.wait_ge(sems["cpv"].h, cpv_at[prevh])
                    t.matmul(prep[0:64, :], ones[0:1, 0:64], rden[:],
                             start=True, stop=True).then_inc(sems["rmm"].h, 1)
                for tt4 in range(4):
                    t.wait_ge(sems["muv"].h, muv_at[(qb, NH - 1)])
                    if (qb, tt4) > (0, 0):
                        prev = (qb, tt4 - 1) if tt4 >= 1 else (qb - 1, 3)
                        t.wait_ge(sems["pev"].h, pev_at[prev])
                    for dt in range(8):
                        for half in range(2):
                            ins = t.matmul(pproj[:, half * 512:(half + 1) * 512],
                                           fm_slice(OTB, 512, dt * 128, 128, tt4 * 128, 128),
                                           fm_slice(WPT, 1024, dt * 128, 128, half * 512, 512),
                                           start=(dt == 0), stop=(dt == 7))
                    ins.then_inc(sems["pmm"].h, 1)

        @block.scalar
        def _(s):
            for qb in range(4):
                if qt_full is None:
                    for ot in range(8):
                        i = qb * 8 + ot
                        s.wait_ge(sems["qmm"].h, B0["qmm"] + i + 1)
                        if qb >= 1:
                            s.wait_ge(sems["smm"].h, smm_at[(qb - 1, NH - 1, kmax[qb - 1] - 1)])
                        if bq_pp is not None:
                            ins = s.activation(fm_slice(QT, 512, ot * 128, 128, 0, 512),
                                               ps01[ot % 2][:], AF.Identity,
                                               bias=bq_pp[:, ot:ot + 1])
                        else:
                            ins = s.activation(fm_slice(QT, 512, ot * 128, 128, 0, 512),
                                               ps01[ot % 2][:], AF.Identity)
                        ins.then_inc(sems["qev"].h, 1)
                for h in range(NH):
                    for kt in range(kmax[qb]):
                        s.wait_ge(sems["smm"].h, smm_at[(qb, h, kt)])
                        if (qb, h, kt) >= (0, 0, 2):
                            if kt >= 2:
                                s.wait_ge(sems["amm"].h, amm_at[(qb, h, kt - 2)])
                            elif (qb, h) > (0, 0):
                                prevh = (qb, h - 1) if h >= 1 else (qb - 1, NH - 1)
                                s.wait_ge(sems["amm"].h,
                                          amm_at[(prevh[0], prevh[1],
                                                  kmax[prevh[0]] - (2 - kt))])
                        s.activation(ptile[kt % 2][:], ps01[kt % 2][:], AF.Exp,
                                     scale=scale).then_inc(sems["ex"].h, 1)

        @block.vector
        def _(v):
            for qb in range(4):
                for h in range(NH):
                    if causal:
                        for kt in range(4 * qb, kmax[qb]):
                            dt = kt - 4 * qb
                            v.wait_ge(sems["ex"].h, ex_at[(qb, h, kt)])
                            v.tensor_mul(ptile[kt % 2][:], ptile[kt % 2][:],
                                         caus[:, dt * 512:(dt + 1) * 512]
                                         ).then_inc(sems["mk"].h, 1)
                    v.wait_ge(sems["amm"].h, amm_at[(qb, h, kmax[qb] - 1)])
                    if (qb, h) > (0, 0):
                        prevh = (qb, h - 1) if h >= 1 else (qb - 1, NH - 1)
                        v.wait_ge(sems["rmm"].h, rmm_at[prevh])
                    with nc.allow_low_precision(reason="1/denominator in bf16 == cast(fp32 recip)"):
                        v.reciprocal(rden[:], pav[h % 2][64:65, :]).then_inc(sems["rc"].h, 1)
                    v.wait_ge(sems["rmm"].h, rmm_at[(qb, h)])
                    v.tensor_copy(prepS[:], prep[0:64, :]).then_inc(sems["cpv"].h, 1)
                    v.wait_ge(sems["cpv"].h, cpv_at[(qb, h)])
                    if qb >= 1:
                        v.wait_ge(sems["pmm"].h, pmm_at[(qb - 1, 3)])
                    v.tensor_mul(fm_slice(OTB, 512, 64 * h, 64, 0, 512),
                                 pav[h % 2][0:64, :], prepS[:]
                                 ).then_inc(sems["muv"].h, 1)
                for tt4 in range(4):
                    v.wait_ge(sems["pmm"].h, pmm_at[(qb, tt4)])
                    i = qb * 4 + tt4
                    if i >= 2:
                        v.wait_ge(sems["pst"].h, B0["pst"] + (i - 1) * 16)
                    if bo_rep is not None and addin_dram is not None:
                        v.tensor_add(accst[tt4 % 2][:], pproj[:], bo_rep[:]).then_inc(sems["e1"].h, 1)
                        v.wait_ge(sems["e1"].h, B0["e1"] + i + 1)
                        v.wait_ge(sems["lx"].h, B0["lx"] + (i + 1) * 16)
                        v.tensor_add(accst[tt4 % 2][:], accst[tt4 % 2][:],
                                     axd[tt4 % 2][:]).then_inc(sems["pev"].h, 1)
                    elif bo_rep is not None:
                        v.tensor_add(accst[tt4 % 2][:], pproj[:], bo_rep[:]).then_inc(sems["pev"].h, 1)
                    else:
                        v.tensor_copy(accst[tt4 % 2][:], pproj[:]).then_inc(sems["pev"].h, 1)

    if qt_full is None:
        sems["xl"].v = B0["xl"] + 4 * 8 * 16
        sems["qmm"].v = nq
        sems["qev"].v = nqe
    if bo_rep is not None and addin_dram is not None:
        sems["e1"].v = B0["e1"] + 16
    if addin_dram is not None:
        sems["lx"].v = B0["lx"] + 16 * 16
    sems["smm"].v = ns
    sems["ex"].v = ne
    sems["mk"].v = nm
    sems["amm"].v = na
    sems["rc"].v = nr
    sems["rmm"].v = nrm
    sems["cpv"].v = ncp
    sems["muv"].v = nmu
    sems["pmm"].v = npm
    sems["pev"].v = npe
    sems["pst"].v = nps
    ctx.barrier()
    les.close()


def emit_colvec_to_rowrep(ctx, col_sbuf, ncols, scratch_dram, dst_rep, scale=1.0):
    nc = ctx.nc
    u = ctx.uid()
    les = contextlib.ExitStack()
    mm, ev, st = ctx.pool(3)
    pt = ctx.ps_(f"cr_p{u}", [128, 128], F32, es=les)
    row = ctx.sb(f"cr_r{u}", [ncols, 128], F32, es=les)
    identf = ctx.consts["c_identf"]
    with nc.Block() as block:
        @block.tensor
        def _(t):
            t.transpose(pt[0:ncols, :], col_sbuf[:, 0:ncols], identf[:]).then_inc(mm.h, 1)

        @block.vector
        def _(v):
            v.wait_ge(mm.h, mm.v + 1)
            if scale != 1.0:
                ins = v.tensor_scalar_mul(row[:], pt[0:ncols, :], float(scale))
            else:
                ins = v.tensor_copy(row[:], pt[0:ncols, :])
            ins.then_inc(ev.h, 1)

        @block.sync
        def _(sy):
            sy.wait_ge(ev.h, ev.v + 1)
            sy.dma_start(out=scratch_dram.rearrange("(a p) -> a p", p=128), in_=row[:]).then_inc(st.h, 16)
            sy.wait_ge(st.h, st.v + 16)
    mm.v += 1
    ev.v += 1
    st.v += 16
    ctx.barrier()
    les.close()
    emit_replicate_rows(ctx, dst_rep, scratch_dram, 128 * ncols)


def emit_load_pp(ctx, dst_pp, vec_dram, ncols):
    nc = ctx.nc
    u = ctx.uid()
    les = contextlib.ExitStack()
    ld, mm, ev = ctx.pool(3)
    stg = ctx.sb(f"pp_s{u}", [ncols, 128], F32, es=les)
    pt = ctx.ps_(f"pp_p{u}", [128, 128], F32, es=les)
    identf = ctx.consts["c_identf"]
    with nc.Block() as block:
        @block.sync
        def _(sy):
            sy.dma_start(out=stg[:], in_=vec_dram.rearrange("(a p) -> a p", p=128)).then_inc(ld.h, 16)

        @block.tensor
        def _(t):
            t.wait_ge(ld.h, ld.v + 16)
            t.transpose(pt[:, 0:ncols], stg[:], identf[0:ncols, 0:ncols]).then_inc(mm.h, 1)

        @block.vector
        def _(v):
            v.wait_ge(mm.h, mm.v + 1)
            v.tensor_copy(dst_pp[:, 0:ncols], pt[:, 0:ncols]).then_inc(ev.h, 1)
    ld.v += 16
    mm.v += 1
    ev.v += 1
    ctx.barrier()
    les.close()


def emit_binarize_prep(ctx, w_dram, nrows, ncol, a_col, les):
    nc = ctx.nc
    u = ctx.uid()
    lles = contextlib.ExitStack()
    rv, sv = ctx.pool(2)
    ldp = ctx.gpool(2)
    nrt = nrows // 128
    stg = [ctx.sb(f"bz_g{u}_{i}", [128, ncol], BF16, es=lles) for i in range(2)]
    b_ld = [ldp[0].v, ldp[1].v]
    b_rv, b_sv = rv.v, sv.v
    with nc.Block() as block:
        @block.gpsimd
        def _(gp):
            for r in range(nrt):
                if r >= 2:
                    gp.wait_ge(rv.h, b_rv + r - 1)
                gp.dma_start(out=stg[r % 2][:], in_=w_dram[r * 128:(r + 1) * 128, :]).then_inc(ldp[r % 2].h, 16)

        @block.vector
        def _(v):
            for r in range(nrt):
                v.wait_ge(ldp[r % 2].h, b_ld[r % 2] + (r // 2 + 1) * 16)
                v.reduce_sum(a_col[:, r:r + 1], stg[r % 2][:], axis=AXX,
                             apply_absolute_value=True).then_inc(rv.h, 1)
    ldp[0].v = b_ld[0] + ((nrt + 1) // 2) * 16
    ldp[1].v = b_ld[1] + (nrt // 2) * 16
    rv.v = b_rv + nrt
    ctx.barrier()
    lles.close()


def emit_sign_fm(ctx, dst_fm, w_dram, nrows, ncol, arep, arep_C):
    nc = ctx.nc
    u = ctx.uid()
    les = contextlib.ExitStack()
    sg, mm, ev = ctx.pool(3)
    ldp = ctx.gpool(2)
    nrt = nrows // 128
    nct = ncol // 128
    GRP = min(nct, 8)
    stg = [ctx.sb(f"sg_g{u}_{i}", [128, ncol], BF16, es=les) for i in range(2)]
    sgb = [ctx.sb(f"sg_b{u}_{i}", [128, ncol], BF16, es=les) for i in range(2)]
    pst = [ctx.ps_(f"sg_p{u}_{i}", [128, GRP * 128], BF16, es=les) for i in range(2)]
    ident = ctx.consts["c_ident"]
    ngrp_per_r = nct // GRP
    b_ld = [ldp[0].v, ldp[1].v]
    b_sg, b_mm, b_ev = sg.v, mm.v, ev.v
    with nc.Block() as block:
        @block.gpsimd
        def _(gp):
            for r in range(nrt):
                if r >= 2:
                    gp.wait_ge(sg.h, b_sg + r - 1)
                gp.dma_start(out=stg[r % 2][:], in_=w_dram[r * 128:(r + 1) * 128, :]).then_inc(ldp[r % 2].h, 16)

        @block.scalar
        def _(s):
            for r in range(nrt):
                s.wait_ge(ldp[r % 2].h, b_ld[r % 2] + (r // 2 + 1) * 16)
                if r >= 2:
                    s.wait_ge(mm.h, b_mm + (r - 1) * nct)
                s.activation(sgb[r % 2][:], stg[r % 2][:], AF.Sign).then_inc(sg.h, 1)

        @block.tensor
        def _(t):
            for r in range(nrt):
                t.wait_ge(sg.h, b_sg + r + 1)
                for g in range(ngrp_per_r):
                    gi = r * ngrp_per_r + g
                    if gi >= 2:
                        t.wait_ge(ev.h, b_ev + (gi - 1) * GRP)
                    for cb in range(GRP):
                        c = g * GRP + cb
                        t.transpose(pst[gi % 2][:, cb * 128:(cb + 1) * 128],
                                    sgb[r % 2][:, c * 128:(c + 1) * 128],
                                    ident[:]).then_inc(mm.h, 1)

        @block.vector
        def _(v):
            for r in range(nrt):
                for g in range(ngrp_per_r):
                    gi = r * ngrp_per_r + g
                    v.wait_ge(mm.h, b_mm + (gi + 1) * GRP)
                    for cb in range(GRP):
                        c = g * GRP + cb
                        v.tensor_mul(fm_slice(dst_fm, nrows, c * 128, 128, r * 128, 128),
                                     pst[gi % 2][:, cb * 128:(cb + 1) * 128],
                                     arep[:, r * 128:(r + 1) * 128]).then_inc(ev.h, 1)
    ldp[0].v = b_ld[0] + ((nrt + 1) // 2) * 16
    ldp[1].v = b_ld[1] + (nrt // 2) * 16
    sg.v = b_sg + nrt
    mm.v = b_mm + nrt * nct
    ev.v = b_ev + nrt * nct
    ctx.barrier()
    les.close()


def emit_l0_ffn_main(ctx, xnt_dram, x_dram, acc_dram, xout_dram,
                     S1T, S2T, b1pp, B2rep, NH1=4096):
    nc = ctx.nc
    u = ctx.uid()
    les = contextlib.ExitStack()
    _names = ["xl", "mmm", "ge", "ym", "e1", "e2", "e3", "lx", "stx"]
    sems = dict(zip(_names, ctx.pool(9)))
    B0 = {n: s.v for n, s in sems.items()}
    NF = NH1 // 128
    NTB = T // 256
    xsl = [[ctx.sb(f"ffx{u}_{s}_{i}", [128, 256], BF16, es=les) for i in range(8)]
           for s in range(2)]
    ghT = ctx.sb(f"ffg{u}", [128, NF * 256], BF16, es=les)
    eb = [ctx.sb(f"ffe{u}_{i}", [128, DM], F32, es=les) for i in range(2)]
    acct = [ctx.sb(f"ffa{u}_{i}", [128, DM], F32, es=les) for i in range(2)]
    xtl = [ctx.sb(f"fft{u}_{i}", [128, DM], F32, es=les) for i in range(2)]
    pm = [ctx.ps_(f"ffp{u}_{i}", [128, 256], F32, es=les) for i in range(2)]
    py = [ctx.ps_(f"ffy{u}_{i}", [128, 1024], F32, es=les) for i in range(2)]

    mmm_at = {}
    ge_at = {}
    ym_at = {}
    e3_at = {}
    n_m = B0["mmm"]; n_g = B0["ge"]; n_y = B0["ym"]; n_3 = B0["e3"]
    for tb in range(NTB):
        for f in range(NF):
            n_m += 1; mmm_at[(tb, f)] = n_m
            n_g += 1; ge_at[(tb, f)] = n_g
        for tt in range(2):
            n_y += 1; ym_at[(tb, tt)] = n_y
            n_3 += 1; e3_at[(tb, tt)] = n_3

    with nc.Block() as block:
        @block.sync
        def _(sy):
            for tb in range(NTB):
                if tb >= 2:
                    sy.wait_ge(sems["mmm"].h, mmm_at[(tb - 2, NF - 1)])
                for it in range(8):
                    sy.dma_start(out=xsl[tb % 2][it][:],
                                 in_=xnt_dram[it * 128:(it + 1) * 128,
                                              tb * 256:(tb + 1) * 256]).then_inc(sems["xl"].h, 16)
                for tt in range(2):
                    i = tb * 2 + tt
                    g = i * 128
                    if i >= 2:
                        sy.wait_ge(sems["e3"].h, B0["e3"] + (i - 2) + 1)
                    sy.dma_start(out=acct[tt][:], in_=acc_dram[g:g + 128, :]).then_inc(sems["lx"].h, 16)
                    sy.dma_start(out=xtl[tt][:], in_=x_dram[g:g + 128, :]).then_inc(sems["lx"].h, 16)
                for tt in range(2):
                    i = tb * 2 + tt
                    sy.wait_ge(sems["e3"].h, e3_at[(tb, tt)])
                    sy.dma_start(out=xout_dram[i * 128:(i + 1) * 128, :],
                                 in_=eb[tt][:]).then_inc(sems["stx"].h, 16)
                    sy.wait_ge(sems["stx"].h, B0["stx"] + (i + 1) * 16)

        @block.tensor
        def _(t):
            for tb in range(NTB):
                t.wait_ge(sems["xl"].h, B0["xl"] + (tb + 1) * 8 * 16)
                for f in range(NF):
                    if (tb, f) >= (0, 2):
                        if f >= 2:
                            t.wait_ge(sems["ge"].h, ge_at[(tb, f - 2)])
                        else:
                            t.wait_ge(sems["ge"].h, ge_at[(tb - 1, NF - 2 + f)])
                    for it in range(8):
                        ins = t.matmul(pm[f % 2][:],
                                       fm_slice(S1T, NH1, it * 128, 128, f * 128, 128),
                                       xsl[tb % 2][it][:],
                                       start=(it == 0), stop=(it == 7))
                    ins.then_inc(sems["mmm"].h, 1)
                for tt in range(2):
                    if tb >= 1:
                        t.wait_ge(sems["e1"].h, B0["e1"] + (tb - 1) * 2 + tt + 1)
                    for f in range(NF):
                        t.wait_ge(sems["ge"].h, ge_at[(tb, f)])
                        for half in range(2):
                            ins = t.matmul(py[tt][:, half * 512:(half + 1) * 512],
                                           fm_slice(ghT, 256, f * 128, 128, tt * 128, 128),
                                           fm_slice(S2T, 1024, f * 128, 128, half * 512, 512),
                                           start=(f == 0), stop=(f == NF - 1))
                    ins.then_inc(sems["ym"].h, 1)

        @block.scalar
        def _(s):
            for tb in range(NTB):
                if tb >= 1:
                    s.wait_ge(sems["ym"].h, ym_at[(tb - 1, 1)])
                for f in range(NF):
                    s.wait_ge(sems["mmm"].h, mmm_at[(tb, f)])
                    s.activation(fm_slice(ghT, 256, f * 128, 128, 0, 256),
                                 pm[f % 2][:], AF.Gelu,
                                 bias=b1pp[:, f:f + 1]).then_inc(sems["ge"].h, 1)

        @block.vector
        def _(v):
            for tb in range(NTB):
                for tt in range(2):
                    i = tb * 2 + tt
                    v.wait_ge(sems["ym"].h, ym_at[(tb, tt)])
                    if i >= 2:
                        v.wait_ge(sems["stx"].h, B0["stx"] + (i - 1) * 16)
                    v.tensor_add(eb[tt][:], py[tt][:], B2rep[:]).then_inc(sems["e1"].h, 1)
                    v.wait_ge(sems["e1"].h, B0["e1"] + i + 1)
                    v.wait_ge(sems["lx"].h, B0["lx"] + (tb + 1) * 64)
                    v.tensor_add(eb[tt][:], eb[tt][:], acct[tt][:]).then_inc(sems["e2"].h, 1)
                    v.wait_ge(sems["e2"].h, B0["e2"] + i + 1)
                    v.tensor_add(eb[tt][:], eb[tt][:], xtl[tt][:]).then_inc(sems["e3"].h, 1)

    sems["xl"].v = B0["xl"] + NTB * 8 * 16
    sems["mmm"].v = n_m
    sems["ge"].v = n_g
    sems["ym"].v = n_y
    sems["e1"].v = B0["e1"] + NTB * 2
    sems["e2"].v = B0["e2"] + NTB * 2
    sems["e3"].v = n_3
    sems["lx"].v = B0["lx"] + NTB * 2 * 32
    sems["stx"].v = B0["stx"] + NTB * 2 * 16
    ctx.barrier()
    les.close()


def emit_l0_layer(ctx, x_dram, xout_dram, W, upto="ffn", dbg=None):
    xnt = ctx.dram("l0_xnt", [DM, T], BF16)
    acc = dbg["acc"] if dbg and "acc" in dbg else ctx.dram("l0_acc", [T, DM], F32)
    emit_norm(ctx, x_dram, xnt)
    if upto == "norm":
        return
    les = contextlib.ExitStack()
    WQT = ctx.sb("l0_wqt", [128, 8 * 3072], BF16, es=les)
    WPT = ctx.sb("l0_wpt", [128, 8 * 1024], BF16, es=les)
    emit_weight_fm(ctx, WQT, W["L0_attn_w_qkv"], 1024, 3072, transpose=True)
    emit_weight_fm(ctx, WPT, W["L0_attn_w_proj"], 1024, 1024, transpose=True)
    KT = ctx.sb("l0_kt", [128, 8 * 2048], BF16, es=les)
    VA = ctx.sb("l0_va", [128, 16 * 1040], BF16, es=les)
    if upto == "weights":
        if dbg is not None and "kt" in dbg:
            emit_dump_sbuf(ctx, dbg["kt"], WQT, 8 * 2048)
        les.close()
        return
    for b in range(BT):
        emit_l0_qkv(ctx, xnt, b, WQT, KT, VA)
        if upto == "qkv" and b == 0:
            if dbg is not None and "kt" in dbg:
                emit_dump_sbuf(ctx, dbg["kt"], KT, 8 * 2048)
                emit_dump_sbuf(ctx, dbg["va"], VA, 16 * 1040)
            les.close()
            return
        emit_l0_attn_core(ctx, xnt, b, WQT, KT, VA, WPT, acc)
        if upto == "attn0" and b == 0:
            les.close()
            return
    les.close()
    if upto == "attn":
        return
    les = contextlib.ExitStack()
    a1col = ctx.sb("l0_a1c", [128, 32], F32, es=les)
    a2col = ctx.sb("l0_a2c", [128, 8], F32, es=les)
    A1rep = ctx.sb("l0_a1r", [128, 4096], BF16, es=les)
    A2rep = ctx.sb("l0_a2r", [128, 1024], BF16, es=les)
    B2rep = ctx.sb("l0_b2r", [128, 1024], F32, es=les)
    b1pp = ctx.sb("l0_b1p", [128, 32], F32, es=les)
    a1row_d = ctx.dram("l0_a1row", [4096], F32)
    a2row_d = ctx.dram("l0_a2row", [1024], F32)
    emit_binarize_prep(ctx, W["L0_ffn_w1"], 4096, 1024, a1col, les)
    emit_colvec_to_rowrep(ctx, a1col, 32, a1row_d, A1rep, scale=1.0 / 1024)
    if upto == "fprep1":
        les.close()
        return
    emit_binarize_prep(ctx, W["L0_ffn_w2"], 1024, 4096, a2col, les)
    emit_colvec_to_rowrep(ctx, a2col, 8, a2row_d, A2rep, scale=1.0 / 4096)
    emit_replicate_rows(ctx, B2rep, W["L0_ffn_b2"], 1024)
    emit_load_pp(ctx, b1pp, W["L0_ffn_b1"], 32)
    if upto == "fprep2":
        les.close()
        return
    S1T = ctx.sb("l0_s1t", [128, 8 * 4096], BF16, es=les)
    S2T = ctx.sb("l0_s2t", [128, 32 * 1024], BF16, es=les)
    emit_sign_fm(ctx, S1T, W["L0_ffn_w1"], 4096, 1024, A1rep, 4096)
    if upto == "fsign1":
        les.close()
        return
    emit_sign_fm(ctx, S2T, W["L0_ffn_w2"], 1024, 4096, A2rep, 1024)
    if upto == "fsign":
        les.close()
        return
    emit_l0_ffn_main(ctx, xnt, x_dram, acc, xout_dram, S1T, S2T, b1pp, B2rep)
    les.close()


def emit_dump_sbuf(ctx, dram_out, sbuf, ncols):
    sems = ctx.pool(1)
    st = sems[0]
    with ctx.nc.Block() as block:
        @block.sync
        def _(sy):
            sy.dma_start(out=dram_out, in_=sbuf[:, 0:ncols]).then_inc(st.h, 16)
            sy.wait_ge(st.h, st.v + 16)
    st.v += 16
    ctx.barrier()


def build_l0(upto="ffn", debug=False):
    ctx = Ctx()
    x_in = ctx.dram("x_in", [T, DM], F32, kind="ExternalInput")
    x_out = ctx.dram("x_out", [T, DM], F32, kind="ExternalOutput")
    W = {}
    for nm, shp in [("L0_attn_w_qkv", (3072, 1024)), ("L0_attn_w_proj", (1024, 1024)),
                    ("L0_ffn_w1", (4096, 1024)), ("L0_ffn_b1", (4096,)),
                    ("L0_ffn_w2", (1024, 4096)), ("L0_ffn_b2", (1024,))]:
        W[nm] = ctx.dram(nm, list(shp), F32, kind="ExternalInput")
    ctx.load_consts()
    dbg = None
    if debug:
        dbg = {"acc": ctx.dram("dbg_acc", [T, DM], F32, kind="ExternalOutput")}
        if upto == "qkv":
            dbg["kt"] = ctx.dram("dbg_kt", [128, 8 * 2048], BF16, kind="ExternalOutput")
            dbg["va"] = ctx.dram("dbg_va", [128, 16 * 1040], BF16, kind="ExternalOutput")
    emit_l0_layer(ctx, x_in, x_out, W, upto=upto, dbg=dbg)
    ctx.close()
    return ctx



def emit_l1_ropeqk(ctx, xnt_dram, b, WQfm, WKfm, bq_rep, bk_rep, QTfull, KTfull):
    nc = ctx.nc
    u = ctx.uid()
    les = contextlib.ExitStack()
    (ld, cs, qm, va_, vb, vc, tq, eq, km2, vk, tk, ek) = ctx.pool(12)
    xsl = [[ctx.sb(f"rqx{u}_{s}_{i}", [128, 512], BF16, es=les) for i in range(8)]
           for s in range(2)]
    cosb = [ctx.sb(f"rqc{u}_{i}", [128, DM], F32, es=les) for i in range(2)]
    sinb = [ctx.sb(f"rqs{u}_{i}", [128, DM], F32, es=les) for i in range(2)]
    xnf = ctx.sb(f"rqn{u}", [128, DM], F32, es=les)
    t1f = ctx.sb(f"rq1{u}", [128, DM], F32, es=les)
    t2f = ctx.sb(f"rq2{u}", [128, DM], F32, es=les)
    xbf = [ctx.sb(f"rqb{u}_{i}", [128, DM], BF16, es=les) for i in range(2)]
    pqk = [ctx.ps_(f"rqp{u}_{i}", [128, 1024], F32, es=les) for i in range(2)]
    ptr = [ctx.ps_(f"rqt{u}_{i}", [128, 1024], BF16, es=les) for i in range(2)]
    ident = ctx.consts["c_ident"]
    cosd = ctx.dram_consts["c_ropecos"]
    sind = ctx.dram_consts["c_ropesin"]

    b_ = {s: s.v for s in (ld, cs, qm, va_, vb, vc, tq, eq, km2, vk, tk, ek)}

    with nc.Block() as block:
        @block.sync
        def _(sy):
            for tb in range(4):
                if tb >= 1:
                    sy.wait_ge(ld.h, b_[ld] + tb * 128)
                if tb >= 2:
                    sy.wait_ge(qm.h, b_[qm] + (tb - 1) * 8)
                for it in range(8):
                    sy.dma_start(out=xsl[tb % 2][it][:],
                                 in_=xnt_dram[it * 128:(it + 1) * 128,
                                              b * S + tb * 512: b * S + (tb + 1) * 512]
                                 ).then_inc(ld.h, 16)
                for tt4 in range(4):
                    tt = tb * 4 + tt4
                    if tt >= 2:
                        sy.wait_ge(vc.h, b_[vc] + (tt - 1) * 2)
                    sy.dma_start(out=cosb[tt % 2][:],
                                 in_=cosd[tt * 128:(tt + 1) * 128, :]).then_inc(cs.h, 16)
                    sy.wait_ge(cs.h, b_[cs] + (tt * 2 + 1) * 16)
                    sy.dma_start(out=sinb[tt % 2][:],
                                 in_=sind[tt * 128:(tt + 1) * 128, :]).then_inc(cs.h, 16)
                    sy.wait_ge(cs.h, b_[cs] + (tt * 2 + 2) * 16)

        @block.tensor
        def _(t):
            for tb in range(4):
                t.wait_ge(ld.h, b_[ld] + (tb + 1) * 128)
                for tt4 in range(4):
                    tt = tb * 4 + tt4
                    for p in range(2):
                        W = WQfm if p == 0 else WKfm
                        if tt >= 1:
                            t.wait_ge(va_.h, b_[va_] + (tt - 1) * 2 + p + 1)
                        for half in range(2):
                            for it in range(8):
                                ins = t.matmul(pqk[p][:, half * 512:(half + 1) * 512],
                                               xsl[tb % 2][it][:, tt4 * 128:(tt4 + 1) * 128],
                                               fm_slice(W, 1024, it * 128, 128, half * 512, 512),
                                               start=(it == 0), stop=(it == 7))
                        ins.then_inc(qm.h, 1)
                    for p in range(2):
                        t.wait_ge(vb.h, b_[vb] + tt * 2 + p + 1)
                        if tt >= 1:
                            t.wait_ge((eq if p == 0 else ek).h,
                                      b_[eq if p == 0 else ek] + tt)
                        for cb in range(8):
                            ins = t.transpose(ptr[p][:, cb * 128:(cb + 1) * 128],
                                              xbf[p][:, cb * 128:(cb + 1) * 128],
                                              ident[:])
                        ins.then_inc((tq if p == 0 else tk).h, 1)

        @block.vector
        def _(v):
            for tb in range(4):
                for tt4 in range(4):
                    tt = tb * 4 + tt4
                    for p in range(2):
                        brep = bq_rep if p == 0 else bk_rep
                        v.wait_ge(qm.h, b_[qm] + tt * 2 + p + 1)
                        v.wait_ge(cs.h, b_[cs] + (tt + 1) * 32)
                        if tt >= 1:
                            v.wait_ge((tq if p == 0 else tk).h,
                                      b_[tq if p == 0 else tk] + tt)
                        v.tensor_add(xnf[:], pqk[p][:], brep[:]).then_inc(va_.h, 1)
                        v.wait_ge(va_.h, b_[va_] + tt * 2 + p + 1)
                        v.tensor_mul(t1f[:], xnf[:], cosb[tt % 2][:])
                        x4 = xnf[:].rearrange("q (a k j) -> q a k j", k=2, j=32)
                        s4 = sinb[tt % 2][:].rearrange("q (a k j) -> q a k j", k=2, j=32)
                        t4 = t2f[:].rearrange("q (a k j) -> q a k j", k=2, j=32)
                        v.tensor_mul(t4[:, :, 0, :], x4[:, :, 1, :], s4[:, :, 0, :])
                        ins = v.tensor_mul(t4[:, :, 1, :], x4[:, :, 0, :], s4[:, :, 1, :])
                        ins.then_inc(vc.h, 1)
                        v.wait_ge(vc.h, b_[vc] + tt * 2 + p + 1)
                        v.tensor_add(xbf[p][:], t1f[:], t2f[:]).then_inc(vb.h, 1)
                    for p in range(2):
                        v.wait_ge((tq if p == 0 else tk).h,
                                  b_[tq if p == 0 else tk] + tt + 1)
                        dst = QTfull if p == 0 else KTfull
                        ins = v.tensor_copy(
                            dst[:, :].rearrange("q (c z) -> q c z", c=8)[:, :, tt * 128:(tt + 1) * 128],
                            ptr[p][:].rearrange("q (c z) -> q c z", c=8))
                        ins.then_inc((eq if p == 0 else ek).h, 1)

    ld.v = b_[ld] + 4 * 128
    cs.v = b_[cs] + 16 * 32
    qm.v = b_[qm] + 32
    va_.v = b_[va_] + 32
    vb.v = b_[vb] + 32
    vc.v = b_[vc] + 32
    tq.v = b_[tq] + 16
    tk.v = b_[tk] + 16
    eq.v = b_[eq] + 16
    ek.v = b_[ek] + 16
    ctx.barrier()
    les.close()


def emit_gate(ctx, hnt_dram, GWfm, gb_rep, MSK, gate_dim=8, with_scores=False,
              SCO=None):
    nc = ctx.nc
    u = ctx.uid()
    les = contextlib.ExitStack()
    gd = gate_dim
    (ld, gm, e1, e2, e3, e4, e5) = ctx.pool(7)
    xsl = [[ctx.sb(f"gx{u}_{s}_{i}", [128, 128], BF16, es=les) for i in range(8)]
           for s in range(2)]
    lg = [ctx.sb(f"glg{u}_{i}", [128, gd], F32, es=les) for i in range(2)]
    mx = ctx.sb(f"gmx{u}", [128, 2], F32, es=les)
    ee = ctx.sb(f"gee{u}", [128, gd], F32, es=les)
    sume = ctx.sb(f"gsu{u}", [128, 1], F32, es=les)
    pg = [ctx.ps_(f"gpg{u}_{i}", [128, 512], F32, es=les) for i in range(2)]
    b_ = {s: s.v for s in (ld, gm, e1, e2, e3, e4, e5)}
    with nc.Block() as block:
        @block.sync
        def _(sy):
            for i in range(NT):
                if i >= 1:
                    sy.wait_ge(ld.h, b_[ld] + i * 128)
                if i >= 2:
                    sy.wait_ge(gm.h, b_[gm] + i - 1)
                for it in range(8):
                    sy.dma_start(out=xsl[i % 2][it][:],
                                 in_=hnt_dram[it * 128:(it + 1) * 128,
                                              i * 128:(i + 1) * 128]).then_inc(ld.h, 16)

        @block.tensor
        def _(t):
            for i in range(NT):
                t.wait_ge(ld.h, b_[ld] + (i + 1) * 128)
                if i >= 2:
                    t.wait_ge(e1.h, b_[e1] + i - 1)
                for it in range(8):
                    ins = t.matmul(pg[i % 2][:, 0:gd],
                                   xsl[i % 2][it][:],
                                   fm_slice(GWfm, gd, it * 128, 128, 0, gd),
                                   start=(it == 0), stop=(it == 7))
                ins.then_inc(gm.h, 1)

        @block.vector
        def _(v):
            for i in range(NT):
                v.wait_ge(gm.h, b_[gm] + i + 1)
                v.tensor_add(lg[i % 2][:], pg[i % 2][:, 0:gd], gb_rep[:, 0:gd]).then_inc(e1.h, 1)
                v.wait_ge(e1.h, b_[e1] + i + 1)
                v.reduce_max(mx[:, 0:1], lg[i % 2][:], axis=AXX).then_inc(e2.h, 1)
                v.wait_ge(e2.h, b_[e2] + i + 1)
                if not with_scores:
                    v.tensor_single_scalar(MSK[:, i * gd:(i + 1) * gd], lg[i % 2][:],
                                           mx[:, 0:1], ALU.is_ge).then_inc(e3.h, 1)
                else:
                    v.wait_ge(e4.h, b_[e4] + i)
                    v.tensor_scalar_mul(mx[:, 1:2], mx[:, 0:1], -1.0).then_inc(e3.h, 1)
                    v.wait_ge(e5.h, b_[e5] + i + 1)
                    v.tensor_single_scalar(MSK[:, i * gd:(i + 1) * gd], ee[:],
                                           1.0, ALU.is_ge)
                    with nc.allow_low_precision(reason="score=1/sum fp32 recip"):
                        v.reciprocal(SCO[:, i:i + 1], sume[:]).then_inc(e4.h, 1)

        @block.scalar
        def _(s):
            if with_scores:
                for i in range(NT):
                    s.wait_ge(e3.h, b_[e3] + i + 1)
                    s.activation(ee[:], lg[i % 2][:], AF.Exp, bias=mx[:, 1:2],
                                 accum_out=sume[:]).then_inc(e5.h, 1)

    ld.v = b_[ld] + NT * 128
    gm.v = b_[gm] + NT
    e1.v = b_[e1] + NT
    e2.v = b_[e2] + NT
    e3.v = b_[e3] + NT
    if with_scores:
        e4.v = b_[e4] + NT
        e5.v = b_[e5] + NT
    ctx.barrier()
    les.close()


def emit_moe_expert(ctx, hnt_dram, W1fm, W2fm, b1pp, B2rep, coef_sb, e, gd,
                    macc_dram, first, last, final_addends, xout_dram, act):
    nc = ctx.nc
    u = ctx.uid()
    les = contextlib.ExitStack()
    _names = ["xl", "mmm", "ge", "ym", "e1", "e2", "e3", "lx", "stx"]
    sems = dict(zip(_names, ctx.pool(9)))
    B0 = {n: s.v for n, s in sems.items()}
    NF = 32
    NTB = T // 256
    nadd = (1 if not first else 0) + (len(final_addends) if last else 0)
    xsl = [[ctx.sb(f"mex{u}_{s}_{i}", [128, 256], BF16, es=les) for i in range(8)]
           for s in range(2)]
    ghT = ctx.sb(f"meg{u}", [128, NF * 256], BF16, es=les)
    eb = [ctx.sb(f"mee{u}_{i}", [128, DM], F32, es=les) for i in range(2)]
    adt = [[ctx.sb(f"mea{u}_{s}_{i}", [128, DM], F32, es=les) for i in range(nadd)]
           for s in range(2)] if nadd else None
    pm = [ctx.ps_(f"mep{u}_{i}", [128, 256], F32, es=les) for i in range(2)]
    py = [ctx.ps_(f"mey{u}_{i}", [128, 1024], F32, es=les) for i in range(2)]

    mmm_at = {}
    ge_at = {}
    ym_at = {}
    e3_at = {}
    n_m = B0["mmm"]; n_g = B0["ge"]; n_y = B0["ym"]; n_3 = B0["e3"]
    for tb in range(NTB):
        for f in range(NF):
            n_m += 1; mmm_at[(tb, f)] = n_m
            n_g += 1; ge_at[(tb, f)] = n_g
        for tt in range(2):
            n_y += 1; ym_at[(tb, tt)] = n_y
            n_3 += max(1, nadd); e3_at[(tb, tt)] = n_3
    out_dram = xout_dram if last else macc_dram

    with nc.Block() as block:
        @block.sync
        def _(sy):
            for tb in range(NTB):
                if tb >= 1:
                    sy.wait_ge(sems["xl"].h, B0["xl"] + tb * 128)
                if tb >= 2:
                    sy.wait_ge(sems["mmm"].h, mmm_at[(tb - 2, NF - 1)])
                for it in range(8):
                    sy.dma_start(out=xsl[tb % 2][it][:],
                                 in_=hnt_dram[it * 128:(it + 1) * 128,
                                              tb * 256:(tb + 1) * 256]).then_inc(sems["xl"].h, 16)
                for tt in range(2):
                    i = tb * 2 + tt
                    g = i * 128
                    if nadd:
                        if i >= 2:
                            sy.wait_ge(sems["e3"].h, e3_at[((i - 2) // 2, (i - 2) % 2)])
                        srcs = ([] if first else [macc_dram]) + (final_addends if last else [])
                        for a_i, src in enumerate(srcs):
                            sy.dma_start(out=adt[tt][a_i][:], in_=src[g:g + 128, :]).then_inc(sems["lx"].h, 16)
                for tt in range(2):
                    i = tb * 2 + tt
                    sy.wait_ge(sems["e3"].h, e3_at[(tb, tt)])
                    sy.dma_start(out=out_dram[i * 128:(i + 1) * 128, :],
                                 in_=eb[tt][:]).then_inc(sems["stx"].h, 16)
                    sy.wait_ge(sems["stx"].h, B0["stx"] + (i + 1) * 16)

        @block.tensor
        def _(t):
            for tb in range(NTB):
                t.wait_ge(sems["xl"].h, B0["xl"] + (tb + 1) * 8 * 16)
                for f in range(NF):
                    if (tb, f) >= (0, 2):
                        if f >= 2:
                            t.wait_ge(sems["ge"].h, ge_at[(tb, f - 2)])
                        else:
                            t.wait_ge(sems["ge"].h, ge_at[(tb - 1, NF - 2 + f)])
                    for it in range(8):
                        ins = t.matmul(pm[f % 2][:],
                                       fm_slice(W1fm, 4096, it * 128, 128, f * 128, 128),
                                       xsl[tb % 2][it][:],
                                       start=(it == 0), stop=(it == 7))
                    ins.then_inc(sems["mmm"].h, 1)
                for tt in range(2):
                    if tb >= 1:
                        t.wait_ge(sems["e1"].h, B0["e1"] + (tb - 1) * 2 + tt + 1)
                    for f in range(NF):
                        t.wait_ge(sems["ge"].h, ge_at[(tb, f)])
                        for half in range(2):
                            ins = t.matmul(py[tt][:, half * 512:(half + 1) * 512],
                                           fm_slice(ghT, 256, f * 128, 128, tt * 128, 128),
                                           fm_slice(W2fm, 1024, f * 128, 128, half * 512, 512),
                                           start=(f == 0), stop=(f == NF - 1))
                    ins.then_inc(sems["ym"].h, 1)

        @block.scalar
        def _(s):
            for tb in range(NTB):
                if tb >= 1:
                    s.wait_ge(sems["ym"].h, ym_at[(tb - 1, 1)])
                for f in range(NF):
                    s.wait_ge(sems["mmm"].h, mmm_at[(tb, f)])
                    s.activation(fm_slice(ghT, 256, f * 128, 128, 0, 256),
                                 pm[f % 2][:], act,
                                 bias=b1pp[:, f:f + 1]).then_inc(sems["ge"].h, 1)

        @block.vector
        def _(v):
            for tb in range(NTB):
                for tt in range(2):
                    i = tb * 2 + tt
                    v.wait_ge(sems["ym"].h, ym_at[(tb, tt)])
                    if i >= 2:
                        v.wait_ge(sems["stx"].h, B0["stx"] + (i - 1) * 16)
                    v.tensor_add(eb[tt][:], py[tt][:], B2rep[:]).then_inc(sems["e1"].h, 1)
                    v.wait_ge(sems["e1"].h, B0["e1"] + i + 1)
                    ins = v.tensor_scalar_mul(eb[tt][:], eb[tt][:],
                                              coef_sb[:, i * gd + e:i * gd + e + 1])
                    if nadd:
                        ins.then_inc(sems["e2"].h, 1)
                        v.wait_ge(sems["e2"].h, B0["e2"] + i + 1)
                        v.wait_ge(sems["lx"].h, B0["lx"] + (tb + 1) * nadd * 2 * 16)
                        base3 = e3_at[(tb, tt)] - nadd
                        for a_i in range(nadd):
                            v.tensor_add(eb[tt][:], eb[tt][:],
                                         adt[tt][a_i][:]).then_inc(sems["e3"].h, 1)
                            if a_i < nadd - 1:
                                v.wait_ge(sems["e3"].h, base3 + a_i + 1)
                    else:
                        ins.then_inc(sems["e3"].h, 1)

    sems["xl"].v = B0["xl"] + NTB * 128
    sems["mmm"].v = n_m
    sems["ge"].v = n_g
    sems["ym"].v = n_y
    sems["e1"].v = B0["e1"] + NTB * 2
    if nadd:
        sems["e2"].v = B0["e2"] + NTB * 2
    sems["e3"].v = B0["e3"] + NTB * 2 * max(1, nadd)
    if nadd:
        sems["lx"].v = B0["lx"] + NTB * 2 * nadd * 16
    sems["stx"].v = B0["stx"] + NTB * 2 * 16
    ctx.barrier()
    les.close()


L1_WNAMES = ["L1_attn_Wq", "L1_attn_bq", "L1_attn_Wk", "L1_attn_bk", "L1_attn_Wv",
             "L1_attn_bv", "L1_attn_Wo", "L1_attn_bo", "L1_ffn_ln1_g", "L1_ffn_ln1_b",
             "L1_ffn_ln2_g", "L1_ffn_ln2_b", "L1_ffn_wq", "L1_ffn_bq", "L1_ffn_wk",
             "L1_ffn_bk", "L1_ffn_wv", "L1_ffn_bv", "L1_ffn_wo", "L1_ffn_bo",
             "L1_ffn_gate_w", "L1_ffn_gate_b", "L1_ffn_w1", "L1_ffn_b1",
             "L1_ffn_w2", "L1_ffn_b2"]


def emit_l1_layer(ctx, x_dram, xout_dram, W, upto="all", dbg=None):
    xnt = ctx.dram("l1_xnt", [DM, T], BF16)
    xnf = ctx.dram("l1_xnf", [T, DM], F32)
    acc = dbg["acc"] if dbg and "acc" in dbg else ctx.dram("l1_acc", [T, DM], F32)
    u2d = dbg["u2"] if dbg and "u2" in dbg else ctx.dram("l1_u2", [T, DM], F32)
    hnt = ctx.dram("l1_hnt", [DM, T], BF16)
    hn2t = ctx.dram("l1_hn2t", [DM, T], BF16)
    macc = ctx.dram("l1_macc", [T, DM], F32)

    emit_norm(ctx, x_dram, xnt, xnf_dram=xnf)
    if upto == "norm":
        return
    les = contextlib.ExitStack()
    QTfull = ctx.sb("l1_qt", [128, 8 * 2048], BF16, es=les)
    KT = ctx.sb("l1_kt", [128, 8 * 2048], BF16, es=les)
    VA = ctx.sb("l1_va", [128, 16 * 1040], BF16, es=les)
    bo_rep = ctx.sb("l1_bor", [128, DM], F32, es=les)
    emit_replicate_rows(ctx, bo_rep, W["L1_attn_bo"], 1024)
    for b in range(BT):
        lesb = contextlib.ExitStack()
        WQfm = ctx.sb(f"l1_wq{b}", [128, 8 * 1024], BF16, es=lesb)
        WKfm = ctx.sb(f"l1_wk{b}", [128, 8 * 1024], BF16, es=lesb)
        bq_rep = ctx.sb(f"l1_bqr{b}", [128, DM], F32, es=lesb)
        bk_rep = ctx.sb(f"l1_bkr{b}", [128, DM], F32, es=lesb)
        emit_weight_fm(ctx, WQfm, W["L1_attn_Wq"], 1024, 1024, transpose=False)
        emit_weight_fm(ctx, WKfm, W["L1_attn_Wk"], 1024, 1024, transpose=False)
        emit_replicate_rows(ctx, bq_rep, W["L1_attn_bq"], 1024)
        emit_replicate_rows(ctx, bk_rep, W["L1_attn_bk"], 1024)
        emit_l1_ropeqk(ctx, xnt, b, WQfm, WKfm, bq_rep, bk_rep, QTfull, KT)
        lesb.close()
        lesv = contextlib.ExitStack()
        WVfm = ctx.sb(f"l1_wv{b}", [128, 8 * 1024], BF16, es=lesv)
        bv_rep = ctx.sb(f"l1_bvr{b}", [128, DM], F32, es=lesv)
        emit_weight_fm(ctx, WVfm, W["L1_attn_Wv"], 1024, 1024, transpose=False)
        emit_replicate_rows(ctx, bv_rep, W["L1_attn_bv"], 1024)
        emit_l0_qkv(ctx, xnt, b, WVfm, KT, VA, WC=1024, parts=2,
                    wv_spec=(WVfm, 1024, 0), bv_rep=bv_rep)
        lesv.close()
        lesc = contextlib.ExitStack()
        WOfm = ctx.sb(f"l1_wo{b}", [128, 8 * 1024], BF16, es=lesc)
        emit_weight_fm(ctx, WOfm, W["L1_attn_Wo"], 1024, 1024, transpose=False)
        emit_l0_attn_core(ctx, xnt, b, None, KT, VA, WOfm, acc, WC=1024,
                          causal=False, scale=0.125, qt_full=QTfull, bo_rep=bo_rep)
        lesc.close()
    les.close()
    if upto == "oattn":
        return
    les = contextlib.ExitStack()
    G1 = ctx.sb("l1_g1", [128, DM], F32, es=les)
    B1 = ctx.sb("l1_b1", [128, DM], F32, es=les)
    emit_replicate_rows(ctx, G1, W["L1_ffn_ln1_g"], 1024)
    emit_replicate_rows(ctx, B1, W["L1_ffn_ln1_b"], 1024)
    emit_norm(ctx, xnf, hnt, grep=G1, brep=B1)
    les.close()
    les = contextlib.ExitStack()
    KT2 = ctx.sb("l1_kt2", [128, 8 * 2048], BF16, es=les)
    VA2 = ctx.sb("l1_va2", [128, 16 * 1040], BF16, es=les)
    bo2 = ctx.sb("l1_bo2", [128, DM], F32, es=les)
    emit_replicate_rows(ctx, bo2, W["L1_ffn_bo"], 1024)
    for b in range(BT):
        lesb = contextlib.ExitStack()
        WKfm2 = ctx.sb(f"l1_iwk{b}", [128, 8 * 1024], BF16, es=lesb)
        WVfm2 = ctx.sb(f"l1_iwv{b}", [128, 8 * 1024], BF16, es=lesb)
        bk2pp = ctx.sb(f"l1_ibk{b}", [128, 8], F32, es=lesb)
        bv2rep = ctx.sb(f"l1_ibv{b}", [128, DM], F32, es=lesb)
        emit_weight_fm(ctx, WKfm2, W["L1_ffn_wk"], 1024, 1024, transpose=False)
        emit_weight_fm(ctx, WVfm2, W["L1_ffn_wv"], 1024, 1024, transpose=False)
        emit_load_pp(ctx, bk2pp, W["L1_ffn_bk"], 8)
        emit_replicate_rows(ctx, bv2rep, W["L1_ffn_bv"], 1024)
        emit_l0_qkv(ctx, hnt, b, WKfm2, KT2, VA2, k_off=0, WC=1024, parts=3,
                    wv_spec=(WVfm2, 1024, 0), bk_pp=bk2pp, bv_rep=bv2rep)
        lesb.close()
        lesc = contextlib.ExitStack()
        WQfm2 = ctx.sb(f"l1_iwq{b}", [128, 8 * 1024], BF16, es=lesc)
        WOfm2 = ctx.sb(f"l1_iwo{b}", [128, 8 * 1024], BF16, es=lesc)
        bq2pp = ctx.sb(f"l1_ibq{b}", [128, 8], F32, es=lesc)
        emit_weight_fm(ctx, WQfm2, W["L1_ffn_wq"], 1024, 1024, transpose=False)
        emit_weight_fm(ctx, WOfm2, W["L1_ffn_wo"], 1024, 1024, transpose=False)
        emit_load_pp(ctx, bq2pp, W["L1_ffn_bq"], 8)
        emit_l0_attn_core(ctx, hnt, b, WQfm2, KT2, VA2, WOfm2, u2d, q_off=0,
                          WC=1024, causal=False, scale=0.125, bq_pp=bq2pp,
                          bo_rep=bo2, addin_dram=xnf)
        lesc.close()
    les.close()
    if upto == "inner":
        return
    les = contextlib.ExitStack()
    G2 = ctx.sb("l1_g2", [128, DM], F32, es=les)
    B2 = ctx.sb("l1_b2", [128, DM], F32, es=les)
    emit_replicate_rows(ctx, G2, W["L1_ffn_ln2_g"], 1024)
    emit_replicate_rows(ctx, B2, W["L1_ffn_ln2_b"], 1024)
    emit_norm(ctx, u2d, hn2t, grep=G2, brep=B2)
    les.close()
    lay = contextlib.ExitStack()
    MSK = ctx.sb("l1_msk", [128, NT * 8], F32, es=lay)
    les = contextlib.ExitStack()
    GWfm = ctx.sb("l1_gw", [128, 8 * 8], BF16, es=les)
    gbrep = ctx.sb("l1_gb", [128, 8], F32, es=les)
    emit_weight_fm(ctx, GWfm, W["L1_ffn_gate_w"], 1024, 8, transpose=False)
    emit_replicate_rows(ctx, gbrep, W["L1_ffn_gate_b"], 8)
    emit_gate(ctx, hn2t, GWfm, gbrep, MSK, gate_dim=8, with_scores=False)
    les.close()
    if upto == "gate":
        lay.close()
        return
    for e in range(8):
        lese = contextlib.ExitStack()
        W1fm = ctx.sb(f"l1_e{e}w1", [128, 8 * 4096], BF16, es=lese)
        W2fm = ctx.sb(f"l1_e{e}w2", [128, 32 * 1024], BF16, es=lese)
        b1pp = ctx.sb(f"l1_e{e}b1", [128, 32], F32, es=lese)
        B2e = ctx.sb(f"l1_e{e}b2", [128, DM], F32, es=lese)
        emit_weight_fm(ctx, W1fm, W["L1_ffn_w1"][e], 1024, 4096, transpose=False)
        emit_weight_fm(ctx, W2fm, W["L1_ffn_w2"][e], 4096, 1024, transpose=False)
        emit_load_pp(ctx, b1pp, W["L1_ffn_b1"][e], 32)
        emit_replicate_rows(ctx, B2e, W["L1_ffn_b2"][e], 1024)
        emit_moe_expert(ctx, hn2t, W1fm, W2fm, b1pp, B2e, MSK, e, 8, macc,
                        first=(e == 0), last=(e == 7),
                        final_addends=[x_dram, acc, u2d],
                        xout_dram=xout_dram, act=AF.Relu)
        lese.close()
    lay.close()


def build_l1(upto="all", debug=False):
    ctx = Ctx()
    x_in = ctx.dram("x_in", [T, DM], F32, kind="ExternalInput")
    x_out = ctx.dram("x_out", [T, DM], F32, kind="ExternalOutput")
    W = {}
    shapes = {"L1_attn_Wq": (1024, 1024), "L1_attn_bq": (1024,),
              "L1_attn_Wk": (1024, 1024), "L1_attn_bk": (1024,),
              "L1_attn_Wv": (1024, 1024), "L1_attn_bv": (1024,),
              "L1_attn_Wo": (1024, 1024), "L1_attn_bo": (1024,),
              "L1_ffn_ln1_g": (1024,), "L1_ffn_ln1_b": (1024,),
              "L1_ffn_ln2_g": (1024,), "L1_ffn_ln2_b": (1024,),
              "L1_ffn_wq": (1024, 1024), "L1_ffn_bq": (1024,),
              "L1_ffn_wk": (1024, 1024), "L1_ffn_bk": (1024,),
              "L1_ffn_wv": (1024, 1024), "L1_ffn_bv": (1024,),
              "L1_ffn_wo": (1024, 1024), "L1_ffn_bo": (1024,),
              "L1_ffn_gate_w": (1024, 8), "L1_ffn_gate_b": (8,),
              "L1_ffn_w1": (8, 1024, 4096), "L1_ffn_b1": (8, 4096),
              "L1_ffn_w2": (8, 4096, 1024), "L1_ffn_b2": (8, 1024)}
    for nm in L1_WNAMES:
        W[nm] = ctx.dram(nm, list(shapes[nm]), F32, kind="ExternalInput")
    ctx.load_consts()
    dbg = None
    if debug:
        dbg = {"acc": ctx.dram("dbg_acc", [T, DM], F32, kind="ExternalOutput"),
               "u2": ctx.dram("dbg_u2", [T, DM], F32, kind="ExternalOutput")}
    emit_l1_layer(ctx, x_in, x_out, W, upto=upto, dbg=dbg)
    ctx.close()
    return ctx



PSL = S // NCORE


def emit_l2_gather_xpm(ctx, xn_all_dram, core_p0, XPM):
    nc = ctx.nc
    u = ctx.uid()
    les = contextlib.ExitStack()
    ld, cp = ctx.pool(2)
    stg = [ctx.sb(f"gx{u}_{i}", [128, PSL], BF16, es=les) for i in range(4)]
    b_ld, b_cp = ld.v, cp.v
    with nc.Block() as block:
        @block.sync
        def _(sy):
            n = 0
            for ft in range(8):
                for cb in range(8):
                    for lb in range(2):
                        if n >= 4:
                            sy.wait_ge(cp.h, b_cp + n - 3)
                        sy.dma_start(out=stg[n % 4][:],
                                     in_=xn_all_dram[cb * 1024 + ft * 128: cb * 1024 + (ft + 1) * 128,
                                                     lb * S + core_p0: lb * S + core_p0 + PSL]
                                     ).then_inc(ld.h, 16)
                        sy.wait_ge(ld.h, b_ld + (n + 1) * 16)
                        n += 1

        @block.vector
        def _(v):
            n = 0
            for ft in range(8):
                for cb in range(8):
                    for lb in range(2):
                        v.wait_ge(ld.h, b_ld + (n + 1) * 16)
                        bidx = cb * 2 + lb
                        dst = XPM[:, ft * 4096 + bidx: ft * 4096 + bidx + (PSL - 1) * 16 + 1]
                        dst = bass.AP(dst.tensor, dst.offset, [list(dst.ap[0]), [16, PSL]])
                        v.tensor_copy(dst, stg[n % 4][:]).then_inc(cp.h, 1)
                        n += 1

    ld.v = b_ld + 128 * 16
    cp.v = b_cp + 128
    ctx.barrier()
    les.close()


def emit_l2_attn(ctx, XPM, WQT2, WKT2, WVT2, bq_pp, bk_pp, bv_rep, o_out_dram,
                 o_is_a2a=False):
    nc = ctx.nc
    u = ctx.uid()
    les = contextlib.ExitStack()
    _n = ["qm", "qe", "km", "ke", "vm", "ve", "sm", "ex", "mk", "am", "d1",
          "d2", "mu", "st", "ms"]
    sems = dict(zip(_n, ctx.pool(15)))
    B0 = {k: s.v for k, s in sems.items()}
    QTt = ctx.sb(f"l2q{u}", [128, 1024], BF16, es=les)
    KTt = ctx.sb(f"l2k{u}", [128, 1024], BF16, es=les)
    VAt = ctx.sb(f"l2v{u}", [128, 8 * 129], BF16, es=les)
    pt = [ctx.sb(f"l2p{u}_{i}", [128, 128], BF16, es=les) for i in range(2)]
    dnr = ctx.sb(f"l2d{u}", [128, 2], F32, es=les)
    onat = [ctx.sb(f"l2o{u}_{i}", [128, DM], F32, es=les) for i in range(2)]
    psQK = ctx.ps_(f"l2pq{u}", [128, 1024], F32, es=les)
    psV = ctx.ps_(f"l2pv{u}", [128, 1024], F32, es=les)
    psS = [ctx.ps_(f"l2ps{u}_{i}", [128, 512], F32, es=les) for i in range(2)]
    psO = [ctx.ps_(f"l2po{u}_{i}", [128, 512], F32, es=les) for i in range(2)]
    bd = ctx.consts["c_blockdiag"]
    SC = 0.08838834764831845

    c = {k: B0[k] for k in _n}
    qm_at = {}; km_at = {}; vm_at = {}; qe_at = {}; ke_at = {}; ve_at = {}
    sm_at = {}; ex_at = {}; mk_at = {}; am_at = {}; d1_at = {}; d2_at = {}
    mu_at = {}; st_at = {}
    for i in range(NT):
        c["qm"] += 1; qm_at[i] = c["qm"]
        c["qe"] += 1; qe_at[i] = c["qe"]
        c["km"] += 1; km_at[i] = c["km"]
        c["ke"] += 1; ke_at[i] = c["ke"]
        c["vm"] += 1; vm_at[i] = c["vm"]
        c["ve"] += 1; ve_at[i] = c["ve"]
        for h in range(8):
            c["sm"] += 1; sm_at[(i, h)] = c["sm"]
            c["ex"] += 1; ex_at[(i, h)] = c["ex"]
            c["mk"] += 1; mk_at[(i, h)] = c["mk"]
            c["am"] += 1; am_at[(i, h)] = c["am"]
            c["d1"] += 1; d1_at[(i, h)] = c["d1"]
            c["d2"] += 1; d2_at[(i, h)] = c["d2"]
            c["mu"] += 1; mu_at[(i, h)] = c["mu"]
        c["st"] += 16; st_at[i] = c["st"]

    with nc.Block() as block:
        @block.tensor
        def _(t):
            t.wait_ge(sems["ms"].h, B0["ms"] + 1)
            for i in range(NT):
                for p, (W, grp) in enumerate([(WQT2, "qm"), (WKT2, "km")]):
                    if p == 0:
                        if i >= 1:
                            t.wait_ge(sems["ke"].h, ke_at[i - 1])
                    else:
                        t.wait_ge(sems["qe"].h, qe_at[i])
                    for ot in range(8):
                        for it in range(8):
                            ins = t.matmul(psQK[:, ot * 128:(ot + 1) * 128],
                                           fm_slice(W, 1024, it * 128, 128, ot * 128, 128),
                                           XPM[:, it * 4096 + i * 128: it * 4096 + (i + 1) * 128],
                                           start=(it == 0), stop=(it == 7))
                    ins.then_inc(sems[grp].h, 1)
                if i >= 1:
                    t.wait_ge(sems["ve"].h, ve_at[i - 1])
                for half in range(2):
                    for it in range(8):
                        ins = t.matmul(psV[:, half * 512:(half + 1) * 512],
                                       XPM[:, it * 4096 + i * 128: it * 4096 + (i + 1) * 128],
                                       fm_slice(WVT2, 1024, it * 128, 128, half * 512, 512),
                                       start=(it == 0), stop=(it == 7))
                ins.then_inc(sems["vm"].h, 1)
                for h in range(8):
                    t.wait_ge(sems["qe"].h, qe_at[i])
                    t.wait_ge(sems["ke"].h, ke_at[i])
                    if (i, h) > (0, 1):
                        ph = (i, h - 2) if h >= 2 else (i - 1, 6 + h)
                        t.wait_ge(sems["ex"].h, ex_at[ph])
                    t.matmul(psS[h % 2][:, 0:128],
                             KTt[:, h * 128:(h + 1) * 128],
                             QTt[:, h * 128:(h + 1) * 128],
                             start=True, stop=True).then_inc(sems["sm"].h, 1)
                    t.wait_ge(sems["mk"].h, mk_at[(i, h)])
                    if (i, h) > (0, 1):
                        ph = (i, h - 2) if h >= 2 else (i - 1, 6 + h)
                        t.wait_ge(sems["mu"].h, mu_at[ph])
                    t.matmul(psO[h % 2][:, 0:129],
                             pt[h % 2][:],
                             VAt[:, h * 129:(h + 1) * 129],
                             start=True, stop=True).then_inc(sems["am"].h, 1)

        @block.scalar
        def _(s):
            for i in range(NT):
                s.wait_ge(sems["qm"].h, qm_at[i])
                for ot in range(8):
                    ins = s.activation(QTt[:, ot * 128:(ot + 1) * 128],
                                       psQK[:, ot * 128:(ot + 1) * 128], AF.Identity,
                                       bias=bq_pp[:, ot:ot + 1])
                ins.then_inc(sems["qe"].h, 1)
                s.wait_ge(sems["km"].h, km_at[i])
                for ot in range(8):
                    ins = s.activation(KTt[:, ot * 128:(ot + 1) * 128],
                                       psQK[:, ot * 128:(ot + 1) * 128], AF.Identity,
                                       bias=bk_pp[:, ot:ot + 1])
                ins.then_inc(sems["ke"].h, 1)
                for h in range(8):
                    s.wait_ge(sems["sm"].h, sm_at[(i, h)])
                    if (i, h) > (0, 1):
                        ph = (i, h - 2) if h >= 2 else (i - 1, 6 + h)
                        s.wait_ge(sems["am"].h, am_at[ph])
                    s.activation(pt[h % 2][:], psS[h % 2][:, 0:128], AF.Exp,
                                 scale=SC).then_inc(sems["ex"].h, 1)

        @block.vector
        def _(v):
            v.memset(VAt[:], 1.0).then_inc(sems["ms"].h, 1)
            for i in range(NT):
                v.wait_ge(sems["vm"].h, vm_at[i])
                dst = VAt[:, :].rearrange("q (h j) -> q h j", j=129)[:, :, 0:128]
                srcp = psV[:, :].rearrange("q (h j) -> q h j", j=128)
                bvr = bv_rep[:, :].rearrange("q (h j) -> q h j", j=128)
                v.tensor_add(dst, srcp, bvr).then_inc(sems["ve"].h, 1)
                for h in range(8):
                    if h == 0 and i >= 2:
                        v.wait_ge(sems["st"].h, st_at[i - 2])
                    v.wait_ge(sems["ex"].h, ex_at[(i, h)])
                    v.tensor_mul(pt[h % 2][:], pt[h % 2][:], bd[:]).then_inc(sems["mk"].h, 1)
                    v.wait_ge(sems["am"].h, am_at[(i, h)])
                    v.tensor_scalar_add(dnr[:, 0:1], psO[h % 2][:, 128:129], 497.0).then_inc(sems["d1"].h, 1)
                    v.wait_ge(sems["d1"].h, d1_at[(i, h)])
                    v.reciprocal(dnr[:, 1:2], dnr[:, 0:1]).then_inc(sems["d2"].h, 1)
                    v.wait_ge(sems["d2"].h, d2_at[(i, h)])
                    v.tensor_scalar_mul(onat[i % 2][:, h * 128:(h + 1) * 128],
                                        psO[h % 2][:, 0:128],
                                        dnr[:, 1:2]).then_inc(sems["mu"].h, 1)

        @block.sync
        def _(sy):
            for i in range(NT):
                sy.wait_ge(sems["mu"].h, mu_at[(i, 7)])
                if o_is_a2a:
                    dst = bass.AP(o_out_dram.tensor, i * 16 * 1024,
                                  [[2 * 1024, 8], [512 * 1024, 8], [1024, 2], [1, 1024]])
                    sy.dma_start(out=dst, in_=onat[i % 2][:]).then_inc(sems["st"].h, 16)
                else:
                    sy.dma_start(out=o_out_dram[i * 128:(i + 1) * 128, :],
                                 in_=onat[i % 2][:]).then_inc(sems["st"].h, 16)
                sy.wait_ge(sems["st"].h, st_at[i])

    for k in _n:
        sems[k].v = c[k] if k != "ms" else B0["ms"] + 1
    ctx.barrier()
    les.close()


def emit_l2_ffn(ctx, xnt_dram, x_dram, xout_dram, WRT, WKT, WVT, mur_pp, muk_pp):
    nc = ctx.nc
    u = ctx.uid()
    les = contextlib.ExitStack()
    _n = ["ld", "pz", "sh", "km", "kr", "ks", "rm", "rs", "vm", "fm", "fx", "lx", "st"]
    sems = dict(zip(_n, ctx.pool(13)))
    B0 = {k: s.v for k, s in sems.items()}
    xc = [[ctx.sb(f"ffc{u}_{s}_{i}", [128, 512], BF16, es=les) for i in range(8)]
          for s in range(2)]
    xp = [[ctx.sb(f"ffp{u}_{s}_{i}", [128, 512], BF16, es=les) for i in range(8)]
          for s in range(2)]
    dtb = ctx.sb(f"ffd{u}", [128, 512], BF16, es=les)
    XRT = ctx.sb(f"ffr{u}", [128, 8 * 512], BF16, es=les)
    XKT = ctx.sb(f"ffk{u}", [128, 8 * 512], BF16, es=les)
    KAT = ctx.sb(f"ffa{u}", [128, 8 * 512], BF16, es=les)
    rb = ctx.sb(f"ffb{u}", [128, 512], BF16, es=les)
    rnat = ctx.sb(f"ffn{u}", [128, DM], BF16, es=les)
    fout = [ctx.sb(f"ffo{u}_{i}", [128, DM], F32, es=les) for i in range(2)]
    xtl = [ctx.sb(f"ffx{u}_{i}", [128, DM], F32, es=les) for i in range(2)]
    pk = [ctx.ps_(f"ffpk{u}_{i}", [128, 512], F32, es=les) for i in range(2)]
    pr = ctx.ps_(f"ffpr{u}", [128, 1024], F32, es=les)
    pv = ctx.ps_(f"ffpv{u}", [128, 1024], F32, es=les)

    NTB = T // 512
    c = {k: B0[k] for k in _n}
    sh_at = {}; km_at = {}; kr_at = {}; ks_at = {}; rm_at = {}; rs_at = {}
    vm_at = {}; fm_at = {}; fx_at = {}; st_at = {}; pz_at = {}
    for tb in range(NTB):
        if tb % 4 == 0:
            c["pz"] += 1
        pz_at[tb] = c["pz"]
        c["sh"] += 24; sh_at[tb] = c["sh"]
        for dt in range(8):
            c["km"] += 1; km_at[(tb, dt)] = c["km"]
            c["kr"] += 1; kr_at[(tb, dt)] = c["kr"]
            c["ks"] += 1; ks_at[(tb, dt)] = c["ks"]
        for tt in range(4):
            c["rm"] += 1; rm_at[(tb, tt)] = c["rm"]
            c["rs"] += 1; rs_at[(tb, tt)] = c["rs"]
            c["vm"] += 1; vm_at[(tb, tt)] = c["vm"]
            c["fm"] += 1; fm_at[(tb, tt)] = c["fm"]
            c["fx"] += 1; fx_at[(tb, tt)] = c["fx"]
            c["st"] += 16; st_at[(tb, tt)] = c["st"]

    with nc.Block() as block:
        @block.sync
        def _(sy):
            for tb in range(NTB):
                c0 = tb * 512
                first = (tb % 4 == 0)
                if tb >= 1:
                    sy.wait_ge(sems["ld"].h, B0["ld"] + tb * 256)
                if tb >= 2:
                    sy.wait_ge(sems["sh"].h, sh_at[tb - 1])
                for it in range(8):
                    sy.dma_start(out=xc[tb % 2][it][:],
                                 in_=xnt_dram[it * 128:(it + 1) * 128, c0:c0 + 512]
                                 ).then_inc(sems["ld"].h, 16)
                for it in range(8):
                    if first:
                        sy.dma_start(out=xp[tb % 2][it][:, 1:512],
                                     in_=xnt_dram[it * 128:(it + 1) * 128, c0:c0 + 511]
                                     ).then_inc(sems["ld"].h, 16)
                    else:
                        sy.dma_start(out=xp[tb % 2][it][:],
                                     in_=xnt_dram[it * 128:(it + 1) * 128, c0 - 1:c0 + 511]
                                     ).then_inc(sems["ld"].h, 16)
                for tt in range(4):
                    i = tb * 4 + tt
                    if i >= 2:
                        sy.wait_ge(sems["fx"].h, fx_at[((i - 2) // 4, (i - 2) % 4)])
                    sy.dma_start(out=xtl[tt % 2][:],
                                 in_=x_dram[i * 128:(i + 1) * 128, :]).then_inc(sems["lx"].h, 16)
                    sy.wait_ge(sems["fx"].h, fx_at[(tb, tt)])
                    sy.dma_start(out=xout_dram[i * 128:(i + 1) * 128, :],
                                 in_=fout[tt % 2][:]).then_inc(sems["st"].h, 16)
                    sy.wait_ge(sems["st"].h, st_at[(tb, tt)])

        @block.vector
        def _(v):
            for tb in range(NTB):
                first = (tb % 4 == 0)
                v.wait_ge(sems["ld"].h, B0["ld"] + (tb + 1) * 256)
                if first:
                    ins = None
                    for it in range(8):
                        ins = v.memset(xp[tb % 2][it][:, 0:1], 0.0)
                    ins.then_inc(sems["pz"].h, 1)
                    v.wait_ge(sems["pz"].h, pz_at[tb])
                if tb >= 2:
                    v.wait_ge(sems["vm"].h, vm_at[(tb - 1, 3)])
                for it in range(8):
                    if it >= 1:
                        v.wait_ge(sems["sh"].h, sh_at[tb] - 24 + it * 3)
                    v.tensor_sub(dtb[:], xc[tb % 2][it][:], xp[tb % 2][it][:]).then_inc(sems["sh"].h, 1)
                    v.wait_ge(sems["sh"].h, sh_at[tb] - 24 + it * 3 + 1)
                    v.scalar_tensor_tensor(XRT[:, it * 512:(it + 1) * 512], dtb[:],
                                           mur_pp[:, it:it + 1], xp[tb % 2][it][:],
                                           ALU.mult, ALU.add).then_inc(sems["sh"].h, 1)
                    v.scalar_tensor_tensor(XKT[:, it * 512:(it + 1) * 512], dtb[:],
                                           muk_pp[:, it:it + 1], xp[tb % 2][it][:],
                                           ALU.mult, ALU.add).then_inc(sems["sh"].h, 1)
                for dt in range(8):
                    v.wait_ge(sems["kr"].h, kr_at[(tb, dt)])
                    v.tensor_mul(KAT[:, dt * 512:(dt + 1) * 512], rb[:], rb[:]).then_inc(sems["ks"].h, 1)
                for tt in range(4):
                    i = tb * 4 + tt
                    v.wait_ge(sems["rs"].h, rs_at[(tb, tt)])
                    v.wait_ge(sems["vm"].h, vm_at[(tb, tt)])
                    if i >= 2:
                        v.wait_ge(sems["st"].h, st_at[((i - 2) // 4, (i - 2) % 4)])
                    v.tensor_mul(fout[tt % 2][:], pv[:], rnat[:]).then_inc(sems["fm"].h, 1)
                    v.wait_ge(sems["fm"].h, fm_at[(tb, tt)])
                    v.wait_ge(sems["lx"].h, B0["lx"] + (i + 1) * 16)
                    v.tensor_add(fout[tt % 2][:], fout[tt % 2][:], xtl[tt % 2][:]).then_inc(sems["fx"].h, 1)

        @block.scalar
        def _(s):
            for tb in range(NTB):
                for dt in range(8):
                    s.wait_ge(sems["km"].h, km_at[(tb, dt)])
                    if (tb, dt) > (0, 0):
                        pd = (tb, dt - 1) if dt >= 1 else (tb - 1, 7)
                        s.wait_ge(sems["ks"].h, ks_at[pd])
                    s.activation(rb[:], pk[dt % 2][:], AF.Relu).then_inc(sems["kr"].h, 1)
                for tt in range(4):
                    s.wait_ge(sems["rm"].h, rm_at[(tb, tt)])
                    if (tb, tt) > (0, 0):
                        pt_ = (tb, tt - 1) if tt >= 1 else (tb - 1, 3)
                        s.wait_ge(sems["fm"].h, fm_at[pt_])
                    s.activation(rnat[:], pr[:], AF.Sigmoid).then_inc(sems["rs"].h, 1)

        @block.tensor
        def _(t):
            for tb in range(NTB):
                t.wait_ge(sems["sh"].h, sh_at[tb])
                for dt in range(8):
                    if (tb, dt) >= (0, 2):
                        pd = (tb, dt - 2) if dt >= 2 else (tb - 1, 6 + dt)
                        t.wait_ge(sems["kr"].h, kr_at[pd])
                    for it in range(8):
                        ins = t.matmul(pk[dt % 2][:],
                                       fm_slice(WKT, 1024, it * 128, 128, dt * 128, 128),
                                       XKT[:, it * 512:(it + 1) * 512],
                                       start=(it == 0), stop=(it == 7))
                    ins.then_inc(sems["km"].h, 1)
                for tt in range(4):
                    if (tb, tt) > (0, 0):
                        pt_ = (tb, tt - 1) if tt >= 1 else (tb - 1, 3)
                        t.wait_ge(sems["rs"].h, rs_at[pt_])
                    for it in range(8):
                        for half in range(2):
                            ins = t.matmul(pr[:, half * 512:(half + 1) * 512],
                                           XRT[:, it * 512 + tt * 128: it * 512 + tt * 128 + 128],
                                           fm_slice(WRT, 1024, it * 128, 128, half * 512, 512),
                                           start=(it == 0), stop=(it == 7))
                    ins.then_inc(sems["rm"].h, 1)
                    t.wait_ge(sems["ks"].h, ks_at[(tb, 7)])
                    if (tb, tt) > (0, 0):
                        pt_ = (tb, tt - 1) if tt >= 1 else (tb - 1, 3)
                        t.wait_ge(sems["fm"].h, fm_at[pt_])
                    for dt in range(8):
                        for half in range(2):
                            ins = t.matmul(pv[:, half * 512:(half + 1) * 512],
                                           KAT[:, dt * 512 + tt * 128: dt * 512 + tt * 128 + 128],
                                           fm_slice(WVT, 1024, dt * 128, 128, half * 512, 512),
                                           start=(dt == 0), stop=(dt == 7))
                    ins.then_inc(sems["vm"].h, 1)

    for k in _n:
        if k == "ld":
            sems[k].v = B0[k] + NTB * 256
        elif k == "lx":
            sems[k].v = B0[k] + NTB * 4 * 16
        else:
            sems[k].v = c[k]
    ctx.barrier()
    les.close()


def emit_add(ctx, a_dram, b_dram, out_dram):
    nc = ctx.nc
    u = ctx.uid()
    les = contextlib.ExitStack()
    ld, ad, st = ctx.pool(3)
    at = [ctx.sb(f"ad{u}a{i}", [128, DM], F32, es=les) for i in range(2)]
    bt = [ctx.sb(f"ad{u}b{i}", [128, DM], F32, es=les) for i in range(2)]
    ot = [ctx.sb(f"ad{u}o{i}", [128, DM], F32, es=les) for i in range(2)]
    b_ld, b_ad, b_st = ld.v, ad.v, st.v
    with nc.Block() as block:
        @block.sync
        def _(sy):
            for i in range(NT):
                if i >= 2:
                    sy.wait_ge(ad.h, b_ad + i - 1)
                sy.dma_start(out=at[i % 2][:], in_=a_dram[i * 128:(i + 1) * 128, :]).then_inc(ld.h, 16)
                sy.dma_start(out=bt[i % 2][:], in_=b_dram[i * 128:(i + 1) * 128, :]).then_inc(ld.h, 16)
                sy.wait_ge(ld.h, b_ld + (i + 1) * 32)
                sy.wait_ge(ad.h, b_ad + i + 1)
                sy.dma_start(out=out_dram[i * 128:(i + 1) * 128, :], in_=ot[i % 2][:]).then_inc(st.h, 16)
                sy.wait_ge(st.h, b_st + (i + 1) * 16)

        @block.vector
        def _(v):
            for i in range(NT):
                v.wait_ge(ld.h, b_ld + (i + 1) * 32)
                if i >= 2:
                    v.wait_ge(st.h, b_st + (i - 1) * 16)
                v.tensor_add(ot[i % 2][:], at[i % 2][:], bt[i % 2][:]).then_inc(ad.h, 1)
    ld.v = b_ld + NT * 32
    ad.v = b_ad + NT
    st.v = b_st + NT * 16
    ctx.barrier()
    les.close()


L2_WNAMES = ["L2_attn_Wq", "L2_attn_bq", "L2_attn_Wk", "L2_attn_bk",
             "L2_attn_Wv", "L2_attn_bv", "L2_ffn_W_r", "L2_ffn_W_k",
             "L2_ffn_W_v", "L2_ffn_mu_r", "L2_ffn_mu_k"]


def build_l2a(debug=False):
    ctx = Ctx()
    x_in = ctx.dram("x_in", [T, DM], F32, kind="ExternalInput")
    xnt = ctx.dram("xnt_out", [DM, T], BF16, kind="ExternalOutput")
    xo = ctx.dram("xo_out", [T, DM], F32, kind="ExternalOutput")
    W = {nm: ctx.dram(nm, [1024, 1024] if "W_" in nm or "_W" in nm else [1024],
                      F32, kind="ExternalInput")
         for nm in ["L2_ffn_W_r", "L2_ffn_W_k", "L2_ffn_W_v", "L2_ffn_mu_r", "L2_ffn_mu_k"]}
    ctx.load_consts()
    emit_norm(ctx, x_in, xnt)
    les = contextlib.ExitStack()
    WRT = ctx.sb("l2_wrt", [128, 8 * 1024], BF16, es=les)
    WKT = ctx.sb("l2_wkt", [128, 8 * 1024], BF16, es=les)
    WVT = ctx.sb("l2_wvt", [128, 8 * 1024], BF16, es=les)
    mur = ctx.sb("l2_mur", [128, 8], F32, es=les)
    muk = ctx.sb("l2_muk", [128, 8], F32, es=les)
    emit_weight_fm(ctx, WRT, W["L2_ffn_W_r"], 1024, 1024, transpose=True)
    emit_weight_fm(ctx, WKT, W["L2_ffn_W_k"], 1024, 1024, transpose=True)
    emit_weight_fm(ctx, WVT, W["L2_ffn_W_v"], 1024, 1024, transpose=True)
    emit_load_pp(ctx, mur, W["L2_ffn_mu_r"], 8)
    emit_load_pp(ctx, muk, W["L2_ffn_mu_k"], 8)
    emit_l2_ffn(ctx, xnt, x_in, xo, WRT, WKT, WVT, mur, muk)
    les.close()
    ctx.close()
    return ctx


def build_l2b():
    ctx = Ctx()
    xpm_in = ctx.dram("xpm_in", [DM, T], BF16, kind="ExternalInput")
    o_out = ctx.dram("o_out", [T, DM], F32, kind="ExternalOutput")
    W = {}
    for nm in ["L2_attn_Wq", "L2_attn_Wk", "L2_attn_Wv"]:
        W[nm] = ctx.dram(nm, [1024, 1024], F32, kind="ExternalInput")
    for nm in ["L2_attn_bq", "L2_attn_bk", "L2_attn_bv"]:
        W[nm] = ctx.dram(nm, [1024], F32, kind="ExternalInput")
    ctx.load_consts()
    les = contextlib.ExitStack()
    XPM = ctx.sb("l2_xpm", [128, 8 * 4096], BF16, es=les)
    WQT2 = ctx.sb("l2_wq", [128, 8 * 1024], BF16, es=les)
    WKT2 = ctx.sb("l2_wk", [128, 8 * 1024], BF16, es=les)
    WVT2 = ctx.sb("l2_wv", [128, 8 * 1024], BF16, es=les)
    bqp = ctx.sb("l2_bq", [128, 8], F32, es=les)
    bkp = ctx.sb("l2_bk", [128, 8], F32, es=les)
    bvr = ctx.sb("l2_bv", [128, DM], F32, es=les)
    s0 = ctx.pool(1)[0]
    with ctx.nc.Block() as block:
        @block.sync
        def _(sy):
            for ft in range(8):
                sy.dma_start(out=XPM[:, ft * 4096:(ft + 1) * 4096],
                             in_=xpm_in[ft * 128:(ft + 1) * 128, :]).then_inc(s0.h, 16)
            sy.wait_ge(s0.h, s0.v + 128)
    s0.v += 128
    ctx.barrier()
    emit_weight_fm(ctx, WQT2, W["L2_attn_Wq"], 1024, 1024, transpose=True)
    emit_weight_fm(ctx, WKT2, W["L2_attn_Wk"], 1024, 1024, transpose=True)
    emit_weight_fm(ctx, WVT2, W["L2_attn_Wv"], 1024, 1024, transpose=True)
    emit_load_pp(ctx, bqp, W["L2_attn_bq"], 8)
    emit_load_pp(ctx, bkp, W["L2_attn_bk"], 8)
    emit_replicate_rows(ctx, bvr, W["L2_attn_bv"], 1024)
    emit_l2_attn(ctx, XPM, WQT2, WKT2, WVT2, bqp, bkp, bvr, o_out)
    les.close()
    ctx.close()
    return ctx



def emit_l3_qk(ctx, xnt_dram, b, WQfm, WKfm, bq_pp, bk_pp, QTf, KTf, KNat):
    nc = ctx.nc
    u = ctx.uid()
    les = contextlib.ExitStack()
    _n = ["ld", "qm", "f1", "f2", "f3", "f4", "tm", "te"]
    sems = dict(zip(_n, ctx.pool(8)))
    B0 = {k: s.v for k, s in sems.items()}
    xsl = [[ctx.sb(f"l3x{u}_{s}_{i}", [128, 512], BF16, es=les) for i in range(8)]
           for s in range(2)]
    tmn = ctx.sb(f"l3n{u}", [128, 512], F32, es=les)
    tme = ctx.sb(f"l3e{u}", [128, 512], F32, es=les)
    tmr = ctx.sb(f"l3r{u}", [128, 512], F32, es=les)
    ps01 = [ctx.ps_(f"l3p{u}_{i}", [128, 512], F32, es=les) for i in range(2)]
    ptr = [ctx.ps_(f"l3t{u}_{i}", [128, 1024], BF16, es=les) for i in range(2)]
    ident = ctx.consts["c_ident"]

    c = {k: B0[k] for k in _n}
    qm_at = {}; f4_at = {}; tm_at = {}; te_at = {}
    for tb in range(4):
        for p in range(2):
            for ot in range(8):
                c["qm"] += 1; qm_at[(tb, p, ot)] = c["qm"]
                c["f4"] += 1; f4_at[(tb, p, ot)] = c["f4"]
        for tt4 in range(4):
            c["tm"] += 1; tm_at[(tb, tt4)] = c["tm"]
            c["te"] += 1; te_at[(tb, tt4)] = c["te"]

    with nc.Block() as block:
        @block.sync
        def _(sy):
            for tb in range(4):
                if tb >= 1:
                    sy.wait_ge(sems["ld"].h, B0["ld"] + tb * 128)
                if tb >= 2:
                    sy.wait_ge(sems["qm"].h, qm_at[(tb - 1, 1, 7)])
                for it in range(8):
                    sy.dma_start(out=xsl[tb % 2][it][:],
                                 in_=xnt_dram[it * 128:(it + 1) * 128,
                                              b * S + tb * 512: b * S + (tb + 1) * 512]
                                 ).then_inc(sems["ld"].h, 16)

        @block.tensor
        def _(t):
            for tb in range(4):
                t.wait_ge(sems["ld"].h, B0["ld"] + (tb + 1) * 128)
                for p in range(2):
                    W = WQfm if p == 0 else WKfm
                    for ot in range(8):
                        gi = tb * 16 + p * 8 + ot
                        if gi >= 2:
                            pk = gi - 2
                            pot = pk % 8
                            pp = (pk // 8) % 2
                            ptb = pk // 16
                            t.wait_ge(sems["f4"].h, f4_at[(ptb, pp, pot)])
                        for it in range(8):
                            ins = t.matmul(ps01[gi % 2][:],
                                           fm_slice(W, 1024, it * 128, 128, ot * 128, 128),
                                           xsl[tb % 2][it][:],
                                           start=(it == 0), stop=(it == 7))
                        ins.then_inc(sems["qm"].h, 1)
                for tt4 in range(4):
                    t.wait_ge(sems["f4"].h, f4_at[(tb, 1, 7)])
                    if tb * 4 + tt4 >= 2:
                        i2 = tb * 4 + tt4 - 2
                        t.wait_ge(sems["te"].h, te_at[(i2 // 4, i2 % 4)])
                    tok0 = tb * 512 + tt4 * 128
                    for ft in range(8):
                        ins = t.transpose(ptr[tt4 % 2][:, ft * 128:(ft + 1) * 128],
                                          fm_slice(KTf, 2048, ft * 128, 128, tok0, 128),
                                          ident[:])
                    ins.then_inc(sems["tm"].h, 1)

        @block.vector
        def _(v):
            for tb in range(4):
                for p in range(2):
                    bpp = bq_pp if p == 0 else bk_pp
                    dstT = QTf if p == 0 else KTf
                    for ot in range(8):
                        gi = tb * 16 + p * 8 + ot
                        v.wait_ge(sems["qm"].h, qm_at[(tb, p, ot)])
                        if gi >= 1:
                            v.wait_ge(sems["f4"].h, B0["f4"] + gi)
                            v.wait_ge(sems["f3"].h, B0["f3"] + gi)
                        v.tensor_scalar(tmn[:], ps01[gi % 2][:], bpp[:, ot:ot + 1], 0.0,
                                        ALU.add, ALU.min).then_inc(sems["f1"].h, 1)
                        v.tensor_scalar(tmr[:], ps01[gi % 2][:], bpp[:, ot:ot + 1], 0.0,
                                        ALU.add, ALU.max).then_inc(sems["f2"].h, 1)
                        v.wait_ge(sems["f3"].h, B0["f3"] + gi + 1)
                        v.wait_ge(sems["f2"].h, B0["f2"] + gi + 1)
                        v.tensor_add(fm_slice(dstT, 2048, ot * 128, 128, tb * 512, 512),
                                     tme[:], tmr[:]).then_inc(sems["f4"].h, 1)
                for tt4 in range(4):
                    v.wait_ge(sems["tm"].h, tm_at[(tb, tt4)])
                    tok0 = tb * 512 + tt4 * 128
                    v.tensor_copy(
                        KNat[:, (tb * 4 + tt4) * 1024:(tb * 4 + tt4 + 1) * 1024],
                        ptr[tt4 % 2][:]).then_inc(sems["te"].h, 1)

        @block.scalar
        def _(s):
            for tb in range(4):
                for p in range(2):
                    for ot in range(8):
                        gi = tb * 16 + p * 8 + ot
                        s.wait_ge(sems["f1"].h, B0["f1"] + gi + 1)
                        if gi >= 1:
                            s.wait_ge(sems["f4"].h, B0["f4"] + gi)
                        s.activation(tme[:], tmn[:], AF.Exp).then_inc(sems["f3"].h, 1)

    for k in _n:
        if k == "ld":
            sems[k].v = B0[k] + 4 * 128
        elif k in ("f1", "f2", "f3"):
            sems[k].v = B0[k] + 64
        elif k == "qm":
            sems[k].v = c["qm"]
        elif k == "f4":
            sems[k].v = c["f4"]
        elif k in ("tm", "te"):
            sems[k].v = c[k]
    ctx.barrier()
    les.close()


def emit_l3_scan(ctx, QTf, KTf, KNat, VA, ONAT):
    nc = ctx.nc
    u = ctx.uid()
    les = contextlib.ExitStack()
    _n = ["c1", "a1", "a2", "n1", "s1", "d1", "d2", "d3"]
    sems = dict(zip(_n, ctx.pool(8)))
    B0 = {k: s.v for k, s in sems.items()}
    Sprev = [ctx.sb(f"scS{u}_{i}", [128, 65], BF16, es=les) for i in range(2)]
    Amb = [ctx.sb(f"scA{u}_{i}", [128, 128], BF16, es=les) for i in range(2)]
    dnr = ctx.sb(f"scD{u}", [128, 2], F32, es=les)
    psS = [ctx.ps_(f"scPS{u}_{i}", [128, 512], F32, es=les) for i in range(2)]
    psA = [ctx.ps_(f"scPA{u}_{i}", [128, 512], F32, es=les) for i in range(2)]
    psN = [ctx.ps_(f"scPN{u}_{i}", [128, 512], F32, es=les) for i in range(2)]
    cmask = ctx.consts["c_causincl"]
    NHD = 16
    NCH = 16
    EPS3 = 1e-6

    c = {k: B0[k] for k in _n}
    at = {k: {} for k in _n}
    for h in range(NHD):
        for j in range(NCH):
            for k in ("c1", "a1", "a2", "n1", "s1", "d1", "d2", "d3"):
                c[k] += 1
                at[k][(h, j)] = c[k]

    with nc.Block() as block:
        @block.vector
        def _(v):
            for h in range(NHD):
                for j in range(NCH):
                    sp = Sprev[j % 2][64 * (h % 2):64 * (h % 2) + 64, :]
                    spo = Sprev[(j + 1) % 2][64 * (h % 2):64 * (h % 2) + 64, :]
                    if j == 0:
                        if h > 0:
                            v.wait_ge(sems["n1"].h, at["n1"][(h - 1, NCH - 1)])
                        v.memset(sp, 0.0).then_inc(sems["c1"].h, 1)
                    else:
                        v.wait_ge(sems["s1"].h, at["s1"][(h, j - 1)])
                        v.wait_ge(sems["n1"].h, at["n1"][(h, j - 1)])
                        v.tensor_add(sp, psS[(j - 1) % 2][0:64, 0:65],
                                     spo).then_inc(sems["c1"].h, 1)
                    v.wait_ge(sems["a1"].h, at["a1"][(h, j)])
                    gia = h * NCH + j
                    if gia >= 2:
                        p2 = gia - 2
                        v.wait_ge(sems["n1"].h, at["n1"][(p2 // NCH, p2 % NCH)])
                    v.tensor_mul(Amb[j % 2][:], psA[j % 2][0:128, 0:128],
                                 cmask[:]).then_inc(sems["a2"].h, 1)
                    v.wait_ge(sems["n1"].h, at["n1"][(h, j)])
                    gi = h * NCH + j
                    if gi >= 1:
                        p1 = gi - 1
                        v.wait_ge(sems["d2"].h, at["d2"][(p1 // NCH, p1 % NCH)])
                    v.tensor_scalar_add(dnr[:, 0:1], psN[j % 2][0:128, 64:65],
                                        EPS3).then_inc(sems["d1"].h, 1)
                    v.wait_ge(sems["d1"].h, at["d1"][(h, j)])
                    if gi >= 1:
                        p1 = gi - 1
                        v.wait_ge(sems["d3"].h, at["d3"][(p1 // NCH, p1 % NCH)])
                    v.reciprocal(dnr[:, 1:2], dnr[:, 0:1]).then_inc(sems["d2"].h, 1)
                    v.wait_ge(sems["d2"].h, at["d2"][(h, j)])
                    v.tensor_scalar_mul(ONAT[:, j * 1024 + 64 * h: j * 1024 + 64 * h + 64],
                                        psN[j % 2][0:128, 0:64],
                                        dnr[:, 1:2]).then_inc(sems["d3"].h, 1)

        @block.tensor
        def _(t):
            for h in range(NHD):
                for j in range(NCH):
                    gi = h * NCH + j
                    if gi >= 2:
                        p2 = gi - 2
                        t.wait_ge(sems["a2"].h, at["a2"][(p2 // NCH, p2 % NCH)])
                    t.matmul(psA[j % 2][0:128, 0:128],
                             fm_slice(KTf, 2048, 64 * h, 64, j * 128, 128),
                             fm_slice(QTf, 2048, 64 * h, 64, j * 128, 128),
                             start=True, stop=True).then_inc(sems["a1"].h, 1)
                    t.wait_ge(sems["a2"].h, at["a2"][(h, j)])
                    t.wait_ge(sems["c1"].h, at["c1"][(h, j)])
                    if gi >= 2:
                        p2 = gi - 2
                        t.wait_ge(sems["d3"].h, at["d3"][(p2 // NCH, p2 % NCH)])
                    t.matmul(psN[j % 2][0:128, 0:65],
                             Amb[j % 2][:],
                             VA[:, j * 1040 + 65 * h: j * 1040 + 65 * h + 65],
                             start=True, stop=False)
                    t.matmul(psN[j % 2][0:128, 0:65],
                             fm_slice(QTf, 2048, 64 * h, 64, j * 128, 128),
                             Sprev[j % 2][64 * (h % 2):64 * (h % 2) + 64, :],
                             start=False, stop=True).then_inc(sems["n1"].h, 1)
                    if h * NCH + j >= 2:
                        p2 = h * NCH + j - 2
                        t.wait_ge(sems["c1"].h, at["c1"][(p2 // NCH, p2 % NCH)] + 1
                                  if False else at["c1"][((h * NCH + j - 1) // NCH,
                                                          (h * NCH + j - 1) % NCH)])
                    t.matmul(psS[j % 2][0:64, 0:65],
                             KNat[:, j * 1024 + 64 * h: j * 1024 + 64 * h + 64],
                             VA[:, j * 1040 + 65 * h: j * 1040 + 65 * h + 65],
                             start=True, stop=True).then_inc(sems["s1"].h, 1)

    for k in _n:
        sems[k].v = c[k]
    ctx.barrier()
    les.close()


def emit_l3_proj(ctx, ONAT, WOfm, bo_rep, b, acc_dram):
    nc = ctx.nc
    u = ctx.uid()
    les = contextlib.ExitStack()
    (tm, te, pm, pe, st) = ctx.pool(5)
    OTt = ctx.sb(f"pjO{u}", [128, 1024], BF16, es=les)
    accst = [ctx.sb(f"pjA{u}_{i}", [128, DM], F32, es=les) for i in range(2)]
    ptr = [ctx.ps_(f"pjT{u}_{i}", [128, 1024], BF16, es=les) for i in range(2)]
    pproj = [ctx.ps_(f"pjP{u}_{i}", [128, 1024], F32, es=les) for i in range(2)]
    ident = ctx.consts["c_ident"]
    b_ = {s: s.v for s in (tm, te, pm, pe, st)}
    NCH = 16
    with nc.Block() as block:
        @block.tensor
        def _(t):
            for j in range(NCH):
                if j >= 2:
                    t.wait_ge(te.h, b_[te] + j - 1)
                for ft in range(8):
                    ins = t.transpose(ptr[j % 2][:, ft * 128:(ft + 1) * 128],
                                      ONAT[:, j * 1024 + ft * 128: j * 1024 + (ft + 1) * 128],
                                      ident[:])
                ins.then_inc(tm.h, 1)
                t.wait_ge(te.h, b_[te] + j + 1)
                if j >= 2:
                    t.wait_ge(pe.h, b_[pe] + j - 1)
                for et in range(8):
                    for half in range(2):
                        ins = t.matmul(pproj[j % 2][:, half * 512:(half + 1) * 512],
                                       OTt[:, et * 128:(et + 1) * 128],
                                       fm_slice(WOfm, 1024, et * 128, 128, half * 512, 512),
                                       start=(et == 0), stop=(et == 7))
                ins.then_inc(pm.h, 1)

        @block.vector
        def _(v):
            for j in range(NCH):
                v.wait_ge(tm.h, b_[tm] + j + 1)
                if j >= 1:
                    v.wait_ge(pm.h, b_[pm] + j)
                v.tensor_copy(OTt[:], ptr[j % 2][:]).then_inc(te.h, 1)
                v.wait_ge(pm.h, b_[pm] + j + 1)
                if j >= 2:
                    v.wait_ge(st.h, b_[st] + (j - 1) * 16)
                v.tensor_add(accst[j % 2][:], pproj[j % 2][:], bo_rep[:]).then_inc(pe.h, 1)

        @block.sync
        def _(sy):
            for j in range(NCH):
                sy.wait_ge(pe.h, b_[pe] + j + 1)
                g = b * S + j * 128
                sy.dma_start(out=acc_dram[g:g + 128, :], in_=accst[j % 2][:]).then_inc(st.h, 16)
                sy.wait_ge(st.h, b_[st] + (j + 1) * 16)

    tm.v = b_[tm] + NCH
    te.v = b_[te] + NCH
    pm.v = b_[pm] + NCH
    pe.v = b_[pe] + NCH
    st.v = b_[st] + NCH * 16
    ctx.barrier()
    les.close()


def emit_l3_gate(ctx, xnt_dram, wgred_dram, wg_dram, MSK, SCO, COEF):
    nc = ctx.nc
    u = ctx.uid()
    les = contextlib.ExitStack()
    _n = ["ld", "w1", "w2", "w3", "w4", "w5", "xm", "xe", "lm", "e1", "e2",
          "e3", "e4", "e5", "lw"]
    sems = dict(zip(_n, ctx.pool(15)))
    B0 = {k: s.v for k, s in sems.items()}
    WGR = ctx.sb(f"g3r{u}", [16, 1024], F32, es=les)
    WGRT = ctx.sb(f"g3t{u}", [128, 8 * 16], BF16, es=les)
    WGU = ctx.sb(f"g3u{u}", [8, 16], F32, es=les)
    WGUt = ctx.sb(f"g3v{u}", [16, 8], BF16, es=les)
    wgs = ctx.sb(f"g3w{u}", [8, 16], F32, es=les)
    n2 = ctx.sb(f"g3n{u}", [8, 3], F32, es=les)
    xrT = ctx.sb(f"g3x{u}", [16, 512], BF16, es=les)
    lg = ctx.sb(f"g3l{u}", [128, 8], F32, es=les)
    mx = ctx.sb(f"g3m{u}", [128, 2], F32, es=les)
    ee = ctx.sb(f"g3e{u}", [128, 8], F32, es=les)
    sume = ctx.sb(f"g3s{u}", [128, 1], F32, es=les)
    xsl = [[ctx.sb(f"g3a{u}_{s}_{i}", [128, 512], BF16, es=les) for i in range(8)]
           for s in range(2)]
    pg = [ctx.ps_(f"g3p{u}_{i}", [128, 512], F32, es=les) for i in range(2)]
    identf = ctx.consts["c_identf"]
    ident = ctx.consts["c_ident"]
    NB = T // 512

    with nc.Block() as block:
        @block.sync
        def _(sy):
            sy.dma_start(out=WGR[:], in_=wgred_dram).then_inc(sems["lw"].h, 16)
            sy.dma_start(out=wgs[:], in_=wg_dram).then_inc(sems["lw"].h, 16)
            for nb in range(NB):
                if nb >= 1:
                    sy.wait_ge(sems["ld"].h, B0["ld"] + nb * 128)
                if nb >= 2:
                    sy.wait_ge(sems["xm"].h, B0["xm"] + nb - 1)
                for it in range(8):
                    sy.dma_start(out=xsl[nb % 2][it][:],
                                 in_=xnt_dram[it * 128:(it + 1) * 128,
                                              nb * 512:(nb + 1) * 512]).then_inc(sems["ld"].h, 16)

        @block.tensor
        def _(t):
            t.wait_ge(sems["lw"].h, B0["lw"] + 32)
            for cb in range(8):
                ins = t.transpose(pg[0][:, cb * 16:(cb + 1) * 16],
                                  WGR[:, cb * 128:(cb + 1) * 128],
                                  identf[0:16, 0:16])
            ins.then_inc(sems["w1"].h, 1)
            t.wait_ge(sems["w4"].h, B0["w4"] + 1)
            t.transpose(pg[1][0:16, 0:8], WGU[:], identf[0:8, 0:8]).then_inc(sems["w5"].h, 1)
            t.wait_ge(sems["w2"].h, B0["w2"] + 2)
            for nb in range(NB):
                t.wait_ge(sems["ld"].h, B0["ld"] + (nb + 1) * 128)
                if nb >= 1:
                    t.wait_ge(sems["xe"].h, B0["xe"] + nb)
                for it in range(8):
                    ins = t.matmul(pg[0][0:16, :],
                                   fm_slice(WGRT, 16, it * 128, 128, 0, 16),
                                   xsl[nb % 2][it][:],
                                   start=(it == 0), stop=(it == 7))
                ins.then_inc(sems["xm"].h, 1)
                for q in range(4):
                    i = nb * 4 + q
                    t.wait_ge(sems["xe"].h, B0["xe"] + nb + 1)
                    if i >= 1:
                        t.wait_ge(sems["e1"].h, B0["e1"] + i)
                    t.matmul(pg[1][0:128, 0:8], xrT[:, q * 128:(q + 1) * 128],
                             WGUt[:], start=True, stop=True).then_inc(sems["lm"].h, 1)

        @block.vector
        def _(v):
            v.wait_ge(sems["w1"].h, B0["w1"] + 1)
            for cb in range(8):
                ins = v.tensor_copy(WGRT[:, cb * 16:(cb + 1) * 16],
                                    pg[0][:, cb * 16:(cb + 1) * 16])
            ins.then_inc(sems["w2"].h, 1)
            v.wait_ge(sems["lw"].h, B0["lw"] + 32)
            v.tensor_mul(WGU[:], wgs[:], wgs[:]).then_inc(sems["w3"].h, 1)
            v.wait_ge(sems["w3"].h, B0["w3"] + 1)
            v.reduce_sum(n2[:, 0:1], WGU[:], axis=AXX).then_inc(sems["e5"].h, 1)
            v.wait_ge(sems["w3"].h, B0["w3"] + 2)
            v.tensor_scalar_max(n2[:, 1:2], n2[:, 1:2], 1e-4).then_inc(sems["w3"].h, 1)
            v.wait_ge(sems["w3"].h, B0["w3"] + 3)
            v.reciprocal(n2[:, 2:3], n2[:, 1:2]).then_inc(sems["w3"].h, 1)
            v.wait_ge(sems["w3"].h, B0["w3"] + 4)
            v.tensor_scalar_mul(WGU[:], wgs[:], n2[:, 2:3]).then_inc(sems["w4"].h, 1)
            v.wait_ge(sems["w5"].h, B0["w5"] + 1)
            v.tensor_copy(WGUt[:], pg[1][0:16, 0:8]).then_inc(sems["w2"].h, 1)
            for nb in range(NB):
                for q in range(4):
                    i = nb * 4 + q
                    v.wait_ge(sems["lm"].h, B0["lm"] + i + 1)
                    if i >= 1:
                        v.wait_ge(sems["e3"].h, B0["e3"] + i)
                    v.tensor_copy(lg[:], pg[1][0:128, 0:8]).then_inc(sems["e1"].h, 1)
                    v.wait_ge(sems["e1"].h, B0["e1"] + i + 1)
                    v.reduce_max(mx[:, 0:1], lg[:], axis=AXX).then_inc(sems["e5"].h, 1)
                    v.wait_ge(sems["e5"].h, B0["e5"] + 1 + i + 1)
                    v.tensor_scalar_mul(mx[:, 1:2], mx[:, 0:1], -1.0).then_inc(sems["e2"].h, 1)
                    v.wait_ge(sems["e3"].h, B0["e3"] + i + 1)
                    v.tensor_single_scalar(MSK[:, i * 8:(i + 1) * 8], ee[:], 1.0,
                                           ALU.is_ge).then_inc(sems["e4"].h, 1)
                    with nc.allow_low_precision(reason="scores recip"):
                        v.reciprocal(SCO[:, i:i + 1], sume[:]).then_inc(sems["e4"].h, 1)
                    v.wait_ge(sems["e4"].h, B0["e4"] + 3 * i + 2)
                    v.tensor_scalar_mul(COEF[:, i * 8:(i + 1) * 8],
                                        MSK[:, i * 8:(i + 1) * 8],
                                        SCO[:, i:i + 1]).then_inc(sems["e4"].h, 1)

        @block.scalar
        def _(s):
            s.wait_ge(sems["e5"].h, B0["e5"] + 1)
            s.activation(n2[:, 1:2], n2[:, 0:1], AF.Sqrt).then_inc(sems["w3"].h, 1)
            for nb in range(NB):
                s.wait_ge(sems["xm"].h, B0["xm"] + nb + 1)
                if nb >= 1:
                    s.wait_ge(sems["lm"].h, B0["lm"] + nb * 4)
                s.activation(xrT[:], pg[0][0:16, :], AF.Identity).then_inc(sems["xe"].h, 1)
                for q in range(4):
                    i = nb * 4 + q
                    s.wait_ge(sems["e2"].h, B0["e2"] + i + 1)
                    if i >= 1:
                        s.wait_ge(sems["e4"].h, B0["e4"] + 3 * i)
                    s.activation(ee[:], lg[:], AF.Exp, bias=mx[:, 1:2],
                                 accum_out=sume[:]).then_inc(sems["e3"].h, 1)

    sems["ld"].v = B0["ld"] + NB * 128
    sems["lw"].v = B0["lw"] + 32
    sems["w1"].v = B0["w1"] + 1
    sems["w2"].v = B0["w2"] + 2
    sems["w3"].v = B0["w3"] + 4
    sems["w4"].v = B0["w4"] + 1
    sems["w5"].v = B0["w5"] + 1
    sems["xm"].v = B0["xm"] + NB
    sems["xe"].v = B0["xe"] + NB
    sems["lm"].v = B0["lm"] + NB * 4
    sems["e1"].v = B0["e1"] + NT
    sems["e2"].v = B0["e2"] + NT
    sems["e3"].v = B0["e3"] + NT
    sems["e4"].v = B0["e4"] + NT * 3
    sems["e5"].v = B0["e5"] + 1 + NT
    ctx.barrier()
    les.close()


L3_WNAMES = ["L3_attn_wq", "L3_attn_bq", "L3_attn_wk", "L3_attn_bk",
             "L3_attn_wv", "L3_attn_bv", "L3_attn_wo", "L3_attn_bo",
             "L3_ffn_wg_red", "L3_ffn_wg", "L3_ffn_w1", "L3_ffn_b1",
             "L3_ffn_w2", "L3_ffn_b2"]


def emit_l3_layer(ctx, x_dram, xout_dram, W, upto="all", dbg=None):
    xnt = ctx.dram("l3_xnt", [DM, T], BF16)
    acc = dbg["acc"] if dbg and "acc" in dbg else ctx.dram("l3_acc", [T, DM], F32)
    macc = ctx.dram("l3_macc", [T, DM], F32)

    emit_norm(ctx, x_dram, xnt)
    if upto == "norm":
        return
    lay = contextlib.ExitStack()
    QTf = ctx.sb("l3_qtf", [128, 8 * 2048], BF16, es=lay)
    KTf = ctx.sb("l3_ktf", [128, 8 * 2048], BF16, es=lay)
    KNat = ctx.sb("l3_kn", [128, 16 * 1024], BF16, es=lay)
    for b in range(BT):
        les = contextlib.ExitStack()
        WQfm = ctx.sb(f"l3_wq{b}", [128, 8 * 1024], BF16, es=les)
        WKfm = ctx.sb(f"l3_wk{b}", [128, 8 * 1024], BF16, es=les)
        bqp = ctx.sb(f"l3_bq{b}", [128, 8], F32, es=les)
        bkp = ctx.sb(f"l3_bk{b}", [128, 8], F32, es=les)
        emit_weight_fm(ctx, WQfm, W["L3_attn_wq"], 1024, 1024, transpose=False)
        emit_weight_fm(ctx, WKfm, W["L3_attn_wk"], 1024, 1024, transpose=False)
        emit_load_pp(ctx, bqp, W["L3_attn_bq"], 8)
        emit_load_pp(ctx, bkp, W["L3_attn_bk"], 8)
        emit_l3_qk(ctx, xnt, b, WQfm, WKfm, bqp, bkp, QTf, KTf, KNat)
        les.close()
        les = contextlib.ExitStack()
        VA = ctx.sb(f"l3_va{b}", [128, 16 * 1040], BF16, es=les)
        lesv = contextlib.ExitStack()
        WVfm = ctx.sb(f"l3_wv{b}", [128, 8 * 1024], BF16, es=lesv)
        bvr = ctx.sb(f"l3_bv{b}", [128, DM], F32, es=lesv)
        emit_weight_fm(ctx, WVfm, W["L3_attn_wv"], 1024, 1024, transpose=False)
        emit_replicate_rows(ctx, bvr, W["L3_attn_bv"], 1024)
        emit_l0_qkv(ctx, xnt, b, WVfm, None, VA, WC=1024, parts=2,
                    wv_spec=(WVfm, 1024, 0), bv_rep=bvr)
        lesv.close()
        ONAT = ctx.sb(f"l3_on{b}", [128, 16 * 1024], BF16, es=les)
        emit_l3_scan(ctx, QTf, KTf, KNat, VA, ONAT)
        lesp = contextlib.ExitStack()
        WOfm = ctx.sb(f"l3_wo{b}", [128, 8 * 1024], BF16, es=lesp)
        bor = ctx.sb(f"l3_bo{b}", [128, DM], F32, es=lesp)
        emit_weight_fm(ctx, WOfm, W["L3_attn_wo"], 1024, 1024, transpose=False)
        emit_replicate_rows(ctx, bor, W["L3_attn_bo"], 1024)
        emit_l3_proj(ctx, ONAT, WOfm, bor, b, acc)
        lesp.close()
        les.close()
    lay.close()
    if upto == "attn":
        return
    lay = contextlib.ExitStack()
    MSK = ctx.sb("l3_msk", [128, NT * 8], F32, es=lay)
    SCO = ctx.sb("l3_sco", [128, NT], F32, es=lay)
    COEF = ctx.sb("l3_coef", [128, NT * 8], F32, es=lay)
    emit_l3_gate(ctx, xnt, W["L3_ffn_wg_red"], W["L3_ffn_wg"], MSK, SCO, COEF)
    if upto == "gate":
        lay.close()
        return
    for e in range(8):
        lese = contextlib.ExitStack()
        W1fm = ctx.sb(f"l3_e{e}w1", [128, 8 * 4096], BF16, es=lese)
        W2fm = ctx.sb(f"l3_e{e}w2", [128, 32 * 1024], BF16, es=lese)
        b1pp = ctx.sb(f"l3_e{e}b1", [128, 32], F32, es=lese)
        B2e = ctx.sb(f"l3_e{e}b2", [128, DM], F32, es=lese)
        emit_weight_fm(ctx, W1fm, W["L3_ffn_w1"][e], 1024, 4096, transpose=False)
        emit_weight_fm(ctx, W2fm, W["L3_ffn_w2"][e], 4096, 1024, transpose=False)
        emit_load_pp(ctx, b1pp, W["L3_ffn_b1"][e], 32)
        emit_replicate_rows(ctx, B2e, W["L3_ffn_b2"][e], 1024)
        emit_moe_expert(ctx, xnt, W1fm, W2fm, b1pp, B2e, COEF, e, 8, macc,
                        first=(e == 0), last=(e == 7),
                        final_addends=[x_dram, acc],
                        xout_dram=xout_dram, act=AF.Gelu)
        lese.close()
    lay.close()


def build_l3(upto="all", debug=False):
    ctx = Ctx()
    xo_in = ctx.dram("xo_in", [T, DM], F32, kind="ExternalInput")
    o_in = ctx.dram("o_in", [T, DM], F32, kind="ExternalInput")
    x_out = ctx.dram("x_out", [T, DM], F32, kind="ExternalOutput")
    shapes = {"L3_attn_wq": (1024, 1024), "L3_attn_bq": (1024,),
              "L3_attn_wk": (1024, 1024), "L3_attn_bk": (1024,),
              "L3_attn_wv": (1024, 1024), "L3_attn_bv": (1024,),
              "L3_attn_wo": (1024, 1024), "L3_attn_bo": (1024,),
              "L3_ffn_wg_red": (16, 1024), "L3_ffn_wg": (8, 16),
              "L3_ffn_w1": (8, 1024, 4096), "L3_ffn_b1": (8, 4096),
              "L3_ffn_w2": (8, 4096, 1024), "L3_ffn_b2": (8, 1024)}
    W = {nm: ctx.dram(nm, list(shapes[nm]), F32, kind="ExternalInput")
         for nm in L3_WNAMES}
    ctx.load_consts()
    x3 = ctx.dram("l3_x3", [T, DM], F32)
    emit_add(ctx, xo_in, o_in, x3)
    dbg = None
    if debug:
        dbg = {"acc": ctx.dram("dbg_acc", [T, DM], F32, kind="ExternalOutput")}
    emit_l3_layer(ctx, x3, x_out, W, upto=upto, dbg=dbg)
    ctx.close()
    return ctx



L0_WNAMES = ["L0_attn_w_qkv", "L0_attn_w_proj", "L0_ffn_w1", "L0_ffn_b1",
             "L0_ffn_w2", "L0_ffn_b2"]
L2A_WNAMES = ["L2_ffn_W_r", "L2_ffn_W_k", "L2_ffn_W_v", "L2_ffn_mu_r", "L2_ffn_mu_k"]
L2B_WNAMES = ["L2_attn_Wq", "L2_attn_bq", "L2_attn_Wk", "L2_attn_bk",
              "L2_attn_Wv", "L2_attn_bv"]

_BUILD_CACHE = {}


def _get(name, fn):
    if name not in _BUILD_CACHE:
        _BUILD_CACHE[name] = fn()
    return _BUILD_CACHE[name]


def kernel(**inputs):
    consts = host_constants()
    x = np.ascontiguousarray(inputs["L0_attn_x"].reshape(16, 2048, 1024))
    ctx = _get("fused", build_fused)
    in_maps = []
    for c in range(NCORE):
        m = dict(consts)
        for nm in ALL_WNAMES:
            m[nm] = inputs[nm]
        m["x_in"] = x[2 * c:2 * c + 2].reshape(T, DM)
        in_maps.append(m)
    res = run_bass_kernel_spmd(ctx.nc, in_maps, core_ids=list(range(NCORE)))
    x4 = [np.asarray(res.results[c]["x_out"]) for c in range(NCORE)]
    return np.stack(x4).reshape(16, 2048, 1024).astype(np.float32, copy=False)


def kernel_multilaunch(**inputs):
    consts = host_constants()
    x = np.ascontiguousarray(inputs["L0_attn_x"].reshape(16, 2048, 1024))

    def launch(ctx, per_core_extra, wnames):
        in_maps = []
        for c in range(NCORE):
            m = dict(consts)
            for nm in wnames:
                m[nm] = inputs[nm]
            m.update(per_core_extra[c])
            in_maps.append(m)
        return run_bass_kernel_spmd(ctx.nc, in_maps, core_ids=list(range(NCORE)))

    ctx = _get("l0", build_l0)
    res = launch(ctx, [{"x_in": x[2 * c:2 * c + 2].reshape(T, DM)} for c in range(NCORE)],
                 L0_WNAMES)
    x1 = [np.asarray(res.results[c]["x_out"]) for c in range(NCORE)]
    ctx = _get("l1", lambda: build_l1())
    res = launch(ctx, [{"x_in": x1[c]} for c in range(NCORE)], L1_WNAMES)
    x2 = [np.asarray(res.results[c]["x_out"]) for c in range(NCORE)]
    ctx = _get("l2a", build_l2a)
    res = launch(ctx, [{"x_in": x2[c]} for c in range(NCORE)], L2A_WNAMES)
    xo = [np.asarray(res.results[c]["xo_out"]) for c in range(NCORE)]
    xnt = [np.asarray(res.results[c]["xnt_out"]) for c in range(NCORE)]
    xnt_all = np.stack(xnt)
    xnt_all = xnt_all.reshape(8, DM, 2, S)
    xpm_full = xnt_all.transpose(1, 3, 0, 2).reshape(DM, S, 16)
    xpm = [np.ascontiguousarray(
        xpm_full[:, c * PSL:(c + 1) * PSL, :].reshape(DM, T))
        for c in range(NCORE)]
    ctx = _get("l2b", build_l2b)
    res = launch(ctx, [{"xpm_in": xpm[c]} for c in range(NCORE)], L2B_WNAMES)
    o_pm = [np.asarray(res.results[c]["o_out"]) for c in range(NCORE)]
    o_all = np.stack(o_pm).reshape(8, PSL, 16, DM)
    o_all = o_all.transpose(2, 0, 1, 3).reshape(16, S, DM)
    ctx = _get("l3", lambda: build_l3())
    res = launch(ctx, [{"xo_in": xo[c],
                        "o_in": o_all[2 * c:2 * c + 2].reshape(T, DM)}
                       for c in range(NCORE)], L3_WNAMES)
    x4 = [np.asarray(res.results[c]["x_out"]) for c in range(NCORE)]
    out = np.stack(x4).reshape(16, 2048, 1024).astype(np.float32, copy=False)
    return out



from concourse.aot_env import allow_ici_all_to_all


def emit_a2a(ctx, in_ap, out_ap, semname):
    s = ctx.sem(semname)
    with ctx.nc.Block() as block:
        @block.gpsimd
        def _(gp):
            with allow_ici_all_to_all():
                gp.collective_compute("AllToAll", ALU.bypass,
                                      replica_groups=[list(range(NCORE))],
                                      ins=[in_ap.opt()],
                                      outs=[out_ap.opt()]).then_inc(s.h)
    s.v += 1


def emit_l2_gather_v2(ctx, a2a_out, XPM, ccsem):
    nc = ctx.nc
    u = ctx.uid()
    les = contextlib.ExitStack()
    ld, cp = ctx.pool(2)
    stg = [ctx.sb(f"g2{u}_{i}", [128, PSL], BF16, es=les) for i in range(4)]
    b_ld, b_cp = ld.v, cp.v
    with nc.Block() as block:
        @block.sync
        def _(sy):
            sy.wait_ge(ccsem.h, ccsem.v)
            n = 0
            for ft in range(8):
                for b in range(16):
                    if n >= 4:
                        sy.wait_ge(cp.h, b_cp + n - 3)
                    sy.dma_start(out=stg[n % 4][:],
                                 in_=a2a_out[(b // 2) * 1024 + ft * 128:
                                             (b // 2) * 1024 + (ft + 1) * 128,
                                             (b % 2) * 256:(b % 2) * 256 + PSL]
                                 ).then_inc(ld.h, 16)
                    sy.wait_ge(ld.h, b_ld + (n + 1) * 16)
                    n += 1

        @block.vector
        def _(v):
            n = 0
            for ft in range(8):
                for b in range(16):
                    v.wait_ge(ld.h, b_ld + (n + 1) * 16)
                    base = XPM[:, ft * 4096 + b: ft * 4096 + b + 1]
                    dst = bass.AP(base.tensor, base.offset,
                                  [list(base.ap[0]), [16, PSL]])
                    v.tensor_copy(dst, stg[n % 4][:]).then_inc(cp.h, 1)
                    n += 1

    ld.v = b_ld + 128 * 16
    cp.v = b_cp + 128
    ctx.barrier()
    les.close()


def emit_o_unpack(ctx, o_a2a_out, o_b_dram, ccsem):
    nc = ctx.nc
    u = ctx.uid()
    les = contextlib.ExitStack()
    ld, st = ctx.pool(2)
    stg = [ctx.sb(f"ou{u}_{i}", [128, DM], F32, es=les) for i in range(2)]
    b_ld, b_st = ld.v, st.v
    with nc.Block() as block:
        @block.sync
        def _(sy):
            sy.wait_ge(ccsem.h, ccsem.v)
            n = 0
            for lb in range(2):
                for k in range(16):
                    off = (512 * (k // 2) + 128 * (k % 2) * 2 + lb) * 1024
                    src = bass.AP(o_a2a_out.tensor, off, [[2 * 1024, 128], [1, 1024]])
                    sy.dma_start(out=stg[n % 2][:], in_=src).then_inc(ld.h, 16)
                    sy.wait_ge(ld.h, b_ld + (n + 1) * 16)
                    sy.dma_start(out=o_b_dram[(lb * 16 + k) * 128:(lb * 16 + k + 1) * 128, :],
                                 in_=stg[n % 2][:]).then_inc(st.h, 16)
                    sy.wait_ge(st.h, b_st + (n + 1) * 16)
                    n += 1
    ld.v = b_ld + 32 * 16
    st.v = b_st + 32 * 16
    ctx.barrier()
    les.close()


def emit_o_pack(ctx, o_pm, o_a2a_in):
    nc = ctx.nc
    u = ctx.uid()
    les = contextlib.ExitStack()
    ld, st = ctx.pool(2)
    stg = [ctx.sb(f"op{u}_{i}", [128, DM], F32, es=les) for i in range(2)]
    b_ld, b_st = ld.v, st.v
    with nc.Block() as block:
        @block.sync
        def _(sy):
            n = 0
            for c in range(8):
                for lb in range(2):
                    for half in range(2):
                        src = bass.AP(o_pm.tensor,
                                      (half * 128 * 16 + 2 * c + lb) * 1024,
                                      [[16 * 1024, 128], [1, 1024]])
                        sy.dma_start(out=stg[n % 2][:], in_=src).then_inc(ld.h, 16)
                        sy.wait_ge(ld.h, b_ld + (n + 1) * 16)
                        dst = bass.AP(o_a2a_in.tensor,
                                      (512 * c + half * 256 + lb) * 1024,
                                      [[2 * 1024, 128], [1, 1024]])
                        sy.dma_start(out=dst, in_=stg[n % 2][:]).then_inc(st.h, 16)
                        sy.wait_ge(st.h, b_st + (n + 1) * 16)
                        n += 1
    ld.v = b_ld + 32 * 16
    st.v = b_st + 32 * 16
    ctx.barrier()
    les.close()


ALL_WNAMES = (L0_WNAMES + L1_WNAMES + L2A_WNAMES + L2B_WNAMES + L3_WNAMES)


def build_fused():
    ctx = Ctx()
    x_in = ctx.dram("x_in", [T, DM], F32, kind="ExternalInput")
    x_out = ctx.dram("x_out", [T, DM], F32, kind="ExternalOutput")
    shapes = {"L0_attn_w_qkv": (3072, 1024), "L0_attn_w_proj": (1024, 1024),
              "L0_ffn_w1": (4096, 1024), "L0_ffn_b1": (4096,),
              "L0_ffn_w2": (1024, 4096), "L0_ffn_b2": (1024,),
              "L1_attn_Wq": (1024, 1024), "L1_attn_bq": (1024,),
              "L1_attn_Wk": (1024, 1024), "L1_attn_bk": (1024,),
              "L1_attn_Wv": (1024, 1024), "L1_attn_bv": (1024,),
              "L1_attn_Wo": (1024, 1024), "L1_attn_bo": (1024,),
              "L1_ffn_ln1_g": (1024,), "L1_ffn_ln1_b": (1024,),
              "L1_ffn_ln2_g": (1024,), "L1_ffn_ln2_b": (1024,),
              "L1_ffn_wq": (1024, 1024), "L1_ffn_bq": (1024,),
              "L1_ffn_wk": (1024, 1024), "L1_ffn_bk": (1024,),
              "L1_ffn_wv": (1024, 1024), "L1_ffn_bv": (1024,),
              "L1_ffn_wo": (1024, 1024), "L1_ffn_bo": (1024,),
              "L1_ffn_gate_w": (1024, 8), "L1_ffn_gate_b": (8,),
              "L1_ffn_w1": (8, 1024, 4096), "L1_ffn_b1": (8, 4096),
              "L1_ffn_w2": (8, 4096, 1024), "L1_ffn_b2": (8, 1024),
              "L2_attn_Wq": (1024, 1024), "L2_attn_bq": (1024,),
              "L2_attn_Wk": (1024, 1024), "L2_attn_bk": (1024,),
              "L2_attn_Wv": (1024, 1024), "L2_attn_bv": (1024,),
              "L2_ffn_W_r": (1024, 1024), "L2_ffn_W_k": (1024, 1024),
              "L2_ffn_W_v": (1024, 1024), "L2_ffn_mu_r": (1024,),
              "L2_ffn_mu_k": (1024,),
              "L3_attn_wq": (1024, 1024), "L3_attn_bq": (1024,),
              "L3_attn_wk": (1024, 1024), "L3_attn_bk": (1024,),
              "L3_attn_wv": (1024, 1024), "L3_attn_bv": (1024,),
              "L3_attn_wo": (1024, 1024), "L3_attn_bo": (1024,),
              "L3_ffn_wg_red": (16, 1024), "L3_ffn_wg": (8, 16),
              "L3_ffn_w1": (8, 1024, 4096), "L3_ffn_b1": (8, 4096),
              "L3_ffn_w2": (8, 4096, 1024), "L3_ffn_b2": (8, 1024)}
    W = {nm: ctx.dram(nm, list(shapes[nm]), F32, kind="ExternalInput")
         for nm in ALL_WNAMES}
    ctx.load_consts()
    xA = ctx.dram("fz_x1", [T, DM], F32)
    xB = ctx.dram("fz_x2", [T, DM], F32)
    emit_l0_layer(ctx, x_in, xA, W)
    emit_l1_layer(ctx, xA, xB, W)
    xnt2 = ctx.dram("fz_xnt2", [DM, T], BF16)
    a2a_xin = ctx.dram("fz_a2a_xi", [8 * DM, 512], BF16)
    a2a_xout = ctx.dram("fz_a2a_xo", [8 * DM, 512], BF16)
    o_a2a_in = ctx.dram("fz_a2a_oi", [8 * 512, DM], F32)
    o_a2a_out = ctx.dram("fz_a2a_oo", [8 * 512, DM], F32)
    xo = ctx.dram("fz_xo", [T, DM], F32)
    o_b = ctx.dram("fz_ob", [T, DM], F32)
    x3 = ctx.dram("fz_x3", [T, DM], F32)
    emit_norm(ctx, xB, xnt2, a2a_dram=a2a_xin)
    emit_a2a(ctx, a2a_xin, a2a_xout, "cc1")
    les = contextlib.ExitStack()
    WRT = ctx.sb("f2_wrt", [128, 8 * 1024], BF16, es=les)
    WKTf = ctx.sb("f2_wkt", [128, 8 * 1024], BF16, es=les)
    WVTf = ctx.sb("f2_wvt", [128, 8 * 1024], BF16, es=les)
    murp = ctx.sb("f2_mur", [128, 8], F32, es=les)
    mukp = ctx.sb("f2_muk", [128, 8], F32, es=les)
    emit_weight_fm(ctx, WRT, W["L2_ffn_W_r"], 1024, 1024, transpose=True)
    emit_weight_fm(ctx, WKTf, W["L2_ffn_W_k"], 1024, 1024, transpose=True)
    emit_weight_fm(ctx, WVTf, W["L2_ffn_W_v"], 1024, 1024, transpose=True)
    emit_load_pp(ctx, murp, W["L2_ffn_mu_r"], 8)
    emit_load_pp(ctx, mukp, W["L2_ffn_mu_k"], 8)
    emit_l2_ffn(ctx, xnt2, xB, xo, WRT, WKTf, WVTf, murp, mukp)
    les.close()
    les = contextlib.ExitStack()
    XPM = ctx.sb("f2_xpm", [128, 8 * 4096], BF16, es=les)
    WQT2 = ctx.sb("f2_wq", [128, 8 * 1024], BF16, es=les)
    WKT2 = ctx.sb("f2_wk", [128, 8 * 1024], BF16, es=les)
    WVT2 = ctx.sb("f2_wv", [128, 8 * 1024], BF16, es=les)
    bqp2 = ctx.sb("f2_bq", [128, 8], F32, es=les)
    bkp2 = ctx.sb("f2_bk", [128, 8], F32, es=les)
    bvr2 = ctx.sb("f2_bv", [128, DM], F32, es=les)
    emit_l2_gather_v2(ctx, a2a_xout, XPM, ctx.sem("cc1"))
    emit_weight_fm(ctx, WQT2, W["L2_attn_Wq"], 1024, 1024, transpose=True)
    emit_weight_fm(ctx, WKT2, W["L2_attn_Wk"], 1024, 1024, transpose=True)
    emit_weight_fm(ctx, WVT2, W["L2_attn_Wv"], 1024, 1024, transpose=True)
    emit_load_pp(ctx, bqp2, W["L2_attn_bq"], 8)
    emit_load_pp(ctx, bkp2, W["L2_attn_bk"], 8)
    emit_replicate_rows(ctx, bvr2, W["L2_attn_bv"], 1024)
    o_pm = ctx.dram("fz_opm", [T, DM], F32)
    emit_l2_attn(ctx, XPM, WQT2, WKT2, WVT2, bqp2, bkp2, bvr2, o_pm)
    les.close()
    emit_o_pack(ctx, o_pm, o_a2a_in)
    emit_a2a(ctx, o_a2a_in, o_a2a_out, "cc2")
    emit_o_unpack(ctx, o_a2a_out, o_b, ctx.sem("cc2"))
    emit_add(ctx, xo, o_b, x3)
    emit_l3_layer(ctx, x3, x_out, W)
    ctx.close()
    return ctx


# version 04
# speedup vs baseline: 1.1308x; 1.0583x over previous
import contextlib
import numpy as np
import ml_dtypes

import concourse.bass as bass
import concourse.mybir as mybir
from concourse.bass_utils import run_bass_kernel_spmd

F32 = mybir.dt.float32
BF16 = mybir.dt.bfloat16
AF = mybir.ActivationFunctionType
ALU = mybir.AluOpType
AXX = mybir.AxisListType.X

NCORE = 8
BT = 2
S = 2048
DM = 1024
T = BT * S
NT = T // 128
EPS_LN = 1e-5

BF = ml_dtypes.bfloat16



def host_constants():
    c = {}
    c["c_ident"] = np.eye(128, dtype=BF)
    c["c_identf"] = np.eye(128, dtype=np.float32)
    kp = np.arange(128)[:, None]
    ql = np.arange(512)[None, :]
    caus = np.concatenate([(kp + 128 * dt <= ql).astype(BF) for dt in range(4)], axis=1)
    c["c_caus"] = caus
    c["c_ones"] = np.ones((1, 512), dtype=BF)
    pos = np.arange(S, dtype=np.float64)
    inv = 1.0 / (10000.0 ** (np.arange(0, 64, 2, dtype=np.float64) / 64.0))
    ang = pos[:, None] * inv[None, :]
    cosv = np.cos(ang)
    sinv = np.sin(ang)
    cos_full = np.concatenate([cosv, cosv], axis=1)
    sin_full = np.concatenate([-sinv, sinv], axis=1)
    c["c_ropecos"] = np.tile(cos_full, (1, 16)).astype(np.float32)
    c["c_ropesin"] = np.tile(sin_full, (1, 16)).astype(np.float32)
    tau = np.arange(128)[:, None]
    tt = np.arange(128)[None, :]
    c["c_causincl"] = (tau <= tt).astype(BF)
    bd = np.zeros((128, 128), dtype=BF)
    for j in range(8):
        bd[16 * j:16 * j + 16, 16 * j:16 * j + 16] = 1
    c["c_blockdiag"] = bd
    return c


CONST_SPECS = [
    ("c_ident", (128, 128), BF16),
    ("c_identf", (128, 128), F32),
    ("c_caus", (128, 2048), BF16),
    ("c_ones", (1, 512), BF16),
    ("c_causincl", (128, 128), BF16),
    ("c_blockdiag", (128, 128), BF16),
]
DRAM_CONST_SPECS = [
    ("c_ropecos", (S, DM), F32),
    ("c_ropesin", (S, DM), F32),
]



class Sem:
    __slots__ = ("h", "v")

    def __init__(self, h):
        self.h = h
        self.v = 0


class Ctx:
    def __init__(self):
        self.nc = bass.Bass("TRN2", target_bir_lowering=False)
        self.es = contextlib.ExitStack()
        self.sems = {}
        self.consts = {}
        self._uid = 0

    def uid(self):
        self._uid += 1
        return self._uid

    def sem(self, name):
        if name not in self.sems:
            self.sems[name] = Sem(self.es.enter_context(self.nc.semaphore(name)))
        return self.sems[name]

    def pool(self, k):
        return [self.sem(f"p{i}") for i in range(k)]

    def gpool(self, k):
        return [self.sem(f"g{i}") for i in range(k)]

    def sb(self, name, shape, dtype, es=None):
        return (es or self.es).enter_context(self.nc.sbuf_tensor(name, shape, dtype))

    def ps_(self, name, shape, dtype=F32, es=None):
        return (es or self.es).enter_context(self.nc.psum_tensor(name, shape, dtype))

    def ps(self, name, shape, dtype=F32):
        return self.es.enter_context(self.nc.psum_tensor(name, shape, dtype))

    def dram(self, name, shape, dtype, kind="Internal"):
        return self.nc.dram_tensor(name, list(shape), dtype, kind=kind).ap()

    def barrier(self):
        self.nc.all_engine_barrier()

    def close(self):
        self.es.close()

    def load_consts(self):
        ld = self.sem("c_ld")
        aps = {}
        for name, shape, dtype in CONST_SPECS:
            self.consts[name] = self.sb("sb_" + name, list(shape), dtype)
            aps[name] = self.dram(name, shape, dtype, kind="ExternalInput")
        self.dram_consts = {}
        for name, shape, dtype in DRAM_CONST_SPECS:
            self.dram_consts[name] = self.dram(name, shape, dtype, kind="ExternalInput")
        with self.nc.Block() as block:
            @block.sync
            def _(sy):
                for name, shape, dtype in CONST_SPECS:
                    sy.dma_start(out=self.consts[name][:], in_=aps[name]).then_inc(ld.h, 16)
                    ld.v += 16
                sy.wait_ge(ld.h, ld.v)
        self.barrier()


def fm_slice(tensor, C, f0, nf, c0, ncol):
    ti = f0 // 128
    p0 = f0 % 128
    assert p0 + nf <= 128, (f0, nf)
    return tensor[p0:p0 + nf, ti * C + c0: ti * C + c0 + ncol]



def emit_weight_fm(ctx, dst, w_dram, NF, C, transpose):
    nc = ctx.nc
    u = ctx.uid()
    mm, ev = ctx.pool(2)
    ldp = ctx.gpool(2)
    ld = ldp[0]
    les = contextlib.ExitStack()
    if not transpose:
        nrt = NF // 128
        with nc.Block() as block:
            @block.gpsimd
            def _(gp):
                for r in range(nrt):
                    gp.dma_start(out=fm_slice(dst, C, r * 128, 128, 0, C),
                                 in_=w_dram[r * 128:(r + 1) * 128, :]).then_inc(ld.h, 16)
                gp.wait_ge(ld.h, ld.v + nrt * 16)
        ld.v += nrt * 16
    else:
        stg = [ctx.sb(f"wstg{u}_{i}", [128, NF], BF16, es=les) for i in range(2)]
        ptr = [ctx.ps_(f"wps{u}_{i}", [128, 512], BF16, es=les) for i in range(2)]
        nrow_t = C // 128
        ncol_t = NF // 128
        ident = ctx.consts["c_ident"]
        ld0 = [ldp[0].v, ldp[1].v]
        mm0, ev0 = mm.v, ev.v
        with nc.Block() as block:
            @block.gpsimd
            def _(gp):
                for r in range(nrow_t):
                    if r >= 2:
                        gp.wait_ge(mm.h, mm0 + (r - 1) * ncol_t)
                    gp.dma_start(out=stg[r % 2][:, :],
                                 in_=w_dram[r * 128:(r + 1) * 128, :]).then_inc(ldp[r % 2].h, 16)

            @block.tensor
            def _(t):
                for r in range(nrow_t):
                    t.wait_ge(ldp[r % 2].h, ld0[r % 2] + (r // 2 + 1) * 16)
                    for cb in range(ncol_t):
                        i = r * ncol_t + cb
                        if i >= 2:
                            t.wait_ge(ev.h, ev0 + i - 1)
                        t.transpose(ptr[i % 2][:, 0:128],
                                    stg[r % 2][:, cb * 128:(cb + 1) * 128],
                                    ident[:]).then_inc(mm.h, 1)

            @block.vector
            def _(v):
                for r in range(nrow_t):
                    for cb in range(ncol_t):
                        i = r * ncol_t + cb
                        v.wait_ge(mm.h, mm0 + i + 1)
                        v.tensor_copy(fm_slice(dst, C, cb * 128, 128, r * 128, 128),
                                      ptr[i % 2][:, 0:128]).then_inc(ev.h, 1)
        ldp[0].v = ld0[0] + ((nrow_t + 1) // 2) * 16
        ldp[1].v = ld0[1] + (nrow_t // 2) * 16
        mm.v = mm0 + nrow_t * ncol_t
        ev.v = ev0 + nrow_t * ncol_t
    ctx.barrier()
    les.close()


def emit_replicate_rows(ctx, dst, src_dram_rows, n):
    nc = ctx.nc
    u = ctx.uid()
    mm, ev = ctx.pool(2)
    ld = ctx.gpool(1)[0]
    les = contextlib.ExitStack()
    row = ctx.sb(f"rrow{u}", [1, n], BF16, es=les)
    pr = ctx.ps_(f"rps{u}", [128, 512], F32, es=les)
    ones = ctx.consts["c_ones"]
    nchunk = (n + 511) // 512
    src = src_dram_rows
    if len(src.shape) == 1:
        src = src.rearrange("(o n) -> o n", o=1)
    with nc.Block() as block:
        @block.gpsimd
        def _(gp):
            gp.dma_start(out=row[:], in_=src).then_inc(ld.h, 16)

        @block.tensor
        def _(t):
            t.wait_ge(ld.h, ld.v + 16)
            for cidx in range(nchunk):
                w = min(512, n - cidx * 512)
                if cidx >= 1:
                    t.wait_ge(ev.h, ev.v + cidx)
                t.matmul(pr[:, 0:w], ones[:, 0:128], row[:, cidx * 512:cidx * 512 + w],
                         start=True, stop=True).then_inc(mm.h, 1)

        @block.vector
        def _(v):
            for cidx in range(nchunk):
                w = min(512, n - cidx * 512)
                v.wait_ge(mm.h, mm.v + cidx + 1)
                v.tensor_copy(dst[:, cidx * 512:cidx * 512 + w], pr[:, 0:w]).then_inc(ev.h, 1)
    ld.v += 16
    mm.v += nchunk
    ev.v += nchunk
    ctx.barrier()
    les.close()


def emit_norm(ctx, x_dram, xnt_dram, xnf_dram=None, grep=None, brep=None,
              n_tiles=NT, x_is_input=True, a2a_dram=None):
    nc = ctx.nc
    u = ctx.uid()
    _p = ctx.pool(14)
    ld, mm, ev, st = _p[0], _p[1], _p[2], _p[3]
    ee = _p[4:14]
    G = 4
    NG = n_tiles // G
    ident = ctx.consts["c_ident"]
    affine = grep is not None

    les = contextlib.ExitStack()
    xt = [ctx.sb(f"nxt{u}_{j}", [128, DM], F32, es=les) for j in range(G)]
    xc = [ctx.sb(f"nxc{u}_{j}", [128, DM], F32, es=les) for j in range(G)]
    xnbf = [ctx.sb(f"nxb{u}_{j}", [128, DM], BF16, es=les) for j in range(G)]
    xnst = [ctx.sb(f"nxs{u}_{j}", [128, DM], BF16, es=les) for j in range(G)]
    xaf = [ctx.sb(f"nxa{u}_{j}", [128, DM], F32, es=les) for j in range(G)] if (affine or xnf_dram is not None) else None
    sv = ctx.sb(f"nsv{u}", [128, G], F32, es=les)
    nmean = ctx.sb(f"nnm{u}", [128, G], F32, es=les)
    v2 = ctx.sb(f"nv2{u}", [128, G], F32, es=les)
    vart = ctx.sb(f"nvt{u}", [128, G], F32, es=les)
    sdv = ctx.sb(f"nsd{u}", [128, G], F32, es=les)
    rstd = ctx.sb(f"nrs{u}", [128, G], F32, es=les)
    ptr = [ctx.ps_(f"nps{u}_{j}", [128, DM], BF16, es=les) for j in range(G)]

    n_stores = 1 + (1 if a2a_dram is not None else 0) + (1 if xnf_dram is not None else 0)
    b = {id(s): s.v for s in [ld, mm, ev, st] + ee}

    def bv(s):
        return b[id(s)]

    with nc.Block() as block:
        @block.sync
        def _(sy):
            for g in range(NG):
                if g >= 1:
                    sy.wait_ge(st.h, bv(st) + g * G * 16 * n_stores)
                for j in range(G):
                    i = g * G + j
                    sy.dma_start(out=xt[j][:], in_=x_dram[i * 128:(i + 1) * 128, :]).then_inc(ld.h, 16)
                for j in range(G):
                    i = g * G + j
                    sy.wait_ge(ev.h, bv(ev) + i + 1)
                    dst = xnt_dram.rearrange("(c p) t -> p c t", p=128)
                    sy.dma_start(out=dst[:, :, i * 128:(i + 1) * 128],
                                 in_=xnst[j][:].rearrange("p (c q) -> p c q", c=8)).then_inc(st.h, 16)
                    if a2a_dram is not None:
                        jj = (i % 16) // 2
                        cb0 = (i // 16) * 256 + 128 * (i % 2)
                        dst2 = a2a_dram.rearrange("(jx c p) q -> p c jx q", c=8, p=128)
                        sy.dma_start(out=dst2[:, :, jj, cb0:cb0 + 128],
                                     in_=xnst[j][:].rearrange("p (c q) -> p c q", c=8)).then_inc(st.h, 16)
                    if xnf_dram is not None:
                        sy.wait_ge(ee[9].h, bv(ee[9]) + g + 1)
                        sy.dma_start(out=xnf_dram[i * 128:(i + 1) * 128, :],
                                     in_=xaf[j][:]).then_inc(st.h, 16)
            sy.wait_ge(st.h, bv(st) + NG * G * 16 * n_stores)

        @block.vector
        def _(v):
            for g in range(NG):
                v.wait_ge(ld.h, bv(ld) + (g + 1) * G * 16)
                for j in range(G):
                    ins = v.reduce_sum(sv[:, j:j + 1], xt[j][:], axis=AXX)
                ins.then_inc(ee[1].h, 1)
                v.wait_ge(ee[4].h, bv(ee[4]) + g + 1)
                for j in range(G):
                    ins = v.tensor_scalar(vart[:, j:j + 1], v2[:, j:j + 1], 1.0 / DM, EPS_LN,
                                          ALU.mult, ALU.add)
                ins.then_inc(ee[5].h, 1)
                v.wait_ge(ee[6].h, bv(ee[6]) + g + 1)
                for j in range(G):
                    ins = v.reciprocal(rstd[:, j:j + 1], sdv[:, j:j + 1])
                ins.then_inc(ee[7].h, 1)
                if affine or xnf_dram is not None:
                    v.wait_ge(ee[7].h, bv(ee[7]) + g + 1)
                    for j in range(G):
                        if affine:
                            ins = v.scalar_tensor_tensor(xaf[j][:], xc[j][:], rstd[:, j:j + 1],
                                                         grep[:], ALU.mult, ALU.mult)
                        else:
                            ins = v.tensor_scalar_mul(xaf[j][:], xc[j][:], rstd[:, j:j + 1])
                    ins.then_inc(ee[8].h, 1)
                    if affine:
                        v.wait_ge(ee[8].h, bv(ee[8]) + g + 1)
                        for j in range(G):
                            ins = v.tensor_add(xaf[j][:], xaf[j][:], brep[:])
                        ins.then_inc(ee[9].h, 1)
                    else:
                        v.wait_ge(ee[8].h, bv(ee[8]) + g + 1)
                        v.engine_nop().then_inc(ee[9].h, 1)
                for j in range(G):
                    i = g * G + j
                    v.wait_ge(mm.h, bv(mm) + (i + 1) * 8)
                    v.tensor_copy(xnst[j][:], ptr[j][:]).then_inc(ev.h, 1)

        @block.scalar
        def _(s):
            for g in range(NG):
                s.wait_ge(ee[1].h, bv(ee[1]) + g + 1)
                for j in range(G):
                    ins = s.mul(nmean[:, j:j + 1], sv[:, j:j + 1], -1.0 / DM)
                ins.then_inc(ee[2].h, 1)
                s.wait_ge(ee[2].h, bv(ee[2]) + g + 1)
                for j in range(G):
                    ins = s.activation(xc[j][:], xt[j][:], AF.Identity, bias=nmean[:, j:j + 1])
                ins.then_inc(ee[3].h, 1)
                s.wait_ge(ee[3].h, bv(ee[3]) + g + 1)
                for j in range(G):
                    ins = s.activation(xt[j][:], xc[j][:], AF.Square, accum_out=v2[:, j:j + 1])
                ins.then_inc(ee[4].h, 1)
                s.wait_ge(ee[5].h, bv(ee[5]) + g + 1)
                for j in range(G):
                    ins = s.activation(sdv[:, j:j + 1], vart[:, j:j + 1], AF.Sqrt)
                ins.then_inc(ee[6].h, 1)
                if not affine:
                    s.wait_ge(ee[7].h, bv(ee[7]) + g + 1)
                    for j in range(G):
                        ins = s.activation(xnbf[j][:], xc[j][:], AF.Identity, scale=rstd[:, j:j + 1])
                else:
                    s.wait_ge(ee[9].h, bv(ee[9]) + g + 1)
                    for j in range(G):
                        ins = s.copy(xnbf[j][:], xaf[j][:])
                ins.then_inc(ee[0].h, 1)

        @block.tensor
        def _(t):
            for g in range(NG):
                t.wait_ge(ee[0].h, bv(ee[0]) + g + 1)
                for j in range(G):
                    for cb in range(8):
                        t.transpose(ptr[j][:, cb * 128:(cb + 1) * 128],
                                    xnbf[j][:, cb * 128:(cb + 1) * 128],
                                    ident[:]).then_inc(mm.h, 1)

    ld.v += NG * G * 16
    for k in range(10):
        if k == 9 and not (affine or xnf_dram is not None):
            continue
        if k == 8 and not (affine or xnf_dram is not None):
            continue
        ee[k].v += NG
    mm.v += NG * G * 8
    ev.v += NG * G
    st.v += NG * G * 16 * n_stores
    ctx.barrier()
    les.close()



def emit_l0_qkv(ctx, xnt_dram, b, WQT, KT, VA, k_off=1024, v_off=2048, WC=3072,
                parts=3, wv_spec=None, bk_pp=None, bv_rep=None):
    nc = ctx.nc
    u = ctx.uid()
    les = contextlib.ExitStack()
    ld, km, ke, vm, ve, ms = ctx.pool(6)
    if wv_spec is None:
        wv_spec = (WQT, WC, v_off)
    wv_t, wv_C, wv_off = wv_spec
    xsl = [[ctx.sb(f"qkx{u}_{s}_{i}", [128, 512], BF16, es=les) for i in range(8)]
           for s in range(2)]
    ps01 = [ctx.ps_(f"qkp{u}_{i}", [128, 512], F32, es=les) for i in range(2)]
    pv2 = [ctx.ps_(f"qkv{u}_{i}", [128, 1024], F32, es=les) for i in range(2)]

    b_ld, b_km, b_ke, b_vm, b_ve = ld.v, km.v, ke.v, vm.v, ve.v
    with nc.Block() as block:
        @block.sync
        def _(sy):
            for tb in range(4):
                if tb >= 1:
                    sy.wait_ge(ld.h, b_ld + tb * 128)
                if tb >= 2:
                    if parts & 1:
                        sy.wait_ge(km.h, b_km + (tb - 1) * 8)
                    if parts & 2:
                        sy.wait_ge(vm.h, b_vm + (tb - 1) * 8)
                for it in range(8):
                    sy.dma_start(out=xsl[tb % 2][it][:],
                                 in_=xnt_dram[it * 128:(it + 1) * 128,
                                              b * S + tb * 512: b * S + (tb + 1) * 512]
                                 ).then_inc(ld.h, 16)

        @block.vector
        def _(v):
            v.memset(VA[:], 1.0).then_inc(ms.h, 1)
            for tb in range(4):
                if (parts & 1) and bk_pp is None:
                    for ot in range(8):
                        i = tb * 8 + ot
                        v.wait_ge(km.h, b_km + i + 1)
                        v.tensor_copy(fm_slice(KT, 2048, ot * 128, 128, tb * 512, 512),
                                      ps01[ot % 2][:]).then_inc(ke.h, 1)
                if parts & 2:
                    for tt4 in range(4):
                        tt = tb * 4 + tt4
                        for half in range(2):
                            i = (tt * 2 + half)
                            v.wait_ge(vm.h, b_vm + i + 1)
                            dst = VA[:, tt * 1040 + half * 520: tt * 1040 + (half + 1) * 520]
                            dst = dst.rearrange("p (h j) -> p h j", j=65)[:, :, 0:64]
                            srcp = pv2[tt4 % 2][:, half * 512:(half + 1) * 512]
                            srcp = srcp.rearrange("p (h j) -> p h j", j=64)
                            if bv_rep is not None:
                                bvr = bv_rep[:, half * 512:(half + 1) * 512]
                                bvr = bvr.rearrange("p (h j) -> p h j", j=64)
                                v.tensor_add(dst, srcp, bvr).then_inc(ve.h, 1)
                            else:
                                v.tensor_copy(dst, srcp).then_inc(ve.h, 1)

        @block.scalar
        def _(s):
            if (parts & 1) and bk_pp is not None:
                for tb in range(4):
                    for ot in range(8):
                        i = tb * 8 + ot
                        s.wait_ge(km.h, b_km + i + 1)
                        s.activation(fm_slice(KT, 2048, ot * 128, 128, tb * 512, 512),
                                     ps01[ot % 2][:], AF.Identity,
                                     bias=bk_pp[:, ot:ot + 1]).then_inc(ke.h, 1)

        @block.tensor
        def _(t):
            t.wait_ge(ms.h, ms.v + 1)
            for tb in range(4):
                t.wait_ge(ld.h, b_ld + (tb + 1) * 8 * 16)
                if parts & 1:
                    for ot in range(8):
                        i = tb * 8 + ot
                        if i >= 2:
                            t.wait_ge(ke.h, b_ke + i - 1)
                        for it in range(8):
                            ins = t.matmul(ps01[ot % 2][:],
                                           fm_slice(WQT, WC, it * 128, 128, k_off + ot * 128, 128),
                                           xsl[tb % 2][it][:],
                                           start=(it == 0), stop=(it == 7))
                        ins.then_inc(km.h, 1)
                if parts & 2:
                    for tt4 in range(4):
                        tt = tb * 4 + tt4
                        if tt >= 2:
                            t.wait_ge(ve.h, b_ve + (tt - 1) * 2)
                        for half in range(2):
                            for it in range(8):
                                ins = t.matmul(pv2[tt4 % 2][:, half * 512:(half + 1) * 512],
                                               xsl[tb % 2][it][:, tt4 * 128:(tt4 + 1) * 128],
                                               fm_slice(wv_t, wv_C, it * 128, 128, wv_off + half * 512, 512),
                                               start=(it == 0), stop=(it == 7))
                            ins.then_inc(vm.h, 1)

    ld.v = b_ld + 4 * 8 * 16
    if parts & 1:
        km.v = b_km + 32
        ke.v = b_ke + 32
    if parts & 2:
        vm.v = b_vm + 32
        ve.v = b_ve + 32
    ms.v += 1
    ctx.barrier()
    les.close()


def emit_l0_attn_core(ctx, xnt_dram, b, WQT, KT, VA, WPT, acc_dram,
                      q_off=0, WC=3072, causal=True, scale=0.125,
                      bq_pp=None, qt_full=None, bo_rep=None, addin_dram=None):
    nc = ctx.nc
    u = ctx.uid()
    les = contextlib.ExitStack()
    _names = ["xl", "qmm", "qev", "smm", "ex", "mk", "amm", "rc", "rmm", "cpv",
              "muv", "pmm", "pev", "pst", "lx", "e1"]
    sems = dict(zip(_names, ctx.pool(16)))
    B0 = {n: s.v for n, s in sems.items()}

    xq = [[ctx.sb(f"atx{u}_{s}_{i}", [128, 512], BF16, es=les) for i in range(8)]
          for s in range(2)]
    QT = ctx.sb(f"atqt{u}", [128, 8 * 512], BF16, es=les)
    OTB = ctx.sb(f"atot{u}", [128, 8 * 512], BF16, es=les)
    ptile = [ctx.sb(f"atp{u}_{i}", [128, 512], BF16, es=les) for i in range(2)]
    rden = ctx.sb(f"atr{u}", [1, 512], BF16, es=les)
    prepS = ctx.sb(f"atps{u}", [64, 512], BF16, es=les)
    accst = [ctx.sb(f"atac{u}_{i}", [128, DM], F32, es=les) for i in range(2)]
    axd = [ctx.sb(f"atax{u}_{i}", [128, DM], F32, es=les) for i in range(2)] if addin_dram is not None else None
    ps01 = [ctx.ps_(f"atps01{u}_{i}", [128, 512], F32, es=les) for i in range(2)]
    pav = [ctx.ps_(f"atpav{u}_{i}", [128, 512], F32, es=les) for i in range(2)]
    prep = ctx.ps_(f"atprep{u}", [128, 512], F32, es=les)
    pproj = ctx.ps_(f"atproj{u}", [128, 1024], F32, es=les)
    caus = ctx.consts["c_caus"]
    ones = ctx.consts["c_ones"]

    NH = 16
    kmax = [(4 * (qb + 1)) if causal else 16 for qb in range(4)]

    qmm_after = {}
    qev_after = {}
    smm_at = {}
    ex_at = {}
    mk_at = {}
    amm_at = {}
    rc_at = {}
    rmm_at = {}
    cpv_at = {}
    muv_at = {}
    pmm_at = {}
    pev_at = {}
    pst_at = {}
    nq = B0["qmm"]; nqe = B0["qev"]; ns = B0["smm"]; ne = B0["ex"]
    nm = B0["mk"]; na = B0["amm"]; nr = B0["rc"]; nrm = B0["rmm"]
    ncp = B0["cpv"]; nmu = B0["muv"]; npm = B0["pmm"]; npe = B0["pev"]; nps = B0["pst"]
    for qb in range(4):
        nq += 8
        qmm_after[qb] = nq
        nqe += 8
        qev_after[qb] = nqe
        for h in range(NH):
            for kt in range(kmax[qb]):
                ns += 1; smm_at[(qb, h, kt)] = ns
                ne += 1; ex_at[(qb, h, kt)] = ne
                if causal and kt >= 4 * qb:
                    nm += 1; mk_at[(qb, h, kt)] = nm
                na += 1; amm_at[(qb, h, kt)] = na
            nr += 1; rc_at[(qb, h)] = nr
            nrm += 1; rmm_at[(qb, h)] = nrm
            ncp += 1; cpv_at[(qb, h)] = ncp
            nmu += 1; muv_at[(qb, h)] = nmu
        for tt4 in range(4):
            npm += 1
            pmm_at[(qb, tt4)] = npm
            npe += 1; pev_at[(qb, tt4)] = npe
            nps += 16; pst_at[(qb, tt4)] = nps

    def diag(qb, kt):
        return causal and kt >= 4 * qb

    with nc.Block() as block:
        @block.sync
        def _(sy):
            for qb in range(4):
                if qt_full is None:
                    if qb >= 2:
                        sy.wait_ge(sems["qmm"].h, qmm_after[qb - 2])
                    for it in range(8):
                        sy.dma_start(out=xq[qb % 2][it][:],
                                     in_=xnt_dram[it * 128:(it + 1) * 128,
                                                  b * S + qb * 512: b * S + (qb + 1) * 512]
                                     ).then_inc(sems["xl"].h, 16)
                for tt4 in range(4):
                    g = b * S + qb * 512 + tt4 * 128
                    if addin_dram is not None:
                        i = qb * 4 + tt4
                        if i >= 2:
                            sy.wait_ge(sems["e1"].h, B0["e1"] + (i - 2) + 1)
                        sy.dma_start(out=axd[tt4 % 2][:],
                                     in_=addin_dram[g:g + 128, :]).then_inc(sems["lx"].h, 16)
                    sy.wait_ge(sems["pev"].h, pev_at[(qb, tt4)])
                    sy.dma_start(out=acc_dram[g:g + 128, :],
                                 in_=accst[tt4 % 2][:]).then_inc(sems["pst"].h, 16)
                    sy.wait_ge(sems["pst"].h, pst_at[(qb, tt4)])

        @block.tensor
        def _(t):
            for qb in range(4):
                if qt_full is None:
                    t.wait_ge(sems["xl"].h, B0["xl"] + (qb + 1) * 8 * 16)
                    if qb >= 1:
                        t.wait_ge(sems["ex"].h, ex_at[(qb - 1, NH - 1, kmax[qb - 1] - 1)])
                    for ot in range(8):
                        i = qb * 8 + ot
                        if i >= 2:
                            t.wait_ge(sems["qev"].h, B0["qev"] + i - 1)
                        for it in range(8):
                            ins = t.matmul(ps01[ot % 2][:],
                                           fm_slice(WQT, WC, it * 128, 128, q_off + ot * 128, 128),
                                           xq[qb % 2][it][:],
                                           start=(it == 0), stop=(it == 7))
                        ins.then_inc(sems["qmm"].h, 1)
                elif qb >= 1:
                    t.wait_ge(sems["ex"].h, ex_at[(qb - 1, NH - 1, kmax[qb - 1] - 1)])
                for h in range(NH):
                    if qt_full is None:
                        t.wait_ge(sems["qev"].h, qev_after[qb])
                    if (qb, h) >= (0, 2):
                        hh = h - 2 if h >= 2 else NH - 2 + h
                        qq = qb if h >= 2 else qb - 1
                        t.wait_ge(sems["muv"].h, muv_at[(qq, hh)])
                    for kt in range(kmax[qb] + 1):
                        if kt < kmax[qb]:
                            if kt >= 2:
                                t.wait_ge(sems["ex"].h, ex_at[(qb, h, kt - 2)])
                            elif h >= 1:
                                t.wait_ge(sems["ex"].h, ex_at[(qb, h - 1, kmax[qb] - 1 - (1 - kt))])
                            qsrc = (fm_slice(QT, 512, 64 * h, 64, 0, 512)
                                    if qt_full is None else
                                    fm_slice(qt_full, S, 64 * h, 64, qb * 512, 512))
                            t.matmul(ps01[kt % 2][:],
                                     fm_slice(KT, 2048, 64 * h, 64, kt * 128, 128),
                                     qsrc,
                                     start=True, stop=True).then_inc(sems["smm"].h, 1)
                        if kt >= 1:
                            k2 = kt - 1
                            if diag(qb, k2):
                                t.wait_ge(sems["mk"].h, mk_at[(qb, h, k2)])
                            else:
                                t.wait_ge(sems["ex"].h, ex_at[(qb, h, k2)])
                            t.matmul(pav[h % 2][0:65, :],
                                     VA[:, k2 * 1040 + 65 * h: k2 * 1040 + 65 * h + 65],
                                     ptile[k2 % 2][:],
                                     start=(k2 == 0), stop=(k2 == kmax[qb] - 1)
                                     ).then_inc(sems["amm"].h, 1)
                    t.wait_ge(sems["rc"].h, rc_at[(qb, h)])
                    if (qb, h) > (0, 0):
                        prevh = (qb, h - 1) if h >= 1 else (qb - 1, NH - 1)
                        t.wait_ge(sems["cpv"].h, cpv_at[prevh])
                    t.matmul(prep[0:64, :], ones[0:1, 0:64], rden[:],
                             start=True, stop=True).then_inc(sems["rmm"].h, 1)
                for tt4 in range(4):
                    t.wait_ge(sems["muv"].h, muv_at[(qb, NH - 1)])
                    if (qb, tt4) > (0, 0):
                        prev = (qb, tt4 - 1) if tt4 >= 1 else (qb - 1, 3)
                        t.wait_ge(sems["pev"].h, pev_at[prev])
                    for dt in range(8):
                        for half in range(2):
                            ins = t.matmul(pproj[:, half * 512:(half + 1) * 512],
                                           fm_slice(OTB, 512, dt * 128, 128, tt4 * 128, 128),
                                           fm_slice(WPT, 1024, dt * 128, 128, half * 512, 512),
                                           start=(dt == 0), stop=(dt == 7))
                    ins.then_inc(sems["pmm"].h, 1)

        @block.scalar
        def _(s):
            for qb in range(4):
                if qt_full is None:
                    for ot in range(8):
                        i = qb * 8 + ot
                        s.wait_ge(sems["qmm"].h, B0["qmm"] + i + 1)
                        if qb >= 1:
                            s.wait_ge(sems["smm"].h, smm_at[(qb - 1, NH - 1, kmax[qb - 1] - 1)])
                        if bq_pp is not None:
                            ins = s.activation(fm_slice(QT, 512, ot * 128, 128, 0, 512),
                                               ps01[ot % 2][:], AF.Identity,
                                               bias=bq_pp[:, ot:ot + 1])
                        else:
                            ins = s.activation(fm_slice(QT, 512, ot * 128, 128, 0, 512),
                                               ps01[ot % 2][:], AF.Identity)
                        ins.then_inc(sems["qev"].h, 1)
                for h in range(NH):
                    for kt in range(kmax[qb]):
                        s.wait_ge(sems["smm"].h, smm_at[(qb, h, kt)])
                        if (qb, h, kt) >= (0, 0, 2):
                            if kt >= 2:
                                s.wait_ge(sems["amm"].h, amm_at[(qb, h, kt - 2)])
                            elif (qb, h) > (0, 0):
                                prevh = (qb, h - 1) if h >= 1 else (qb - 1, NH - 1)
                                s.wait_ge(sems["amm"].h,
                                          amm_at[(prevh[0], prevh[1],
                                                  kmax[prevh[0]] - (2 - kt))])
                        s.activation(ptile[kt % 2][:], ps01[kt % 2][:], AF.Exp,
                                     scale=scale).then_inc(sems["ex"].h, 1)

        @block.vector
        def _(v):
            for qb in range(4):
                for h in range(NH):
                    if causal:
                        for kt in range(4 * qb, kmax[qb]):
                            dt = kt - 4 * qb
                            v.wait_ge(sems["ex"].h, ex_at[(qb, h, kt)])
                            v.tensor_mul(ptile[kt % 2][:], ptile[kt % 2][:],
                                         caus[:, dt * 512:(dt + 1) * 512]
                                         ).then_inc(sems["mk"].h, 1)
                    v.wait_ge(sems["amm"].h, amm_at[(qb, h, kmax[qb] - 1)])
                    if (qb, h) > (0, 0):
                        prevh = (qb, h - 1) if h >= 1 else (qb - 1, NH - 1)
                        v.wait_ge(sems["rmm"].h, rmm_at[prevh])
                    with nc.allow_low_precision(reason="1/denominator in bf16 == cast(fp32 recip)"):
                        v.reciprocal(rden[:], pav[h % 2][64:65, :]).then_inc(sems["rc"].h, 1)
                    v.wait_ge(sems["rmm"].h, rmm_at[(qb, h)])
                    v.tensor_copy(prepS[:], prep[0:64, :]).then_inc(sems["cpv"].h, 1)
                    v.wait_ge(sems["cpv"].h, cpv_at[(qb, h)])
                    if qb >= 1:
                        v.wait_ge(sems["pmm"].h, pmm_at[(qb - 1, 3)])
                    v.tensor_mul(fm_slice(OTB, 512, 64 * h, 64, 0, 512),
                                 pav[h % 2][0:64, :], prepS[:]
                                 ).then_inc(sems["muv"].h, 1)
                for tt4 in range(4):
                    v.wait_ge(sems["pmm"].h, pmm_at[(qb, tt4)])
                    i = qb * 4 + tt4
                    if i >= 2:
                        v.wait_ge(sems["pst"].h, B0["pst"] + (i - 1) * 16)
                    if bo_rep is not None and addin_dram is not None:
                        v.tensor_add(accst[tt4 % 2][:], pproj[:], bo_rep[:]).then_inc(sems["e1"].h, 1)
                        v.wait_ge(sems["e1"].h, B0["e1"] + i + 1)
                        v.wait_ge(sems["lx"].h, B0["lx"] + (i + 1) * 16)
                        v.tensor_add(accst[tt4 % 2][:], accst[tt4 % 2][:],
                                     axd[tt4 % 2][:]).then_inc(sems["pev"].h, 1)
                    elif bo_rep is not None:
                        v.tensor_add(accst[tt4 % 2][:], pproj[:], bo_rep[:]).then_inc(sems["pev"].h, 1)
                    else:
                        v.tensor_copy(accst[tt4 % 2][:], pproj[:]).then_inc(sems["pev"].h, 1)

    if qt_full is None:
        sems["xl"].v = B0["xl"] + 4 * 8 * 16
        sems["qmm"].v = nq
        sems["qev"].v = nqe
    if bo_rep is not None and addin_dram is not None:
        sems["e1"].v = B0["e1"] + 16
    if addin_dram is not None:
        sems["lx"].v = B0["lx"] + 16 * 16
    sems["smm"].v = ns
    sems["ex"].v = ne
    sems["mk"].v = nm
    sems["amm"].v = na
    sems["rc"].v = nr
    sems["rmm"].v = nrm
    sems["cpv"].v = ncp
    sems["muv"].v = nmu
    sems["pmm"].v = npm
    sems["pev"].v = npe
    sems["pst"].v = nps
    ctx.barrier()
    les.close()


def emit_colvec_to_rowrep(ctx, col_sbuf, ncols, scratch_dram, dst_rep, scale=1.0):
    nc = ctx.nc
    u = ctx.uid()
    les = contextlib.ExitStack()
    mm, ev, st = ctx.pool(3)
    pt = ctx.ps_(f"cr_p{u}", [128, 128], F32, es=les)
    row = ctx.sb(f"cr_r{u}", [ncols, 128], F32, es=les)
    identf = ctx.consts["c_identf"]
    with nc.Block() as block:
        @block.tensor
        def _(t):
            t.transpose(pt[0:ncols, :], col_sbuf[:, 0:ncols], identf[:]).then_inc(mm.h, 1)

        @block.vector
        def _(v):
            v.wait_ge(mm.h, mm.v + 1)
            if scale != 1.0:
                ins = v.tensor_scalar_mul(row[:], pt[0:ncols, :], float(scale))
            else:
                ins = v.tensor_copy(row[:], pt[0:ncols, :])
            ins.then_inc(ev.h, 1)

        @block.sync
        def _(sy):
            sy.wait_ge(ev.h, ev.v + 1)
            sy.dma_start(out=scratch_dram.rearrange("(a p) -> a p", p=128), in_=row[:]).then_inc(st.h, 16)
            sy.wait_ge(st.h, st.v + 16)
    mm.v += 1
    ev.v += 1
    st.v += 16
    ctx.barrier()
    les.close()
    emit_replicate_rows(ctx, dst_rep, scratch_dram, 128 * ncols)


def emit_load_pp(ctx, dst_pp, vec_dram, ncols):
    nc = ctx.nc
    u = ctx.uid()
    les = contextlib.ExitStack()
    ld, mm, ev = ctx.pool(3)
    stg = ctx.sb(f"pp_s{u}", [ncols, 128], F32, es=les)
    pt = ctx.ps_(f"pp_p{u}", [128, 128], F32, es=les)
    identf = ctx.consts["c_identf"]
    with nc.Block() as block:
        @block.sync
        def _(sy):
            sy.dma_start(out=stg[:], in_=vec_dram.rearrange("(a p) -> a p", p=128)).then_inc(ld.h, 16)

        @block.tensor
        def _(t):
            t.wait_ge(ld.h, ld.v + 16)
            t.transpose(pt[:, 0:ncols], stg[:], identf[0:ncols, 0:ncols]).then_inc(mm.h, 1)

        @block.vector
        def _(v):
            v.wait_ge(mm.h, mm.v + 1)
            v.tensor_copy(dst_pp[:, 0:ncols], pt[:, 0:ncols]).then_inc(ev.h, 1)
    ld.v += 16
    mm.v += 1
    ev.v += 1
    ctx.barrier()
    les.close()


def emit_binarize_prep(ctx, w_dram, nrows, ncol, a_col, les):
    nc = ctx.nc
    u = ctx.uid()
    lles = contextlib.ExitStack()
    rv, sv = ctx.pool(2)
    ldp = ctx.gpool(2)
    nrt = nrows // 128
    stg = [ctx.sb(f"bz_g{u}_{i}", [128, ncol], BF16, es=lles) for i in range(2)]
    b_ld = [ldp[0].v, ldp[1].v]
    b_rv, b_sv = rv.v, sv.v
    with nc.Block() as block:
        @block.gpsimd
        def _(gp):
            for r in range(nrt):
                if r >= 2:
                    gp.wait_ge(rv.h, b_rv + r - 1)
                gp.dma_start(out=stg[r % 2][:], in_=w_dram[r * 128:(r + 1) * 128, :]).then_inc(ldp[r % 2].h, 16)

        @block.vector
        def _(v):
            for r in range(nrt):
                v.wait_ge(ldp[r % 2].h, b_ld[r % 2] + (r // 2 + 1) * 16)
                v.reduce_sum(a_col[:, r:r + 1], stg[r % 2][:], axis=AXX,
                             apply_absolute_value=True).then_inc(rv.h, 1)
    ldp[0].v = b_ld[0] + ((nrt + 1) // 2) * 16
    ldp[1].v = b_ld[1] + (nrt // 2) * 16
    rv.v = b_rv + nrt
    ctx.barrier()
    lles.close()


def emit_sign_fm(ctx, dst_fm, w_dram, nrows, ncol, arep, arep_C):
    nc = ctx.nc
    u = ctx.uid()
    les = contextlib.ExitStack()
    sg, mm, ev = ctx.pool(3)
    ldp = ctx.gpool(2)
    nrt = nrows // 128
    nct = ncol // 128
    GRP = min(nct, 8)
    stg = [ctx.sb(f"sg_g{u}_{i}", [128, ncol], BF16, es=les) for i in range(2)]
    sgb = [ctx.sb(f"sg_b{u}_{i}", [128, ncol], BF16, es=les) for i in range(2)]
    pst = [ctx.ps_(f"sg_p{u}_{i}", [128, GRP * 128], BF16, es=les) for i in range(2)]
    ident = ctx.consts["c_ident"]
    ngrp_per_r = nct // GRP
    b_ld = [ldp[0].v, ldp[1].v]
    b_sg, b_mm, b_ev = sg.v, mm.v, ev.v
    with nc.Block() as block:
        @block.gpsimd
        def _(gp):
            for r in range(nrt):
                if r >= 2:
                    gp.wait_ge(sg.h, b_sg + r - 1)
                gp.dma_start(out=stg[r % 2][:], in_=w_dram[r * 128:(r + 1) * 128, :]).then_inc(ldp[r % 2].h, 16)

        @block.scalar
        def _(s):
            for r in range(nrt):
                s.wait_ge(ldp[r % 2].h, b_ld[r % 2] + (r // 2 + 1) * 16)
                if r >= 2:
                    s.wait_ge(mm.h, b_mm + (r - 1) * nct)
                s.activation(sgb[r % 2][:], stg[r % 2][:], AF.Sign).then_inc(sg.h, 1)

        @block.tensor
        def _(t):
            for r in range(nrt):
                t.wait_ge(sg.h, b_sg + r + 1)
                for g in range(ngrp_per_r):
                    gi = r * ngrp_per_r + g
                    if gi >= 2:
                        t.wait_ge(ev.h, b_ev + (gi - 1) * GRP)
                    for cb in range(GRP):
                        c = g * GRP + cb
                        t.transpose(pst[gi % 2][:, cb * 128:(cb + 1) * 128],
                                    sgb[r % 2][:, c * 128:(c + 1) * 128],
                                    ident[:]).then_inc(mm.h, 1)

        @block.vector
        def _(v):
            for r in range(nrt):
                for g in range(ngrp_per_r):
                    gi = r * ngrp_per_r + g
                    v.wait_ge(mm.h, b_mm + (gi + 1) * GRP)
                    for cb in range(GRP):
                        c = g * GRP + cb
                        v.tensor_mul(fm_slice(dst_fm, nrows, c * 128, 128, r * 128, 128),
                                     pst[gi % 2][:, cb * 128:(cb + 1) * 128],
                                     arep[:, r * 128:(r + 1) * 128]).then_inc(ev.h, 1)
    ldp[0].v = b_ld[0] + ((nrt + 1) // 2) * 16
    ldp[1].v = b_ld[1] + (nrt // 2) * 16
    sg.v = b_sg + nrt
    mm.v = b_mm + nrt * nct
    ev.v = b_ev + nrt * nct
    ctx.barrier()
    les.close()


def emit_l0_ffn_main(ctx, xnt_dram, x_dram, acc_dram, xout_dram,
                     S1T, S2T, b1pp, B2rep, NH1=4096):
    nc = ctx.nc
    u = ctx.uid()
    les = contextlib.ExitStack()
    _names = ["xl", "mmm", "ge", "ym", "e1", "e2", "e3", "lx", "stx", "xe1", "xe2"]
    sems = dict(zip(_names, ctx.pool(11)))
    B0 = {n: s.v for n, s in sems.items()}
    NF = NH1 // 128
    NTB = T // 256
    xsl = [[ctx.sb(f"ffx{u}_{s}_{i}", [128, 256], BF16, es=les) for i in range(8)]
           for s in range(2)]
    ghT = ctx.sb(f"ffg{u}", [128, NF * 256], BF16, es=les)
    eb = [ctx.sb(f"ffe{u}_{i}", [128, DM], F32, es=les) for i in range(2)]
    acct = [ctx.sb(f"ffa{u}_{i}", [128, DM], F32, es=les) for i in range(2)]
    xtl = [ctx.sb(f"fft{u}_{i}", [128, DM], F32, es=les) for i in range(2)]
    pm = [ctx.ps_(f"ffp{u}_{i}", [128, 256], F32, es=les) for i in range(2)]
    py = [ctx.ps_(f"ffy{u}_{i}", [128, 1024], F32, es=les) for i in range(2)]

    mmm_at = {}
    ge_at = {}
    ym_at = {}
    e3_at = {}
    n_m = B0["mmm"]; n_g = B0["ge"]; n_y = B0["ym"]; n_3 = B0["e3"]
    for tb in range(NTB):
        for f in range(NF):
            n_m += 1; mmm_at[(tb, f)] = n_m
            n_g += 1; ge_at[(tb, f)] = n_g
        for tt in range(2):
            n_y += 1; ym_at[(tb, tt)] = n_y
            n_3 += 1; e3_at[(tb, tt)] = n_3

    with nc.Block() as block:
        @block.sync
        def _(sy):
            for tb in range(NTB + 1):
                if tb < NTB:
                    if tb >= 2:
                        sy.wait_ge(sems["mmm"].h, mmm_at[(tb - 2, NF - 1)])
                    for it in range(8):
                        sy.dma_start(out=xsl[tb % 2][it][:],
                                     in_=xnt_dram[it * 128:(it + 1) * 128,
                                                  tb * 256:(tb + 1) * 256]).then_inc(sems["xl"].h, 16)
                    for tt in range(2):
                        i = tb * 2 + tt
                        g = i * 128
                        if i >= 2:
                            sy.wait_ge(sems["e3"].h, B0["e3"] + (i - 2) + 1)
                        sy.dma_start(out=acct[i % 2][:], in_=acc_dram[g:g + 128, :]).then_inc(sems["lx"].h, 16)
                        sy.dma_start(out=xtl[i % 2][:], in_=x_dram[g:g + 128, :]).then_inc(sems["lx"].h, 16)
                    sy.wait_ge(sems["xl"].h, B0["xl"] + (tb + 1) * 128)
                    sy.sem_inc(sems["xe1"].h, 1)
                    sy.wait_ge(sems["lx"].h, B0["lx"] + (tb + 1) * 64)
                    sy.sem_inc(sems["xe2"].h, 1)
                if tb >= 1:
                    for tt in range(2):
                        i = (tb - 1) * 2 + tt
                        sy.wait_ge(sems["e3"].h, e3_at[(tb - 1, tt)])
                        sy.dma_start(out=xout_dram[i * 128:(i + 1) * 128, :],
                                     in_=eb[tt][:]).then_inc(sems["stx"].h, 16)
                        sy.wait_ge(sems["stx"].h, B0["stx"] + (i + 1) * 16)

        @block.tensor
        def _(t):
            for tb in range(NTB):
                t.wait_ge(sems["xe1"].h, B0["xe1"] + tb + 1)
                for f in range(NF):
                    if (tb, f) >= (0, 2):
                        if f >= 2:
                            t.wait_ge(sems["ge"].h, ge_at[(tb, f - 2)])
                        else:
                            t.wait_ge(sems["ge"].h, ge_at[(tb - 1, NF - 2 + f)])
                    for it in range(8):
                        ins = t.matmul(pm[f % 2][:],
                                       fm_slice(S1T, NH1, it * 128, 128, f * 128, 128),
                                       xsl[tb % 2][it][:],
                                       start=(it == 0), stop=(it == 7))
                    ins.then_inc(sems["mmm"].h, 1)
                for tt in range(2):
                    if tb >= 1:
                        t.wait_ge(sems["e1"].h, B0["e1"] + (tb - 1) * 2 + tt + 1)
                    for f in range(NF):
                        t.wait_ge(sems["ge"].h, ge_at[(tb, f)])
                        for half in range(2):
                            ins = t.matmul(py[tt][:, half * 512:(half + 1) * 512],
                                           fm_slice(ghT, 256, f * 128, 128, tt * 128, 128),
                                           fm_slice(S2T, 1024, f * 128, 128, half * 512, 512),
                                           start=(f == 0), stop=(f == NF - 1))
                    ins.then_inc(sems["ym"].h, 1)

        @block.scalar
        def _(s):
            for tb in range(NTB):
                if tb >= 1:
                    s.wait_ge(sems["ym"].h, ym_at[(tb - 1, 1)])
                for f in range(NF):
                    s.wait_ge(sems["mmm"].h, mmm_at[(tb, f)])
                    s.activation(fm_slice(ghT, 256, f * 128, 128, 0, 256),
                                 pm[f % 2][:], AF.Gelu,
                                 bias=b1pp[:, f:f + 1]).then_inc(sems["ge"].h, 1)

        @block.vector
        def _(v):
            for tb in range(NTB):
                for tt in range(2):
                    i = tb * 2 + tt
                    v.wait_ge(sems["ym"].h, ym_at[(tb, tt)])
                    if i >= 2:
                        v.wait_ge(sems["stx"].h, B0["stx"] + (i - 1) * 16)
                    v.tensor_add(eb[tt][:], py[tt][:], B2rep[:]).then_inc(sems["e1"].h, 1)
                    v.wait_ge(sems["e1"].h, B0["e1"] + i + 1)
                    v.wait_ge(sems["xe2"].h, B0["xe2"] + tb + 1)
                    v.tensor_add(eb[tt][:], eb[tt][:], acct[i % 2][:]).then_inc(sems["e2"].h, 1)
                    v.wait_ge(sems["e2"].h, B0["e2"] + i + 1)
                    v.tensor_add(eb[tt][:], eb[tt][:], xtl[i % 2][:]).then_inc(sems["e3"].h, 1)

    sems["xl"].v = B0["xl"] + NTB * 8 * 16
    sems["xe1"].v = B0["xe1"] + NTB
    sems["xe2"].v = B0["xe2"] + NTB
    sems["mmm"].v = n_m
    sems["ge"].v = n_g
    sems["ym"].v = n_y
    sems["e1"].v = B0["e1"] + NTB * 2
    sems["e2"].v = B0["e2"] + NTB * 2
    sems["e3"].v = n_3
    sems["lx"].v = B0["lx"] + NTB * 2 * 32
    sems["stx"].v = B0["stx"] + NTB * 2 * 16
    ctx.barrier()
    les.close()


def emit_l0_layer(ctx, x_dram, xout_dram, W, upto="ffn", dbg=None):
    xnt = ctx.dram("l0_xnt", [DM, T], BF16)
    acc = dbg["acc"] if dbg and "acc" in dbg else ctx.dram("l0_acc", [T, DM], F32)
    emit_norm(ctx, x_dram, xnt)
    if upto == "norm":
        return
    les = contextlib.ExitStack()
    WQT = ctx.sb("l0_wqt", [128, 8 * 3072], BF16, es=les)
    WPT = ctx.sb("l0_wpt", [128, 8 * 1024], BF16, es=les)
    emit_weight_fm(ctx, WQT, W["L0_attn_w_qkv"], 1024, 3072, transpose=True)
    emit_weight_fm(ctx, WPT, W["L0_attn_w_proj"], 1024, 1024, transpose=True)
    KT = ctx.sb("l0_kt", [128, 8 * 2048], BF16, es=les)
    VA = ctx.sb("l0_va", [128, 16 * 1040], BF16, es=les)
    if upto == "weights":
        if dbg is not None and "kt" in dbg:
            emit_dump_sbuf(ctx, dbg["kt"], WQT, 8 * 2048)
        les.close()
        return
    for b in range(BT):
        emit_l0_qkv(ctx, xnt, b, WQT, KT, VA)
        if upto == "qkv" and b == 0:
            if dbg is not None and "kt" in dbg:
                emit_dump_sbuf(ctx, dbg["kt"], KT, 8 * 2048)
                emit_dump_sbuf(ctx, dbg["va"], VA, 16 * 1040)
            les.close()
            return
        emit_l0_attn_core(ctx, xnt, b, WQT, KT, VA, WPT, acc)
        if upto == "attn0" and b == 0:
            les.close()
            return
    les.close()
    if upto == "attn":
        return
    les = contextlib.ExitStack()
    a1col = ctx.sb("l0_a1c", [128, 32], F32, es=les)
    a2col = ctx.sb("l0_a2c", [128, 8], F32, es=les)
    A1rep = ctx.sb("l0_a1r", [128, 4096], BF16, es=les)
    A2rep = ctx.sb("l0_a2r", [128, 1024], BF16, es=les)
    B2rep = ctx.sb("l0_b2r", [128, 1024], F32, es=les)
    b1pp = ctx.sb("l0_b1p", [128, 32], F32, es=les)
    a1row_d = ctx.dram("l0_a1row", [4096], F32)
    a2row_d = ctx.dram("l0_a2row", [1024], F32)
    emit_binarize_prep(ctx, W["L0_ffn_w1"], 4096, 1024, a1col, les)
    emit_colvec_to_rowrep(ctx, a1col, 32, a1row_d, A1rep, scale=1.0 / 1024)
    if upto == "fprep1":
        les.close()
        return
    emit_binarize_prep(ctx, W["L0_ffn_w2"], 1024, 4096, a2col, les)
    emit_colvec_to_rowrep(ctx, a2col, 8, a2row_d, A2rep, scale=1.0 / 4096)
    emit_replicate_rows(ctx, B2rep, W["L0_ffn_b2"], 1024)
    emit_load_pp(ctx, b1pp, W["L0_ffn_b1"], 32)
    if upto == "fprep2":
        les.close()
        return
    S1T = ctx.sb("l0_s1t", [128, 8 * 4096], BF16, es=les)
    S2T = ctx.sb("l0_s2t", [128, 32 * 1024], BF16, es=les)
    emit_sign_fm(ctx, S1T, W["L0_ffn_w1"], 4096, 1024, A1rep, 4096)
    if upto == "fsign1":
        les.close()
        return
    emit_sign_fm(ctx, S2T, W["L0_ffn_w2"], 1024, 4096, A2rep, 1024)
    if upto == "fsign":
        les.close()
        return
    emit_l0_ffn_main(ctx, xnt, x_dram, acc, xout_dram, S1T, S2T, b1pp, B2rep)
    les.close()


def emit_dump_sbuf(ctx, dram_out, sbuf, ncols):
    sems = ctx.pool(1)
    st = sems[0]
    with ctx.nc.Block() as block:
        @block.sync
        def _(sy):
            sy.dma_start(out=dram_out, in_=sbuf[:, 0:ncols]).then_inc(st.h, 16)
            sy.wait_ge(st.h, st.v + 16)
    st.v += 16
    ctx.barrier()


def build_l0(upto="ffn", debug=False):
    ctx = Ctx()
    x_in = ctx.dram("x_in", [T, DM], F32, kind="ExternalInput")
    x_out = ctx.dram("x_out", [T, DM], F32, kind="ExternalOutput")
    W = {}
    for nm, shp in [("L0_attn_w_qkv", (3072, 1024)), ("L0_attn_w_proj", (1024, 1024)),
                    ("L0_ffn_w1", (4096, 1024)), ("L0_ffn_b1", (4096,)),
                    ("L0_ffn_w2", (1024, 4096)), ("L0_ffn_b2", (1024,))]:
        W[nm] = ctx.dram(nm, list(shp), F32, kind="ExternalInput")
    ctx.load_consts()
    dbg = None
    if debug:
        dbg = {"acc": ctx.dram("dbg_acc", [T, DM], F32, kind="ExternalOutput")}
        if upto == "qkv":
            dbg["kt"] = ctx.dram("dbg_kt", [128, 8 * 2048], BF16, kind="ExternalOutput")
            dbg["va"] = ctx.dram("dbg_va", [128, 16 * 1040], BF16, kind="ExternalOutput")
    emit_l0_layer(ctx, x_in, x_out, W, upto=upto, dbg=dbg)
    ctx.close()
    return ctx



def emit_l1_ropeqk(ctx, xnt_dram, b, WQfm, WKfm, bq_rep, bk_rep, QTfull, KTfull):
    nc = ctx.nc
    u = ctx.uid()
    les = contextlib.ExitStack()
    (ld, cs, qm, va_, vb, vc, tq, eq, km2, vk, tk, ek) = ctx.pool(12)
    xsl = [[ctx.sb(f"rqx{u}_{s}_{i}", [128, 512], BF16, es=les) for i in range(8)]
           for s in range(2)]
    cosb = [ctx.sb(f"rqc{u}_{i}", [128, DM], F32, es=les) for i in range(2)]
    sinb = [ctx.sb(f"rqs{u}_{i}", [128, DM], F32, es=les) for i in range(2)]
    xnf = ctx.sb(f"rqn{u}", [128, DM], F32, es=les)
    t1f = ctx.sb(f"rq1{u}", [128, DM], F32, es=les)
    t2f = ctx.sb(f"rq2{u}", [128, DM], F32, es=les)
    xbf = [ctx.sb(f"rqb{u}_{i}", [128, DM], BF16, es=les) for i in range(2)]
    pqk = [ctx.ps_(f"rqp{u}_{i}", [128, 1024], F32, es=les) for i in range(2)]
    ptr = [ctx.ps_(f"rqt{u}_{i}", [128, 1024], BF16, es=les) for i in range(2)]
    ident = ctx.consts["c_ident"]
    cosd = ctx.dram_consts["c_ropecos"]
    sind = ctx.dram_consts["c_ropesin"]

    b_ = {s: s.v for s in (ld, cs, qm, va_, vb, vc, tq, eq, km2, vk, tk, ek)}

    with nc.Block() as block:
        @block.sync
        def _(sy):
            for tb in range(4):
                if tb >= 1:
                    sy.wait_ge(ld.h, b_[ld] + tb * 128)
                if tb >= 2:
                    sy.wait_ge(qm.h, b_[qm] + (tb - 1) * 8)
                for it in range(8):
                    sy.dma_start(out=xsl[tb % 2][it][:],
                                 in_=xnt_dram[it * 128:(it + 1) * 128,
                                              b * S + tb * 512: b * S + (tb + 1) * 512]
                                 ).then_inc(ld.h, 16)
                for tt4 in range(4):
                    tt = tb * 4 + tt4
                    if tt >= 2:
                        sy.wait_ge(vc.h, b_[vc] + (tt - 1) * 2)
                    sy.dma_start(out=cosb[tt % 2][:],
                                 in_=cosd[tt * 128:(tt + 1) * 128, :]).then_inc(cs.h, 16)
                    sy.wait_ge(cs.h, b_[cs] + (tt * 2 + 1) * 16)
                    sy.dma_start(out=sinb[tt % 2][:],
                                 in_=sind[tt * 128:(tt + 1) * 128, :]).then_inc(cs.h, 16)
                    sy.wait_ge(cs.h, b_[cs] + (tt * 2 + 2) * 16)

        @block.tensor
        def _(t):
            for tb in range(4):
                t.wait_ge(ld.h, b_[ld] + (tb + 1) * 128)
                for tt4 in range(4):
                    tt = tb * 4 + tt4
                    for p in range(2):
                        W = WQfm if p == 0 else WKfm
                        if tt >= 1:
                            t.wait_ge(va_.h, b_[va_] + (tt - 1) * 2 + p + 1)
                        for half in range(2):
                            for it in range(8):
                                ins = t.matmul(pqk[p][:, half * 512:(half + 1) * 512],
                                               xsl[tb % 2][it][:, tt4 * 128:(tt4 + 1) * 128],
                                               fm_slice(W, 1024, it * 128, 128, half * 512, 512),
                                               start=(it == 0), stop=(it == 7))
                        ins.then_inc(qm.h, 1)
                    for p in range(2):
                        t.wait_ge(vb.h, b_[vb] + tt * 2 + p + 1)
                        if tt >= 1:
                            t.wait_ge((eq if p == 0 else ek).h,
                                      b_[eq if p == 0 else ek] + tt)
                        for cb in range(8):
                            ins = t.transpose(ptr[p][:, cb * 128:(cb + 1) * 128],
                                              xbf[p][:, cb * 128:(cb + 1) * 128],
                                              ident[:])
                        ins.then_inc((tq if p == 0 else tk).h, 1)

        @block.vector
        def _(v):
            for tb in range(4):
                for tt4 in range(4):
                    tt = tb * 4 + tt4
                    for p in range(2):
                        brep = bq_rep if p == 0 else bk_rep
                        v.wait_ge(qm.h, b_[qm] + tt * 2 + p + 1)
                        v.wait_ge(cs.h, b_[cs] + (tt + 1) * 32)
                        if tt >= 1:
                            v.wait_ge((tq if p == 0 else tk).h,
                                      b_[tq if p == 0 else tk] + tt)
                        v.tensor_add(xnf[:], pqk[p][:], brep[:]).then_inc(va_.h, 1)
                        v.wait_ge(va_.h, b_[va_] + tt * 2 + p + 1)
                        v.tensor_mul(t1f[:], xnf[:], cosb[tt % 2][:])
                        x4 = xnf[:].rearrange("q (a k j) -> q a k j", k=2, j=32)
                        s4 = sinb[tt % 2][:].rearrange("q (a k j) -> q a k j", k=2, j=32)
                        t4 = t2f[:].rearrange("q (a k j) -> q a k j", k=2, j=32)
                        v.tensor_mul(t4[:, :, 0, :], x4[:, :, 1, :], s4[:, :, 0, :])
                        ins = v.tensor_mul(t4[:, :, 1, :], x4[:, :, 0, :], s4[:, :, 1, :])
                        ins.then_inc(vc.h, 1)
                        v.wait_ge(vc.h, b_[vc] + tt * 2 + p + 1)
                        v.tensor_add(xbf[p][:], t1f[:], t2f[:]).then_inc(vb.h, 1)
                    for p in range(2):
                        v.wait_ge((tq if p == 0 else tk).h,
                                  b_[tq if p == 0 else tk] + tt + 1)
                        dst = QTfull if p == 0 else KTfull
                        ins = v.tensor_copy(
                            dst[:, :].rearrange("q (c z) -> q c z", c=8)[:, :, tt * 128:(tt + 1) * 128],
                            ptr[p][:].rearrange("q (c z) -> q c z", c=8))
                        ins.then_inc((eq if p == 0 else ek).h, 1)

    ld.v = b_[ld] + 4 * 128
    cs.v = b_[cs] + 16 * 32
    qm.v = b_[qm] + 32
    va_.v = b_[va_] + 32
    vb.v = b_[vb] + 32
    vc.v = b_[vc] + 32
    tq.v = b_[tq] + 16
    tk.v = b_[tk] + 16
    eq.v = b_[eq] + 16
    ek.v = b_[ek] + 16
    ctx.barrier()
    les.close()


def emit_gate(ctx, hnt_dram, GWfm, gb_rep, MSK, gate_dim=8, with_scores=False,
              SCO=None):
    nc = ctx.nc
    u = ctx.uid()
    les = contextlib.ExitStack()
    gd = gate_dim
    (ld, gm, e1, e2, e3, e4, e5) = ctx.pool(7)
    xsl = [[ctx.sb(f"gx{u}_{s}_{i}", [128, 128], BF16, es=les) for i in range(8)]
           for s in range(2)]
    lg = [ctx.sb(f"glg{u}_{i}", [128, gd], F32, es=les) for i in range(2)]
    mx = ctx.sb(f"gmx{u}", [128, 2], F32, es=les)
    ee = ctx.sb(f"gee{u}", [128, gd], F32, es=les)
    sume = ctx.sb(f"gsu{u}", [128, 1], F32, es=les)
    pg = [ctx.ps_(f"gpg{u}_{i}", [128, 512], F32, es=les) for i in range(2)]
    b_ = {s: s.v for s in (ld, gm, e1, e2, e3, e4, e5)}
    with nc.Block() as block:
        @block.sync
        def _(sy):
            for i in range(NT):
                if i >= 1:
                    sy.wait_ge(ld.h, b_[ld] + i * 128)
                if i >= 2:
                    sy.wait_ge(gm.h, b_[gm] + i - 1)
                for it in range(8):
                    sy.dma_start(out=xsl[i % 2][it][:],
                                 in_=hnt_dram[it * 128:(it + 1) * 128,
                                              i * 128:(i + 1) * 128]).then_inc(ld.h, 16)

        @block.tensor
        def _(t):
            for i in range(NT):
                t.wait_ge(ld.h, b_[ld] + (i + 1) * 128)
                if i >= 2:
                    t.wait_ge(e1.h, b_[e1] + i - 1)
                for it in range(8):
                    ins = t.matmul(pg[i % 2][:, 0:gd],
                                   xsl[i % 2][it][:],
                                   fm_slice(GWfm, gd, it * 128, 128, 0, gd),
                                   start=(it == 0), stop=(it == 7))
                ins.then_inc(gm.h, 1)

        @block.vector
        def _(v):
            for i in range(NT):
                v.wait_ge(gm.h, b_[gm] + i + 1)
                v.tensor_add(lg[i % 2][:], pg[i % 2][:, 0:gd], gb_rep[:, 0:gd]).then_inc(e1.h, 1)
                v.wait_ge(e1.h, b_[e1] + i + 1)
                v.reduce_max(mx[:, 0:1], lg[i % 2][:], axis=AXX).then_inc(e2.h, 1)
                v.wait_ge(e2.h, b_[e2] + i + 1)
                if not with_scores:
                    v.tensor_single_scalar(MSK[:, i * gd:(i + 1) * gd], lg[i % 2][:],
                                           mx[:, 0:1], ALU.is_ge).then_inc(e3.h, 1)
                else:
                    v.wait_ge(e4.h, b_[e4] + i)
                    v.tensor_scalar_mul(mx[:, 1:2], mx[:, 0:1], -1.0).then_inc(e3.h, 1)
                    v.wait_ge(e5.h, b_[e5] + i + 1)
                    v.tensor_single_scalar(MSK[:, i * gd:(i + 1) * gd], ee[:],
                                           1.0, ALU.is_ge)
                    with nc.allow_low_precision(reason="score=1/sum fp32 recip"):
                        v.reciprocal(SCO[:, i:i + 1], sume[:]).then_inc(e4.h, 1)

        @block.scalar
        def _(s):
            if with_scores:
                for i in range(NT):
                    s.wait_ge(e3.h, b_[e3] + i + 1)
                    s.activation(ee[:], lg[i % 2][:], AF.Exp, bias=mx[:, 1:2],
                                 accum_out=sume[:]).then_inc(e5.h, 1)

    ld.v = b_[ld] + NT * 128
    gm.v = b_[gm] + NT
    e1.v = b_[e1] + NT
    e2.v = b_[e2] + NT
    e3.v = b_[e3] + NT
    if with_scores:
        e4.v = b_[e4] + NT
        e5.v = b_[e5] + NT
    ctx.barrier()
    les.close()


def emit_moe_expert(ctx, hnt_dram, W1fm, W2fm, b1pp, B2rep, coef_sb, e, gd,
                    macc_dram, first, last, final_addends, xout_dram, act):
    nc = ctx.nc
    u = ctx.uid()
    les = contextlib.ExitStack()
    _names = ["xl", "mmm", "ge", "ym", "e1", "e2", "e3", "lx", "stx", "xe1", "xe2"]
    sems = dict(zip(_names, ctx.pool(11)))
    B0 = {n: s.v for n, s in sems.items()}
    NF = 32
    NTB = T // 256
    nadd = (1 if not first else 0) + (len(final_addends) if last else 0)
    xsl = [[ctx.sb(f"mex{u}_{s}_{i}", [128, 256], BF16, es=les) for i in range(8)]
           for s in range(2)]
    ghT = ctx.sb(f"meg{u}", [128, NF * 256], BF16, es=les)
    eb = [ctx.sb(f"mee{u}_{i}", [128, DM], F32, es=les) for i in range(2)]
    nadt = 2
    adt = [[ctx.sb(f"mea{u}_{s}_{i}", [128, DM], F32, es=les) for i in range(nadd)]
           for s in range(nadt)] if nadd else None
    pm = [ctx.ps_(f"mep{u}_{i}", [128, 256], F32, es=les) for i in range(2)]
    py = [ctx.ps_(f"mey{u}_{i}", [128, 1024], F32, es=les) for i in range(2)]

    mmm_at = {}
    ge_at = {}
    ym_at = {}
    e3_at = {}
    n_m = B0["mmm"]; n_g = B0["ge"]; n_y = B0["ym"]; n_3 = B0["e3"]
    for tb in range(NTB):
        for f in range(NF):
            n_m += 1; mmm_at[(tb, f)] = n_m
            n_g += 1; ge_at[(tb, f)] = n_g
        for tt in range(2):
            n_y += 1; ym_at[(tb, tt)] = n_y
            n_3 += max(1, nadd); e3_at[(tb, tt)] = n_3
    out_dram = xout_dram if last else macc_dram

    srcs = ([] if first else [macc_dram]) + (final_addends if last else [])
    with nc.Block() as block:
        @block.sync
        def _(sy):
            for tb in range(NTB + 1):
                if tb < NTB:
                    if tb >= 2:
                        sy.wait_ge(sems["mmm"].h, mmm_at[(tb - 2, NF - 1)])
                    for it in range(8):
                        sy.dma_start(out=xsl[tb % 2][it][:],
                                     in_=hnt_dram[it * 128:(it + 1) * 128,
                                                  tb * 256:(tb + 1) * 256]).then_inc(sems["xl"].h, 16)
                    if nadd:
                        for tt in range(2):
                            i = tb * 2 + tt
                            g = i * 128
                            if i >= 2:
                                sy.wait_ge(sems["e3"].h, e3_at[((i - 2) // 2, (i - 2) % 2)])
                            for a_i, src in enumerate(srcs):
                                sy.dma_start(out=adt[i % 2][a_i][:], in_=src[g:g + 128, :]).then_inc(sems["lx"].h, 16)
                    sy.wait_ge(sems["xl"].h, B0["xl"] + (tb + 1) * 128)
                    sy.sem_inc(sems["xe1"].h, 1)
                    if nadd:
                        sy.wait_ge(sems["lx"].h, B0["lx"] + (tb + 1) * nadd * 32)
                        sy.sem_inc(sems["xe2"].h, 1)
                if tb >= 1:
                    for tt in range(2):
                        i = (tb - 1) * 2 + tt
                        sy.wait_ge(sems["e3"].h, e3_at[(tb - 1, tt)])
                        sy.dma_start(out=out_dram[i * 128:(i + 1) * 128, :],
                                     in_=eb[tt][:]).then_inc(sems["stx"].h, 16)
                        sy.wait_ge(sems["stx"].h, B0["stx"] + (i + 1) * 16)

        @block.tensor
        def _(t):
            for tb in range(NTB):
                t.wait_ge(sems["xe1"].h, B0["xe1"] + tb + 1)
                for f in range(NF):
                    if (tb, f) >= (0, 2):
                        if f >= 2:
                            t.wait_ge(sems["ge"].h, ge_at[(tb, f - 2)])
                        else:
                            t.wait_ge(sems["ge"].h, ge_at[(tb - 1, NF - 2 + f)])
                    for it in range(8):
                        ins = t.matmul(pm[f % 2][:],
                                       fm_slice(W1fm, 4096, it * 128, 128, f * 128, 128),
                                       xsl[tb % 2][it][:],
                                       start=(it == 0), stop=(it == 7))
                    ins.then_inc(sems["mmm"].h, 1)
                for tt in range(2):
                    if tb >= 1:
                        t.wait_ge(sems["e1"].h, B0["e1"] + (tb - 1) * 2 + tt + 1)
                    for f in range(NF):
                        t.wait_ge(sems["ge"].h, ge_at[(tb, f)])
                        for half in range(2):
                            ins = t.matmul(py[tt][:, half * 512:(half + 1) * 512],
                                           fm_slice(ghT, 256, f * 128, 128, tt * 128, 128),
                                           fm_slice(W2fm, 1024, f * 128, 128, half * 512, 512),
                                           start=(f == 0), stop=(f == NF - 1))
                    ins.then_inc(sems["ym"].h, 1)

        @block.scalar
        def _(s):
            for tb in range(NTB):
                if tb >= 1:
                    s.wait_ge(sems["ym"].h, ym_at[(tb - 1, 1)])
                for f in range(NF):
                    s.wait_ge(sems["mmm"].h, mmm_at[(tb, f)])
                    s.activation(fm_slice(ghT, 256, f * 128, 128, 0, 256),
                                 pm[f % 2][:], act,
                                 bias=b1pp[:, f:f + 1]).then_inc(sems["ge"].h, 1)

        @block.vector
        def _(v):
            for tb in range(NTB):
                for tt in range(2):
                    i = tb * 2 + tt
                    v.wait_ge(sems["ym"].h, ym_at[(tb, tt)])
                    if i >= 2:
                        v.wait_ge(sems["stx"].h, B0["stx"] + (i - 1) * 16)
                    v.tensor_add(eb[tt][:], py[tt][:], B2rep[:]).then_inc(sems["e1"].h, 1)
                    v.wait_ge(sems["e1"].h, B0["e1"] + i + 1)
                    ins = v.tensor_scalar_mul(eb[tt][:], eb[tt][:],
                                              coef_sb[:, i * gd + e:i * gd + e + 1])
                    if nadd:
                        ins.then_inc(sems["e2"].h, 1)
                        v.wait_ge(sems["e2"].h, B0["e2"] + i + 1)
                        v.wait_ge(sems["xe2"].h, B0["xe2"] + tb + 1)
                        base3 = e3_at[(tb, tt)] - nadd
                        for a_i in range(nadd):
                            v.tensor_add(eb[tt][:], eb[tt][:],
                                         adt[i % 2][a_i][:]).then_inc(sems["e3"].h, 1)
                            if a_i < nadd - 1:
                                v.wait_ge(sems["e3"].h, base3 + a_i + 1)
                    else:
                        ins.then_inc(sems["e3"].h, 1)

    sems["xl"].v = B0["xl"] + NTB * 128
    sems["xe1"].v = B0["xe1"] + NTB
    if nadd:
        sems["xe2"].v = B0["xe2"] + NTB
    sems["mmm"].v = n_m
    sems["ge"].v = n_g
    sems["ym"].v = n_y
    sems["e1"].v = B0["e1"] + NTB * 2
    if nadd:
        sems["e2"].v = B0["e2"] + NTB * 2
    sems["e3"].v = B0["e3"] + NTB * 2 * max(1, nadd)
    if nadd:
        sems["lx"].v = B0["lx"] + NTB * 2 * nadd * 16
    sems["stx"].v = B0["stx"] + NTB * 2 * 16
    ctx.barrier()
    les.close()


L1_WNAMES = ["L1_attn_Wq", "L1_attn_bq", "L1_attn_Wk", "L1_attn_bk", "L1_attn_Wv",
             "L1_attn_bv", "L1_attn_Wo", "L1_attn_bo", "L1_ffn_ln1_g", "L1_ffn_ln1_b",
             "L1_ffn_ln2_g", "L1_ffn_ln2_b", "L1_ffn_wq", "L1_ffn_bq", "L1_ffn_wk",
             "L1_ffn_bk", "L1_ffn_wv", "L1_ffn_bv", "L1_ffn_wo", "L1_ffn_bo",
             "L1_ffn_gate_w", "L1_ffn_gate_b", "L1_ffn_w1", "L1_ffn_b1",
             "L1_ffn_w2", "L1_ffn_b2"]


def emit_l1_layer(ctx, x_dram, xout_dram, W, upto="all", dbg=None):
    xnt = ctx.dram("l1_xnt", [DM, T], BF16)
    xnf = ctx.dram("l1_xnf", [T, DM], F32)
    acc = dbg["acc"] if dbg and "acc" in dbg else ctx.dram("l1_acc", [T, DM], F32)
    u2d = dbg["u2"] if dbg and "u2" in dbg else ctx.dram("l1_u2", [T, DM], F32)
    hnt = ctx.dram("l1_hnt", [DM, T], BF16)
    hn2t = ctx.dram("l1_hn2t", [DM, T], BF16)
    macc = ctx.dram("l1_macc", [T, DM], F32)

    emit_norm(ctx, x_dram, xnt, xnf_dram=xnf)
    if upto == "norm":
        return
    les = contextlib.ExitStack()
    QTfull = ctx.sb("l1_qt", [128, 8 * 2048], BF16, es=les)
    KT = ctx.sb("l1_kt", [128, 8 * 2048], BF16, es=les)
    VA = ctx.sb("l1_va", [128, 16 * 1040], BF16, es=les)
    bo_rep = ctx.sb("l1_bor", [128, DM], F32, es=les)
    emit_replicate_rows(ctx, bo_rep, W["L1_attn_bo"], 1024)
    for b in range(BT):
        lesb = contextlib.ExitStack()
        WQfm = ctx.sb(f"l1_wq{b}", [128, 8 * 1024], BF16, es=lesb)
        WKfm = ctx.sb(f"l1_wk{b}", [128, 8 * 1024], BF16, es=lesb)
        bq_rep = ctx.sb(f"l1_bqr{b}", [128, DM], F32, es=lesb)
        bk_rep = ctx.sb(f"l1_bkr{b}", [128, DM], F32, es=lesb)
        emit_weight_fm(ctx, WQfm, W["L1_attn_Wq"], 1024, 1024, transpose=False)
        emit_weight_fm(ctx, WKfm, W["L1_attn_Wk"], 1024, 1024, transpose=False)
        emit_replicate_rows(ctx, bq_rep, W["L1_attn_bq"], 1024)
        emit_replicate_rows(ctx, bk_rep, W["L1_attn_bk"], 1024)
        emit_l1_ropeqk(ctx, xnt, b, WQfm, WKfm, bq_rep, bk_rep, QTfull, KT)
        lesb.close()
        lesv = contextlib.ExitStack()
        WVfm = ctx.sb(f"l1_wv{b}", [128, 8 * 1024], BF16, es=lesv)
        bv_rep = ctx.sb(f"l1_bvr{b}", [128, DM], F32, es=lesv)
        emit_weight_fm(ctx, WVfm, W["L1_attn_Wv"], 1024, 1024, transpose=False)
        emit_replicate_rows(ctx, bv_rep, W["L1_attn_bv"], 1024)
        emit_l0_qkv(ctx, xnt, b, WVfm, KT, VA, WC=1024, parts=2,
                    wv_spec=(WVfm, 1024, 0), bv_rep=bv_rep)
        lesv.close()
        lesc = contextlib.ExitStack()
        WOfm = ctx.sb(f"l1_wo{b}", [128, 8 * 1024], BF16, es=lesc)
        emit_weight_fm(ctx, WOfm, W["L1_attn_Wo"], 1024, 1024, transpose=False)
        emit_l0_attn_core(ctx, xnt, b, None, KT, VA, WOfm, acc, WC=1024,
                          causal=False, scale=0.125, qt_full=QTfull, bo_rep=bo_rep)
        lesc.close()
    les.close()
    if upto == "oattn":
        return
    les = contextlib.ExitStack()
    G1 = ctx.sb("l1_g1", [128, DM], F32, es=les)
    B1 = ctx.sb("l1_b1", [128, DM], F32, es=les)
    emit_replicate_rows(ctx, G1, W["L1_ffn_ln1_g"], 1024)
    emit_replicate_rows(ctx, B1, W["L1_ffn_ln1_b"], 1024)
    emit_norm(ctx, xnf, hnt, grep=G1, brep=B1)
    les.close()
    les = contextlib.ExitStack()
    KT2 = ctx.sb("l1_kt2", [128, 8 * 2048], BF16, es=les)
    VA2 = ctx.sb("l1_va2", [128, 16 * 1040], BF16, es=les)
    bo2 = ctx.sb("l1_bo2", [128, DM], F32, es=les)
    emit_replicate_rows(ctx, bo2, W["L1_ffn_bo"], 1024)
    for b in range(BT):
        lesb = contextlib.ExitStack()
        WKfm2 = ctx.sb(f"l1_iwk{b}", [128, 8 * 1024], BF16, es=lesb)
        WVfm2 = ctx.sb(f"l1_iwv{b}", [128, 8 * 1024], BF16, es=lesb)
        bk2pp = ctx.sb(f"l1_ibk{b}", [128, 8], F32, es=lesb)
        bv2rep = ctx.sb(f"l1_ibv{b}", [128, DM], F32, es=lesb)
        emit_weight_fm(ctx, WKfm2, W["L1_ffn_wk"], 1024, 1024, transpose=False)
        emit_weight_fm(ctx, WVfm2, W["L1_ffn_wv"], 1024, 1024, transpose=False)
        emit_load_pp(ctx, bk2pp, W["L1_ffn_bk"], 8)
        emit_replicate_rows(ctx, bv2rep, W["L1_ffn_bv"], 1024)
        emit_l0_qkv(ctx, hnt, b, WKfm2, KT2, VA2, k_off=0, WC=1024, parts=3,
                    wv_spec=(WVfm2, 1024, 0), bk_pp=bk2pp, bv_rep=bv2rep)
        lesb.close()
        lesc = contextlib.ExitStack()
        WQfm2 = ctx.sb(f"l1_iwq{b}", [128, 8 * 1024], BF16, es=lesc)
        WOfm2 = ctx.sb(f"l1_iwo{b}", [128, 8 * 1024], BF16, es=lesc)
        bq2pp = ctx.sb(f"l1_ibq{b}", [128, 8], F32, es=lesc)
        emit_weight_fm(ctx, WQfm2, W["L1_ffn_wq"], 1024, 1024, transpose=False)
        emit_weight_fm(ctx, WOfm2, W["L1_ffn_wo"], 1024, 1024, transpose=False)
        emit_load_pp(ctx, bq2pp, W["L1_ffn_bq"], 8)
        emit_l0_attn_core(ctx, hnt, b, WQfm2, KT2, VA2, WOfm2, u2d, q_off=0,
                          WC=1024, causal=False, scale=0.125, bq_pp=bq2pp,
                          bo_rep=bo2, addin_dram=xnf)
        lesc.close()
    les.close()
    if upto == "inner":
        return
    les = contextlib.ExitStack()
    G2 = ctx.sb("l1_g2", [128, DM], F32, es=les)
    B2 = ctx.sb("l1_b2", [128, DM], F32, es=les)
    emit_replicate_rows(ctx, G2, W["L1_ffn_ln2_g"], 1024)
    emit_replicate_rows(ctx, B2, W["L1_ffn_ln2_b"], 1024)
    emit_norm(ctx, u2d, hn2t, grep=G2, brep=B2)
    les.close()
    lay = contextlib.ExitStack()
    MSK = ctx.sb("l1_msk", [128, NT * 8], F32, es=lay)
    les = contextlib.ExitStack()
    GWfm = ctx.sb("l1_gw", [128, 8 * 8], BF16, es=les)
    gbrep = ctx.sb("l1_gb", [128, 8], F32, es=les)
    emit_weight_fm(ctx, GWfm, W["L1_ffn_gate_w"], 1024, 8, transpose=False)
    emit_replicate_rows(ctx, gbrep, W["L1_ffn_gate_b"], 8)
    emit_gate(ctx, hn2t, GWfm, gbrep, MSK, gate_dim=8, with_scores=False)
    les.close()
    if upto == "gate":
        lay.close()
        return
    for e in range(8):
        lese = contextlib.ExitStack()
        W1fm = ctx.sb(f"l1_e{e}w1", [128, 8 * 4096], BF16, es=lese)
        W2fm = ctx.sb(f"l1_e{e}w2", [128, 32 * 1024], BF16, es=lese)
        b1pp = ctx.sb(f"l1_e{e}b1", [128, 32], F32, es=lese)
        B2e = ctx.sb(f"l1_e{e}b2", [128, DM], F32, es=lese)
        emit_weight_fm(ctx, W1fm, W["L1_ffn_w1"][e], 1024, 4096, transpose=False)
        emit_weight_fm(ctx, W2fm, W["L1_ffn_w2"][e], 4096, 1024, transpose=False)
        emit_load_pp(ctx, b1pp, W["L1_ffn_b1"][e], 32)
        emit_replicate_rows(ctx, B2e, W["L1_ffn_b2"][e], 1024)
        emit_moe_expert(ctx, hn2t, W1fm, W2fm, b1pp, B2e, MSK, e, 8, macc,
                        first=(e == 0), last=(e == 7),
                        final_addends=[x_dram, acc, u2d],
                        xout_dram=xout_dram, act=AF.Relu)
        lese.close()
    lay.close()


def build_l1(upto="all", debug=False):
    ctx = Ctx()
    x_in = ctx.dram("x_in", [T, DM], F32, kind="ExternalInput")
    x_out = ctx.dram("x_out", [T, DM], F32, kind="ExternalOutput")
    W = {}
    shapes = {"L1_attn_Wq": (1024, 1024), "L1_attn_bq": (1024,),
              "L1_attn_Wk": (1024, 1024), "L1_attn_bk": (1024,),
              "L1_attn_Wv": (1024, 1024), "L1_attn_bv": (1024,),
              "L1_attn_Wo": (1024, 1024), "L1_attn_bo": (1024,),
              "L1_ffn_ln1_g": (1024,), "L1_ffn_ln1_b": (1024,),
              "L1_ffn_ln2_g": (1024,), "L1_ffn_ln2_b": (1024,),
              "L1_ffn_wq": (1024, 1024), "L1_ffn_bq": (1024,),
              "L1_ffn_wk": (1024, 1024), "L1_ffn_bk": (1024,),
              "L1_ffn_wv": (1024, 1024), "L1_ffn_bv": (1024,),
              "L1_ffn_wo": (1024, 1024), "L1_ffn_bo": (1024,),
              "L1_ffn_gate_w": (1024, 8), "L1_ffn_gate_b": (8,),
              "L1_ffn_w1": (8, 1024, 4096), "L1_ffn_b1": (8, 4096),
              "L1_ffn_w2": (8, 4096, 1024), "L1_ffn_b2": (8, 1024)}
    for nm in L1_WNAMES:
        W[nm] = ctx.dram(nm, list(shapes[nm]), F32, kind="ExternalInput")
    ctx.load_consts()
    dbg = None
    if debug:
        dbg = {"acc": ctx.dram("dbg_acc", [T, DM], F32, kind="ExternalOutput"),
               "u2": ctx.dram("dbg_u2", [T, DM], F32, kind="ExternalOutput")}
    emit_l1_layer(ctx, x_in, x_out, W, upto=upto, dbg=dbg)
    ctx.close()
    return ctx



PSL = S // NCORE


def emit_l2_gather_xpm(ctx, xn_all_dram, core_p0, XPM):
    nc = ctx.nc
    u = ctx.uid()
    les = contextlib.ExitStack()
    ld, cp = ctx.pool(2)
    stg = [ctx.sb(f"gx{u}_{i}", [128, PSL], BF16, es=les) for i in range(4)]
    b_ld, b_cp = ld.v, cp.v
    with nc.Block() as block:
        @block.sync
        def _(sy):
            n = 0
            for ft in range(8):
                for cb in range(8):
                    for lb in range(2):
                        if n >= 4:
                            sy.wait_ge(cp.h, b_cp + n - 3)
                        sy.dma_start(out=stg[n % 4][:],
                                     in_=xn_all_dram[cb * 1024 + ft * 128: cb * 1024 + (ft + 1) * 128,
                                                     lb * S + core_p0: lb * S + core_p0 + PSL]
                                     ).then_inc(ld.h, 16)
                        sy.wait_ge(ld.h, b_ld + (n + 1) * 16)
                        n += 1

        @block.vector
        def _(v):
            n = 0
            for ft in range(8):
                for cb in range(8):
                    for lb in range(2):
                        v.wait_ge(ld.h, b_ld + (n + 1) * 16)
                        bidx = cb * 2 + lb
                        dst = XPM[:, ft * 4096 + bidx: ft * 4096 + bidx + (PSL - 1) * 16 + 1]
                        dst = bass.AP(dst.tensor, dst.offset, [list(dst.ap[0]), [16, PSL]])
                        v.tensor_copy(dst, stg[n % 4][:]).then_inc(cp.h, 1)
                        n += 1

    ld.v = b_ld + 128 * 16
    cp.v = b_cp + 128
    ctx.barrier()
    les.close()


def emit_l2_attn(ctx, XPM, WQT2, WKT2, WVT2, bq_pp, bk_pp, bv_rep, o_out_dram,
                 o_is_a2a=False):
    nc = ctx.nc
    u = ctx.uid()
    les = contextlib.ExitStack()
    _n = ["qm", "qe", "km", "ke", "vm", "ve", "sm", "ex", "mk", "am", "d1",
          "d2", "mu", "st", "ms"]
    sems = dict(zip(_n, ctx.pool(15)))
    B0 = {k: s.v for k, s in sems.items()}
    QTt = ctx.sb(f"l2q{u}", [128, 1024], BF16, es=les)
    KTt = ctx.sb(f"l2k{u}", [128, 1024], BF16, es=les)
    VAt = ctx.sb(f"l2v{u}", [128, 8 * 129], BF16, es=les)
    pt = [ctx.sb(f"l2p{u}_{i}", [128, 128], BF16, es=les) for i in range(2)]
    dnr = ctx.sb(f"l2d{u}", [128, 2], F32, es=les)
    onat = [ctx.sb(f"l2o{u}_{i}", [128, DM], F32, es=les) for i in range(2)]
    psQK = ctx.ps_(f"l2pq{u}", [128, 1024], F32, es=les)
    psV = ctx.ps_(f"l2pv{u}", [128, 1024], F32, es=les)
    psS = [ctx.ps_(f"l2ps{u}_{i}", [128, 512], F32, es=les) for i in range(2)]
    psO = [ctx.ps_(f"l2po{u}_{i}", [128, 512], F32, es=les) for i in range(2)]
    bd = ctx.consts["c_blockdiag"]
    SC = 0.08838834764831845

    c = {k: B0[k] for k in _n}
    qm_at = {}; km_at = {}; vm_at = {}; qe_at = {}; ke_at = {}; ve_at = {}
    sm_at = {}; ex_at = {}; mk_at = {}; am_at = {}; d1_at = {}; d2_at = {}
    mu_at = {}; st_at = {}
    for i in range(NT):
        c["qm"] += 1; qm_at[i] = c["qm"]
        c["qe"] += 1; qe_at[i] = c["qe"]
        c["km"] += 1; km_at[i] = c["km"]
        c["ke"] += 1; ke_at[i] = c["ke"]
        c["vm"] += 1; vm_at[i] = c["vm"]
        c["ve"] += 1; ve_at[i] = c["ve"]
        for h in range(8):
            c["sm"] += 1; sm_at[(i, h)] = c["sm"]
            c["ex"] += 1; ex_at[(i, h)] = c["ex"]
            c["mk"] += 1; mk_at[(i, h)] = c["mk"]
            c["am"] += 1; am_at[(i, h)] = c["am"]
            c["d1"] += 1; d1_at[(i, h)] = c["d1"]
            c["d2"] += 1; d2_at[(i, h)] = c["d2"]
            c["mu"] += 1; mu_at[(i, h)] = c["mu"]
        c["st"] += 16; st_at[i] = c["st"]

    with nc.Block() as block:
        @block.tensor
        def _(t):
            t.wait_ge(sems["ms"].h, B0["ms"] + 1)
            for i in range(NT):
                for p, (W, grp) in enumerate([(WQT2, "qm"), (WKT2, "km")]):
                    if p == 0:
                        if i >= 1:
                            t.wait_ge(sems["ke"].h, ke_at[i - 1])
                    else:
                        t.wait_ge(sems["qe"].h, qe_at[i])
                    for ot in range(8):
                        for it in range(8):
                            ins = t.matmul(psQK[:, ot * 128:(ot + 1) * 128],
                                           fm_slice(W, 1024, it * 128, 128, ot * 128, 128),
                                           XPM[:, it * 4096 + i * 128: it * 4096 + (i + 1) * 128],
                                           start=(it == 0), stop=(it == 7))
                    ins.then_inc(sems[grp].h, 1)
                if i >= 1:
                    t.wait_ge(sems["ve"].h, ve_at[i - 1])
                for half in range(2):
                    for it in range(8):
                        ins = t.matmul(psV[:, half * 512:(half + 1) * 512],
                                       XPM[:, it * 4096 + i * 128: it * 4096 + (i + 1) * 128],
                                       fm_slice(WVT2, 1024, it * 128, 128, half * 512, 512),
                                       start=(it == 0), stop=(it == 7))
                ins.then_inc(sems["vm"].h, 1)
                for h in range(8):
                    t.wait_ge(sems["qe"].h, qe_at[i])
                    t.wait_ge(sems["ke"].h, ke_at[i])
                    if (i, h) > (0, 1):
                        ph = (i, h - 2) if h >= 2 else (i - 1, 6 + h)
                        t.wait_ge(sems["ex"].h, ex_at[ph])
                    t.matmul(psS[h % 2][:, 0:128],
                             KTt[:, h * 128:(h + 1) * 128],
                             QTt[:, h * 128:(h + 1) * 128],
                             start=True, stop=True).then_inc(sems["sm"].h, 1)
                    t.wait_ge(sems["mk"].h, mk_at[(i, h)])
                    if (i, h) > (0, 1):
                        ph = (i, h - 2) if h >= 2 else (i - 1, 6 + h)
                        t.wait_ge(sems["mu"].h, mu_at[ph])
                    t.matmul(psO[h % 2][:, 0:129],
                             pt[h % 2][:],
                             VAt[:, h * 129:(h + 1) * 129],
                             start=True, stop=True).then_inc(sems["am"].h, 1)

        @block.scalar
        def _(s):
            for i in range(NT):
                s.wait_ge(sems["qm"].h, qm_at[i])
                for ot in range(8):
                    ins = s.activation(QTt[:, ot * 128:(ot + 1) * 128],
                                       psQK[:, ot * 128:(ot + 1) * 128], AF.Identity,
                                       bias=bq_pp[:, ot:ot + 1])
                ins.then_inc(sems["qe"].h, 1)
                s.wait_ge(sems["km"].h, km_at[i])
                for ot in range(8):
                    ins = s.activation(KTt[:, ot * 128:(ot + 1) * 128],
                                       psQK[:, ot * 128:(ot + 1) * 128], AF.Identity,
                                       bias=bk_pp[:, ot:ot + 1])
                ins.then_inc(sems["ke"].h, 1)
                for h in range(8):
                    s.wait_ge(sems["sm"].h, sm_at[(i, h)])
                    if (i, h) > (0, 1):
                        ph = (i, h - 2) if h >= 2 else (i - 1, 6 + h)
                        s.wait_ge(sems["am"].h, am_at[ph])
                    s.activation(pt[h % 2][:], psS[h % 2][:, 0:128], AF.Exp,
                                 scale=SC).then_inc(sems["ex"].h, 1)

        @block.vector
        def _(v):
            v.memset(VAt[:], 1.0).then_inc(sems["ms"].h, 1)
            for i in range(NT):
                v.wait_ge(sems["vm"].h, vm_at[i])
                dst = VAt[:, :].rearrange("q (h j) -> q h j", j=129)[:, :, 0:128]
                srcp = psV[:, :].rearrange("q (h j) -> q h j", j=128)
                bvr = bv_rep[:, :].rearrange("q (h j) -> q h j", j=128)
                v.tensor_add(dst, srcp, bvr).then_inc(sems["ve"].h, 1)
                for h in range(8):
                    if h == 0 and i >= 2:
                        v.wait_ge(sems["st"].h, st_at[i - 2])
                    v.wait_ge(sems["ex"].h, ex_at[(i, h)])
                    v.tensor_mul(pt[h % 2][:], pt[h % 2][:], bd[:]).then_inc(sems["mk"].h, 1)
                    v.wait_ge(sems["am"].h, am_at[(i, h)])
                    v.tensor_scalar_add(dnr[:, 0:1], psO[h % 2][:, 128:129], 497.0).then_inc(sems["d1"].h, 1)
                    v.wait_ge(sems["d1"].h, d1_at[(i, h)])
                    v.reciprocal(dnr[:, 1:2], dnr[:, 0:1]).then_inc(sems["d2"].h, 1)
                    v.wait_ge(sems["d2"].h, d2_at[(i, h)])
                    v.tensor_scalar_mul(onat[i % 2][:, h * 128:(h + 1) * 128],
                                        psO[h % 2][:, 0:128],
                                        dnr[:, 1:2]).then_inc(sems["mu"].h, 1)

        @block.sync
        def _(sy):
            for i in range(NT):
                sy.wait_ge(sems["mu"].h, mu_at[(i, 7)])
                if o_is_a2a:
                    dst = bass.AP(o_out_dram.tensor, i * 16 * 1024,
                                  [[2 * 1024, 8], [512 * 1024, 8], [1024, 2], [1, 1024]])
                    sy.dma_start(out=dst, in_=onat[i % 2][:]).then_inc(sems["st"].h, 16)
                else:
                    sy.dma_start(out=o_out_dram[i * 128:(i + 1) * 128, :],
                                 in_=onat[i % 2][:]).then_inc(sems["st"].h, 16)
                sy.wait_ge(sems["st"].h, st_at[i])

    for k in _n:
        sems[k].v = c[k] if k != "ms" else B0["ms"] + 1
    ctx.barrier()
    les.close()


def emit_l2_ffn(ctx, xnt_dram, x_dram, xout_dram, WRT, WKT, WVT, mur_pp, muk_pp):
    nc = ctx.nc
    u = ctx.uid()
    les = contextlib.ExitStack()
    _n = ["ld", "pz", "sh", "km", "kr", "ks", "rm", "rs", "vm", "fm", "fx", "lx", "st"]
    sems = dict(zip(_n, ctx.pool(13)))
    B0 = {k: s.v for k, s in sems.items()}
    xc = [[ctx.sb(f"ffc{u}_{s}_{i}", [128, 512], BF16, es=les) for i in range(8)]
          for s in range(2)]
    xp = [[ctx.sb(f"ffp{u}_{s}_{i}", [128, 512], BF16, es=les) for i in range(8)]
          for s in range(2)]
    dtb = ctx.sb(f"ffd{u}", [128, 512], BF16, es=les)
    XRT = ctx.sb(f"ffr{u}", [128, 8 * 512], BF16, es=les)
    XKT = ctx.sb(f"ffk{u}", [128, 8 * 512], BF16, es=les)
    KAT = ctx.sb(f"ffa{u}", [128, 8 * 512], BF16, es=les)
    rb = ctx.sb(f"ffb{u}", [128, 512], BF16, es=les)
    rnat = ctx.sb(f"ffn{u}", [128, DM], BF16, es=les)
    fout = [ctx.sb(f"ffo{u}_{i}", [128, DM], F32, es=les) for i in range(2)]
    xtl = [ctx.sb(f"ffx{u}_{i}", [128, DM], F32, es=les) for i in range(2)]
    pk = [ctx.ps_(f"ffpk{u}_{i}", [128, 512], F32, es=les) for i in range(2)]
    pr = ctx.ps_(f"ffpr{u}", [128, 1024], F32, es=les)
    pv = ctx.ps_(f"ffpv{u}", [128, 1024], F32, es=les)

    NTB = T // 512
    c = {k: B0[k] for k in _n}
    sh_at = {}; km_at = {}; kr_at = {}; ks_at = {}; rm_at = {}; rs_at = {}
    vm_at = {}; fm_at = {}; fx_at = {}; st_at = {}; pz_at = {}
    for tb in range(NTB):
        if tb % 4 == 0:
            c["pz"] += 1
        pz_at[tb] = c["pz"]
        c["sh"] += 24; sh_at[tb] = c["sh"]
        for dt in range(8):
            c["km"] += 1; km_at[(tb, dt)] = c["km"]
            c["kr"] += 1; kr_at[(tb, dt)] = c["kr"]
            c["ks"] += 1; ks_at[(tb, dt)] = c["ks"]
        for tt in range(4):
            c["rm"] += 1; rm_at[(tb, tt)] = c["rm"]
            c["rs"] += 1; rs_at[(tb, tt)] = c["rs"]
            c["vm"] += 1; vm_at[(tb, tt)] = c["vm"]
            c["fm"] += 1; fm_at[(tb, tt)] = c["fm"]
            c["fx"] += 1; fx_at[(tb, tt)] = c["fx"]
            c["st"] += 16; st_at[(tb, tt)] = c["st"]

    with nc.Block() as block:
        @block.sync
        def _(sy):
            for tb in range(NTB):
                c0 = tb * 512
                first = (tb % 4 == 0)
                if tb >= 1:
                    sy.wait_ge(sems["ld"].h, B0["ld"] + tb * 256)
                if tb >= 2:
                    sy.wait_ge(sems["sh"].h, sh_at[tb - 1])
                for it in range(8):
                    sy.dma_start(out=xc[tb % 2][it][:],
                                 in_=xnt_dram[it * 128:(it + 1) * 128, c0:c0 + 512]
                                 ).then_inc(sems["ld"].h, 16)
                for it in range(8):
                    if first:
                        sy.dma_start(out=xp[tb % 2][it][:, 1:512],
                                     in_=xnt_dram[it * 128:(it + 1) * 128, c0:c0 + 511]
                                     ).then_inc(sems["ld"].h, 16)
                    else:
                        sy.dma_start(out=xp[tb % 2][it][:],
                                     in_=xnt_dram[it * 128:(it + 1) * 128, c0 - 1:c0 + 511]
                                     ).then_inc(sems["ld"].h, 16)
                for tt in range(4):
                    i = tb * 4 + tt
                    if i >= 2:
                        sy.wait_ge(sems["fx"].h, fx_at[((i - 2) // 4, (i - 2) % 4)])
                    sy.dma_start(out=xtl[tt % 2][:],
                                 in_=x_dram[i * 128:(i + 1) * 128, :]).then_inc(sems["lx"].h, 16)
                    sy.wait_ge(sems["fx"].h, fx_at[(tb, tt)])
                    sy.dma_start(out=xout_dram[i * 128:(i + 1) * 128, :],
                                 in_=fout[tt % 2][:]).then_inc(sems["st"].h, 16)
                    sy.wait_ge(sems["st"].h, st_at[(tb, tt)])

        @block.vector
        def _(v):
            for tb in range(NTB):
                first = (tb % 4 == 0)
                v.wait_ge(sems["ld"].h, B0["ld"] + (tb + 1) * 256)
                if first:
                    ins = None
                    for it in range(8):
                        ins = v.memset(xp[tb % 2][it][:, 0:1], 0.0)
                    ins.then_inc(sems["pz"].h, 1)
                    v.wait_ge(sems["pz"].h, pz_at[tb])
                if tb >= 2:
                    v.wait_ge(sems["vm"].h, vm_at[(tb - 1, 3)])
                for it in range(8):
                    if it >= 1:
                        v.wait_ge(sems["sh"].h, sh_at[tb] - 24 + it * 3)
                    v.tensor_sub(dtb[:], xc[tb % 2][it][:], xp[tb % 2][it][:]).then_inc(sems["sh"].h, 1)
                    v.wait_ge(sems["sh"].h, sh_at[tb] - 24 + it * 3 + 1)
                    v.scalar_tensor_tensor(XRT[:, it * 512:(it + 1) * 512], dtb[:],
                                           mur_pp[:, it:it + 1], xp[tb % 2][it][:],
                                           ALU.mult, ALU.add).then_inc(sems["sh"].h, 1)
                    v.scalar_tensor_tensor(XKT[:, it * 512:(it + 1) * 512], dtb[:],
                                           muk_pp[:, it:it + 1], xp[tb % 2][it][:],
                                           ALU.mult, ALU.add).then_inc(sems["sh"].h, 1)
                for dt in range(8):
                    v.wait_ge(sems["kr"].h, kr_at[(tb, dt)])
                    v.tensor_mul(KAT[:, dt * 512:(dt + 1) * 512], rb[:], rb[:]).then_inc(sems["ks"].h, 1)
                for tt in range(4):
                    i = tb * 4 + tt
                    v.wait_ge(sems["rs"].h, rs_at[(tb, tt)])
                    v.wait_ge(sems["vm"].h, vm_at[(tb, tt)])
                    if i >= 2:
                        v.wait_ge(sems["st"].h, st_at[((i - 2) // 4, (i - 2) % 4)])
                    v.tensor_mul(fout[tt % 2][:], pv[:], rnat[:]).then_inc(sems["fm"].h, 1)
                    v.wait_ge(sems["fm"].h, fm_at[(tb, tt)])
                    v.wait_ge(sems["lx"].h, B0["lx"] + (i + 1) * 16)
                    v.tensor_add(fout[tt % 2][:], fout[tt % 2][:], xtl[tt % 2][:]).then_inc(sems["fx"].h, 1)

        @block.scalar
        def _(s):
            for tb in range(NTB):
                for dt in range(8):
                    s.wait_ge(sems["km"].h, km_at[(tb, dt)])
                    if (tb, dt) > (0, 0):
                        pd = (tb, dt - 1) if dt >= 1 else (tb - 1, 7)
                        s.wait_ge(sems["ks"].h, ks_at[pd])
                    s.activation(rb[:], pk[dt % 2][:], AF.Relu).then_inc(sems["kr"].h, 1)
                for tt in range(4):
                    s.wait_ge(sems["rm"].h, rm_at[(tb, tt)])
                    if (tb, tt) > (0, 0):
                        pt_ = (tb, tt - 1) if tt >= 1 else (tb - 1, 3)
                        s.wait_ge(sems["fm"].h, fm_at[pt_])
                    s.activation(rnat[:], pr[:], AF.Sigmoid).then_inc(sems["rs"].h, 1)

        @block.tensor
        def _(t):
            for tb in range(NTB):
                t.wait_ge(sems["sh"].h, sh_at[tb])
                for dt in range(8):
                    if (tb, dt) >= (0, 2):
                        pd = (tb, dt - 2) if dt >= 2 else (tb - 1, 6 + dt)
                        t.wait_ge(sems["kr"].h, kr_at[pd])
                    for it in range(8):
                        ins = t.matmul(pk[dt % 2][:],
                                       fm_slice(WKT, 1024, it * 128, 128, dt * 128, 128),
                                       XKT[:, it * 512:(it + 1) * 512],
                                       start=(it == 0), stop=(it == 7))
                    ins.then_inc(sems["km"].h, 1)
                for tt in range(4):
                    if (tb, tt) > (0, 0):
                        pt_ = (tb, tt - 1) if tt >= 1 else (tb - 1, 3)
                        t.wait_ge(sems["rs"].h, rs_at[pt_])
                    for it in range(8):
                        for half in range(2):
                            ins = t.matmul(pr[:, half * 512:(half + 1) * 512],
                                           XRT[:, it * 512 + tt * 128: it * 512 + tt * 128 + 128],
                                           fm_slice(WRT, 1024, it * 128, 128, half * 512, 512),
                                           start=(it == 0), stop=(it == 7))
                    ins.then_inc(sems["rm"].h, 1)
                    t.wait_ge(sems["ks"].h, ks_at[(tb, 7)])
                    if (tb, tt) > (0, 0):
                        pt_ = (tb, tt - 1) if tt >= 1 else (tb - 1, 3)
                        t.wait_ge(sems["fm"].h, fm_at[pt_])
                    for dt in range(8):
                        for half in range(2):
                            ins = t.matmul(pv[:, half * 512:(half + 1) * 512],
                                           KAT[:, dt * 512 + tt * 128: dt * 512 + tt * 128 + 128],
                                           fm_slice(WVT, 1024, dt * 128, 128, half * 512, 512),
                                           start=(dt == 0), stop=(dt == 7))
                    ins.then_inc(sems["vm"].h, 1)

    for k in _n:
        if k == "ld":
            sems[k].v = B0[k] + NTB * 256
        elif k == "lx":
            sems[k].v = B0[k] + NTB * 4 * 16
        else:
            sems[k].v = c[k]
    ctx.barrier()
    les.close()


def emit_add(ctx, a_dram, b_dram, out_dram):
    nc = ctx.nc
    u = ctx.uid()
    les = contextlib.ExitStack()
    ld, ad, st = ctx.pool(3)
    at = [ctx.sb(f"ad{u}a{i}", [128, DM], F32, es=les) for i in range(2)]
    bt = [ctx.sb(f"ad{u}b{i}", [128, DM], F32, es=les) for i in range(2)]
    ot = [ctx.sb(f"ad{u}o{i}", [128, DM], F32, es=les) for i in range(2)]
    b_ld, b_ad, b_st = ld.v, ad.v, st.v
    with nc.Block() as block:
        @block.sync
        def _(sy):
            for i in range(NT):
                if i >= 2:
                    sy.wait_ge(ad.h, b_ad + i - 1)
                sy.dma_start(out=at[i % 2][:], in_=a_dram[i * 128:(i + 1) * 128, :]).then_inc(ld.h, 16)
                sy.dma_start(out=bt[i % 2][:], in_=b_dram[i * 128:(i + 1) * 128, :]).then_inc(ld.h, 16)
                sy.wait_ge(ld.h, b_ld + (i + 1) * 32)
                sy.wait_ge(ad.h, b_ad + i + 1)
                sy.dma_start(out=out_dram[i * 128:(i + 1) * 128, :], in_=ot[i % 2][:]).then_inc(st.h, 16)
                sy.wait_ge(st.h, b_st + (i + 1) * 16)

        @block.vector
        def _(v):
            for i in range(NT):
                v.wait_ge(ld.h, b_ld + (i + 1) * 32)
                if i >= 2:
                    v.wait_ge(st.h, b_st + (i - 1) * 16)
                v.tensor_add(ot[i % 2][:], at[i % 2][:], bt[i % 2][:]).then_inc(ad.h, 1)
    ld.v = b_ld + NT * 32
    ad.v = b_ad + NT
    st.v = b_st + NT * 16
    ctx.barrier()
    les.close()


L2_WNAMES = ["L2_attn_Wq", "L2_attn_bq", "L2_attn_Wk", "L2_attn_bk",
             "L2_attn_Wv", "L2_attn_bv", "L2_ffn_W_r", "L2_ffn_W_k",
             "L2_ffn_W_v", "L2_ffn_mu_r", "L2_ffn_mu_k"]


def build_l2a(debug=False):
    ctx = Ctx()
    x_in = ctx.dram("x_in", [T, DM], F32, kind="ExternalInput")
    xnt = ctx.dram("xnt_out", [DM, T], BF16, kind="ExternalOutput")
    xo = ctx.dram("xo_out", [T, DM], F32, kind="ExternalOutput")
    W = {nm: ctx.dram(nm, [1024, 1024] if "W_" in nm or "_W" in nm else [1024],
                      F32, kind="ExternalInput")
         for nm in ["L2_ffn_W_r", "L2_ffn_W_k", "L2_ffn_W_v", "L2_ffn_mu_r", "L2_ffn_mu_k"]}
    ctx.load_consts()
    emit_norm(ctx, x_in, xnt)
    les = contextlib.ExitStack()
    WRT = ctx.sb("l2_wrt", [128, 8 * 1024], BF16, es=les)
    WKT = ctx.sb("l2_wkt", [128, 8 * 1024], BF16, es=les)
    WVT = ctx.sb("l2_wvt", [128, 8 * 1024], BF16, es=les)
    mur = ctx.sb("l2_mur", [128, 8], F32, es=les)
    muk = ctx.sb("l2_muk", [128, 8], F32, es=les)
    emit_weight_fm(ctx, WRT, W["L2_ffn_W_r"], 1024, 1024, transpose=True)
    emit_weight_fm(ctx, WKT, W["L2_ffn_W_k"], 1024, 1024, transpose=True)
    emit_weight_fm(ctx, WVT, W["L2_ffn_W_v"], 1024, 1024, transpose=True)
    emit_load_pp(ctx, mur, W["L2_ffn_mu_r"], 8)
    emit_load_pp(ctx, muk, W["L2_ffn_mu_k"], 8)
    emit_l2_ffn(ctx, xnt, x_in, xo, WRT, WKT, WVT, mur, muk)
    les.close()
    ctx.close()
    return ctx


def build_l2b():
    ctx = Ctx()
    xpm_in = ctx.dram("xpm_in", [DM, T], BF16, kind="ExternalInput")
    o_out = ctx.dram("o_out", [T, DM], F32, kind="ExternalOutput")
    W = {}
    for nm in ["L2_attn_Wq", "L2_attn_Wk", "L2_attn_Wv"]:
        W[nm] = ctx.dram(nm, [1024, 1024], F32, kind="ExternalInput")
    for nm in ["L2_attn_bq", "L2_attn_bk", "L2_attn_bv"]:
        W[nm] = ctx.dram(nm, [1024], F32, kind="ExternalInput")
    ctx.load_consts()
    les = contextlib.ExitStack()
    XPM = ctx.sb("l2_xpm", [128, 8 * 4096], BF16, es=les)
    WQT2 = ctx.sb("l2_wq", [128, 8 * 1024], BF16, es=les)
    WKT2 = ctx.sb("l2_wk", [128, 8 * 1024], BF16, es=les)
    WVT2 = ctx.sb("l2_wv", [128, 8 * 1024], BF16, es=les)
    bqp = ctx.sb("l2_bq", [128, 8], F32, es=les)
    bkp = ctx.sb("l2_bk", [128, 8], F32, es=les)
    bvr = ctx.sb("l2_bv", [128, DM], F32, es=les)
    s0 = ctx.pool(1)[0]
    with ctx.nc.Block() as block:
        @block.sync
        def _(sy):
            for ft in range(8):
                sy.dma_start(out=XPM[:, ft * 4096:(ft + 1) * 4096],
                             in_=xpm_in[ft * 128:(ft + 1) * 128, :]).then_inc(s0.h, 16)
            sy.wait_ge(s0.h, s0.v + 128)
    s0.v += 128
    ctx.barrier()
    emit_weight_fm(ctx, WQT2, W["L2_attn_Wq"], 1024, 1024, transpose=True)
    emit_weight_fm(ctx, WKT2, W["L2_attn_Wk"], 1024, 1024, transpose=True)
    emit_weight_fm(ctx, WVT2, W["L2_attn_Wv"], 1024, 1024, transpose=True)
    emit_load_pp(ctx, bqp, W["L2_attn_bq"], 8)
    emit_load_pp(ctx, bkp, W["L2_attn_bk"], 8)
    emit_replicate_rows(ctx, bvr, W["L2_attn_bv"], 1024)
    emit_l2_attn(ctx, XPM, WQT2, WKT2, WVT2, bqp, bkp, bvr, o_out)
    les.close()
    ctx.close()
    return ctx



def emit_l3_qk(ctx, xnt_dram, b, WQfm, WKfm, bq_pp, bk_pp, QTf, KTf, KNat):
    nc = ctx.nc
    u = ctx.uid()
    les = contextlib.ExitStack()
    _n = ["ld", "qm", "f1", "f2", "f3", "f4", "tm", "te"]
    sems = dict(zip(_n, ctx.pool(8)))
    B0 = {k: s.v for k, s in sems.items()}
    xsl = [[ctx.sb(f"l3x{u}_{s}_{i}", [128, 512], BF16, es=les) for i in range(8)]
           for s in range(2)]
    tmn = ctx.sb(f"l3n{u}", [128, 512], F32, es=les)
    tme = ctx.sb(f"l3e{u}", [128, 512], F32, es=les)
    tmr = ctx.sb(f"l3r{u}", [128, 512], F32, es=les)
    ps01 = [ctx.ps_(f"l3p{u}_{i}", [128, 512], F32, es=les) for i in range(2)]
    ptr = [ctx.ps_(f"l3t{u}_{i}", [128, 1024], BF16, es=les) for i in range(2)]
    ident = ctx.consts["c_ident"]

    c = {k: B0[k] for k in _n}
    qm_at = {}; f4_at = {}; tm_at = {}; te_at = {}
    for tb in range(4):
        for p in range(2):
            for ot in range(8):
                c["qm"] += 1; qm_at[(tb, p, ot)] = c["qm"]
                c["f4"] += 1; f4_at[(tb, p, ot)] = c["f4"]
        for tt4 in range(4):
            c["tm"] += 1; tm_at[(tb, tt4)] = c["tm"]
            c["te"] += 1; te_at[(tb, tt4)] = c["te"]

    with nc.Block() as block:
        @block.sync
        def _(sy):
            for tb in range(4):
                if tb >= 1:
                    sy.wait_ge(sems["ld"].h, B0["ld"] + tb * 128)
                if tb >= 2:
                    sy.wait_ge(sems["qm"].h, qm_at[(tb - 1, 1, 7)])
                for it in range(8):
                    sy.dma_start(out=xsl[tb % 2][it][:],
                                 in_=xnt_dram[it * 128:(it + 1) * 128,
                                              b * S + tb * 512: b * S + (tb + 1) * 512]
                                 ).then_inc(sems["ld"].h, 16)

        @block.tensor
        def _(t):
            for tb in range(4):
                t.wait_ge(sems["ld"].h, B0["ld"] + (tb + 1) * 128)
                for p in range(2):
                    W = WQfm if p == 0 else WKfm
                    for ot in range(8):
                        gi = tb * 16 + p * 8 + ot
                        if gi >= 2:
                            pk = gi - 2
                            pot = pk % 8
                            pp = (pk // 8) % 2
                            ptb = pk // 16
                            t.wait_ge(sems["f4"].h, f4_at[(ptb, pp, pot)])
                        for it in range(8):
                            ins = t.matmul(ps01[gi % 2][:],
                                           fm_slice(W, 1024, it * 128, 128, ot * 128, 128),
                                           xsl[tb % 2][it][:],
                                           start=(it == 0), stop=(it == 7))
                        ins.then_inc(sems["qm"].h, 1)
                for tt4 in range(4):
                    t.wait_ge(sems["f4"].h, f4_at[(tb, 1, 7)])
                    if tb * 4 + tt4 >= 2:
                        i2 = tb * 4 + tt4 - 2
                        t.wait_ge(sems["te"].h, te_at[(i2 // 4, i2 % 4)])
                    tok0 = tb * 512 + tt4 * 128
                    for ft in range(8):
                        ins = t.transpose(ptr[tt4 % 2][:, ft * 128:(ft + 1) * 128],
                                          fm_slice(KTf, 2048, ft * 128, 128, tok0, 128),
                                          ident[:])
                    ins.then_inc(sems["tm"].h, 1)

        @block.vector
        def _(v):
            for tb in range(4):
                for p in range(2):
                    bpp = bq_pp if p == 0 else bk_pp
                    dstT = QTf if p == 0 else KTf
                    for ot in range(8):
                        gi = tb * 16 + p * 8 + ot
                        v.wait_ge(sems["qm"].h, qm_at[(tb, p, ot)])
                        if gi >= 1:
                            v.wait_ge(sems["f4"].h, B0["f4"] + gi)
                            v.wait_ge(sems["f3"].h, B0["f3"] + gi)
                        v.tensor_scalar(tmn[:], ps01[gi % 2][:], bpp[:, ot:ot + 1], 0.0,
                                        ALU.add, ALU.min).then_inc(sems["f1"].h, 1)
                        v.tensor_scalar(tmr[:], ps01[gi % 2][:], bpp[:, ot:ot + 1], 0.0,
                                        ALU.add, ALU.max).then_inc(sems["f2"].h, 1)
                        v.wait_ge(sems["f3"].h, B0["f3"] + gi + 1)
                        v.wait_ge(sems["f2"].h, B0["f2"] + gi + 1)
                        v.tensor_add(fm_slice(dstT, 2048, ot * 128, 128, tb * 512, 512),
                                     tme[:], tmr[:]).then_inc(sems["f4"].h, 1)
                for tt4 in range(4):
                    v.wait_ge(sems["tm"].h, tm_at[(tb, tt4)])
                    tok0 = tb * 512 + tt4 * 128
                    v.tensor_copy(
                        KNat[:, (tb * 4 + tt4) * 1024:(tb * 4 + tt4 + 1) * 1024],
                        ptr[tt4 % 2][:]).then_inc(sems["te"].h, 1)

        @block.scalar
        def _(s):
            for tb in range(4):
                for p in range(2):
                    for ot in range(8):
                        gi = tb * 16 + p * 8 + ot
                        s.wait_ge(sems["f1"].h, B0["f1"] + gi + 1)
                        if gi >= 1:
                            s.wait_ge(sems["f4"].h, B0["f4"] + gi)
                        s.activation(tme[:], tmn[:], AF.Exp).then_inc(sems["f3"].h, 1)

    for k in _n:
        if k == "ld":
            sems[k].v = B0[k] + 4 * 128
        elif k in ("f1", "f2", "f3"):
            sems[k].v = B0[k] + 64
        elif k == "qm":
            sems[k].v = c["qm"]
        elif k == "f4":
            sems[k].v = c["f4"]
        elif k in ("tm", "te"):
            sems[k].v = c[k]
    ctx.barrier()
    les.close()


def emit_l3_scan(ctx, QTf, KTf, KNat, VA, ONAT):
    nc = ctx.nc
    u = ctx.uid()
    les = contextlib.ExitStack()
    _n = ["c1", "a1", "a2", "n1", "s1", "d1", "d2", "d3"]
    sems = dict(zip(_n, ctx.pool(8)))
    B0 = {k: s.v for k, s in sems.items()}
    Sprev = [ctx.sb(f"scS{u}_{i}", [128, 65], BF16, es=les) for i in range(2)]
    Amb = [ctx.sb(f"scA{u}_{i}", [128, 128], BF16, es=les) for i in range(2)]
    dnr = ctx.sb(f"scD{u}", [128, 2], F32, es=les)
    psS = [ctx.ps_(f"scPS{u}_{i}", [128, 512], F32, es=les) for i in range(2)]
    psA = [ctx.ps_(f"scPA{u}_{i}", [128, 512], F32, es=les) for i in range(2)]
    psN = [ctx.ps_(f"scPN{u}_{i}", [128, 512], F32, es=les) for i in range(2)]
    cmask = ctx.consts["c_causincl"]
    NHD = 16
    NCH = 16
    EPS3 = 1e-6

    c = {k: B0[k] for k in _n}
    at = {k: {} for k in _n}
    for h in range(NHD):
        for j in range(NCH):
            for k in ("c1", "a1", "a2", "n1", "s1", "d1", "d2", "d3"):
                c[k] += 1
                at[k][(h, j)] = c[k]

    with nc.Block() as block:
        @block.vector
        def _(v):
            for h in range(NHD):
                for j in range(NCH):
                    sp = Sprev[j % 2][64 * (h % 2):64 * (h % 2) + 64, :]
                    spo = Sprev[(j + 1) % 2][64 * (h % 2):64 * (h % 2) + 64, :]
                    if j == 0:
                        if h > 0:
                            v.wait_ge(sems["n1"].h, at["n1"][(h - 1, NCH - 1)])
                        v.memset(sp, 0.0).then_inc(sems["c1"].h, 1)
                    else:
                        v.wait_ge(sems["s1"].h, at["s1"][(h, j - 1)])
                        v.wait_ge(sems["n1"].h, at["n1"][(h, j - 1)])
                        v.tensor_add(sp, psS[(j - 1) % 2][0:64, 0:65],
                                     spo).then_inc(sems["c1"].h, 1)
                    v.wait_ge(sems["a1"].h, at["a1"][(h, j)])
                    gia = h * NCH + j
                    if gia >= 2:
                        p2 = gia - 2
                        v.wait_ge(sems["n1"].h, at["n1"][(p2 // NCH, p2 % NCH)])
                    v.tensor_mul(Amb[j % 2][:], psA[j % 2][0:128, 0:128],
                                 cmask[:]).then_inc(sems["a2"].h, 1)
                    v.wait_ge(sems["n1"].h, at["n1"][(h, j)])
                    gi = h * NCH + j
                    if gi >= 1:
                        p1 = gi - 1
                        v.wait_ge(sems["d2"].h, at["d2"][(p1 // NCH, p1 % NCH)])
                    v.tensor_scalar_add(dnr[:, 0:1], psN[j % 2][0:128, 64:65],
                                        EPS3).then_inc(sems["d1"].h, 1)
                    v.wait_ge(sems["d1"].h, at["d1"][(h, j)])
                    if gi >= 1:
                        p1 = gi - 1
                        v.wait_ge(sems["d3"].h, at["d3"][(p1 // NCH, p1 % NCH)])
                    v.reciprocal(dnr[:, 1:2], dnr[:, 0:1]).then_inc(sems["d2"].h, 1)
                    v.wait_ge(sems["d2"].h, at["d2"][(h, j)])
                    v.tensor_scalar_mul(ONAT[:, j * 1024 + 64 * h: j * 1024 + 64 * h + 64],
                                        psN[j % 2][0:128, 0:64],
                                        dnr[:, 1:2]).then_inc(sems["d3"].h, 1)

        @block.tensor
        def _(t):
            for h in range(NHD):
                for j in range(NCH):
                    gi = h * NCH + j
                    if gi >= 2:
                        p2 = gi - 2
                        t.wait_ge(sems["a2"].h, at["a2"][(p2 // NCH, p2 % NCH)])
                    t.matmul(psA[j % 2][0:128, 0:128],
                             fm_slice(KTf, 2048, 64 * h, 64, j * 128, 128),
                             fm_slice(QTf, 2048, 64 * h, 64, j * 128, 128),
                             start=True, stop=True).then_inc(sems["a1"].h, 1)
                    t.wait_ge(sems["a2"].h, at["a2"][(h, j)])
                    t.wait_ge(sems["c1"].h, at["c1"][(h, j)])
                    if gi >= 2:
                        p2 = gi - 2
                        t.wait_ge(sems["d3"].h, at["d3"][(p2 // NCH, p2 % NCH)])
                    t.matmul(psN[j % 2][0:128, 0:65],
                             Amb[j % 2][:],
                             VA[:, j * 1040 + 65 * h: j * 1040 + 65 * h + 65],
                             start=True, stop=False)
                    t.matmul(psN[j % 2][0:128, 0:65],
                             fm_slice(QTf, 2048, 64 * h, 64, j * 128, 128),
                             Sprev[j % 2][64 * (h % 2):64 * (h % 2) + 64, :],
                             start=False, stop=True).then_inc(sems["n1"].h, 1)
                    if h * NCH + j >= 2:
                        p2 = h * NCH + j - 2
                        t.wait_ge(sems["c1"].h, at["c1"][(p2 // NCH, p2 % NCH)] + 1
                                  if False else at["c1"][((h * NCH + j - 1) // NCH,
                                                          (h * NCH + j - 1) % NCH)])
                    t.matmul(psS[j % 2][0:64, 0:65],
                             KNat[:, j * 1024 + 64 * h: j * 1024 + 64 * h + 64],
                             VA[:, j * 1040 + 65 * h: j * 1040 + 65 * h + 65],
                             start=True, stop=True).then_inc(sems["s1"].h, 1)

    for k in _n:
        sems[k].v = c[k]
    ctx.barrier()
    les.close()


def emit_l3_proj(ctx, ONAT, WOfm, bo_rep, b, acc_dram):
    nc = ctx.nc
    u = ctx.uid()
    les = contextlib.ExitStack()
    (tm, te, pm, pe, st) = ctx.pool(5)
    OTt = ctx.sb(f"pjO{u}", [128, 1024], BF16, es=les)
    accst = [ctx.sb(f"pjA{u}_{i}", [128, DM], F32, es=les) for i in range(2)]
    ptr = [ctx.ps_(f"pjT{u}_{i}", [128, 1024], BF16, es=les) for i in range(2)]
    pproj = [ctx.ps_(f"pjP{u}_{i}", [128, 1024], F32, es=les) for i in range(2)]
    ident = ctx.consts["c_ident"]
    b_ = {s: s.v for s in (tm, te, pm, pe, st)}
    NCH = 16
    with nc.Block() as block:
        @block.tensor
        def _(t):
            for j in range(NCH):
                if j >= 2:
                    t.wait_ge(te.h, b_[te] + j - 1)
                for ft in range(8):
                    ins = t.transpose(ptr[j % 2][:, ft * 128:(ft + 1) * 128],
                                      ONAT[:, j * 1024 + ft * 128: j * 1024 + (ft + 1) * 128],
                                      ident[:])
                ins.then_inc(tm.h, 1)
                t.wait_ge(te.h, b_[te] + j + 1)
                if j >= 2:
                    t.wait_ge(pe.h, b_[pe] + j - 1)
                for et in range(8):
                    for half in range(2):
                        ins = t.matmul(pproj[j % 2][:, half * 512:(half + 1) * 512],
                                       OTt[:, et * 128:(et + 1) * 128],
                                       fm_slice(WOfm, 1024, et * 128, 128, half * 512, 512),
                                       start=(et == 0), stop=(et == 7))
                ins.then_inc(pm.h, 1)

        @block.vector
        def _(v):
            for j in range(NCH):
                v.wait_ge(tm.h, b_[tm] + j + 1)
                if j >= 1:
                    v.wait_ge(pm.h, b_[pm] + j)
                v.tensor_copy(OTt[:], ptr[j % 2][:]).then_inc(te.h, 1)
                v.wait_ge(pm.h, b_[pm] + j + 1)
                if j >= 2:
                    v.wait_ge(st.h, b_[st] + (j - 1) * 16)
                v.tensor_add(accst[j % 2][:], pproj[j % 2][:], bo_rep[:]).then_inc(pe.h, 1)

        @block.sync
        def _(sy):
            for j in range(NCH):
                sy.wait_ge(pe.h, b_[pe] + j + 1)
                g = b * S + j * 128
                sy.dma_start(out=acc_dram[g:g + 128, :], in_=accst[j % 2][:]).then_inc(st.h, 16)
                sy.wait_ge(st.h, b_[st] + (j + 1) * 16)

    tm.v = b_[tm] + NCH
    te.v = b_[te] + NCH
    pm.v = b_[pm] + NCH
    pe.v = b_[pe] + NCH
    st.v = b_[st] + NCH * 16
    ctx.barrier()
    les.close()


def emit_l3_gate(ctx, xnt_dram, wgred_dram, wg_dram, MSK, SCO, COEF):
    nc = ctx.nc
    u = ctx.uid()
    les = contextlib.ExitStack()
    _n = ["ld", "w1", "w2", "w3", "w4", "w5", "xm", "xe", "lm", "e1", "e2",
          "e3", "e4", "e5", "lw"]
    sems = dict(zip(_n, ctx.pool(15)))
    B0 = {k: s.v for k, s in sems.items()}
    WGR = ctx.sb(f"g3r{u}", [16, 1024], F32, es=les)
    WGRT = ctx.sb(f"g3t{u}", [128, 8 * 16], BF16, es=les)
    WGU = ctx.sb(f"g3u{u}", [8, 16], F32, es=les)
    WGUt = ctx.sb(f"g3v{u}", [16, 8], BF16, es=les)
    wgs = ctx.sb(f"g3w{u}", [8, 16], F32, es=les)
    n2 = ctx.sb(f"g3n{u}", [8, 3], F32, es=les)
    xrT = ctx.sb(f"g3x{u}", [16, 512], BF16, es=les)
    lg = ctx.sb(f"g3l{u}", [128, 8], F32, es=les)
    mx = ctx.sb(f"g3m{u}", [128, 2], F32, es=les)
    ee = ctx.sb(f"g3e{u}", [128, 8], F32, es=les)
    sume = ctx.sb(f"g3s{u}", [128, 1], F32, es=les)
    xsl = [[ctx.sb(f"g3a{u}_{s}_{i}", [128, 512], BF16, es=les) for i in range(8)]
           for s in range(2)]
    pg = [ctx.ps_(f"g3p{u}_{i}", [128, 512], F32, es=les) for i in range(2)]
    identf = ctx.consts["c_identf"]
    ident = ctx.consts["c_ident"]
    NB = T // 512

    with nc.Block() as block:
        @block.sync
        def _(sy):
            sy.dma_start(out=WGR[:], in_=wgred_dram).then_inc(sems["lw"].h, 16)
            sy.dma_start(out=wgs[:], in_=wg_dram).then_inc(sems["lw"].h, 16)
            for nb in range(NB):
                if nb >= 1:
                    sy.wait_ge(sems["ld"].h, B0["ld"] + nb * 128)
                if nb >= 2:
                    sy.wait_ge(sems["xm"].h, B0["xm"] + nb - 1)
                for it in range(8):
                    sy.dma_start(out=xsl[nb % 2][it][:],
                                 in_=xnt_dram[it * 128:(it + 1) * 128,
                                              nb * 512:(nb + 1) * 512]).then_inc(sems["ld"].h, 16)

        @block.tensor
        def _(t):
            t.wait_ge(sems["lw"].h, B0["lw"] + 32)
            for cb in range(8):
                ins = t.transpose(pg[0][:, cb * 16:(cb + 1) * 16],
                                  WGR[:, cb * 128:(cb + 1) * 128],
                                  identf[0:16, 0:16])
            ins.then_inc(sems["w1"].h, 1)
            t.wait_ge(sems["w4"].h, B0["w4"] + 1)
            t.transpose(pg[1][0:16, 0:8], WGU[:], identf[0:8, 0:8]).then_inc(sems["w5"].h, 1)
            t.wait_ge(sems["w2"].h, B0["w2"] + 2)
            for nb in range(NB):
                t.wait_ge(sems["ld"].h, B0["ld"] + (nb + 1) * 128)
                if nb >= 1:
                    t.wait_ge(sems["xe"].h, B0["xe"] + nb)
                for it in range(8):
                    ins = t.matmul(pg[0][0:16, :],
                                   fm_slice(WGRT, 16, it * 128, 128, 0, 16),
                                   xsl[nb % 2][it][:],
                                   start=(it == 0), stop=(it == 7))
                ins.then_inc(sems["xm"].h, 1)
                for q in range(4):
                    i = nb * 4 + q
                    t.wait_ge(sems["xe"].h, B0["xe"] + nb + 1)
                    if i >= 1:
                        t.wait_ge(sems["e1"].h, B0["e1"] + i)
                    t.matmul(pg[1][0:128, 0:8], xrT[:, q * 128:(q + 1) * 128],
                             WGUt[:], start=True, stop=True).then_inc(sems["lm"].h, 1)

        @block.vector
        def _(v):
            v.wait_ge(sems["w1"].h, B0["w1"] + 1)
            for cb in range(8):
                ins = v.tensor_copy(WGRT[:, cb * 16:(cb + 1) * 16],
                                    pg[0][:, cb * 16:(cb + 1) * 16])
            ins.then_inc(sems["w2"].h, 1)
            v.wait_ge(sems["lw"].h, B0["lw"] + 32)
            v.tensor_mul(WGU[:], wgs[:], wgs[:]).then_inc(sems["w3"].h, 1)
            v.wait_ge(sems["w3"].h, B0["w3"] + 1)
            v.reduce_sum(n2[:, 0:1], WGU[:], axis=AXX).then_inc(sems["e5"].h, 1)
            v.wait_ge(sems["w3"].h, B0["w3"] + 2)
            v.tensor_scalar_max(n2[:, 1:2], n2[:, 1:2], 1e-4).then_inc(sems["w3"].h, 1)
            v.wait_ge(sems["w3"].h, B0["w3"] + 3)
            v.reciprocal(n2[:, 2:3], n2[:, 1:2]).then_inc(sems["w3"].h, 1)
            v.wait_ge(sems["w3"].h, B0["w3"] + 4)
            v.tensor_scalar_mul(WGU[:], wgs[:], n2[:, 2:3]).then_inc(sems["w4"].h, 1)
            v.wait_ge(sems["w5"].h, B0["w5"] + 1)
            v.tensor_copy(WGUt[:], pg[1][0:16, 0:8]).then_inc(sems["w2"].h, 1)
            for nb in range(NB):
                for q in range(4):
                    i = nb * 4 + q
                    v.wait_ge(sems["lm"].h, B0["lm"] + i + 1)
                    if i >= 1:
                        v.wait_ge(sems["e3"].h, B0["e3"] + i)
                    v.tensor_copy(lg[:], pg[1][0:128, 0:8]).then_inc(sems["e1"].h, 1)
                    v.wait_ge(sems["e1"].h, B0["e1"] + i + 1)
                    v.reduce_max(mx[:, 0:1], lg[:], axis=AXX).then_inc(sems["e5"].h, 1)
                    v.wait_ge(sems["e5"].h, B0["e5"] + 1 + i + 1)
                    v.tensor_scalar_mul(mx[:, 1:2], mx[:, 0:1], -1.0).then_inc(sems["e2"].h, 1)
                    v.wait_ge(sems["e3"].h, B0["e3"] + i + 1)
                    v.tensor_single_scalar(MSK[:, i * 8:(i + 1) * 8], ee[:], 1.0,
                                           ALU.is_ge).then_inc(sems["e4"].h, 1)
                    with nc.allow_low_precision(reason="scores recip"):
                        v.reciprocal(SCO[:, i:i + 1], sume[:]).then_inc(sems["e4"].h, 1)
                    v.wait_ge(sems["e4"].h, B0["e4"] + 3 * i + 2)
                    v.tensor_scalar_mul(COEF[:, i * 8:(i + 1) * 8],
                                        MSK[:, i * 8:(i + 1) * 8],
                                        SCO[:, i:i + 1]).then_inc(sems["e4"].h, 1)

        @block.scalar
        def _(s):
            s.wait_ge(sems["e5"].h, B0["e5"] + 1)
            s.activation(n2[:, 1:2], n2[:, 0:1], AF.Sqrt).then_inc(sems["w3"].h, 1)
            for nb in range(NB):
                s.wait_ge(sems["xm"].h, B0["xm"] + nb + 1)
                if nb >= 1:
                    s.wait_ge(sems["lm"].h, B0["lm"] + nb * 4)
                s.activation(xrT[:], pg[0][0:16, :], AF.Identity).then_inc(sems["xe"].h, 1)
                for q in range(4):
                    i = nb * 4 + q
                    s.wait_ge(sems["e2"].h, B0["e2"] + i + 1)
                    if i >= 1:
                        s.wait_ge(sems["e4"].h, B0["e4"] + 3 * i)
                    s.activation(ee[:], lg[:], AF.Exp, bias=mx[:, 1:2],
                                 accum_out=sume[:]).then_inc(sems["e3"].h, 1)

    sems["ld"].v = B0["ld"] + NB * 128
    sems["lw"].v = B0["lw"] + 32
    sems["w1"].v = B0["w1"] + 1
    sems["w2"].v = B0["w2"] + 2
    sems["w3"].v = B0["w3"] + 4
    sems["w4"].v = B0["w4"] + 1
    sems["w5"].v = B0["w5"] + 1
    sems["xm"].v = B0["xm"] + NB
    sems["xe"].v = B0["xe"] + NB
    sems["lm"].v = B0["lm"] + NB * 4
    sems["e1"].v = B0["e1"] + NT
    sems["e2"].v = B0["e2"] + NT
    sems["e3"].v = B0["e3"] + NT
    sems["e4"].v = B0["e4"] + NT * 3
    sems["e5"].v = B0["e5"] + 1 + NT
    ctx.barrier()
    les.close()


L3_WNAMES = ["L3_attn_wq", "L3_attn_bq", "L3_attn_wk", "L3_attn_bk",
             "L3_attn_wv", "L3_attn_bv", "L3_attn_wo", "L3_attn_bo",
             "L3_ffn_wg_red", "L3_ffn_wg", "L3_ffn_w1", "L3_ffn_b1",
             "L3_ffn_w2", "L3_ffn_b2"]


def emit_l3_layer(ctx, x_dram, xout_dram, W, upto="all", dbg=None):
    xnt = ctx.dram("l3_xnt", [DM, T], BF16)
    acc = dbg["acc"] if dbg and "acc" in dbg else ctx.dram("l3_acc", [T, DM], F32)
    macc = ctx.dram("l3_macc", [T, DM], F32)

    emit_norm(ctx, x_dram, xnt)
    if upto == "norm":
        return
    lay = contextlib.ExitStack()
    QTf = ctx.sb("l3_qtf", [128, 8 * 2048], BF16, es=lay)
    KTf = ctx.sb("l3_ktf", [128, 8 * 2048], BF16, es=lay)
    KNat = ctx.sb("l3_kn", [128, 16 * 1024], BF16, es=lay)
    for b in range(BT):
        les = contextlib.ExitStack()
        WQfm = ctx.sb(f"l3_wq{b}", [128, 8 * 1024], BF16, es=les)
        WKfm = ctx.sb(f"l3_wk{b}", [128, 8 * 1024], BF16, es=les)
        bqp = ctx.sb(f"l3_bq{b}", [128, 8], F32, es=les)
        bkp = ctx.sb(f"l3_bk{b}", [128, 8], F32, es=les)
        emit_weight_fm(ctx, WQfm, W["L3_attn_wq"], 1024, 1024, transpose=False)
        emit_weight_fm(ctx, WKfm, W["L3_attn_wk"], 1024, 1024, transpose=False)
        emit_load_pp(ctx, bqp, W["L3_attn_bq"], 8)
        emit_load_pp(ctx, bkp, W["L3_attn_bk"], 8)
        emit_l3_qk(ctx, xnt, b, WQfm, WKfm, bqp, bkp, QTf, KTf, KNat)
        les.close()
        les = contextlib.ExitStack()
        VA = ctx.sb(f"l3_va{b}", [128, 16 * 1040], BF16, es=les)
        lesv = contextlib.ExitStack()
        WVfm = ctx.sb(f"l3_wv{b}", [128, 8 * 1024], BF16, es=lesv)
        bvr = ctx.sb(f"l3_bv{b}", [128, DM], F32, es=lesv)
        emit_weight_fm(ctx, WVfm, W["L3_attn_wv"], 1024, 1024, transpose=False)
        emit_replicate_rows(ctx, bvr, W["L3_attn_bv"], 1024)
        emit_l0_qkv(ctx, xnt, b, WVfm, None, VA, WC=1024, parts=2,
                    wv_spec=(WVfm, 1024, 0), bv_rep=bvr)
        lesv.close()
        ONAT = ctx.sb(f"l3_on{b}", [128, 16 * 1024], BF16, es=les)
        emit_l3_scan(ctx, QTf, KTf, KNat, VA, ONAT)
        lesp = contextlib.ExitStack()
        WOfm = ctx.sb(f"l3_wo{b}", [128, 8 * 1024], BF16, es=lesp)
        bor = ctx.sb(f"l3_bo{b}", [128, DM], F32, es=lesp)
        emit_weight_fm(ctx, WOfm, W["L3_attn_wo"], 1024, 1024, transpose=False)
        emit_replicate_rows(ctx, bor, W["L3_attn_bo"], 1024)
        emit_l3_proj(ctx, ONAT, WOfm, bor, b, acc)
        lesp.close()
        les.close()
    lay.close()
    if upto == "attn":
        return
    lay = contextlib.ExitStack()
    MSK = ctx.sb("l3_msk", [128, NT * 8], F32, es=lay)
    SCO = ctx.sb("l3_sco", [128, NT], F32, es=lay)
    COEF = ctx.sb("l3_coef", [128, NT * 8], F32, es=lay)
    emit_l3_gate(ctx, xnt, W["L3_ffn_wg_red"], W["L3_ffn_wg"], MSK, SCO, COEF)
    if upto == "gate":
        lay.close()
        return
    for e in range(8):
        lese = contextlib.ExitStack()
        W1fm = ctx.sb(f"l3_e{e}w1", [128, 8 * 4096], BF16, es=lese)
        W2fm = ctx.sb(f"l3_e{e}w2", [128, 32 * 1024], BF16, es=lese)
        b1pp = ctx.sb(f"l3_e{e}b1", [128, 32], F32, es=lese)
        B2e = ctx.sb(f"l3_e{e}b2", [128, DM], F32, es=lese)
        emit_weight_fm(ctx, W1fm, W["L3_ffn_w1"][e], 1024, 4096, transpose=False)
        emit_weight_fm(ctx, W2fm, W["L3_ffn_w2"][e], 4096, 1024, transpose=False)
        emit_load_pp(ctx, b1pp, W["L3_ffn_b1"][e], 32)
        emit_replicate_rows(ctx, B2e, W["L3_ffn_b2"][e], 1024)
        emit_moe_expert(ctx, xnt, W1fm, W2fm, b1pp, B2e, COEF, e, 8, macc,
                        first=(e == 0), last=(e == 7),
                        final_addends=[x_dram, acc],
                        xout_dram=xout_dram, act=AF.Gelu)
        lese.close()
    lay.close()


def build_l3(upto="all", debug=False):
    ctx = Ctx()
    xo_in = ctx.dram("xo_in", [T, DM], F32, kind="ExternalInput")
    o_in = ctx.dram("o_in", [T, DM], F32, kind="ExternalInput")
    x_out = ctx.dram("x_out", [T, DM], F32, kind="ExternalOutput")
    shapes = {"L3_attn_wq": (1024, 1024), "L3_attn_bq": (1024,),
              "L3_attn_wk": (1024, 1024), "L3_attn_bk": (1024,),
              "L3_attn_wv": (1024, 1024), "L3_attn_bv": (1024,),
              "L3_attn_wo": (1024, 1024), "L3_attn_bo": (1024,),
              "L3_ffn_wg_red": (16, 1024), "L3_ffn_wg": (8, 16),
              "L3_ffn_w1": (8, 1024, 4096), "L3_ffn_b1": (8, 4096),
              "L3_ffn_w2": (8, 4096, 1024), "L3_ffn_b2": (8, 1024)}
    W = {nm: ctx.dram(nm, list(shapes[nm]), F32, kind="ExternalInput")
         for nm in L3_WNAMES}
    ctx.load_consts()
    x3 = ctx.dram("l3_x3", [T, DM], F32)
    emit_add(ctx, xo_in, o_in, x3)
    dbg = None
    if debug:
        dbg = {"acc": ctx.dram("dbg_acc", [T, DM], F32, kind="ExternalOutput")}
    emit_l3_layer(ctx, x3, x_out, W, upto=upto, dbg=dbg)
    ctx.close()
    return ctx



L0_WNAMES = ["L0_attn_w_qkv", "L0_attn_w_proj", "L0_ffn_w1", "L0_ffn_b1",
             "L0_ffn_w2", "L0_ffn_b2"]
L2A_WNAMES = ["L2_ffn_W_r", "L2_ffn_W_k", "L2_ffn_W_v", "L2_ffn_mu_r", "L2_ffn_mu_k"]
L2B_WNAMES = ["L2_attn_Wq", "L2_attn_bq", "L2_attn_Wk", "L2_attn_bk",
              "L2_attn_Wv", "L2_attn_bv"]

_BUILD_CACHE = {}


def _get(name, fn):
    if name not in _BUILD_CACHE:
        _BUILD_CACHE[name] = fn()
    return _BUILD_CACHE[name]


def kernel(**inputs):
    consts = host_constants()
    x = np.ascontiguousarray(inputs["L0_attn_x"].reshape(16, 2048, 1024))
    ctx = _get("fused", build_fused)
    in_maps = []
    for c in range(NCORE):
        m = dict(consts)
        for nm in ALL_WNAMES:
            m[nm] = inputs[nm]
        m["x_in"] = x[2 * c:2 * c + 2].reshape(T, DM)
        in_maps.append(m)
    res = run_bass_kernel_spmd(ctx.nc, in_maps, core_ids=list(range(NCORE)))
    x4 = [np.asarray(res.results[c]["x_out"]) for c in range(NCORE)]
    return np.stack(x4).reshape(16, 2048, 1024).astype(np.float32, copy=False)


def kernel_multilaunch(**inputs):
    consts = host_constants()
    x = np.ascontiguousarray(inputs["L0_attn_x"].reshape(16, 2048, 1024))

    def launch(ctx, per_core_extra, wnames):
        in_maps = []
        for c in range(NCORE):
            m = dict(consts)
            for nm in wnames:
                m[nm] = inputs[nm]
            m.update(per_core_extra[c])
            in_maps.append(m)
        return run_bass_kernel_spmd(ctx.nc, in_maps, core_ids=list(range(NCORE)))

    ctx = _get("l0", build_l0)
    res = launch(ctx, [{"x_in": x[2 * c:2 * c + 2].reshape(T, DM)} for c in range(NCORE)],
                 L0_WNAMES)
    x1 = [np.asarray(res.results[c]["x_out"]) for c in range(NCORE)]
    ctx = _get("l1", lambda: build_l1())
    res = launch(ctx, [{"x_in": x1[c]} for c in range(NCORE)], L1_WNAMES)
    x2 = [np.asarray(res.results[c]["x_out"]) for c in range(NCORE)]
    ctx = _get("l2a", build_l2a)
    res = launch(ctx, [{"x_in": x2[c]} for c in range(NCORE)], L2A_WNAMES)
    xo = [np.asarray(res.results[c]["xo_out"]) for c in range(NCORE)]
    xnt = [np.asarray(res.results[c]["xnt_out"]) for c in range(NCORE)]
    xnt_all = np.stack(xnt)
    xnt_all = xnt_all.reshape(8, DM, 2, S)
    xpm_full = xnt_all.transpose(1, 3, 0, 2).reshape(DM, S, 16)
    xpm = [np.ascontiguousarray(
        xpm_full[:, c * PSL:(c + 1) * PSL, :].reshape(DM, T))
        for c in range(NCORE)]
    ctx = _get("l2b", build_l2b)
    res = launch(ctx, [{"xpm_in": xpm[c]} for c in range(NCORE)], L2B_WNAMES)
    o_pm = [np.asarray(res.results[c]["o_out"]) for c in range(NCORE)]
    o_all = np.stack(o_pm).reshape(8, PSL, 16, DM)
    o_all = o_all.transpose(2, 0, 1, 3).reshape(16, S, DM)
    ctx = _get("l3", lambda: build_l3())
    res = launch(ctx, [{"xo_in": xo[c],
                        "o_in": o_all[2 * c:2 * c + 2].reshape(T, DM)}
                       for c in range(NCORE)], L3_WNAMES)
    x4 = [np.asarray(res.results[c]["x_out"]) for c in range(NCORE)]
    out = np.stack(x4).reshape(16, 2048, 1024).astype(np.float32, copy=False)
    return out



from concourse.aot_env import allow_ici_all_to_all


def emit_a2a(ctx, in_ap, out_ap, semname):
    s = ctx.sem(semname)
    with ctx.nc.Block() as block:
        @block.gpsimd
        def _(gp):
            with allow_ici_all_to_all():
                gp.collective_compute("AllToAll", ALU.bypass,
                                      replica_groups=[list(range(NCORE))],
                                      ins=[in_ap.opt()],
                                      outs=[out_ap.opt()]).then_inc(s.h)
    s.v += 1


def emit_l2_gather_v2(ctx, a2a_out, XPM, ccsem):
    nc = ctx.nc
    u = ctx.uid()
    les = contextlib.ExitStack()
    ld, cp = ctx.pool(2)
    stg = [ctx.sb(f"g2{u}_{i}", [128, PSL], BF16, es=les) for i in range(4)]
    b_ld, b_cp = ld.v, cp.v
    with nc.Block() as block:
        @block.sync
        def _(sy):
            sy.wait_ge(ccsem.h, ccsem.v)
            n = 0
            for ft in range(8):
                for b in range(16):
                    if n >= 4:
                        sy.wait_ge(cp.h, b_cp + n - 3)
                    sy.dma_start(out=stg[n % 4][:],
                                 in_=a2a_out[(b // 2) * 1024 + ft * 128:
                                             (b // 2) * 1024 + (ft + 1) * 128,
                                             (b % 2) * 256:(b % 2) * 256 + PSL]
                                 ).then_inc(ld.h, 16)
                    sy.wait_ge(ld.h, b_ld + (n + 1) * 16)
                    n += 1

        @block.vector
        def _(v):
            n = 0
            for ft in range(8):
                for b in range(16):
                    v.wait_ge(ld.h, b_ld + (n + 1) * 16)
                    base = XPM[:, ft * 4096 + b: ft * 4096 + b + 1]
                    dst = bass.AP(base.tensor, base.offset,
                                  [list(base.ap[0]), [16, PSL]])
                    v.tensor_copy(dst, stg[n % 4][:]).then_inc(cp.h, 1)
                    n += 1

    ld.v = b_ld + 128 * 16
    cp.v = b_cp + 128
    ctx.barrier()
    les.close()


def emit_o_unpack(ctx, o_a2a_out, o_b_dram, ccsem):
    nc = ctx.nc
    u = ctx.uid()
    les = contextlib.ExitStack()
    ld, st = ctx.pool(2)
    stg = [ctx.sb(f"ou{u}_{i}", [128, DM], F32, es=les) for i in range(2)]
    b_ld, b_st = ld.v, st.v
    with nc.Block() as block:
        @block.sync
        def _(sy):
            sy.wait_ge(ccsem.h, ccsem.v)
            n = 0
            for lb in range(2):
                for k in range(16):
                    off = (512 * (k // 2) + 128 * (k % 2) * 2 + lb) * 1024
                    src = bass.AP(o_a2a_out.tensor, off, [[2 * 1024, 128], [1, 1024]])
                    sy.dma_start(out=stg[n % 2][:], in_=src).then_inc(ld.h, 16)
                    sy.wait_ge(ld.h, b_ld + (n + 1) * 16)
                    sy.dma_start(out=o_b_dram[(lb * 16 + k) * 128:(lb * 16 + k + 1) * 128, :],
                                 in_=stg[n % 2][:]).then_inc(st.h, 16)
                    sy.wait_ge(st.h, b_st + (n + 1) * 16)
                    n += 1
    ld.v = b_ld + 32 * 16
    st.v = b_st + 32 * 16
    ctx.barrier()
    les.close()


def emit_o_pack(ctx, o_pm, o_a2a_in):
    nc = ctx.nc
    u = ctx.uid()
    les = contextlib.ExitStack()
    ld, st = ctx.pool(2)
    stg = [ctx.sb(f"op{u}_{i}", [128, DM], F32, es=les) for i in range(2)]
    b_ld, b_st = ld.v, st.v
    with nc.Block() as block:
        @block.sync
        def _(sy):
            n = 0
            for c in range(8):
                for lb in range(2):
                    for half in range(2):
                        src = bass.AP(o_pm.tensor,
                                      (half * 128 * 16 + 2 * c + lb) * 1024,
                                      [[16 * 1024, 128], [1, 1024]])
                        sy.dma_start(out=stg[n % 2][:], in_=src).then_inc(ld.h, 16)
                        sy.wait_ge(ld.h, b_ld + (n + 1) * 16)
                        dst = bass.AP(o_a2a_in.tensor,
                                      (512 * c + half * 256 + lb) * 1024,
                                      [[2 * 1024, 128], [1, 1024]])
                        sy.dma_start(out=dst, in_=stg[n % 2][:]).then_inc(st.h, 16)
                        sy.wait_ge(st.h, b_st + (n + 1) * 16)
                        n += 1
    ld.v = b_ld + 32 * 16
    st.v = b_st + 32 * 16
    ctx.barrier()
    les.close()


ALL_WNAMES = (L0_WNAMES + L1_WNAMES + L2A_WNAMES + L2B_WNAMES + L3_WNAMES)


def build_fused():
    ctx = Ctx()
    x_in = ctx.dram("x_in", [T, DM], F32, kind="ExternalInput")
    x_out = ctx.dram("x_out", [T, DM], F32, kind="ExternalOutput")
    shapes = {"L0_attn_w_qkv": (3072, 1024), "L0_attn_w_proj": (1024, 1024),
              "L0_ffn_w1": (4096, 1024), "L0_ffn_b1": (4096,),
              "L0_ffn_w2": (1024, 4096), "L0_ffn_b2": (1024,),
              "L1_attn_Wq": (1024, 1024), "L1_attn_bq": (1024,),
              "L1_attn_Wk": (1024, 1024), "L1_attn_bk": (1024,),
              "L1_attn_Wv": (1024, 1024), "L1_attn_bv": (1024,),
              "L1_attn_Wo": (1024, 1024), "L1_attn_bo": (1024,),
              "L1_ffn_ln1_g": (1024,), "L1_ffn_ln1_b": (1024,),
              "L1_ffn_ln2_g": (1024,), "L1_ffn_ln2_b": (1024,),
              "L1_ffn_wq": (1024, 1024), "L1_ffn_bq": (1024,),
              "L1_ffn_wk": (1024, 1024), "L1_ffn_bk": (1024,),
              "L1_ffn_wv": (1024, 1024), "L1_ffn_bv": (1024,),
              "L1_ffn_wo": (1024, 1024), "L1_ffn_bo": (1024,),
              "L1_ffn_gate_w": (1024, 8), "L1_ffn_gate_b": (8,),
              "L1_ffn_w1": (8, 1024, 4096), "L1_ffn_b1": (8, 4096),
              "L1_ffn_w2": (8, 4096, 1024), "L1_ffn_b2": (8, 1024),
              "L2_attn_Wq": (1024, 1024), "L2_attn_bq": (1024,),
              "L2_attn_Wk": (1024, 1024), "L2_attn_bk": (1024,),
              "L2_attn_Wv": (1024, 1024), "L2_attn_bv": (1024,),
              "L2_ffn_W_r": (1024, 1024), "L2_ffn_W_k": (1024, 1024),
              "L2_ffn_W_v": (1024, 1024), "L2_ffn_mu_r": (1024,),
              "L2_ffn_mu_k": (1024,),
              "L3_attn_wq": (1024, 1024), "L3_attn_bq": (1024,),
              "L3_attn_wk": (1024, 1024), "L3_attn_bk": (1024,),
              "L3_attn_wv": (1024, 1024), "L3_attn_bv": (1024,),
              "L3_attn_wo": (1024, 1024), "L3_attn_bo": (1024,),
              "L3_ffn_wg_red": (16, 1024), "L3_ffn_wg": (8, 16),
              "L3_ffn_w1": (8, 1024, 4096), "L3_ffn_b1": (8, 4096),
              "L3_ffn_w2": (8, 4096, 1024), "L3_ffn_b2": (8, 1024)}
    W = {nm: ctx.dram(nm, list(shapes[nm]), F32, kind="ExternalInput")
         for nm in ALL_WNAMES}
    ctx.load_consts()
    xA = ctx.dram("fz_x1", [T, DM], F32)
    xB = ctx.dram("fz_x2", [T, DM], F32)
    emit_l0_layer(ctx, x_in, xA, W)
    emit_l1_layer(ctx, xA, xB, W)
    xnt2 = ctx.dram("fz_xnt2", [DM, T], BF16)
    a2a_xin = ctx.dram("fz_a2a_xi", [8 * DM, 512], BF16)
    a2a_xout = ctx.dram("fz_a2a_xo", [8 * DM, 512], BF16)
    o_a2a_in = ctx.dram("fz_a2a_oi", [8 * 512, DM], F32)
    o_a2a_out = ctx.dram("fz_a2a_oo", [8 * 512, DM], F32)
    xo = ctx.dram("fz_xo", [T, DM], F32)
    o_b = ctx.dram("fz_ob", [T, DM], F32)
    x3 = ctx.dram("fz_x3", [T, DM], F32)
    emit_norm(ctx, xB, xnt2, a2a_dram=a2a_xin)
    emit_a2a(ctx, a2a_xin, a2a_xout, "cc1")
    les = contextlib.ExitStack()
    WRT = ctx.sb("f2_wrt", [128, 8 * 1024], BF16, es=les)
    WKTf = ctx.sb("f2_wkt", [128, 8 * 1024], BF16, es=les)
    WVTf = ctx.sb("f2_wvt", [128, 8 * 1024], BF16, es=les)
    murp = ctx.sb("f2_mur", [128, 8], F32, es=les)
    mukp = ctx.sb("f2_muk", [128, 8], F32, es=les)
    emit_weight_fm(ctx, WRT, W["L2_ffn_W_r"], 1024, 1024, transpose=True)
    emit_weight_fm(ctx, WKTf, W["L2_ffn_W_k"], 1024, 1024, transpose=True)
    emit_weight_fm(ctx, WVTf, W["L2_ffn_W_v"], 1024, 1024, transpose=True)
    emit_load_pp(ctx, murp, W["L2_ffn_mu_r"], 8)
    emit_load_pp(ctx, mukp, W["L2_ffn_mu_k"], 8)
    emit_l2_ffn(ctx, xnt2, xB, xo, WRT, WKTf, WVTf, murp, mukp)
    les.close()
    les = contextlib.ExitStack()
    XPM = ctx.sb("f2_xpm", [128, 8 * 4096], BF16, es=les)
    WQT2 = ctx.sb("f2_wq", [128, 8 * 1024], BF16, es=les)
    WKT2 = ctx.sb("f2_wk", [128, 8 * 1024], BF16, es=les)
    WVT2 = ctx.sb("f2_wv", [128, 8 * 1024], BF16, es=les)
    bqp2 = ctx.sb("f2_bq", [128, 8], F32, es=les)
    bkp2 = ctx.sb("f2_bk", [128, 8], F32, es=les)
    bvr2 = ctx.sb("f2_bv", [128, DM], F32, es=les)
    emit_l2_gather_v2(ctx, a2a_xout, XPM, ctx.sem("cc1"))
    emit_weight_fm(ctx, WQT2, W["L2_attn_Wq"], 1024, 1024, transpose=True)
    emit_weight_fm(ctx, WKT2, W["L2_attn_Wk"], 1024, 1024, transpose=True)
    emit_weight_fm(ctx, WVT2, W["L2_attn_Wv"], 1024, 1024, transpose=True)
    emit_load_pp(ctx, bqp2, W["L2_attn_bq"], 8)
    emit_load_pp(ctx, bkp2, W["L2_attn_bk"], 8)
    emit_replicate_rows(ctx, bvr2, W["L2_attn_bv"], 1024)
    o_pm = ctx.dram("fz_opm", [T, DM], F32)
    emit_l2_attn(ctx, XPM, WQT2, WKT2, WVT2, bqp2, bkp2, bvr2, o_pm)
    les.close()
    emit_o_pack(ctx, o_pm, o_a2a_in)
    emit_a2a(ctx, o_a2a_in, o_a2a_out, "cc2")
    emit_o_unpack(ctx, o_a2a_out, o_b, ctx.sem("cc2"))
    emit_add(ctx, xo, o_b, x3)
    emit_l3_layer(ctx, x3, x_out, W)
    ctx.close()
    return ctx
